# version 03
# baseline (speedup 1.0000x reference)
import numpy as np
import ml_dtypes
from contextlib import ExitStack

import concourse.bass as bass
import concourse.mybir as mybir
from concourse.bass_utils import run_bass_kernel_spmd

F32 = mybir.dt.float32
BF16 = mybir.dt.bfloat16
AF = mybir.ActivationFunctionType
ALU = mybir.AluOpType
AX = mybir.AxisListType

DT_SIZE = {F32: 4, BF16: 2}


class TV:
    __slots__ = ("t", "ap", "p0", "p1", "e0", "e1")

    def __init__(self, t, ap, p0, p1, e0, e1):
        self.t = t
        self.ap = ap
        self.p0, self.p1, self.e0, self.e1 = p0, p1, e0, e1


def V(th, ps=None, es=None):
    shape = list(th.shape)
    p0, p1 = (0, shape[0]) if ps is None else ps
    e0, e1 = (0, shape[1]) if es is None else es
    return TV(th, th[p0:p1, e0:e1], p0, p1, e0, e1)


def V3(th, ps, es, ap):
    return TV(th, ap, ps[0], ps[1], es[0], es[1])


def DV(th_ap, tname, region=None):
    if region is None:
        region = (0, 1, 0, 1)
    return TV(_Named(tname), th_ap, *region)


class _Named:
    __slots__ = ("name",)

    def __init__(self, name):
        self.name = name



ENGINES = ("t", "v", "s", "p", "q")


class KB:
    def __init__(self, nc):
        self.nc = nc
        self.stack = ExitStack()
        self.prog = {e: [] for e in ENGINES}
        self.semh = {}
        for name in ("ct", "cv", "cs", "cp"):
            self.semh[name] = self.stack.enter_context(nc.semaphore("sem_" + name))
        self.csem = {"t": "ct", "v": "cv", "s": "cs", "p": "cp"}
        self.dring = {}
        for q, r in (("q", 8), ("p", 8), ("s", 4), ("v", 2), ("t", 2)):
            names = []
            for i in range(r):
                n = f"d{q}{i}"
                self.semh[n] = self.stack.enter_context(nc.semaphore("sem_" + n))
                names.append(n)
            self.dring[q] = names
        self.dma_i = {q: 0 for q in self.dring}
        self.cnt = {k: 0 for k in self.semh}
        self.clock = {e: {} for e in ENGINES}
        self.snap = {k: {} for k in self.semh}
        self.writes = {}
        self.reads = {}
        self.psum_names = set()
        self.eng_obj = {"t": nc.tensor, "v": nc.vector, "s": nc.scalar,
                        "p": nc.gpsimd, "q": nc.sync}
        self._tensors = []
        self.fconst_t = self.sbuf("fconst_tab", [1, 32], F32)
        self.fconst_vals = {}

    def fconst(self, val):
        val = float(val)
        if val in (0.0, 1.0):
            return val, None
        if val not in self.fconst_vals:
            idx = len(self.fconst_vals)
            assert idx < 32
            self.fconst_vals[val] = idx
            view = V(self.fconst_t, (0, 1), (idx, idx + 1))
            self.memset("p", view, val)
            self.nc.const_aps.register(F32, val, self.fconst_t[0:1, idx:idx + 1])
        idx = self.fconst_vals[val]
        return val, V(self.fconst_t, (0, 1), (idx, idx + 1))

    def sbuf(self, name, shape, dtype):
        th = self.stack.enter_context(self.nc.sbuf_tensor(name, list(shape), dtype))
        self._tensors.append(th)
        return th

    def psum(self, name, shape, dtype):
        th = self.stack.enter_context(self.nc.psum_tensor(name, list(shape), dtype))
        self._tensors.append(th)
        self.psum_names.add(name)
        return th

    @staticmethod
    def _overlap(a, b):
        return not (a[1] <= b[0] or b[1] <= a[0] or a[3] <= b[2] or b[3] <= a[2])

    def _dep_events(self, eng, rviews, wviews):
        evs = []
        own = self.csem.get(eng)
        for v_ in rviews:
            for ent in self.writes.get(v_.t.name, ()):
                if self._overlap((v_.p0, v_.p1, v_.e0, v_.e1), ent):
                    evs.extend(ent[4])
            if v_.t.name in self.psum_names:
                for ent in self.reads.get(v_.t.name, ()):
                    if self._overlap((v_.p0, v_.p1, v_.e0, v_.e1), ent):
                        evs.extend(e for e in ent[4] if e[0] != own)
        for v_ in wviews:
            box = (v_.p0, v_.p1, v_.e0, v_.e1)
            for ent in self.writes.get(v_.t.name, ()):
                if self._overlap(box, ent):
                    evs.extend(ent[4])
            for ent in self.reads.get(v_.t.name, ()):
                if self._overlap(box, ent):
                    evs.extend(ent[4])
        if eng == "t":
            evs = [e for e in evs if e[0] != "ct"]
        return evs

    def _emit_waits(self, eng, evs):
        clk = self.clock[eng]
        need = {}
        for s_, v_ in evs:
            if clk.get(s_, 0) < v_ and need.get(s_, 0) < v_:
                need[s_] = v_
        for s_, v_ in sorted(need.items()):
            semh = self.semh[s_]
            self.prog[eng].append(("w", semh, v_))
            snap = self.snap[s_].get(v_)
            if snap:
                for ks, kv in snap.items():
                    if clk.get(ks, 0) < kv:
                        clk[ks] = kv
            if clk.get(s_, 0) < v_:
                clk[s_] = v_

    def _record(self, views, table, ev):
        for v_ in views:
            lst = table.setdefault(v_.t.name, [])
            box = (v_.p0, v_.p1, v_.e0, v_.e1)
            if table is self.writes:
                for tab in (self.writes, self.reads):
                    l2 = tab.get(v_.t.name)
                    if l2:
                        tab[v_.t.name] = [e for e in l2 if not (
                            box[0] <= e[0] and e[1] <= box[1] and box[2] <= e[2] and e[3] <= box[3])]
                lst = table.setdefault(v_.t.name, [])
            lst.append(box + ((ev,),))
            if len(lst) > 120:
                old, rest = lst[:60], lst[60:]
                p0 = min(e[0] for e in old); p1 = max(e[1] for e in old)
                e0 = min(e[2] for e in old); e1 = max(e[3] for e in old)
                evs = tuple(sorted({x for e in old for x in e[4]}, key=lambda t: (t[0], -t[1])))
                seen = {}
                for s_, vv in evs:
                    seen[s_] = max(seen.get(s_, 0), vv)
                table[v_.t.name] = [(p0, p1, e0, e1, tuple(seen.items()))] + rest

    def op(self, eng, fn, reads=(), writes=(), dma=False):
        evs = self._dep_events(eng, reads, writes)
        if dma:
            ring = self.dring[eng]
            sname = ring[self.dma_i[eng] % len(ring)]
            self.dma_i[eng] += 1
            if self.cnt[sname] > 0:
                evs = list(evs) + [(sname, self.cnt[sname])]
        else:
            sname = self.csem[eng]
        self._emit_waits(eng, evs)
        inc = 16 if dma else 1
        self.cnt[sname] += inc
        ev = (sname, self.cnt[sname])
        semh = self.semh[sname]
        self.prog[eng].append(("c", fn, semh, inc))
        clk = dict(self.clock[eng])
        clk[sname] = self.cnt[sname]
        self.snap[sname][self.cnt[sname]] = clk
        self._record(reads, self.reads, ev)
        self._record(writes, self.writes, ev)
        return ev

    def raw_wait(self, eng, sname, val):
        self.prog[eng].append(("w", self.semh[sname], val))
        clk = self.clock[eng]
        if clk.get(sname, 0) < val:
            clk[sname] = val

    def barrier_all_on(self, eng):
        evs = [(s, c) for s, c in self.cnt.items() if c > 0]
        self._emit_waits(eng, evs)

    def dma(self, q, dst, src):
        return self.op(q, lambda e: e.dma_start(out=dst.ap, in_=src.ap),
                       reads=(src,), writes=(dst,), dma=True)

    def dma_slow(self, q, dst, src):
        return self.op(q, lambda e: e.dma_start(out=dst.ap, in_=src.ap,
                                                allow_slow_non_contiguous=True),
                       reads=(src,), writes=(dst,), dma=True)

    def copy(self, eng, dst, src):
        if eng == "s":
            return self.op(eng, lambda e: e.copy(out=dst.ap, in_=src.ap),
                           reads=(src,), writes=(dst,))
        return self.op(eng, lambda e: e.tensor_copy(out=dst.ap, in_=src.ap),
                       reads=(src,), writes=(dst,))

    def memset(self, eng, dst, c):
        return self.op(eng, lambda e: e.memset(dst.ap, c), writes=(dst,))

    def tt(self, eng, out, in0, in1, alu):
        return self.op(eng, lambda e: e.tensor_tensor(out=out.ap, in0=in0.ap, in1=in1.ap, op=alu),
                       reads=(in0, in1), writes=(out,))

    def ts(self, eng, out, in0, s1, s2=None, op0=ALU.mult, op1=None, s1v=None, s2v=None):
        reads = [in0]
        a1 = s1
        if isinstance(s1, TV):
            reads.append(s1); a1 = s1.ap
        a2 = s2
        if isinstance(s2, TV):
            reads.append(s2); a2 = s2.ap
        if op1 is None:
            fn = lambda e: e.tensor_scalar(out=out.ap, in0=in0.ap, scalar1=a1,
                                           scalar2=None, op0=op0)
        else:
            fn = lambda e: e.tensor_scalar(out=out.ap, in0=in0.ap, scalar1=a1,
                                           scalar2=a2, op0=op0, op1=op1)
        return self.op(eng, fn, reads=tuple(reads), writes=(out,))

    def stt(self, eng, out, in0, scalar, in1, op0, op1):
        reads = [in0, in1]
        a = scalar
        if isinstance(scalar, TV):
            reads.append(scalar); a = scalar.ap
        return self.op(eng, lambda e: e.scalar_tensor_tensor(
            out=out.ap, in0=in0.ap, scalar=a, in1=in1.ap, op0=op0, op1=op1),
            reads=tuple(reads), writes=(out,))

    def scan(self, out, d0, d1, initial, op0=ALU.mult, op1=ALU.add):
        reads = [d0, d1]
        ini = initial
        if isinstance(initial, TV):
            reads.append(initial); ini = initial.ap
        return self.op("v", lambda e: e.tensor_tensor_scan(
            out=out.ap, data0=d0.ap, data1=d1.ap, initial=ini, op0=op0, op1=op1),
            reads=tuple(reads), writes=(out,))

    def act(self, out, in_, func, bias=0.0, scale=1.0, accum=None):
        reads = [in_]
        b = bias
        if isinstance(bias, TV):
            reads.append(bias); b = bias.ap
        else:
            b, cv = self.fconst(bias)
            if cv is not None:
                reads.append(cv)
        sc = scale
        if isinstance(scale, TV):
            reads.append(scale); sc = scale.ap
        else:
            sc, cv = self.fconst(scale)
            if cv is not None:
                reads.append(cv)
        writes = [out]
        if accum is not None:
            writes.append(accum)
            fn = lambda e: e.activation(out=out.ap, in_=in_.ap, func=func, bias=b,
                                        scale=sc, accum_out=accum.ap)
        else:
            fn = lambda e: e.activation(out=out.ap, in_=in_.ap, func=func, bias=b, scale=sc)
        return self.op("s", fn, reads=tuple(reads), writes=tuple(writes))

    def reduce(self, eng, out, in_, alu, negate=False):
        return self.op(eng, lambda e: e.tensor_reduce(
            out=out.ap, in_=in_.ap, axis=AX.X, op=alu, negate=negate),
            reads=(in_,), writes=(out,))

    def recip(self, out, in_):
        return self.op("v", lambda e: e.reciprocal(out=out.ap, in_=in_.ap),
                       reads=(in_,), writes=(out,))

    def mm(self, out, lhsT, rhs, start, stop, tile_position=None):
        return self.op("t", lambda e: e.matmul(out.ap, lhsT.ap, rhs.ap, start=start,
                                               stop=stop, tile_position=tile_position),
                       reads=(lhsT, rhs), writes=(out,))

    def trans(self, out, in_, ident):
        return self.op("t", lambda e: e.transpose(out.ap, in_.ap, ident.ap),
                       reads=(in_, ident), writes=(out,))

    def finalize(self):
        self.barrier_all_on("q")
        nc = self.nc
        with nc.Block() as block:
            def runner(stream):
                def go(engobj):
                    for item in stream:
                        if item[0] == "w":
                            engobj.wait_ge(item[1], item[2])
                        else:
                            _, fn, semh, inc = item
                            fn(engobj).then_inc(semh, inc)
                return go

            block.tensor(runner(self.prog["t"]))
            block.vector(runner(self.prog["v"]))
            block.scalar(runner(self.prog["s"]))
            block.gpsimd(runner(self.prog["p"]))
            block.sync(runner(self.prog["q"]))
        self.stack.close()



T = 2048
D = 1024
NCT = D // 128
NTT = T // 128


class Slot:
    __slots__ = ("th", "off", "ln")

    def __init__(self, th, off, ln):
        self.th, self.off, self.ln = th, off, ln

    def v(self, ps=None, es=None):
        p0, p1 = (0, self.th.shape[0]) if ps is None else ps
        e0, e1 = (0, self.ln) if es is None else es
        a, b = self.off + e0, self.off + e1
        return TV(self.th, self.th[p0:p1, a:b], p0, p1, a, b)

    def vap(self, rel_off, dims, ps, es):
        pitch = self.th.shape[1]
        ap = bass.AP(self.th, self.off + rel_off, [[pitch, ps[1] - ps[0]]] + dims)
        if ps[0] != 0:
            ap = bass.AP(self.th, (self.off + rel_off), [[pitch, ps]] )
        return TV(self.th, ap, ps[0], ps[1], self.off + es[0], self.off + es[1])


AB_KC = 0
AB_VTM = 16384
AB_KR4 = 33024
AB_SCL = 35072
AB_CQ = 35840
AB_XBC = 44032
AB_QC = 46080
AB_QR = 48128
AB_WSM = 49152
AB_END = 56576

AF_ACC = 0
AF_WSTG = 4096
AF_SQ = 5120
AF_LNT = 6144
AF_END = 6656


def host_consts():
    c = {}
    c["ident_f32"] = np.eye(128, dtype=np.float32)
    c["ident_bf16"] = np.eye(128, dtype=ml_dtypes.bfloat16)
    c["ones_f32"] = np.ones((1, 128), dtype=np.float32)
    i = np.arange(128)
    c["tri_mask"] = np.where(i[:, None] <= i[None, :], 0.0, -1e30).astype(np.float32)
    DHR = 32
    t = np.arange(T, dtype=np.float32)
    inv = 1.0 / 10000.0 ** (np.arange(0, DHR, 2, dtype=np.float32) / DHR)
    freqs = t[:, None] * inv[None, :]
    emb = np.concatenate([freqs, freqs], -1)
    scale = (np.cos(emb) + np.sin(emb)).astype(np.float32)
    c["l0_scale_k"] = np.ascontiguousarray(scale.T).astype(ml_dtypes.bfloat16)
    c["l0_scale_q4"] = (np.tile(scale.T, (4, 1)) /
                        np.sqrt(np.float32(96.0))).astype(ml_dtypes.bfloat16)
    c["tri2_mask"] = np.where(i[None, :] <= i[:, None], 0.0, -1e30).astype(np.float32)
    e2 = np.zeros((128, 2), np.float32)
    e2[0:64, 0] = 1.0
    e2[64:128, 1] = 1.0
    c["e2_mat"] = e2
    c["e2t_mat"] = np.ascontiguousarray(e2.T)
    DH, WIN = 64, 512
    half = np.arange(0, DH, 2, dtype=np.float32)
    invf = 1.0 / 10000.0 ** (half / DH)
    tt = np.arange(T, dtype=np.float32)
    fr = tt[:, None] * invf[None, :]
    fr2 = np.concatenate([fr, fr], -1)
    basev = (half + 0.4 * DH) / (1.4 * DH)
    power = (tt - T // 2) / float(WIN // 2)
    sc = basev[None, :] ** power[:, None]
    sc2 = np.concatenate([sc, sc], -1)
    cosv, sinv = np.cos(fr2), np.sin(fr2)
    sgn = np.where(np.arange(DH) < DH // 2, -1.0, 1.0).astype(np.float32)
    csq = (8.0 * cosv * sc2).T
    ssq = (8.0 * sinv * sc2).T * sgn[:, None]
    csk = (cosv / sc2).T
    ssk = (sinv / sc2).T * sgn[:, None]
    c["xp_csq"] = np.ascontiguousarray(np.tile(csq, (2, 1)).astype(np.float32))
    c["xp_ssq"] = np.ascontiguousarray(np.tile(ssq, (2, 1)).astype(np.float32))
    c["xp_csk"] = np.ascontiguousarray(np.tile(csk, (2, 1)).astype(np.float32))
    c["xp_ssk"] = np.ascontiguousarray(np.tile(ssk, (2, 1)).astype(np.float32))
    return c


CONST_NAMES = ("ident_f32", "ident_bf16", "ones_f32", "tri_mask",
               "l0_scale_k", "l0_scale_q4", "tri2_mask", "e2_mat", "e2t_mat",
               "xp_csq", "xp_ssq", "xp_csk", "xp_ssk")


class Ctx:
    def __init__(self, kb):
        self.kb = kb
        self.d = {}

    def __getattr__(self, k):
        try:
            return self.d[k]
        except KeyError:
            raise AttributeError(k)


def alloc_common(kb, ctx):
    ctx.d["x"] = kb.sbuf("x_cm", [128, NCT * T], F32)
    ctx.d["AB"] = kb.sbuf("arena_bf", [128, AB_END], BF16)
    ctx.d["AF"] = kb.sbuf("arena_f32", [128, AF_END], F32)
    ctx.d["pf"] = [kb.psum(f"psf{i}", [128, 512], F32) for i in range(6)]
    ctx.d["pb"] = [kb.psum(f"psb{i}", [128, 1024], BF16) for i in range(2)]
    ctx.d["idf"] = kb.sbuf("idf", [128, 128], F32)
    ctx.d["idb"] = kb.sbuf("idb", [128, 128], BF16)
    ctx.d["ones"] = kb.sbuf("ones_c", [1, 128], F32)
    ctx.d["ones_col"] = kb.sbuf("ones_col", [128, 1], F32)
    ctx.d["tri"] = kb.sbuf("tri_c", [128, 128], F32)
    ctx.d["tri2"] = kb.sbuf("tri2_c", [128, 128], F32)
    ctx.d["e2"] = kb.sbuf("e2_c", [128, 2], F32)
    ctx.d["e2t"] = kb.sbuf("e2t_c", [2, 128], F32)
    AB, AFt = ctx.AB, ctx.AF
    ctx.d["stage"] = Slot(AFt, AF_ACC, 2048)


def emit_load_consts(kb, ctx, hb):
    kb.dma("q", V(ctx.idf), DV(hb["ident_f32"], "h_identf"))
    kb.dma("q", V(ctx.idb), DV(hb["ident_bf16"], "h_identb"))
    kb.dma("q", V(ctx.ones), DV(hb["ones_f32"], "h_ones"))
    kb.dma_slow("q", V(ctx.ones_col), DV(hb["ones_f32"].rearrange("o p -> p o"), "h_ones2"))
    kb.dma("q", V(ctx.tri), DV(hb["tri_mask"], "h_tri"))
    kb.dma("q", V(ctx.tri2), DV(hb["tri2_mask"], "h_tri2"))
    kb.dma("q", V(ctx.e2), DV(hb["e2_mat"], "h_e2"))
    kb.dma("q", V(ctx.e2t), DV(hb["e2t_mat"], "h_e2t"))


def emit_load_x(kb, ctx, x_hbm):
    x, stage, pf = ctx.x, ctx.stage, ctx.pf
    for ti in range(NTT):
        sb = ti % 2
        stv = stage.v(es=(sb * 1024, (sb + 1) * 1024))
        kb.dma("q", stv, DV(x_hbm[ti * 128:(ti + 1) * 128, :], "x_in",
                            (0, 128, ti * 1024, (ti + 1) * 1024)))
        for half in range(2):
            bank = pf[half]
            for cj in range(4):
                ci = half * 4 + cj
                src = stage.v(es=(sb * 1024 + ci * 128, sb * 1024 + (ci + 1) * 128))
                kb.trans(V(bank, (0, 128), (cj * 128, (cj + 1) * 128)), src, V(ctx.idf))
            base = (half * 4) * T + ti * 128
            ap = bass.AP(x, base, [[NCT * T, 128], [T, 4], [1, 128]])
            dtv = TV(x, ap, 0, 128, base, base + 3 * T + 128)
            kb.op("s", lambda e, o=ap, i=bank: e.copy(out=o, in_=i[:, 0:512]),
                  reads=(V(bank, (0, 128), (0, 512)),), writes=(dtv,))


def emit_store_x(kb, ctx, out_hbm):
    x, stage, pf = ctx.x, ctx.stage, ctx.pf
    for ti in range(NTT):
        sb = ti % 2
        for half in range(2):
            bank = pf[2 + half]
            for cj in range(4):
                ci = half * 4 + cj
                src = TV(x, x[:, ci * T + ti * 128: ci * T + (ti + 1) * 128],
                         0, 128, ci * T + ti * 128, ci * T + (ti + 1) * 128)
                kb.trans(V(bank, (0, 128), (cj * 128, (cj + 1) * 128)), src, V(ctx.idf))
            kb.copy("s", stage.v(es=(sb * 1024 + half * 512, sb * 1024 + (half + 1) * 512)),
                    V(bank, (0, 128), (0, 512)))
        kb.dma("q", DV(out_hbm[ti * 128:(ti + 1) * 128, :], "x_out",
                       (0, 128, ti * 1024, (ti + 1) * 1024)),
               stage.v(es=(sb * 1024, (sb + 1) * 1024)))



def emit_postln(kb, ctx, eps=1e-5):
    x, pf, oc = ctx.x, ctx.pf, ctx.ones_col
    AFt = ctx.AF
    for tc in range(T // 512):
        cs = tc * 512
        ps_a, ps_b, ps_c, ps_d = pf[0], pf[1], pf[2], pf[3]
        for ci in range(NCT):
            kb.mm(V(ps_a, (0, 1), (0, 512)), V(oc),
                  V(x, (0, 128), (ci * T + cs, ci * T + cs + 512)),
                  start=(ci == 0), stop=(ci == NCT - 1))
        for ci in range(NCT):
            sq = Slot(AFt, AF_SQ, 1024).v(es=((ci % 2) * 512, (ci % 2) * 512 + 512))
            kb.act(sq, V(x, (0, 128), (ci * T + cs, ci * T + cs + 512)), AF.Square)
            kb.mm(V(ps_b, (0, 1), (0, 512)), V(oc), sq,
                  start=(ci == 0), stop=(ci == NCT - 1))
        A = Slot(ctx.AF, AF_SQ, 1024).v((0, 1), (0, 512))
        B = Slot(ctx.AF, AF_SQ, 1024).v((0, 1), (512, 1024))
        kb.ts("v", A, V(ps_a, (0, 1), (0, 512)), 1.0 / D, op0=ALU.mult)
        kb.tt("v", B, A, A, ALU.mult)
        kb.stt("v", B, V(ps_b, (0, 1), (0, 512)), 1.0 / D, B, ALU.mult, ALU.subtract)
        kb.act(B, B, AF.Sqrt, bias=float(eps))
        kb.recip(B, B)
        kb.tt("v", A, A, B, ALU.mult)
        kb.mm(V(ps_c, (0, 128), (0, 512)), V(ctx.ones, (0, 1), (0, 128)), B,
              start=True, stop=True)
        kb.mm(V(ps_d, (0, 128), (0, 512)), V(ctx.ones, (0, 1), (0, 128)), A,
              start=True, stop=True)
        for ci in range(NCT):
            xv = V(x, (0, 128), (ci * T + cs, ci * T + cs + 512))
            t1 = Slot(AFt, AF_LNT, 512).v()
            kb.tt("v", t1, xv, V(ps_c, (0, 128), (0, 512)), ALU.mult)
            kb.tt("v", xv, t1, V(ps_d, (0, 128), (0, 512)), ALU.subtract)



QCW = 256
NQC = T // QCW


def emit_l0_attn(kb, ctx, W):
    x, pf, AB = ctx.x, ctx.pf, ctx.AB
    kc = Slot(AB, AB_KC, 16384)
    vtm = Slot(AB, AB_VTM, 16640)
    kr4 = Slot(AB, AB_KR4, 2048)
    scl = Slot(AB, AB_SCL, 768)
    cqs = Slot(AB, AB_CQ, 8192)
    expt = Slot(AB, AB_CQ, 1024)
    ycm = Slot(AB, AB_CQ + 1024, 2048)
    xbc = Slot(AB, AB_XBC, 2048)
    qcc = Slot(AB, AB_QC, 2048)
    qrr = Slot(AB, AB_QR, 1024)
    wbf = Slot(AB, AB_WSM, 1024)
    wsm = Slot(AB, AB_WSM + 1024, 6400)
    wstg = Slot(ctx.AF, AF_WSTG, 1024)
    rbsb = Slot(ctx.AF, AF_LNT, 512)
    OF_DKV, OF_UK, OF_UV, OF_KR = 0, 8 * 256, 8 * 256 + 2 * 1024, 8 * 256 + 4 * 1024

    def stream_w(hap, hname, r0, c0, ncols, buf, width):
        stg = wstg.v((0, 128), (buf * 512, buf * 512 + ncols))
        kb.dma("q", stg, DV(hap[r0:r0 + 128, c0:c0 + ncols], hname,
                            (0, 128, r0 * 8192 + c0, r0 * 8192 + c0 + ncols)))
        wb = wbf.v((0, 128), (buf * 512, buf * 512 + ncols))
        kb.copy("p", wb, stg)
        return wb

    def load_small(hname, hap, ktiles, cols, off):
        for k in range(ktiles):
            stg = wstg.v((0, 128), ((k % 2) * 512, (k % 2) * 512 + cols))
            kb.dma("q", stg, DV(hap[k * 128:(k + 1) * 128, :], hname,
                                (0, 128, k * cols, (k + 1) * cols)))
            kb.copy("p", wsm.v((0, 128), (off + k * cols, off + (k + 1) * cols)), stg)

    load_small("w_dkv", W["w_dkv"], 8, 256, OF_DKV)
    load_small("w_uk", W["w_uk"], 2, 1024, OF_UK)
    load_small("w_uv", W["w_uv"], 2, 1024, OF_UV)
    load_small("w_kr", W["w_kr"], 8, 32, OF_KR)

    ckv = Slot(AB, AB_CQ + 4096, 4096)
    xb1 = Slot(AB, AB_CQ, 4096)

    for tc in range(4):
        cs = tc * 512
        kb.dma("q", scl.v((0, 32), (256, 768)),
               DV(W["scale_k"][0:32, cs:cs + 512], "h_sclk", (0, 32, cs, cs + 512)))
        for ci in range(NCT):
            kb.copy("v", xb1.v((0, 128), (ci * 512, (ci + 1) * 512)),
                    V(x, (0, 128), (ci * T + cs, ci * T + cs + 512)))
        for m in range(2):
            ps = pf[m]
            for ci in range(NCT):
                kb.mm(V(ps), wsm.v((0, 128), (OF_DKV + ci * 256 + m * 128,
                                              OF_DKV + ci * 256 + (m + 1) * 128)),
                      xb1.v((0, 128), (ci * 512, (ci + 1) * 512)),
                      start=(ci == 0), stop=(ci == NCT - 1))
            kb.copy("s", ckv.v((0, 128), (m * T + cs - 0 * T, m * T + cs + 512))
                    if False else ckv.v((0, 128), (m * 2048 + cs, m * 2048 + cs + 512)), V(ps))
        for m in range(8):
            ps = pf[2 + (m % 2)]
            for ci in range(2):
                kb.mm(V(ps), wsm.v((0, 128), (OF_UK + ci * 1024 + m * 128,
                                              OF_UK + ci * 1024 + (m + 1) * 128)),
                      ckv.v((0, 128), (ci * 2048 + cs, ci * 2048 + cs + 512)),
                      start=(ci == 0), stop=(ci == 1))
            kb.copy("s" if m % 2 == 0 else "v",
                    kc.v((0, 128), (m * T + cs, m * T + cs + 512)), V(ps))
        for tl in range(4):
            tt_ = tc * 4 + tl
            for n in range(2):
                ps = pf[4 + n]
                for ci in range(2):
                    kb.mm(V(ps), ckv.v((0, 128), (ci * 2048 + cs + tl * 128,
                                                  ci * 2048 + cs + (tl + 1) * 128)),
                          wsm.v((0, 128), (OF_UV + ci * 1024 + n * 512,
                                           OF_UV + ci * 1024 + (n + 1) * 512)),
                          start=(ci == 0), stop=(ci == 1))
                base = AB_VTM + tt_ * 1040 + 65 * 8 * n
                oap = bass.AP(AB, base, [[AB_END, 128], [65, 8], [1, 64]])
                kb.op("v", lambda e, o=oap, i=ps: e.tensor_copy(out=o, in_=i[:, 0:512]),
                      reads=(V(ps),), writes=(TV(AB, oap, 0, 128, base, base + 8 * 65),))
        ps = pf[0]
        for ci in range(NCT):
            kb.mm(V(ps, (0, 32), (0, 512)),
                  wsm.v((0, 128), (OF_KR + ci * 32, OF_KR + (ci + 1) * 32)),
                  xb1.v((0, 128), (ci * 512, (ci + 1) * 512)),
                  start=(ci == 0), stop=(ci == NCT - 1))
        kb.tt("v", kr4.v((0, 32), (cs, cs + 512)), V(ps, (0, 32), (0, 512)),
              scl.v((0, 32), (256, 768)), ALU.mult)
    for tt_ in range(16):
        base = AB_VTM + tt_ * 1040 + 64
        oap = bass.AP(AB, base, [[AB_END, 128], [65, 16], [1, 1]])
        kb.op("v", lambda e, o=oap: e.memset(o, 1.0),
              writes=(TV(AB, oap, 0, 128, base, base + 16 * 65),))
    for rep in range(1, 4):
        kb.dma("p", kr4.v((rep * 32, (rep + 1) * 32), (0, T)), kr4.v((0, 32), (0, T)))

    for qc in range(NQC):
        qs = qc * QCW
        kb.dma("q", scl.v((0, 128), (0, QCW)),
               DV(W["scale_q4"][0:128, qs:qs + QCW], "h_sclq", (0, 128, qs, qs + QCW)))
        for ci in range(NCT):
            kb.copy("v", xbc.v((0, 128), (ci * QCW, (ci + 1) * QCW)),
                    V(x, (0, 128), (ci * T + qs, ci * T + qs + QCW)))
        for mg in range(8):
            for k in range(NCT):
                wb = stream_w(W["w_dq"], "w_dq", k * 128, mg * 512, 512, (mg * 8 + k) % 2, 4096)
                for j in range(4):
                    kb.mm(V(pf[j], (0, 128), (0, QCW)),
                          wbf.v((0, 128), (wb.e0 - AB_WSM + j * 128, wb.e0 - AB_WSM + (j + 1) * 128)),
                          xbc.v((0, 128), (k * QCW, (k + 1) * QCW)),
                          start=(k == 0), stop=(k == NCT - 1))
            for j in range(4):
                kb.copy("s" if j % 2 == 0 else "v",
                        cqs.v((0, 128), ((mg * 4 + j) * QCW, (mg * 4 + j + 1) * QCW)),
                        V(pf[j], (0, 128), (0, QCW)))
        for mg in range(2):
            for k in range(32):
                wb = stream_w(W["w_uq"], "w_uq", k * 128, mg * 512, 512, (mg * 32 + k) % 2, 1024)
                for j in range(4):
                    kb.mm(V(pf[j], (0, 128), (0, QCW)),
                          wbf.v((0, 128), (wb.e0 - AB_WSM + j * 128, wb.e0 - AB_WSM + (j + 1) * 128)),
                          cqs.v((0, 128), (k * QCW, (k + 1) * QCW)),
                          start=(k == 0), stop=(k == 31))
            for j in range(4):
                dst = qcc.v((0, 128), ((mg * 4 + j) * QCW, (mg * 4 + j + 1) * QCW))
                kb.act(dst, V(pf[j], (0, 128), (0, QCW)), AF.Copy,
                       scale=float(1.0 / np.sqrt(96.0)))
        for k in range(32):
            wb = stream_w(W["w_qr"], "w_qr", k * 128, 0, 512, k % 2, 512)
            for j in range(4):
                kb.mm(V(pf[j], (0, 128), (0, QCW)),
                      wbf.v((0, 128), (wb.e0 - AB_WSM + j * 128, wb.e0 - AB_WSM + (j + 1) * 128)),
                      cqs.v((0, 128), (k * QCW, (k + 1) * QCW)),
                      start=(k == 0), stop=(k == 31))
        for j in range(4):
            kb.tt("v", qrr.v((0, 128), (j * QCW, (j + 1) * QCW)),
                  V(pf[j], (0, 128), (0, QCW)), scl.v((0, 128), (0, QCW)), ALU.mult)
        njt = (qc + 1) * (QCW // 128)
        for h in range(16):
            hp, ho = h // 2, (h % 2) * 64
            rp = (h % 4) * 32
            y_ps = pf[4]
            for j in range(njt):
                st = pf[j % 2]
                kb.mm(V(st, (0, 128), (0, QCW)),
                      kc.v((ho, ho + 64), (hp * T + j * 128, hp * T + (j + 1) * 128)),
                      qcc.v((ho, ho + 64), (hp * QCW, (hp + 1) * QCW)),
                      start=True, stop=False, tile_position=(ho, 0))
                kb.mm(V(st, (0, 128), (0, QCW)),
                      kr4.v((rp, rp + 32), (j * 128, (j + 1) * 128)),
                      qrr.v((rp, rp + 32), ((h // 4) * QCW, (h // 4 + 1) * QCW)),
                      start=False, stop=True, tile_position=(rp, 0))
                exoff = (j % 2) * 512
                nd = qc * (QCW // 128)
                if j >= nd:
                    a = j - nd
                    if a > 0:
                        kb.memset("v", expt.v((0, 128), (exoff, exoff + a * 128)), 0.0)
                    kb.tt("v", V(st, (0, 128), (a * 128, (a + 1) * 128)),
                          V(st, (0, 128), (a * 128, (a + 1) * 128)), V(ctx.tri), ALU.add)
                    kb.act(expt.v((0, 128), (exoff + a * 128, exoff + QCW)),
                           V(st, (0, 128), (a * 128, QCW)), AF.Exp)
                else:
                    kb.act(expt.v((0, 128), (exoff, exoff + QCW)),
                           V(st, (0, 128), (0, QCW)), AF.Exp)
                kb.mm(V(y_ps, (0, 65), (0, QCW)),
                      vtm.v((0, 128), (j * 1040 + 65 * h, j * 1040 + 65 * h + 65)),
                      expt.v((0, 128), (exoff, exoff + QCW)),
                      start=(j == 0), stop=(j == njt - 1))
            kb.recip(Slot(ctx.AF, AF_SQ, 1024).v((0, 1), (0, QCW)),
                     V(y_ps, (64, 65), (0, QCW)))
            rb_ps = pf[3]
            kb.mm(V(rb_ps, (0, 64), (0, QCW)), V(ctx.ones, (0, 1), (0, 64)),
                  Slot(ctx.AF, AF_SQ, 1024).v((0, 1), (0, QCW)), start=True, stop=True)
            kb.copy("s", rbsb.v((0, 64), (0, QCW)), V(rb_ps, (0, 64), (0, QCW)))
            kb.tt("v", ycm.v((ho, ho + 64), (hp * QCW, (hp + 1) * QCW)),
                  V(y_ps, (0, 64), (0, QCW)), rbsb.v((0, 64), (0, QCW)), ALU.mult)
        for ocg in range(2):
            for k in range(NCT):
                wb = stream_w(W["w_o"], "w_o", k * 128, ocg * 512, 512, (ocg * 8 + k) % 2, 1024)
                for j in range(4):
                    kb.mm(V(pf[j], (0, 128), (0, QCW)),
                          wbf.v((0, 128), (wb.e0 - AB_WSM + j * 128, wb.e0 - AB_WSM + (j + 1) * 128)),
                          ycm.v((0, 128), (k * QCW, (k + 1) * QCW)),
                          start=(k == 0), stop=(k == NCT - 1))
            for j in range(4):
                oc_ = ocg * 4 + j
                xv = V(x, (0, 128), (oc_ * T + qs, oc_ * T + qs + QCW))
                kb.tt("v", xv, V(pf[j], (0, 128), (0, QCW)), xv, ALU.add)



FP4_MIDS = (0.75, 1.25, 1.75, 2.5, 3.5, 5.0)


def emit_fp4_quant(kb, ctx, src, dst, W, bufA, bufB, bmrow, ladder_eng="v"):
    nb = W // 16
    bufA = TV(bufA.t, bufA.t[0:128, bufA.e0:bufA.e0 + W], 0, 128, bufA.e0, bufA.e0 + W)
    bufB = TV(bufB.t, bufB.t[0:128, bufB.e0:bufB.e0 + W], 0, 128, bufB.e0, bufB.e0 + W)

    def b3(view, blk=True):
        if blk:
            ap = bass.AP(view.t, view.e0, [[view.t.shape[1], 128], [16, nb], [1, 16]])
        else:
            ap = bass.AP(view.t, view.e0, [[view.t.shape[1], 128], [1, nb], [0, 16]])
        return TV(view.t, ap, view.p0, view.p1, view.e0, view.e1)

    bm = TV(bmrow.t, bmrow.t[0:128, bmrow.e0:bmrow.e0 + nb], 0, 128, bmrow.e0, bmrow.e0 + nb)
    sc = TV(bmrow.t, bmrow.t[0:128, bmrow.e0 + nb:bmrow.e0 + 2 * nb], 0, 128,
            bmrow.e0 + nb, bmrow.e0 + 2 * nb)
    inv = TV(bmrow.t, bmrow.t[0:128, bmrow.e0 + 2 * nb:bmrow.e0 + 3 * nb], 0, 128,
             bmrow.e0 + 2 * nb, bmrow.e0 + 3 * nb)
    invb = TV(inv.t, bass.AP(inv.t, inv.e0, [[inv.t.shape[1], 128], [1, nb], [0, 16]]),
              0, 128, inv.e0, inv.e1)
    scb = TV(sc.t, bass.AP(sc.t, sc.e0, [[sc.t.shape[1], 128], [1, nb], [0, 16]]),
             0, 128, sc.e0, sc.e1)

    kb.act(bufA, src, AF.Abs)
    kb.reduce("v", bm, b3(bufA), ALU.max)
    kb.ts("v", sc, bm, 1.0 / 6.0, 1e-12, op0=ALU.mult, op1=ALU.max)
    kb.recip(inv, sc)
    kb.tt("v", b3(bufA), b3(src), invb, ALU.mult)
    kb.act(bufB, bufA, AF.Abs)
    kb.ts(ladder_eng, bufA, bufB, 0.25, None, op0=ALU.is_gt)
    for m in FP4_MIDS:
        kb.stt(ladder_eng, bufA, bufB, float(m), bufA, ALU.is_gt, ALU.add)
    kb.ts("v", bufB, bufA, -4.0, 0.0, op0=ALU.add, op1=ALU.max)
    kb.tt("v", bufB, bufA, bufB, ALU.add)
    kb.ts("v", bufA, bufA, -6.0, 0.0, op0=ALU.add, op1=ALU.max)
    kb.stt("v", bufA, bufB, 0.5, bufA, ALU.mult, ALU.add)
    kb.ts("v", bufB, src, 0.0, -2.0, op0=ALU.is_lt, op1=ALU.mult)
    kb.ts("v", bufB, bufB, 1.0, None, op0=ALU.add)
    kb.tt("v", bufA, bufA, bufB, ALU.mult)
    kb.tt("v", b3(dst), b3(bufA), scb, ALU.mult)


FB_XQT = 0
FB_XQC = 4096
FB_HQ = 8192
FB_HQC = 8704
FB_WG = 9216
FB_WD = 17408
FB_WQS = 21504


def emit_l0_ffn(kb, ctx, W):
    x, pf, pb, AB, AFt = ctx.x, ctx.pf, ctx.pb, ctx.AB, ctx.AF
    NCH = [(0, 512), (512, 512), (1024, 512), (1536, 512), (2048, 512), (2560, 256)]

    accs = Slot(AFt, AF_ACC, 4096)
    wqs = Slot(AB, FB_WQS, 1024)

    def quant_w(hap, hname, nrows, ncols, dst_ap, dst_name, dst_cols):
        ccs = []
        c0_ = 0
        while c0_ < ncols:
            cw_ = min(512, ncols - c0_)
            ccs.append((c0_, cw_))
            c0_ += cw_
        for r in range(nrows // 128):
            for c0, cw in ccs:
                piece = accs.v((0, 128), (0, cw))
                kb.dma("q", piece, DV(hap[r * 128:(r + 1) * 128, c0:c0 + cw], hname,
                                      (0, 128, r * ncols + c0, r * ncols + c0 + cw)))
                bufA = accs.v((0, 128), (512, 1024))
                bufB = accs.v((0, 128), (1024, 1536))
                bmr = accs.v((0, 128), (1536, 1536 + 96))
                q = wqs.v((0, 128), (0, cw))
                emit_fp4_quant(kb, ctx, piece, q, cw, bufA, bufB, bmr)
                for j in range(cw // 128):
                    bk = pb[j % 2]
                    kb.trans(V(bk, (0, 128), (0, 128)),
                             wqs.v((0, 128), (j * 128, (j + 1) * 128)), V(ctx.idb))
                    st = wqs.v((0, 128), (512 + j * 128, 512 + (j + 1) * 128))
                    kb.copy("s", st, V(bk, (0, 128), (0, 128)))
                    kb.dma("p", DV(dst_ap[c0 + j * 128:c0 + (j + 1) * 128,
                                          r * 128:(r + 1) * 128], dst_name,
                                   (0, 128, (c0 + j * 128) * dst_cols + r * 128,
                                    (c0 + j * 128) * dst_cols + r * 128 + 128)),
                           st)

    quant_w(W["w_gate"], "w_gate", 2816, 1024, W["wqgT"], "wqgT", 2816)
    quant_w(W["w_up"], "w_up", 2816, 1024, W["wquT"], "wquT", 2816)
    quant_w(W["w_down"], "w_down", 1024, 2816, W["wqdT"], "wqdT", 1024)

    xqt = Slot(AB, FB_XQT, 4096)
    xqc = Slot(AB, FB_XQC, 4096)
    hq = Slot(AB, FB_HQ, 512)
    hqc = Slot(AB, FB_HQC, 512)
    wg = Slot(AB, FB_WG, 8192)
    wd = Slot(AB, FB_WD, 4096)
    sqs = Slot(AFt, AF_SQ, 1024)
    wstg = Slot(AFt, AF_WSTG, 1024)
    lnt = Slot(AFt, AF_LNT, 512)

    for tc in range(4):
        cs = tc * 512
        for tl in range(4):
            for half in range(2):
                bank = pf[half]
                for cj in range(4):
                    ci = half * 4 + cj
                    kb.trans(V(bank, (0, 128), (cj * 128, (cj + 1) * 128)),
                             V(x, (0, 128), (ci * T + cs + tl * 128, ci * T + cs + (tl + 1) * 128)),
                             V(ctx.idf))
                base = tl * 1024 + half * 512
                kb.copy("s", accs.v((0, 128), (base, base + 512)), V(bank, (0, 128), (0, 512)))
        for tl in range(4):
            for half in range(2):
                base = tl * 1024 + half * 512
                emit_fp4_quant(kb, ctx, accs.v((0, 128), (base, base + 512)),
                               xqt.v((0, 128), (base, base + 512)), 512,
                               sqs.v((0, 128), (0, 512)), sqs.v((0, 128), (512, 1024)),
                               lnt.v((0, 128), (0, 96)))
        for tl in range(4):
            for half in range(2):
                bank = pb[half]
                for cj in range(4):
                    ci = half * 4 + cj
                    kb.trans(V(bank, (0, 128), (cj * 128, (cj + 1) * 128)),
                             xqt.v((0, 128), (tl * 1024 + ci * 128, tl * 1024 + (ci + 1) * 128)),
                             V(ctx.idb))
                base = (half * 4) * 512 + tl * 128
                ap = bass.AP(AB, FB_XQC + base, [[AB_END, 128], [512, 4], [1, 128]])
                kb.op("v", lambda e, o=ap, i=bank: e.tensor_copy(out=o, in_=i[:, 0:512]),
                      reads=(V(bank, (0, 128), (0, 512)),),
                      writes=(TV(AB, ap, 0, 128, FB_XQC + base, FB_XQC + base + 3 * 512 + 128),))
        for ci in range(NCT):
            kb.memset("v", accs.v((0, 128), (ci * 512, (ci + 1) * 512)), 0.0)
        for n0, nw in NCH:
            nk = nw // 128
            for k in range(NCT):
                kb.dma("q", wg.v((0, 128), (k * 512, k * 512 + nw)),
                       DV(W["wqgT"][k * 128:(k + 1) * 128, n0:n0 + nw], "wqgT",
                          (0, 128, k * 2816 + n0, k * 2816 + n0 + nw)))
                kb.dma("p", wg.v((0, 128), (4096 + k * 512, 4096 + k * 512 + nw)),
                       DV(W["wquT"][k * 128:(k + 1) * 128, n0:n0 + nw], "wquT",
                          (0, 128, 2 ** 23 + k * 2816 + n0, 2 ** 23 + k * 2816 + n0 + nw)))
            for kk in range(nk):
                kb.dma("q", wd.v((0, 128), (kk * 1024, (kk + 1) * 1024)),
                       DV(W["wqdT"][n0 + kk * 128:n0 + (kk + 1) * 128, 0:1024], "wqdT",
                          (0, 128, (n0 + kk * 128) * 1024, (n0 + kk * 128) * 1024 + 1024)))
            for tl in range(4):
                ps_g, ps_u = pf[2], pf[3]
                for k in range(NCT):
                    kb.mm(V(ps_g, (0, 128), (0, nw)),
                          xqc.v((0, 128), (k * 512 + tl * 128, k * 512 + (tl + 1) * 128)),
                          wg.v((0, 128), (k * 512, k * 512 + nw)),
                          start=(k == 0), stop=(k == NCT - 1))
                for k in range(NCT):
                    kb.mm(V(ps_u, (0, 128), (0, nw)),
                          xqc.v((0, 128), (k * 512 + tl * 128, k * 512 + (tl + 1) * 128)),
                          wg.v((0, 128), (4096 + k * 512, 4096 + k * 512 + nw)),
                          start=(k == 0), stop=(k == NCT - 1))
                sg = wstg.v((0, 128), (512, 512 + nw))
                kb.act(sg, V(ps_g, (0, 128), (0, nw)), AF.Silu)
                hh = wstg.v((0, 128), (0, nw))
                kb.tt("v", hh, sg, V(ps_u, (0, 128), (0, nw)), ALU.mult)
                emit_fp4_quant(kb, ctx, hh, hq.v((0, 128), (0, nw)), nw,
                               sqs.v((0, 128), (0, 512)), sqs.v((0, 128), (512, 1024)),
                               lnt.v((0, 128), (0, 96)), ladder_eng="v")
                for kk in range(nk):
                    bk = pb[kk % 2]
                    kb.trans(V(bk, (0, 128), (0, 128)),
                             hq.v((0, 128), (kk * 128, (kk + 1) * 128)), V(ctx.idb))
                    kb.copy("s", hqc.v((0, 128), (kk * 128, (kk + 1) * 128)),
                            V(bk, (0, 128), (0, 128)))
                for oc in range(NCT):
                    ps_d = pf[4 + (oc % 2)]
                    for kk in range(nk):
                        kb.mm(V(ps_d, (0, 128), (0, 128)),
                              wd.v((0, 128), (kk * 1024 + oc * 128, kk * 1024 + (oc + 1) * 128)),
                              hqc.v((0, 128), (kk * 128, (kk + 1) * 128)),
                              start=(kk == 0), stop=(kk == nk - 1))
                    av = accs.v((0, 128), (oc * 512 + tl * 128, oc * 512 + (tl + 1) * 128))
                    kb.tt("v", av, V(ps_d, (0, 128), (0, 128)), av, ALU.add)
        for ci in range(NCT):
            xv = V(x, (0, 128), (ci * T + cs, ci * T + cs + 512))
            kb.tt("v", xv, accs.v((0, 128), (ci * 512, (ci + 1) * 512)), xv, ALU.add)



LA_WT = 0
LA_XBC = 24576
LA_WST = 28672


def emit_l1_attn(kb, ctx, W):
    x, pf, pb, AB, AFt = ctx.x, ctx.pf, ctx.pb, ctx.AB, ctx.AF
    wts = Slot(AB, LA_WT, 24576)
    xbc = Slot(AB, LA_XBC, 4096)
    wst = Slot(AB, LA_WST, 1024)
    accs = Slot(AFt, AF_ACC, 4096)
    wstg = Slot(AFt, AF_WSTG, 1024)
    sqs = Slot(AFt, AF_SQ, 1024)
    biases = kb.sbuf("l1a_bias", [128, 24], F32)
    carry = kb.sbuf("l1a_carry", [128, 8], F32)

    for wi, bn in enumerate(("b_f", "b_i", "b_h")):
        kb.dma_slow("q", V(biases, (0, 128), (wi * 8, wi * 8 + 8)),
                    DV(W[bn].rearrange("(c p) -> p c", p=128), "h_" + bn))
    kb.memset("v", V(carry), 0.0)

    for wi, wn in enumerate(("w_f", "w_i", "w_h")):
        for r in range(8):
            stg = wstg.v((0, 128), (0, 1024))
            kb.dma("q", stg, DV(W[wn][r * 128:(r + 1) * 128, :], wn,
                                (0, 128, r * 1024, (r + 1) * 1024)))
            wb = wst.v((0, 128), (0, 1024))
            kb.copy("p", wb, stg)
            for k in range(8):
                bk = pb[k % 2]
                kb.trans(V(bk, (0, 128), (0, 128)),
                         wst.v((0, 128), (k * 128, (k + 1) * 128)), V(ctx.idb))
                kb.copy("s", wts.v((0, 128), (wi * 8192 + k * 1024 + r * 128,
                                              wi * 8192 + k * 1024 + (r + 1) * 128)),
                        V(bk, (0, 128), (0, 128)))

    F_, I_, H_ = 0, 512, 1024
    S1, S2, S3 = 1536, 2048, 2560
    for tc in range(4):
        cs = tc * 512
        for ci in range(NCT):
            kb.copy("v", xbc.v((0, 128), (ci * 512, (ci + 1) * 512)),
                    V(x, (0, 128), (ci * T + cs, ci * T + cs + 512)))
        for m in range(NCT):
            for wi in range(3):
                ps = pf[2 + wi]
                for k in range(NCT):
                    kb.mm(V(ps), wts.v((0, 128), (wi * 8192 + k * 1024 + m * 128,
                                                  wi * 8192 + k * 1024 + (m + 1) * 128)),
                          xbc.v((0, 128), (k * 512, (k + 1) * 512)),
                          start=(k == 0), stop=(k == NCT - 1))
            FB = accs.v((0, 128), (F_, F_ + 512))
            IB = accs.v((0, 128), (I_, I_ + 512))
            HB = accs.v((0, 128), (H_, H_ + 512))
            kb.act(FB, V(pf[2]), AF.Identity, bias=V(biases, (0, 128), (m, m + 1)))
            kb.act(IB, V(pf[3]), AF.Identity, bias=V(biases, (0, 128), (8 + m, 8 + m + 1)))
            kb.act(HB, V(pf[4]), AF.Identity, bias=V(biases, (0, 128), (16 + m, 16 + m + 1)))
            A = accs.v((0, 128), (S1, S1 + 512))
            B = accs.v((0, 128), (S2, S2 + 512))
            C = accs.v((0, 128), (S3, S3 + 512))
            kb.act(A, FB, AF.Abs)
            kb.act(A, A, AF.Exp, scale=-1.0)
            kb.act(A, A, AF.Ln, bias=1.0)
            kb.ts("v", B, FB, -1.0, 0.0, op0=ALU.mult, op1=ALU.max)
            kb.tt("v", A, A, B, ALU.add)
            kb.act(B, IB, AF.Abs)
            kb.act(B, B, AF.Exp, scale=-1.0)
            kb.act(B, B, AF.Ln, bias=1.0)
            kb.ts("v", C, IB, -1.0, 0.0, op0=ALU.mult, op1=ALU.max)
            kb.tt("v", B, B, C, ALU.add)
            kb.tt("v", A, A, B, ALU.subtract)
            kb.act(B, A, AF.Sigmoid, scale=-1.0)
            kb.act(A, A, AF.Sigmoid)
            kb.act(C, HB, AF.Sigmoid)
            kb.ts("v", FB, HB, 0.5, None, op0=ALU.add)
            kb.tt("v", FB, FB, C, ALU.subtract)
            kb.ts("v", IB, HB, 0.0, None, op0=ALU.is_ge)
            kb.tt("v", FB, FB, IB, ALU.mult)
            kb.tt("v", C, C, FB, ALU.add)
            kb.tt("v", A, A, C, ALU.mult)
            kb.scan(HB, B, A, V(carry, (0, 128), (m, m + 1)), op0=ALU.mult, op1=ALU.add)
            kb.copy("v", V(carry, (0, 128), (m, m + 1)),
                    TV(accs.th, accs.th[0:128, accs.off + H_ + 511:accs.off + H_ + 512],
                       0, 128, accs.off + H_ + 511, accs.off + H_ + 512))
            xv = V(x, (0, 128), (m * T + cs, m * T + cs + 512))
            kb.tt("v", xv, HB, xv, ALU.add)



LF_QCM = 0
LF_KCM = 8192
LF_VTM = 16384
LF_OCM = 24576
LF_HB1 = 24576
LF_XBC = 35840
LF_WS1 = 39936
LF_PA = 41984
LF_PB = 42496
LF_PT = 42624
LF_WS2 = 43264


def emit_l1_ffn(kb, ctx, W):
    x, pf, pb, AB, AFt = ctx.x, ctx.pf, ctx.pb, ctx.AB, ctx.AF
    qcm = Slot(AB, LF_QCM, 8192)
    kcm = Slot(AB, LF_KCM, 8192)
    vtm = Slot(AB, LF_VTM, 8192)
    ocm = Slot(AB, LF_OCM, 8192)
    hb1 = Slot(AB, LF_HB1, 11264)
    xbc = Slot(AB, LF_XBC, 4096)
    ws1 = Slot(AB, LF_WS1, 2048)
    ws2 = Slot(AB, LF_WS2, 2048)
    pA = Slot(AB, LF_PA, 512)
    pB = Slot(AB, LF_PB, 128)
    pts = Slot(AB, LF_PT, 640)
    accs = Slot(AFt, AF_ACC, 4096)
    sqs = Slot(AFt, AF_SQ, 1024)
    wstg = Slot(AFt, AF_WSTG, 1024)
    lnt = Slot(AFt, AF_LNT, 512)
    qks = kb.sbuf("lf_qks", [128, 4], F32)
    lng = kb.sbuf("lf_lng", [128, 16], F32)
    lnb = kb.sbuf("lf_lnb", [128, 16], F32)
    b1a = kb.sbuf("lf_b1a", [128, 44], F32)
    b1g = kb.sbuf("lf_b1g", [128, 44], F32)
    b2c = kb.sbuf("lf_b2", [128, 16], F32)
    tabs = Slot(AFt, AF_ACC + 2048, 2048)

    for c4 in range(4):
        kb.dma("q", DV(W["xres"][0:128, c4 * 4096:(c4 + 1) * 4096], "xres",
                       (0, 128, c4 * 4096, (c4 + 1) * 4096)),
               V(x, (0, 128), (c4 * 4096, (c4 + 1) * 4096)))
    kb.dma("q", V(qks), DV(W["qk_scale_l"], "h_qks"))
    kb.dma("q", V(lng), DV(W["lng_l"], "h_lng"))
    kb.dma("q", V(lnb), DV(W["lnb_l"], "h_lnb"))
    kb.dma("q", V(b1a), DV(W["b1a_l"], "h_b1a"))
    kb.dma("q", V(b1g), DV(W["b1g_l"], "h_b1g"))
    kb.dma("q", V(b2c), DV(W["b2_l"], "h_b2"))

    def shift_chunk(cs, dst_slot):
        for ci in range(NCT):
            dv = dst_slot.v((0, 128), (ci * 512, (ci + 1) * 512))
            if ci < 4:
                kb.copy("v", dv, V(x, (0, 128), (ci * T + cs, ci * T + cs + 512)))
            elif cs > 0:
                kb.copy("v", dv, V(x, (0, 128), (ci * T + cs - 1, ci * T + cs + 511)))
            else:
                kb.memset("v", dst_slot.v((0, 128), (ci * 512, ci * 512 + 1)), 0.0)
                kb.copy("v", dst_slot.v((0, 128), (ci * 512 + 1, (ci + 1) * 512)),
                        V(x, (0, 128), (ci * T, ci * T + 511)))

    for l in range(2):
        wqkv, wo, w1, w2 = W["wqkv"], W["wo"], W["w1"], W["w2"]
        for tc in range(4):
            cs = tc * 512
            shift_chunk(cs, xbc)
            for tb in range(4):
                kb.dma("q", tabs.v((0, 128), (tb * 512, (tb + 1) * 512)),
                       DV(W[("xp_csq", "xp_ssq", "xp_csk", "xp_ssk")[tb]][0:128, cs:cs + 512],
                          "h_xt%d" % tb, (0, 128, cs, cs + 512)))
            for kind in range(2):
                for k in range(NCT):
                    stg = wstg.v((0, 128), ((k % 2) * 512, (k % 2) * 512 + 512))
                    kb.dma("q", stg, DV(wqkv[l, k * 128:(k + 1) * 128,
                                            kind * 512:(kind + 1) * 512], "wqkv",
                                        (l, l + 1, k * 1536 + kind * 512,
                                         k * 1536 + (kind + 1) * 512)))
                    wb = ws1.v((0, 128), ((k % 2) * 512, (k % 2) * 512 + 512))
                    kb.copy("p", wb, stg)
                    for j in range(4):
                        kb.mm(V(pf[j]), ws1.v((0, 128), ((k % 2) * 512 + j * 128,
                                                         (k % 2) * 512 + (j + 1) * 128)),
                              xbc.v((0, 128), (k * 512, (k + 1) * 512)),
                              start=(k == 0), stop=(k == NCT - 1))
                dst = qcm if kind == 0 else kcm
                co, so = (0, 0) if kind == 0 else (2 * 512, 2)
                for j in range(4):
                    ps = pf[j]
                    qf = accs.v((0, 128), (0, 512))
                    kb.act(qf, V(ps), AF.Copy)
                    sq = sqs.v((0, 128), (0, 512))
                    kb.act(sq, V(ps), AF.Square)
                    n2 = pf[4]
                    kb.mm(V(n2, (0, 2), (0, 512)), V(ctx.e2), sq, start=True, stop=True)
                    nr = lnt.v((0, 2), (0, 512))
                    kb.act(nr, V(n2, (0, 2), (0, 512)), AF.Sqrt)
                    kb.ts("v", nr, nr, 1e-12, None, op0=ALU.max)
                    kb.recip(nr, nr)
                    rb = pf[5]
                    kb.mm(V(rb, (0, 128), (0, 512)), V(ctx.e2t, (0, 2), (0, 128)), nr,
                          start=True, stop=True)
                    qh = accs.v((0, 128), (512, 1024))
                    kb.tt("v", qh, qf, V(rb, (0, 128), (0, 512)), ALU.mult)
                    kb.ts("v", qh, qh, V(qks, (0, 128), (l * 2 + so // 2, l * 2 + so // 2 + 1)),
                          None, op0=ALU.mult)
                    qrot = accs.v((0, 128), (1024, 1536))
                    for (pd, psr) in ((0, 32), (32, 0), (64, 96), (96, 64)):
                        kb.copy("v", TV(AFt, AFt[pd:pd + 32, accs.off + 1024:accs.off + 1536],
                                        pd, pd + 32, accs.off + 1024, accs.off + 1536),
                                TV(AFt, AFt[psr:psr + 32, accs.off + 512:accs.off + 1024],
                                   psr, psr + 32, accs.off + 512, accs.off + 1024))
                    kb.tt("v", qh, qh, tabs.v((0, 128), (co, co + 512)), ALU.mult)
                    kb.tt("v", qrot, qrot, tabs.v((0, 128), (co + 512, co + 1024)), ALU.mult)
                    kb.tt("v", dst.v((0, 128), (j * 2048 + cs, j * 2048 + cs + 512)),
                          qh, qrot, ALU.add)
            for half in range(2):
                for kk in range(4):
                    k = half * 4 + kk
                    stg = wstg.v((0, 128), ((k % 2) * 512, (k % 2) * 512 + 512))
                    kb.dma("q", stg, DV(wqkv[l, k * 128:(k + 1) * 128, 1024:1536], "wqkv",
                                        (l, l + 1, k * 1536 + 1024, k * 1536 + 1536)))
                    kb.copy("p", ws2.v((0, 128), (kk * 512, (kk + 1) * 512)), stg)
                for tl in range(4):
                    ps = pf[2 + tl]
                    for kk in range(4):
                        k = half * 4 + kk
                        kb.mm(V(ps), xbc.v((0, 128), (k * 512 + tl * 128, k * 512 + (tl + 1) * 128)),
                              ws2.v((0, 128), (kk * 512, (kk + 1) * 512)),
                              start=(kk == 0 and half == 0), stop=(kk == 3 and half == 1))
                    if half == 1:
                        kb.copy("s", vtm.v((0, 128), ((tc * 4 + tl) * 512, (tc * 4 + tl + 1) * 512)),
                                V(ps))
        for h in range(8):
            hp, ho = h // 2, (h % 2) * 64
            for qtc in range(16):
                jlo = max(0, qtc - 4)
                noff = qtc - jlo
                bA, bB = pf[0], pf[1]
                if noff > 0:
                    kb.mm(V(bA, (0, 128), (0, noff * 128)),
                          qcm.v((ho, ho + 64), (hp * 2048 + qtc * 128, hp * 2048 + (qtc + 1) * 128)),
                          kcm.v((ho, ho + 64), (hp * 2048 + jlo * 128, hp * 2048 + qtc * 128)),
                          start=True, stop=True, tile_position=(ho, 0))
                kb.mm(V(bB, (0, 128), (0, 128)),
                      qcm.v((ho, ho + 64), (hp * 2048 + qtc * 128, hp * 2048 + (qtc + 1) * 128)),
                      kcm.v((ho, ho + 64), (hp * 2048 + qtc * 128, hp * 2048 + (qtc + 1) * 128)),
                      start=True, stop=True, tile_position=(ho, 0))
                if noff == 4:
                    kb.tt("v", V(bA, (0, 128), (0, 128)), V(bA, (0, 128), (0, 128)),
                          V(ctx.tri), ALU.add)
                kb.tt("v", V(bB, (0, 128), (0, 128)), V(bB, (0, 128), (0, 128)),
                      V(ctx.tri2), ALU.add)
                mx = lnt.v((0, 128), (0, 1))
                m2x = lnt.v((0, 128), (1, 2))
                if noff > 0:
                    kb.reduce("v", mx, V(bA, (0, 128), (0, noff * 128)), ALU.max)
                    kb.reduce("v", m2x, V(bB, (0, 128), (0, 128)), ALU.max)
                    kb.tt("v", mx, mx, m2x, ALU.max)
                else:
                    kb.reduce("v", mx, V(bB, (0, 128), (0, 128)), ALU.max)
                negm = lnt.v((0, 128), (2, 3))
                kb.ts("v", negm, mx, -1.0, None, op0=ALU.mult)
                sA = lnt.v((0, 128), (3, 4))
                sB = lnt.v((0, 128), (4, 5))
                if noff > 0:
                    kb.act(pA.v((0, 128), (0, noff * 128)), V(bA, (0, 128), (0, noff * 128)),
                           AF.Exp, bias=negm, accum=sA)
                kb.act(pB.v((0, 128), (0, 128)), V(bB, (0, 128), (0, 128)),
                       AF.Exp, bias=negm, accum=sB)
                den = lnt.v((0, 128), (5, 6))
                if noff > 0:
                    kb.tt("v", den, sA, sB, ALU.add)
                else:
                    kb.copy("v", den, sB)
                kb.recip(den, den)
                if noff > 0:
                    kb.ts("v", pA.v((0, 128), (0, noff * 128)),
                          pA.v((0, 128), (0, noff * 128)), den, None, op0=ALU.mult)
                kb.ts("v", pB.v((0, 128), (0, 128)), pB.v((0, 128), (0, 128)),
                      den, None, op0=ALU.mult)
                for jj in range(noff + 1):
                    bk = pb[jj % 2]
                    src = (pA.v((0, 128), (jj * 128, (jj + 1) * 128)) if jj < noff
                           else pB.v((0, 128), (0, 128)))
                    kb.trans(V(bk, (0, 128), (0, 128)), src, V(ctx.idb))
                    kb.copy("s", pts.v((0, 128), (jj * 128, (jj + 1) * 128)),
                            V(bk, (0, 128), (0, 128)))
                yb = pf[2]
                for jj in range(noff + 1):
                    j = jlo + jj
                    kb.mm(V(yb, (0, 64), (0, 128)),
                          vtm.v((0, 128), (j * 512 + h * 64, j * 512 + (h + 1) * 64)),
                          pts.v((0, 128), (jj * 128, (jj + 1) * 128)),
                          start=(jj == 0), stop=(jj == noff))
                kb.copy("s", ocm.v((ho, ho + 64), (hp * 2048 + qtc * 128,
                                                   hp * 2048 + (qtc + 1) * 128)),
                        V(yb, (0, 64), (0, 128)))
        for tc in range(4):
            cs = tc * 512
            for mg in range(2):
                for k in range(4):
                    stg = wstg.v((0, 128), ((k % 2) * 512, (k % 2) * 512 + 512))
                    kb.dma("q", stg, DV(wo[l, k * 128:(k + 1) * 128, mg * 512:(mg + 1) * 512],
                                        "wo", (l, l + 1, k * 1024 + mg * 512,
                                               k * 1024 + (mg + 1) * 512)))
                    wb = ws1.v((0, 128), ((k % 2) * 512, (k % 2) * 512 + 512))
                    kb.copy("p", wb, stg)
                    for j in range(4):
                        kb.mm(V(pf[j]), ws1.v((0, 128), ((k % 2) * 512 + j * 128,
                                                         (k % 2) * 512 + (j + 1) * 128)),
                              ocm.v((0, 128), (k * 2048 + cs, k * 2048 + cs + 512)),
                              start=(k == 0), stop=(k == 3))
                for j in range(4):
                    oc_ = mg * 4 + j
                    xv = V(x, (0, 128), (oc_ * T + cs, oc_ * T + cs + 512))
                    kb.tt("v", xv, V(pf[j]), xv, ALU.add)
        AW = [(mi * 512, min(512, 2730 - mi * 512)) for mi in range(6)]
        for tc in range(4):
            cs = tc * 512
            ps_a, ps_b = pf[0], pf[1]
            first = True
            for ci in range(NCT):
                xv = (V(x, (0, 128), (ci * T + cs, ci * T + cs + 512)) if ci < 4 else
                      (V(x, (0, 128), (ci * T + cs - 1, ci * T + cs + 511)) if cs > 0 else
                       V(x, (0, 128), (ci * T, ci * T + 511))))
                ov = (V(ps_a, (0, 1), (0, 512)) if (ci < 4 or cs > 0) else
                      V(ps_a, (0, 1), (1, 512)))
                kb.mm(ov, V(ctx.ones_col), xv, start=first, stop=(ci == NCT - 1))
                first = False
            first = True
            for ci in range(NCT):
                xv = (V(x, (0, 128), (ci * T + cs, ci * T + cs + 512)) if ci < 4 else
                      (V(x, (0, 128), (ci * T + cs - 1, ci * T + cs + 511)) if cs > 0 else
                       V(x, (0, 128), (ci * T, ci * T + 511))))
                w_ = 512 if (ci < 4 or cs > 0) else 511
                sq = sqs.v((0, 128), (0, w_))
                kb.act(sq, xv, AF.Square)
                ov = (V(ps_b, (0, 1), (0, 512)) if w_ == 512 else V(ps_b, (0, 1), (1, 512)))
                kb.mm(ov, V(ctx.ones_col), sq, start=first, stop=(ci == NCT - 1))
                first = False
            if cs == 0:
                pass
            A = sqs.v((0, 1), (0, 512))
            B = sqs.v((0, 1), (512, 1024))
            kb.ts("v", A, V(ps_a, (0, 1), (0, 512)), 1.0 / D, op0=ALU.mult)
            kb.tt("v", B, A, A, ALU.mult)
            kb.stt("v", B, V(ps_b, (0, 1), (0, 512)), 1.0 / D, B, ALU.mult, ALU.subtract)
            kb.act(B, B, AF.Sqrt, bias=1e-5)
            kb.recip(B, B)
            kb.tt("v", A, A, B, ALU.mult)
            kb.mm(V(pf[2], (0, 128), (0, 512)), V(ctx.ones, (0, 1), (0, 128)), B,
                  start=True, stop=True)
            kb.mm(V(pf[3], (0, 128), (0, 512)), V(ctx.ones, (0, 1), (0, 128)), A,
                  start=True, stop=True)
            for ci in range(NCT):
                xv = (V(x, (0, 128), (ci * T + cs, ci * T + cs + 512)) if ci < 4 else
                      (V(x, (0, 128), (ci * T + cs - 1, ci * T + cs + 511)) if cs > 0 else
                       V(x, (0, 128), (ci * T, ci * T + 511))))
                w_ = 512 if (ci < 4 or cs > 0) else 511
                off = 512 - w_
                t1 = accs.v((0, 128), (0, 512))
                t1w = accs.v((0, 128), (off, 512))
                if off:
                    kb.memset("v", accs.v((0, 128), (0, 1)), 0.0)
                kb.tt("v", t1w, xv, V(pf[2], (0, 128), (off, 512)), ALU.mult)
                kb.tt("v", t1, t1, V(pf[3], (0, 128), (0, 512)), ALU.subtract)
                kb.ts("v", xbc.v((0, 128), (ci * 512, (ci + 1) * 512)), t1,
                      V(lng, (0, 128), (l * 8 + ci, l * 8 + ci + 1)),
                      V(lnb, (0, 128), (l * 8 + ci, l * 8 + ci + 1)),
                      op0=ALU.mult, op1=ALU.add)
            for gi, (g0, gw) in enumerate(AW):
                nmc = (gw + 127) // 128
                abuf = accs.v((0, 128), (0, 2048))
                for k in range(NCT):
                    stg = wstg.v((0, 128), ((k % 2) * 512, (k % 2) * 512 + gw))
                    kb.dma("q", stg, DV(w1[l, k * 128:(k + 1) * 128, g0:g0 + gw], "w1",
                                        (l, l + 1, k * 5460 + g0, k * 5460 + g0 + gw)))
                    wb = ws1.v((0, 128), ((k % 2) * 512, (k % 2) * 512 + gw))
                    kb.copy("p", wb, stg)
                    for j in range(nmc):
                        mw = min(128, gw - j * 128)
                        kb.mm(V(pf[j], (0, mw), (0, 512)),
                              ws1.v((0, 128), ((k % 2) * 512 + j * 128, (k % 2) * 512 + j * 128 + mw)),
                              xbc.v((0, 128), (k * 512, (k + 1) * 512)),
                              start=(k == 0), stop=(k == NCT - 1))
                for j in range(nmc):
                    mw = min(128, gw - j * 128)
                    mi = (g0 + j * 128) // 128
                    kb.act(TV(AFt, AFt[0:mw, accs.off + j * 512:accs.off + (j + 1) * 512],
                              0, mw, accs.off + j * 512, accs.off + (j + 1) * 512),
                           V(pf[j], (0, mw), (0, 512)), AF.Identity,
                           bias=V(b1a, (0, mw), (l * 22 + mi, l * 22 + mi + 1)))
                for k in range(NCT):
                    stg = wstg.v((0, 128), ((k % 2) * 512, (k % 2) * 512 + gw))
                    kb.dma("q", stg, DV(w1[l, k * 128:(k + 1) * 128, 2730 + g0:2730 + g0 + gw],
                                        "w1", (l, l + 1, k * 5460 + 2730 + g0,
                                               k * 5460 + 2730 + g0 + gw)))
                    kb.copy("p", ws1.v((0, 128), ((k % 2) * 512, (k % 2) * 512 + gw)), stg)
                    for j in range(nmc):
                        mw = min(128, gw - j * 128)
                        kb.mm(V(pf[j], (0, mw), (0, 512)),
                              ws1.v((0, 128), ((k % 2) * 512 + j * 128, (k % 2) * 512 + j * 128 + mw)),
                              xbc.v((0, 128), (k * 512, (k + 1) * 512)),
                              start=(k == 0), stop=(k == NCT - 1))
                for j in range(nmc):
                    mw = min(128, gw - j * 128)
                    mi = (g0 + j * 128) // 128
                    gb = sqs.v((0, mw), (0, 512))
                    kb.act(gb, V(pf[j], (0, mw), (0, 512)), AF.Gelu,
                           bias=V(b1g, (0, mw), (l * 22 + mi, l * 22 + mi + 1)))
                    kb.tt("v", hb1.v((0, mw), (mi * 512, (mi + 1) * 512)),
                          TV(AFt, AFt[0:mw, accs.off + j * 512:accs.off + (j + 1) * 512],
                             0, mw, accs.off + j * 512, accs.off + (j + 1) * 512),
                          gb, ALU.mult)
            for mg in range(2):
                for kk in range(22):
                    kw = min(128, 2730 - kk * 128)
                    stg = wstg.v((0, kw), ((kk % 2) * 512, (kk % 2) * 512 + 512))
                    kb.dma("q", stg, DV(w2[l, kk * 128:kk * 128 + kw, mg * 512:(mg + 1) * 512],
                                        "w2", (l, l + 1, kk * 1024 + mg * 512,
                                               kk * 1024 + (mg + 1) * 512)))
                    kb.copy("p", ws1.v((0, kw), ((kk % 2) * 512, (kk % 2) * 512 + 512)), stg)
                    for j in range(4):
                        kb.mm(V(pf[j]), ws1.v((0, kw), ((kk % 2) * 512 + j * 128,
                                                        (kk % 2) * 512 + (j + 1) * 128)),
                              hb1.v((0, kw), (kk * 512, (kk + 1) * 512)),
                              start=(kk == 0), stop=(kk == 21))
                for j in range(4):
                    oc_ = mg * 4 + j
                    xv = V(x, (0, 128), (oc_ * T + cs, oc_ * T + cs + 512))
                    kb.stt("v", xv, V(pf[j]),
                           V(b2c, (0, 128), (l * 8 + oc_, l * 8 + oc_ + 1)), xv,
                           ALU.add, ALU.add)
    for c8 in range(8):
        xrs = accs.v((0, 128), (0, 2048))
        kb.dma("q", xrs, DV(W["xres"][0:128, c8 * 2048:(c8 + 1) * 2048], "xres",
                            (0, 128, c8 * 2048, (c8 + 1) * 2048)))
        xv = V(x, (0, 128), (c8 * 2048, (c8 + 1) * 2048))
        kb.tt("v", xv, xrs, xv, ALU.add)



_PROG_CACHE = {}


def _np_dt(a):
    return F32 if a.dtype == np.float32 else BF16


def _declare_consts(nc, consts):
    return {n: nc.dram_tensor(n, list(consts[n].shape), _np_dt(consts[n]),
                              kind="ExternalInput").ap() for n in CONST_NAMES}


def _build_block(block):
    consts = host_consts()
    nc = bass.Bass("TRN2", target_bir_lowering=False)
    x_h = nc.dram_tensor("x", [T, D], F32, kind="ExternalInput").ap()
    o_h = nc.dram_tensor("o", [T, D], F32, kind="ExternalOutput").ap()
    hb = _declare_consts(nc, consts)
    W = {}
    if block == "l0a":
        shapes = dict(w_dq=(1024, 4096), w_uq=(4096, 1024), w_dkv=(1024, 256),
                      w_uk=(256, 1024), w_uv=(256, 1024), w_qr=(4096, 512),
                      w_kr=(1024, 32), w_o=(1024, 1024))
        for n, s in shapes.items():
            W[n] = nc.dram_tensor(n, list(s), F32, kind="ExternalInput").ap()
        W["scale_q4"] = hb["l0_scale_q4"]
        W["scale_k"] = hb["l0_scale_k"]
    elif block == "l0f":
        for n, s in dict(w_gate=(2816, 1024), w_up=(2816, 1024),
                         w_down=(1024, 2816)).items():
            W[n] = nc.dram_tensor(n, list(s), F32, kind="ExternalInput").ap()
        for n, s in dict(wqgT=(1024, 2816), wquT=(1024, 2816),
                         wqdT=(2816, 1024)).items():
            W[n] = nc.dram_tensor(n, list(s), BF16, kind="Internal").ap()
    elif block == "l1a":
        for n, s in dict(w_f=(1024, 1024), w_i=(1024, 1024), w_h=(1024, 1024)).items():
            W[n] = nc.dram_tensor(n, list(s), F32, kind="ExternalInput").ap()
        for n in ("b_f", "b_i", "b_h"):
            W[n] = nc.dram_tensor(n, [1024], F32, kind="ExternalInput").ap()
    elif block == "l1f":
        for n, s in dict(wqkv=(2, 1024, 1536), wo=(2, 512, 1024),
                         w1=(2, 1024, 5460), w2=(2, 2730, 1024)).items():
            W[n] = nc.dram_tensor(n, list(s), F32, kind="ExternalInput").ap()
        for n, s in dict(qk_scale_l=(128, 4), lng_l=(128, 16), lnb_l=(128, 16),
                         b1a_l=(128, 44), b1g_l=(128, 44), b2_l=(128, 16)).items():
            W[n] = nc.dram_tensor(n, list(s), F32, kind="ExternalInput").ap()
        for n in ("xp_csq", "xp_ssq", "xp_csk", "xp_ssk"):
            W[n] = hb[n]
        W["xres"] = nc.dram_tensor("xres", [128, NCT * T], F32, kind="Internal").ap()
    kb = KB(nc)
    ctx = Ctx(kb)
    alloc_common(kb, ctx)
    emit_load_consts(kb, ctx, hb)
    emit_load_x(kb, ctx, x_h)
    if block == "l0a":
        emit_l0_attn(kb, ctx, W)
    elif block == "l0f":
        emit_l0_ffn(kb, ctx, W)
    elif block == "l1a":
        emit_l1_attn(kb, ctx, W)
    elif block == "l1f":
        emit_l1_ffn(kb, ctx, W)
    emit_postln(kb, ctx)
    emit_store_x(kb, ctx, o_h)
    kb.finalize()
    return nc, consts


def _l1f_layouts(q_scale, k_scale, ln_g, ln_b, b1, b2):
    d = {}
    d["qk_scale_l"] = np.stack([np.tile(q_scale[0], 2), np.tile(k_scale[0], 2),
                                np.tile(q_scale[1], 2), np.tile(k_scale[1], 2)],
                               axis=1).astype(np.float32)

    def cols(vec, n):
        out = np.zeros((128, n), np.float32)
        for i in range(n):
            seg = vec[i * 128:(i + 1) * 128]
            out[:len(seg), i] = seg
        return out
    d["lng_l"] = np.concatenate([cols(ln_g[l], 8) for l in range(2)], 1)
    d["lnb_l"] = np.concatenate([cols(ln_b[l], 8) for l in range(2)], 1)
    d["b1a_l"] = np.concatenate([cols(b1[l][:2730], 22) for l in range(2)], 1)
    d["b1g_l"] = np.concatenate([cols(b1[l][2730:], 22) for l in range(2)], 1)
    d["b2_l"] = np.concatenate([cols(b2[l], 8) for l in range(2)], 1)
    return d


def kernel(**inputs):
    inp = {k: np.asarray(v) for k, v in inputs.items()}
    f32 = np.float32
    blocks = [
        ("l0a", {"w_dq": inp["L0_attn_W_DQ"], "w_uq": inp["L0_attn_W_UQ"],
                 "w_dkv": inp["L0_attn_W_DKV"], "w_uk": inp["L0_attn_W_UK"],
                 "w_uv": inp["L0_attn_W_UV"], "w_qr": inp["L0_attn_W_QR"],
                 "w_kr": inp["L0_attn_W_KR"], "w_o": inp["L0_attn_W_O"]}),
        ("l0f", {"w_gate": inp["L0_ffn_w_gate"], "w_up": inp["L0_ffn_w_up"],
                 "w_down": inp["L0_ffn_w_down"]}),
        ("l1a", {"w_f": inp["L1_attn_W_f"], "b_f": inp["L1_attn_b_f"],
                 "w_i": inp["L1_attn_W_i"], "b_i": inp["L1_attn_b_i"],
                 "w_h": inp["L1_attn_W_h"], "b_h": inp["L1_attn_b_h"]}),
        ("l1f", {"wqkv": inp["L1_ffn_wqkv"], "wo": inp["L1_ffn_wo"],
                 "w1": inp["L1_ffn_w1"], "w2": inp["L1_ffn_w2"],
                 **_l1f_layouts(inp["L1_ffn_q_scale"], inp["L1_ffn_k_scale"],
                                inp["L1_ffn_ln_g"], inp["L1_ffn_ln_b"],
                                inp["L1_ffn_b1"], inp["L1_ffn_b2"])}),
    ]
    cur = np.ascontiguousarray(inp["L0_attn_x"].astype(f32))
    for bname, wmap in blocks:
        if bname not in _PROG_CACHE:
            _PROG_CACHE[bname] = _build_block(bname)
        nc, consts = _PROG_CACHE[bname]
        cmap = {n: consts[n] for n in CONST_NAMES}
        wmap2 = {k: np.ascontiguousarray(v.astype(f32) if v.dtype != np.float32 else v)
                 for k, v in wmap.items()}
        in_maps = [{"x": np.ascontiguousarray(cur[c]), **wmap2, **cmap}
                   for c in range(8)]
        res = run_bass_kernel_spmd(nc, in_maps, core_ids=list(range(8)))
        cur = np.stack([res.results[c]["o"] for c in range(8)])
    return cur.astype(np.float32)


# version 04
# speedup vs baseline: 1.0403x; 1.0403x over previous
import numpy as np
import ml_dtypes
from contextlib import ExitStack

import concourse.bass as bass
import concourse.mybir as mybir
from concourse.bass_utils import run_bass_kernel_spmd

F32 = mybir.dt.float32
BF16 = mybir.dt.bfloat16
AF = mybir.ActivationFunctionType
ALU = mybir.AluOpType
AX = mybir.AxisListType

DT_SIZE = {F32: 4, BF16: 2}


class TV:
    __slots__ = ("t", "ap", "p0", "p1", "e0", "e1")

    def __init__(self, t, ap, p0, p1, e0, e1):
        self.t = t
        self.ap = ap
        self.p0, self.p1, self.e0, self.e1 = p0, p1, e0, e1


def V(th, ps=None, es=None):
    shape = list(th.shape)
    p0, p1 = (0, shape[0]) if ps is None else ps
    e0, e1 = (0, shape[1]) if es is None else es
    return TV(th, th[p0:p1, e0:e1], p0, p1, e0, e1)


def V3(th, ps, es, ap):
    return TV(th, ap, ps[0], ps[1], es[0], es[1])


def DV(th_ap, tname, region=None):
    if region is None:
        region = (0, 1, 0, 1)
    return TV(_Named(tname), th_ap, *region)


class _Named:
    __slots__ = ("name",)

    def __init__(self, name):
        self.name = name



ENGINES = ("t", "v", "s", "p", "q")


class KB:
    def __init__(self, nc):
        self.nc = nc
        self.stack = ExitStack()
        self.prog = {e: [] for e in ENGINES}
        self.semh = {}
        for name in ("ct", "cv", "cs", "cp"):
            self.semh[name] = self.stack.enter_context(nc.semaphore("sem_" + name))
        self.csem = {"t": "ct", "v": "cv", "s": "cs", "p": "cp"}
        self.dring = {}
        for q, r in (("q", 8), ("p", 8), ("s", 4), ("v", 2), ("t", 2)):
            names = []
            for i in range(r):
                n = f"d{q}{i}"
                self.semh[n] = self.stack.enter_context(nc.semaphore("sem_" + n))
                names.append(n)
            self.dring[q] = names
        self.dma_i = {q: 0 for q in self.dring}
        self.cnt = {k: 0 for k in self.semh}
        self.clock = {e: {} for e in ENGINES}
        self.snap = {k: {} for k in self.semh}
        self.writes = {}
        self.reads = {}
        self.psum_names = set()
        self.eng_obj = {"t": nc.tensor, "v": nc.vector, "s": nc.scalar,
                        "p": nc.gpsimd, "q": nc.sync}
        self._tensors = []
        self.fconst_t = self.sbuf("fconst_tab", [1, 32], F32)
        self.fconst_vals = {}

    def fconst(self, val):
        val = float(val)
        if val in (0.0, 1.0):
            return val, None
        if val not in self.fconst_vals:
            idx = len(self.fconst_vals)
            assert idx < 32
            self.fconst_vals[val] = idx
            view = V(self.fconst_t, (0, 1), (idx, idx + 1))
            self.memset("p", view, val)
            self.nc.const_aps.register(F32, val, self.fconst_t[0:1, idx:idx + 1])
        idx = self.fconst_vals[val]
        return val, V(self.fconst_t, (0, 1), (idx, idx + 1))

    def sbuf(self, name, shape, dtype):
        th = self.stack.enter_context(self.nc.sbuf_tensor(name, list(shape), dtype))
        self._tensors.append(th)
        return th

    def psum(self, name, shape, dtype):
        th = self.stack.enter_context(self.nc.psum_tensor(name, list(shape), dtype))
        self._tensors.append(th)
        self.psum_names.add(name)
        return th

    @staticmethod
    def _overlap(a, b):
        return not (a[1] <= b[0] or b[1] <= a[0] or a[3] <= b[2] or b[3] <= a[2])

    def _dep_events(self, eng, rviews, wviews):
        evs = []
        own = self.csem.get(eng)
        for v_ in rviews:
            for ent in self.writes.get(v_.t.name, ()):
                if self._overlap((v_.p0, v_.p1, v_.e0, v_.e1), ent):
                    evs.extend(ent[4])
            if v_.t.name in self.psum_names:
                for ent in self.reads.get(v_.t.name, ()):
                    if self._overlap((v_.p0, v_.p1, v_.e0, v_.e1), ent):
                        evs.extend(e for e in ent[4] if e[0] != own)
        for v_ in wviews:
            box = (v_.p0, v_.p1, v_.e0, v_.e1)
            for ent in self.writes.get(v_.t.name, ()):
                if self._overlap(box, ent):
                    evs.extend(ent[4])
            for ent in self.reads.get(v_.t.name, ()):
                if self._overlap(box, ent):
                    evs.extend(ent[4])
        if eng == "t":
            evs = [e for e in evs if e[0] != "ct"]
        return evs

    def _emit_waits(self, eng, evs):
        clk = self.clock[eng]
        need = {}
        for s_, v_ in evs:
            if clk.get(s_, 0) < v_ and need.get(s_, 0) < v_:
                need[s_] = v_
        for s_, v_ in sorted(need.items()):
            semh = self.semh[s_]
            self.prog[eng].append(("w", semh, v_))
            snap = self.snap[s_].get(v_)
            if snap:
                for ks, kv in snap.items():
                    if clk.get(ks, 0) < kv:
                        clk[ks] = kv
            if clk.get(s_, 0) < v_:
                clk[s_] = v_

    def _record(self, views, table, ev):
        for v_ in views:
            lst = table.setdefault(v_.t.name, [])
            box = (v_.p0, v_.p1, v_.e0, v_.e1)
            if table is self.writes:
                for tab in (self.writes, self.reads):
                    l2 = tab.get(v_.t.name)
                    if l2:
                        tab[v_.t.name] = [e for e in l2 if not (
                            box[0] <= e[0] and e[1] <= box[1] and box[2] <= e[2] and e[3] <= box[3])]
                lst = table.setdefault(v_.t.name, [])
            lst.append(box + ((ev,),))
            if len(lst) > 120:
                old, rest = lst[:60], lst[60:]
                p0 = min(e[0] for e in old); p1 = max(e[1] for e in old)
                e0 = min(e[2] for e in old); e1 = max(e[3] for e in old)
                evs = tuple(sorted({x for e in old for x in e[4]}, key=lambda t: (t[0], -t[1])))
                seen = {}
                for s_, vv in evs:
                    seen[s_] = max(seen.get(s_, 0), vv)
                table[v_.t.name] = [(p0, p1, e0, e1, tuple(seen.items()))] + rest

    def op(self, eng, fn, reads=(), writes=(), dma=False):
        evs = self._dep_events(eng, reads, writes)
        if dma:
            ring = self.dring[eng]
            sname = ring[self.dma_i[eng] % len(ring)]
            self.dma_i[eng] += 1
            if self.cnt[sname] > 0:
                evs = list(evs) + [(sname, self.cnt[sname])]
        else:
            sname = self.csem[eng]
        self._emit_waits(eng, evs)
        inc = 16 if dma else 1
        self.cnt[sname] += inc
        ev = (sname, self.cnt[sname])
        semh = self.semh[sname]
        self.prog[eng].append(("c", fn, semh, inc))
        clk = dict(self.clock[eng])
        clk[sname] = self.cnt[sname]
        self.snap[sname][self.cnt[sname]] = clk
        self._record(reads, self.reads, ev)
        self._record(writes, self.writes, ev)
        return ev

    def raw_wait(self, eng, sname, val):
        self.prog[eng].append(("w", self.semh[sname], val))
        clk = self.clock[eng]
        if clk.get(sname, 0) < val:
            clk[sname] = val

    def barrier_all_on(self, eng):
        evs = [(s, c) for s, c in self.cnt.items() if c > 0]
        self._emit_waits(eng, evs)

    def dma(self, q, dst, src):
        return self.op(q, lambda e: e.dma_start(out=dst.ap, in_=src.ap),
                       reads=(src,), writes=(dst,), dma=True)

    def dma_slow(self, q, dst, src):
        return self.op(q, lambda e: e.dma_start(out=dst.ap, in_=src.ap,
                                                allow_slow_non_contiguous=True),
                       reads=(src,), writes=(dst,), dma=True)

    def copy(self, eng, dst, src):
        if eng == "s":
            return self.op(eng, lambda e: e.copy(out=dst.ap, in_=src.ap),
                           reads=(src,), writes=(dst,))
        return self.op(eng, lambda e: e.tensor_copy(out=dst.ap, in_=src.ap),
                       reads=(src,), writes=(dst,))

    def memset(self, eng, dst, c):
        return self.op(eng, lambda e: e.memset(dst.ap, c), writes=(dst,))

    def tt(self, eng, out, in0, in1, alu):
        return self.op(eng, lambda e: e.tensor_tensor(out=out.ap, in0=in0.ap, in1=in1.ap, op=alu),
                       reads=(in0, in1), writes=(out,))

    def ts(self, eng, out, in0, s1, s2=None, op0=ALU.mult, op1=None, s1v=None, s2v=None):
        reads = [in0]
        a1 = s1
        if isinstance(s1, TV):
            reads.append(s1); a1 = s1.ap
        a2 = s2
        if isinstance(s2, TV):
            reads.append(s2); a2 = s2.ap
        if op1 is None:
            fn = lambda e: e.tensor_scalar(out=out.ap, in0=in0.ap, scalar1=a1,
                                           scalar2=None, op0=op0)
        else:
            fn = lambda e: e.tensor_scalar(out=out.ap, in0=in0.ap, scalar1=a1,
                                           scalar2=a2, op0=op0, op1=op1)
        return self.op(eng, fn, reads=tuple(reads), writes=(out,))

    def stt(self, eng, out, in0, scalar, in1, op0, op1):
        reads = [in0, in1]
        a = scalar
        if isinstance(scalar, TV):
            reads.append(scalar); a = scalar.ap
        return self.op(eng, lambda e: e.scalar_tensor_tensor(
            out=out.ap, in0=in0.ap, scalar=a, in1=in1.ap, op0=op0, op1=op1),
            reads=tuple(reads), writes=(out,))

    def scan(self, out, d0, d1, initial, op0=ALU.mult, op1=ALU.add):
        reads = [d0, d1]
        ini = initial
        if isinstance(initial, TV):
            reads.append(initial); ini = initial.ap
        return self.op("v", lambda e: e.tensor_tensor_scan(
            out=out.ap, data0=d0.ap, data1=d1.ap, initial=ini, op0=op0, op1=op1),
            reads=tuple(reads), writes=(out,))

    def act(self, out, in_, func, bias=0.0, scale=1.0, accum=None):
        reads = [in_]
        b = bias
        if isinstance(bias, TV):
            reads.append(bias); b = bias.ap
        else:
            b, cv = self.fconst(bias)
            if cv is not None:
                reads.append(cv)
        sc = scale
        if isinstance(scale, TV):
            reads.append(scale); sc = scale.ap
        else:
            sc, cv = self.fconst(scale)
            if cv is not None:
                reads.append(cv)
        writes = [out]
        if accum is not None:
            writes.append(accum)
            fn = lambda e: e.activation(out=out.ap, in_=in_.ap, func=func, bias=b,
                                        scale=sc, accum_out=accum.ap)
        else:
            fn = lambda e: e.activation(out=out.ap, in_=in_.ap, func=func, bias=b, scale=sc)
        return self.op("s", fn, reads=tuple(reads), writes=tuple(writes))

    def reduce(self, eng, out, in_, alu, negate=False):
        return self.op(eng, lambda e: e.tensor_reduce(
            out=out.ap, in_=in_.ap, axis=AX.X, op=alu, negate=negate),
            reads=(in_,), writes=(out,))

    def recip(self, out, in_):
        return self.op("v", lambda e: e.reciprocal(out=out.ap, in_=in_.ap),
                       reads=(in_,), writes=(out,))

    def mm(self, out, lhsT, rhs, start, stop, tile_position=None):
        return self.op("t", lambda e: e.matmul(out.ap, lhsT.ap, rhs.ap, start=start,
                                               stop=stop, tile_position=tile_position),
                       reads=(lhsT, rhs), writes=(out,))

    def trans(self, out, in_, ident):
        return self.op("t", lambda e: e.transpose(out.ap, in_.ap, ident.ap),
                       reads=(in_, ident), writes=(out,))

    def finalize(self):
        self.barrier_all_on("q")
        nc = self.nc
        with nc.Block() as block:
            def runner(stream):
                def go(engobj):
                    for item in stream:
                        if item[0] == "w":
                            engobj.wait_ge(item[1], item[2])
                        else:
                            _, fn, semh, inc = item
                            fn(engobj).then_inc(semh, inc)
                return go

            block.tensor(runner(self.prog["t"]))
            block.vector(runner(self.prog["v"]))
            block.scalar(runner(self.prog["s"]))
            block.gpsimd(runner(self.prog["p"]))
            block.sync(runner(self.prog["q"]))
        self.stack.close()



T = 2048
D = 1024
NCT = D // 128
NTT = T // 128


class Slot:
    __slots__ = ("th", "off", "ln")

    def __init__(self, th, off, ln):
        self.th, self.off, self.ln = th, off, ln

    def v(self, ps=None, es=None):
        p0, p1 = (0, self.th.shape[0]) if ps is None else ps
        e0, e1 = (0, self.ln) if es is None else es
        a, b = self.off + e0, self.off + e1
        return TV(self.th, self.th[p0:p1, a:b], p0, p1, a, b)

    def vap(self, rel_off, dims, ps, es):
        pitch = self.th.shape[1]
        ap = bass.AP(self.th, self.off + rel_off, [[pitch, ps[1] - ps[0]]] + dims)
        if ps[0] != 0:
            ap = bass.AP(self.th, (self.off + rel_off), [[pitch, ps]] )
        return TV(self.th, ap, ps[0], ps[1], self.off + es[0], self.off + es[1])


AB_KC = 0
AB_VTM = 16384
AB_KR4 = 33024
AB_SCL = 35072
AB_CQ = 35840
AB_XBC = 44032
AB_QC = 46080
AB_QR = 48128
AB_WSM = 49152
AB_END = 56576

AF_ACC = 0
AF_WSTG = 4096
AF_SQ = 5120
AF_LNT = 6144
AF_END = 6656


def host_consts():
    c = {}
    c["ident_f32"] = np.eye(128, dtype=np.float32)
    c["ident_bf16"] = np.eye(128, dtype=ml_dtypes.bfloat16)
    c["ones_f32"] = np.ones((1, 128), dtype=np.float32)
    i = np.arange(128)
    c["tri_mask"] = np.where(i[:, None] <= i[None, :], 0.0, -1e30).astype(np.float32)
    DHR = 32
    t = np.arange(T, dtype=np.float32)
    inv = 1.0 / 10000.0 ** (np.arange(0, DHR, 2, dtype=np.float32) / DHR)
    freqs = t[:, None] * inv[None, :]
    emb = np.concatenate([freqs, freqs], -1)
    scale = (np.cos(emb) + np.sin(emb)).astype(np.float32)
    c["l0_scale_k"] = np.ascontiguousarray(scale.T).astype(ml_dtypes.bfloat16)
    c["l0_scale_q4"] = (np.tile(scale.T, (4, 1)) /
                        np.sqrt(np.float32(96.0))).astype(ml_dtypes.bfloat16)
    c["tri2_mask"] = np.where(i[None, :] <= i[:, None], 0.0, -1e30).astype(np.float32)
    e2 = np.zeros((128, 2), np.float32)
    e2[0:64, 0] = 1.0
    e2[64:128, 1] = 1.0
    c["e2_mat"] = e2
    c["e2t_mat"] = np.ascontiguousarray(e2.T)
    DH, WIN = 64, 512
    half = np.arange(0, DH, 2, dtype=np.float32)
    invf = 1.0 / 10000.0 ** (half / DH)
    tt = np.arange(T, dtype=np.float32)
    fr = tt[:, None] * invf[None, :]
    fr2 = np.concatenate([fr, fr], -1)
    basev = (half + 0.4 * DH) / (1.4 * DH)
    power = (tt - T // 2) / float(WIN // 2)
    sc = basev[None, :] ** power[:, None]
    sc2 = np.concatenate([sc, sc], -1)
    cosv, sinv = np.cos(fr2), np.sin(fr2)
    sgn = np.where(np.arange(DH) < DH // 2, -1.0, 1.0).astype(np.float32)
    csq = (8.0 * cosv * sc2).T
    ssq = (8.0 * sinv * sc2).T * sgn[:, None]
    csk = (cosv / sc2).T
    ssk = (sinv / sc2).T * sgn[:, None]
    c["xp_csq"] = np.ascontiguousarray(np.tile(csq, (2, 1)).astype(np.float32))
    c["xp_ssq"] = np.ascontiguousarray(np.tile(ssq, (2, 1)).astype(np.float32))
    c["xp_csk"] = np.ascontiguousarray(np.tile(csk, (2, 1)).astype(np.float32))
    c["xp_ssk"] = np.ascontiguousarray(np.tile(ssk, (2, 1)).astype(np.float32))
    return c


CONST_NAMES = ("ident_f32", "ident_bf16", "ones_f32", "tri_mask",
               "l0_scale_k", "l0_scale_q4", "tri2_mask", "e2_mat", "e2t_mat",
               "xp_csq", "xp_ssq", "xp_csk", "xp_ssk")


class Ctx:
    def __init__(self, kb):
        self.kb = kb
        self.d = {}

    def __getattr__(self, k):
        try:
            return self.d[k]
        except KeyError:
            raise AttributeError(k)


def alloc_common(kb, ctx):
    ctx.d["x"] = kb.sbuf("x_cm", [128, NCT * T], F32)
    ctx.d["AB"] = kb.sbuf("arena_bf", [128, AB_END], BF16)
    ctx.d["AF"] = kb.sbuf("arena_f32", [128, AF_END], F32)
    ctx.d["pf"] = [kb.psum(f"psf{i}", [128, 512], F32) for i in range(6)]
    ctx.d["pb"] = [kb.psum(f"psb{i}", [128, 1024], BF16) for i in range(2)]
    ctx.d["idf"] = kb.sbuf("idf", [128, 128], F32)
    ctx.d["idb"] = kb.sbuf("idb", [128, 128], BF16)
    ctx.d["ones"] = kb.sbuf("ones_c", [1, 128], F32)
    ctx.d["ones_col"] = kb.sbuf("ones_col", [128, 1], F32)
    ctx.d["tri"] = kb.sbuf("tri_c", [128, 128], F32)
    ctx.d["tri2"] = kb.sbuf("tri2_c", [128, 128], F32)
    ctx.d["e2"] = kb.sbuf("e2_c", [128, 2], F32)
    ctx.d["e2t"] = kb.sbuf("e2t_c", [2, 128], F32)
    AB, AFt = ctx.AB, ctx.AF
    ctx.d["stage"] = Slot(AFt, AF_ACC, 2048)


def emit_load_consts(kb, ctx, hb):
    kb.dma("q", V(ctx.idf), DV(hb["ident_f32"], "h_identf"))
    kb.dma("q", V(ctx.idb), DV(hb["ident_bf16"], "h_identb"))
    kb.dma("q", V(ctx.ones), DV(hb["ones_f32"], "h_ones"))
    kb.dma_slow("q", V(ctx.ones_col), DV(hb["ones_f32"].rearrange("o p -> p o"), "h_ones2"))
    kb.dma("q", V(ctx.tri), DV(hb["tri_mask"], "h_tri"))
    kb.dma("q", V(ctx.tri2), DV(hb["tri2_mask"], "h_tri2"))
    kb.dma("q", V(ctx.e2), DV(hb["e2_mat"], "h_e2"))
    kb.dma("q", V(ctx.e2t), DV(hb["e2t_mat"], "h_e2t"))


def emit_load_x(kb, ctx, x_hbm):
    x, stage, pf = ctx.x, ctx.stage, ctx.pf
    for ti in range(NTT):
        sb = ti % 2
        stv = stage.v(es=(sb * 1024, (sb + 1) * 1024))
        kb.dma("q", stv, DV(x_hbm[ti * 128:(ti + 1) * 128, :], "x_in",
                            (0, 128, ti * 1024, (ti + 1) * 1024)))
        for half in range(2):
            bank = pf[half]
            for cj in range(4):
                ci = half * 4 + cj
                src = stage.v(es=(sb * 1024 + ci * 128, sb * 1024 + (ci + 1) * 128))
                kb.trans(V(bank, (0, 128), (cj * 128, (cj + 1) * 128)), src, V(ctx.idf))
            base = (half * 4) * T + ti * 128
            ap = bass.AP(x, base, [[NCT * T, 128], [T, 4], [1, 128]])
            dtv = TV(x, ap, 0, 128, base, base + 3 * T + 128)
            kb.op("s", lambda e, o=ap, i=bank: e.copy(out=o, in_=i[:, 0:512]),
                  reads=(V(bank, (0, 128), (0, 512)),), writes=(dtv,))


def emit_store_x(kb, ctx, out_hbm):
    x, stage, pf = ctx.x, ctx.stage, ctx.pf
    for ti in range(NTT):
        sb = ti % 2
        for half in range(2):
            bank = pf[2 + half]
            for cj in range(4):
                ci = half * 4 + cj
                src = TV(x, x[:, ci * T + ti * 128: ci * T + (ti + 1) * 128],
                         0, 128, ci * T + ti * 128, ci * T + (ti + 1) * 128)
                kb.trans(V(bank, (0, 128), (cj * 128, (cj + 1) * 128)), src, V(ctx.idf))
            kb.copy("s", stage.v(es=(sb * 1024 + half * 512, sb * 1024 + (half + 1) * 512)),
                    V(bank, (0, 128), (0, 512)))
        kb.dma("q", DV(out_hbm[ti * 128:(ti + 1) * 128, :], "x_out",
                       (0, 128, ti * 1024, (ti + 1) * 1024)),
               stage.v(es=(sb * 1024, (sb + 1) * 1024)))



def emit_postln(kb, ctx, eps=1e-5):
    x, pf, oc = ctx.x, ctx.pf, ctx.ones_col
    AFt = ctx.AF
    for tc in range(T // 512):
        cs = tc * 512
        ps_a, ps_b, ps_c, ps_d = pf[0], pf[1], pf[2], pf[3]
        for ci in range(NCT):
            kb.mm(V(ps_a, (0, 1), (0, 512)), V(oc),
                  V(x, (0, 128), (ci * T + cs, ci * T + cs + 512)),
                  start=(ci == 0), stop=(ci == NCT - 1))
        for ci in range(NCT):
            sq = Slot(AFt, AF_SQ, 1024).v(es=((ci % 2) * 512, (ci % 2) * 512 + 512))
            kb.act(sq, V(x, (0, 128), (ci * T + cs, ci * T + cs + 512)), AF.Square)
            kb.mm(V(ps_b, (0, 1), (0, 512)), V(oc), sq,
                  start=(ci == 0), stop=(ci == NCT - 1))
        A = Slot(ctx.AF, AF_SQ, 1024).v((0, 1), (0, 512))
        B = Slot(ctx.AF, AF_SQ, 1024).v((0, 1), (512, 1024))
        kb.ts("v", A, V(ps_a, (0, 1), (0, 512)), 1.0 / D, op0=ALU.mult)
        kb.tt("v", B, A, A, ALU.mult)
        kb.stt("v", B, V(ps_b, (0, 1), (0, 512)), 1.0 / D, B, ALU.mult, ALU.subtract)
        kb.act(B, B, AF.Sqrt, bias=float(eps))
        kb.recip(B, B)
        kb.tt("v", A, A, B, ALU.mult)
        kb.mm(V(ps_c, (0, 128), (0, 512)), V(ctx.ones, (0, 1), (0, 128)), B,
              start=True, stop=True)
        kb.mm(V(ps_d, (0, 128), (0, 512)), V(ctx.ones, (0, 1), (0, 128)), A,
              start=True, stop=True)
        for ci in range(NCT):
            xv = V(x, (0, 128), (ci * T + cs, ci * T + cs + 512))
            t1 = Slot(AFt, AF_LNT, 512).v()
            kb.tt("v", t1, xv, V(ps_c, (0, 128), (0, 512)), ALU.mult)
            kb.tt("v", xv, t1, V(ps_d, (0, 128), (0, 512)), ALU.subtract)



QCW = 256
NQC = T // QCW


def emit_l0_attn(kb, ctx, W):
    x, pf, AB = ctx.x, ctx.pf, ctx.AB
    kc = Slot(AB, AB_KC, 16384)
    vtm = Slot(AB, AB_VTM, 16640)
    kr4 = Slot(AB, AB_KR4, 2048)
    scl = Slot(AB, AB_SCL, 768)
    cqs = Slot(AB, AB_CQ, 8192)
    expt = Slot(AB, AB_CQ, 1024)
    ycm = Slot(AB, AB_CQ + 1024, 2048)
    xbc = Slot(AB, AB_XBC, 2048)
    qcc = Slot(AB, AB_QC, 2048)
    qrr = Slot(AB, AB_QR, 1024)
    wbf = Slot(AB, AB_WSM, 1024)
    wsm = Slot(AB, AB_WSM + 1024, 6400)
    wstg = Slot(ctx.AF, AF_WSTG, 1024)
    rbsb = Slot(ctx.AF, AF_LNT, 512)
    OF_DKV, OF_UK, OF_UV, OF_KR = 0, 8 * 256, 8 * 256 + 2 * 1024, 8 * 256 + 4 * 1024

    def stream_w(hap, hname, r0, c0, ncols, buf, width):
        stg = wstg.v((0, 128), (buf * 512, buf * 512 + ncols))
        kb.dma("q", stg, DV(hap[r0:r0 + 128, c0:c0 + ncols], hname,
                            (0, 128, r0 * 8192 + c0, r0 * 8192 + c0 + ncols)))
        wb = wbf.v((0, 128), (buf * 512, buf * 512 + ncols))
        kb.copy("p", wb, stg)
        return wb

    def load_small(hname, hap, ktiles, cols, off):
        for k in range(ktiles):
            stg = wstg.v((0, 128), ((k % 2) * 512, (k % 2) * 512 + cols))
            kb.dma("q", stg, DV(hap[k * 128:(k + 1) * 128, :], hname,
                                (0, 128, k * cols, (k + 1) * cols)))
            kb.copy("p", wsm.v((0, 128), (off + k * cols, off + (k + 1) * cols)), stg)

    load_small("w_dkv", W["w_dkv"], 8, 256, OF_DKV)
    load_small("w_uk", W["w_uk"], 2, 1024, OF_UK)
    load_small("w_uv", W["w_uv"], 2, 1024, OF_UV)
    load_small("w_kr", W["w_kr"], 8, 32, OF_KR)

    ckv = Slot(AB, AB_CQ + 4096, 4096)
    xb1 = Slot(AB, AB_CQ, 4096)

    for tc in range(4):
        cs = tc * 512
        kb.dma("q", scl.v((0, 32), (256, 768)),
               DV(W["scale_k"][0:32, cs:cs + 512], "h_sclk", (0, 32, cs, cs + 512)))
        for ci in range(NCT):
            kb.copy("v", xb1.v((0, 128), (ci * 512, (ci + 1) * 512)),
                    V(x, (0, 128), (ci * T + cs, ci * T + cs + 512)))
        for m in range(2):
            ps = pf[m]
            for ci in range(NCT):
                kb.mm(V(ps), wsm.v((0, 128), (OF_DKV + ci * 256 + m * 128,
                                              OF_DKV + ci * 256 + (m + 1) * 128)),
                      xb1.v((0, 128), (ci * 512, (ci + 1) * 512)),
                      start=(ci == 0), stop=(ci == NCT - 1))
            kb.copy("s", ckv.v((0, 128), (m * T + cs - 0 * T, m * T + cs + 512))
                    if False else ckv.v((0, 128), (m * 2048 + cs, m * 2048 + cs + 512)), V(ps))
        for m in range(8):
            ps = pf[2 + (m % 2)]
            for ci in range(2):
                kb.mm(V(ps), wsm.v((0, 128), (OF_UK + ci * 1024 + m * 128,
                                              OF_UK + ci * 1024 + (m + 1) * 128)),
                      ckv.v((0, 128), (ci * 2048 + cs, ci * 2048 + cs + 512)),
                      start=(ci == 0), stop=(ci == 1))
            kb.copy("s" if m % 2 == 0 else "v",
                    kc.v((0, 128), (m * T + cs, m * T + cs + 512)), V(ps))
        for tl in range(4):
            tt_ = tc * 4 + tl
            for n in range(2):
                ps = pf[4 + n]
                for ci in range(2):
                    kb.mm(V(ps), ckv.v((0, 128), (ci * 2048 + cs + tl * 128,
                                                  ci * 2048 + cs + (tl + 1) * 128)),
                          wsm.v((0, 128), (OF_UV + ci * 1024 + n * 512,
                                           OF_UV + ci * 1024 + (n + 1) * 512)),
                          start=(ci == 0), stop=(ci == 1))
                base = AB_VTM + tt_ * 1040 + 65 * 8 * n
                oap = bass.AP(AB, base, [[AB_END, 128], [65, 8], [1, 64]])
                kb.op("v", lambda e, o=oap, i=ps: e.tensor_copy(out=o, in_=i[:, 0:512]),
                      reads=(V(ps),), writes=(TV(AB, oap, 0, 128, base, base + 8 * 65),))
        ps = pf[0]
        for ci in range(NCT):
            kb.mm(V(ps, (0, 32), (0, 512)),
                  wsm.v((0, 128), (OF_KR + ci * 32, OF_KR + (ci + 1) * 32)),
                  xb1.v((0, 128), (ci * 512, (ci + 1) * 512)),
                  start=(ci == 0), stop=(ci == NCT - 1))
        kb.tt("v", kr4.v((0, 32), (cs, cs + 512)), V(ps, (0, 32), (0, 512)),
              scl.v((0, 32), (256, 768)), ALU.mult)
    for tt_ in range(16):
        base = AB_VTM + tt_ * 1040 + 64
        oap = bass.AP(AB, base, [[AB_END, 128], [65, 16], [1, 1]])
        kb.op("v", lambda e, o=oap: e.memset(o, 1.0),
              writes=(TV(AB, oap, 0, 128, base, base + 16 * 65),))
    for rep in range(1, 4):
        kb.dma("p", kr4.v((rep * 32, (rep + 1) * 32), (0, T)), kr4.v((0, 32), (0, T)))

    for qc in range(NQC):
        qs = qc * QCW
        kb.dma("q", scl.v((0, 128), (0, QCW)),
               DV(W["scale_q4"][0:128, qs:qs + QCW], "h_sclq", (0, 128, qs, qs + QCW)))
        for ci in range(NCT):
            kb.copy("v", xbc.v((0, 128), (ci * QCW, (ci + 1) * QCW)),
                    V(x, (0, 128), (ci * T + qs, ci * T + qs + QCW)))
        for mg in range(8):
            for k in range(NCT):
                wb = stream_w(W["w_dq"], "w_dq", k * 128, mg * 512, 512, (mg * 8 + k) % 2, 4096)
                for j in range(4):
                    kb.mm(V(pf[j], (0, 128), (0, QCW)),
                          wbf.v((0, 128), (wb.e0 - AB_WSM + j * 128, wb.e0 - AB_WSM + (j + 1) * 128)),
                          xbc.v((0, 128), (k * QCW, (k + 1) * QCW)),
                          start=(k == 0), stop=(k == NCT - 1))
            for j in range(4):
                kb.copy("s" if j % 2 == 0 else "v",
                        cqs.v((0, 128), ((mg * 4 + j) * QCW, (mg * 4 + j + 1) * QCW)),
                        V(pf[j], (0, 128), (0, QCW)))
        for mg in range(2):
            for k in range(32):
                wb = stream_w(W["w_uq"], "w_uq", k * 128, mg * 512, 512, (mg * 32 + k) % 2, 1024)
                for j in range(4):
                    kb.mm(V(pf[j], (0, 128), (0, QCW)),
                          wbf.v((0, 128), (wb.e0 - AB_WSM + j * 128, wb.e0 - AB_WSM + (j + 1) * 128)),
                          cqs.v((0, 128), (k * QCW, (k + 1) * QCW)),
                          start=(k == 0), stop=(k == 31))
            for j in range(4):
                dst = qcc.v((0, 128), ((mg * 4 + j) * QCW, (mg * 4 + j + 1) * QCW))
                kb.act(dst, V(pf[j], (0, 128), (0, QCW)), AF.Copy,
                       scale=float(1.0 / np.sqrt(96.0)))
        for k in range(32):
            wb = stream_w(W["w_qr"], "w_qr", k * 128, 0, 512, k % 2, 512)
            for j in range(4):
                kb.mm(V(pf[j], (0, 128), (0, QCW)),
                      wbf.v((0, 128), (wb.e0 - AB_WSM + j * 128, wb.e0 - AB_WSM + (j + 1) * 128)),
                      cqs.v((0, 128), (k * QCW, (k + 1) * QCW)),
                      start=(k == 0), stop=(k == 31))
        for j in range(4):
            kb.tt("v", qrr.v((0, 128), (j * QCW, (j + 1) * QCW)),
                  V(pf[j], (0, 128), (0, QCW)), scl.v((0, 128), (0, QCW)), ALU.mult)
        njt = (qc + 1) * (QCW // 128)
        for h in range(16):
            hp, ho = h // 2, (h % 2) * 64
            rp = (h % 4) * 32
            y_ps = pf[4]
            for j in range(njt):
                st = pf[j % 2]
                kb.mm(V(st, (0, 128), (0, QCW)),
                      kc.v((ho, ho + 64), (hp * T + j * 128, hp * T + (j + 1) * 128)),
                      qcc.v((ho, ho + 64), (hp * QCW, (hp + 1) * QCW)),
                      start=True, stop=False, tile_position=(ho, 0))
                kb.mm(V(st, (0, 128), (0, QCW)),
                      kr4.v((rp, rp + 32), (j * 128, (j + 1) * 128)),
                      qrr.v((rp, rp + 32), ((h // 4) * QCW, (h // 4 + 1) * QCW)),
                      start=False, stop=True, tile_position=(rp, 0))
                exoff = (j % 2) * 512
                nd = qc * (QCW // 128)
                if j >= nd:
                    a = j - nd
                    if a > 0:
                        kb.memset("v", expt.v((0, 128), (exoff, exoff + a * 128)), 0.0)
                    kb.tt("v", V(st, (0, 128), (a * 128, (a + 1) * 128)),
                          V(st, (0, 128), (a * 128, (a + 1) * 128)), V(ctx.tri), ALU.add)
                    kb.act(expt.v((0, 128), (exoff + a * 128, exoff + QCW)),
                           V(st, (0, 128), (a * 128, QCW)), AF.Exp)
                else:
                    kb.act(expt.v((0, 128), (exoff, exoff + QCW)),
                           V(st, (0, 128), (0, QCW)), AF.Exp)
                kb.mm(V(y_ps, (0, 65), (0, QCW)),
                      vtm.v((0, 128), (j * 1040 + 65 * h, j * 1040 + 65 * h + 65)),
                      expt.v((0, 128), (exoff, exoff + QCW)),
                      start=(j == 0), stop=(j == njt - 1))
            kb.recip(Slot(ctx.AF, AF_SQ, 1024).v((0, 1), (0, QCW)),
                     V(y_ps, (64, 65), (0, QCW)))
            rb_ps = pf[3]
            kb.mm(V(rb_ps, (0, 64), (0, QCW)), V(ctx.ones, (0, 1), (0, 64)),
                  Slot(ctx.AF, AF_SQ, 1024).v((0, 1), (0, QCW)), start=True, stop=True)
            kb.copy("s", rbsb.v((0, 64), (0, QCW)), V(rb_ps, (0, 64), (0, QCW)))
            kb.tt("v", ycm.v((ho, ho + 64), (hp * QCW, (hp + 1) * QCW)),
                  V(y_ps, (0, 64), (0, QCW)), rbsb.v((0, 64), (0, QCW)), ALU.mult)
        for ocg in range(2):
            for k in range(NCT):
                wb = stream_w(W["w_o"], "w_o", k * 128, ocg * 512, 512, (ocg * 8 + k) % 2, 1024)
                for j in range(4):
                    kb.mm(V(pf[j], (0, 128), (0, QCW)),
                          wbf.v((0, 128), (wb.e0 - AB_WSM + j * 128, wb.e0 - AB_WSM + (j + 1) * 128)),
                          ycm.v((0, 128), (k * QCW, (k + 1) * QCW)),
                          start=(k == 0), stop=(k == NCT - 1))
            for j in range(4):
                oc_ = ocg * 4 + j
                xv = V(x, (0, 128), (oc_ * T + qs, oc_ * T + qs + QCW))
                kb.tt("v", xv, V(pf[j], (0, 128), (0, QCW)), xv, ALU.add)



FP4_MIDS = (0.75, 1.25, 1.75, 2.5, 3.5, 5.0)


def emit_fp4_quant(kb, ctx, src, dst, W, bufA, bufB, bmrow, ladder_eng="v"):
    nb = W // 16
    bufA = TV(bufA.t, bufA.t[0:128, bufA.e0:bufA.e0 + W], 0, 128, bufA.e0, bufA.e0 + W)
    bufB = TV(bufB.t, bufB.t[0:128, bufB.e0:bufB.e0 + W], 0, 128, bufB.e0, bufB.e0 + W)

    def b3(view, blk=True):
        if blk:
            ap = bass.AP(view.t, view.e0, [[view.t.shape[1], 128], [16, nb], [1, 16]])
        else:
            ap = bass.AP(view.t, view.e0, [[view.t.shape[1], 128], [1, nb], [0, 16]])
        return TV(view.t, ap, view.p0, view.p1, view.e0, view.e1)

    bm = TV(bmrow.t, bmrow.t[0:128, bmrow.e0:bmrow.e0 + nb], 0, 128, bmrow.e0, bmrow.e0 + nb)
    sc = TV(bmrow.t, bmrow.t[0:128, bmrow.e0 + nb:bmrow.e0 + 2 * nb], 0, 128,
            bmrow.e0 + nb, bmrow.e0 + 2 * nb)
    inv = TV(bmrow.t, bmrow.t[0:128, bmrow.e0 + 2 * nb:bmrow.e0 + 3 * nb], 0, 128,
             bmrow.e0 + 2 * nb, bmrow.e0 + 3 * nb)
    invb = TV(inv.t, bass.AP(inv.t, inv.e0, [[inv.t.shape[1], 128], [1, nb], [0, 16]]),
              0, 128, inv.e0, inv.e1)
    scb = TV(sc.t, bass.AP(sc.t, sc.e0, [[sc.t.shape[1], 128], [1, nb], [0, 16]]),
             0, 128, sc.e0, sc.e1)

    kb.act(bufA, src, AF.Abs)
    kb.reduce("v", bm, b3(bufA), ALU.max)
    kb.ts("v", sc, bm, 1.0 / 6.0, 1e-12, op0=ALU.mult, op1=ALU.max)
    kb.recip(inv, sc)
    kb.tt("v", b3(bufA), b3(src), invb, ALU.mult)
    kb.act(bufB, bufA, AF.Abs)
    kb.ts(ladder_eng, bufA, bufB, 0.25, None, op0=ALU.is_gt)
    for m in FP4_MIDS:
        kb.stt(ladder_eng, bufA, bufB, float(m), bufA, ALU.is_gt, ALU.add)
    kb.ts("v", bufB, bufA, -4.0, 0.0, op0=ALU.add, op1=ALU.max)
    kb.tt("v", bufB, bufA, bufB, ALU.add)
    kb.ts("v", bufA, bufA, -6.0, 0.0, op0=ALU.add, op1=ALU.max)
    kb.stt("v", bufA, bufB, 0.5, bufA, ALU.mult, ALU.add)
    kb.ts("v", bufB, src, 0.0, -2.0, op0=ALU.is_lt, op1=ALU.mult)
    kb.ts("v", bufB, bufB, 1.0, None, op0=ALU.add)
    kb.tt("v", bufA, bufA, bufB, ALU.mult)
    kb.tt("v", b3(dst), b3(bufA), scb, ALU.mult)


FB_XQT = 0
FB_XQC = 4096
FB_HQ = 8192
FB_HQC = 8704
FB_WG = 9216
FB_WD = 17408
FB_WQS = 21504


def emit_l0_ffn(kb, ctx, W):
    x, pf, pb, AB, AFt = ctx.x, ctx.pf, ctx.pb, ctx.AB, ctx.AF
    NCH = [(0, 512), (512, 512), (1024, 512), (1536, 512), (2048, 512), (2560, 256)]

    accs = Slot(AFt, AF_ACC, 4096)
    wqs = Slot(AB, FB_WQS, 1024)

    def quant_w(hap, hname, nrows, ncols, dst_ap, dst_name, dst_cols):
        ccs = []
        c0_ = 0
        while c0_ < ncols:
            cw_ = min(512, ncols - c0_)
            ccs.append((c0_, cw_))
            c0_ += cw_
        for r in range(nrows // 128):
            for c0, cw in ccs:
                piece = accs.v((0, 128), (0, cw))
                kb.dma("q", piece, DV(hap[r * 128:(r + 1) * 128, c0:c0 + cw], hname,
                                      (0, 128, r * ncols + c0, r * ncols + c0 + cw)))
                bufA = accs.v((0, 128), (512, 1024))
                bufB = accs.v((0, 128), (1024, 1536))
                bmr = accs.v((0, 128), (1536, 1536 + 96))
                q = wqs.v((0, 128), (0, cw))
                emit_fp4_quant(kb, ctx, piece, q, cw, bufA, bufB, bmr)
                for j in range(cw // 128):
                    bk = pb[j % 2]
                    kb.trans(V(bk, (0, 128), (0, 128)),
                             wqs.v((0, 128), (j * 128, (j + 1) * 128)), V(ctx.idb))
                    st = wqs.v((0, 128), (512 + j * 128, 512 + (j + 1) * 128))
                    kb.copy("s", st, V(bk, (0, 128), (0, 128)))
                    kb.dma("p", DV(dst_ap[c0 + j * 128:c0 + (j + 1) * 128,
                                          r * 128:(r + 1) * 128], dst_name,
                                   (0, 128, (c0 + j * 128) * dst_cols + r * 128,
                                    (c0 + j * 128) * dst_cols + r * 128 + 128)),
                           st)

    quant_w(W["w_gate"], "w_gate", 2816, 1024, W["wqgT"], "wqgT", 2816)
    quant_w(W["w_up"], "w_up", 2816, 1024, W["wquT"], "wquT", 2816)
    quant_w(W["w_down"], "w_down", 1024, 2816, W["wqdT"], "wqdT", 1024)

    xqt = Slot(AB, FB_XQT, 4096)
    xqc = Slot(AB, FB_XQC, 4096)
    hq = Slot(AB, FB_HQ, 512)
    hqc = Slot(AB, FB_HQC, 512)
    wg = Slot(AB, FB_WG, 8192)
    wd = Slot(AB, FB_WD, 4096)
    sqs = Slot(AFt, AF_SQ, 1024)
    wstg = Slot(AFt, AF_WSTG, 1024)
    lnt = Slot(AFt, AF_LNT, 512)

    for tc in range(4):
        cs = tc * 512
        for tl in range(4):
            for half in range(2):
                bank = pf[half]
                for cj in range(4):
                    ci = half * 4 + cj
                    kb.trans(V(bank, (0, 128), (cj * 128, (cj + 1) * 128)),
                             V(x, (0, 128), (ci * T + cs + tl * 128, ci * T + cs + (tl + 1) * 128)),
                             V(ctx.idf))
                base = tl * 1024 + half * 512
                kb.copy("s", accs.v((0, 128), (base, base + 512)), V(bank, (0, 128), (0, 512)))
        for tl in range(4):
            for half in range(2):
                base = tl * 1024 + half * 512
                emit_fp4_quant(kb, ctx, accs.v((0, 128), (base, base + 512)),
                               xqt.v((0, 128), (base, base + 512)), 512,
                               sqs.v((0, 128), (0, 512)), sqs.v((0, 128), (512, 1024)),
                               lnt.v((0, 128), (0, 96)))
        for tl in range(4):
            for half in range(2):
                bank = pb[half]
                for cj in range(4):
                    ci = half * 4 + cj
                    kb.trans(V(bank, (0, 128), (cj * 128, (cj + 1) * 128)),
                             xqt.v((0, 128), (tl * 1024 + ci * 128, tl * 1024 + (ci + 1) * 128)),
                             V(ctx.idb))
                base = (half * 4) * 512 + tl * 128
                ap = bass.AP(AB, FB_XQC + base, [[AB_END, 128], [512, 4], [1, 128]])
                kb.op("v", lambda e, o=ap, i=bank: e.tensor_copy(out=o, in_=i[:, 0:512]),
                      reads=(V(bank, (0, 128), (0, 512)),),
                      writes=(TV(AB, ap, 0, 128, FB_XQC + base, FB_XQC + base + 3 * 512 + 128),))
        for ci in range(NCT):
            kb.memset("v", accs.v((0, 128), (ci * 512, (ci + 1) * 512)), 0.0)
        for n0, nw in NCH:
            nk = nw // 128
            for k in range(NCT):
                kb.dma("q", wg.v((0, 128), (k * 512, k * 512 + nw)),
                       DV(W["wqgT"][k * 128:(k + 1) * 128, n0:n0 + nw], "wqgT",
                          (0, 128, k * 2816 + n0, k * 2816 + n0 + nw)))
                kb.dma("p", wg.v((0, 128), (4096 + k * 512, 4096 + k * 512 + nw)),
                       DV(W["wquT"][k * 128:(k + 1) * 128, n0:n0 + nw], "wquT",
                          (0, 128, 2 ** 23 + k * 2816 + n0, 2 ** 23 + k * 2816 + n0 + nw)))
            for kk in range(nk):
                kb.dma("q", wd.v((0, 128), (kk * 1024, (kk + 1) * 1024)),
                       DV(W["wqdT"][n0 + kk * 128:n0 + (kk + 1) * 128, 0:1024], "wqdT",
                          (0, 128, (n0 + kk * 128) * 1024, (n0 + kk * 128) * 1024 + 1024)))
            for tl in range(4):
                ps_g, ps_u = pf[2], pf[3]
                for k in range(NCT):
                    kb.mm(V(ps_g, (0, 128), (0, nw)),
                          xqc.v((0, 128), (k * 512 + tl * 128, k * 512 + (tl + 1) * 128)),
                          wg.v((0, 128), (k * 512, k * 512 + nw)),
                          start=(k == 0), stop=(k == NCT - 1))
                for k in range(NCT):
                    kb.mm(V(ps_u, (0, 128), (0, nw)),
                          xqc.v((0, 128), (k * 512 + tl * 128, k * 512 + (tl + 1) * 128)),
                          wg.v((0, 128), (4096 + k * 512, 4096 + k * 512 + nw)),
                          start=(k == 0), stop=(k == NCT - 1))
                sg = wstg.v((0, 128), (512, 512 + nw))
                kb.act(sg, V(ps_g, (0, 128), (0, nw)), AF.Silu)
                hh = wstg.v((0, 128), (0, nw))
                kb.tt("v", hh, sg, V(ps_u, (0, 128), (0, nw)), ALU.mult)
                emit_fp4_quant(kb, ctx, hh, hq.v((0, 128), (0, nw)), nw,
                               sqs.v((0, 128), (0, 512)), sqs.v((0, 128), (512, 1024)),
                               lnt.v((0, 128), (0, 96)), ladder_eng="v")
                for kk in range(nk):
                    bk = pb[kk % 2]
                    kb.trans(V(bk, (0, 128), (0, 128)),
                             hq.v((0, 128), (kk * 128, (kk + 1) * 128)), V(ctx.idb))
                    kb.copy("s", hqc.v((0, 128), (kk * 128, (kk + 1) * 128)),
                            V(bk, (0, 128), (0, 128)))
                for oc in range(NCT):
                    ps_d = pf[4 + (oc % 2)]
                    for kk in range(nk):
                        kb.mm(V(ps_d, (0, 128), (0, 128)),
                              wd.v((0, 128), (kk * 1024 + oc * 128, kk * 1024 + (oc + 1) * 128)),
                              hqc.v((0, 128), (kk * 128, (kk + 1) * 128)),
                              start=(kk == 0), stop=(kk == nk - 1))
                    av = accs.v((0, 128), (oc * 512 + tl * 128, oc * 512 + (tl + 1) * 128))
                    kb.tt("v", av, V(ps_d, (0, 128), (0, 128)), av, ALU.add)
        for ci in range(NCT):
            xv = V(x, (0, 128), (ci * T + cs, ci * T + cs + 512))
            kb.tt("v", xv, accs.v((0, 128), (ci * 512, (ci + 1) * 512)), xv, ALU.add)



LA_WT = 0
LA_XBC = 24576
LA_WST = 28672


def emit_l1_attn(kb, ctx, W):
    x, pf, pb, AB, AFt = ctx.x, ctx.pf, ctx.pb, ctx.AB, ctx.AF
    wts = Slot(AB, LA_WT, 24576)
    xbc = Slot(AB, LA_XBC, 4096)
    wst = Slot(AB, LA_WST, 1024)
    accs = Slot(AFt, AF_ACC, 4096)
    wstg = Slot(AFt, AF_WSTG, 1024)
    sqs = Slot(AFt, AF_SQ, 1024)
    biases = kb.sbuf("l1a_bias", [128, 24], F32)
    carry = kb.sbuf("l1a_carry", [128, 8], F32)

    for wi, bn in enumerate(("b_f", "b_i", "b_h")):
        kb.dma_slow("q", V(biases, (0, 128), (wi * 8, wi * 8 + 8)),
                    DV(W[bn].rearrange("(c p) -> p c", p=128), "h_" + bn))
    kb.memset("v", V(carry), 0.0)

    for wi, wn in enumerate(("w_f", "w_i", "w_h")):
        for r in range(8):
            stg = wstg.v((0, 128), (0, 1024))
            kb.dma("q", stg, DV(W[wn][r * 128:(r + 1) * 128, :], wn,
                                (0, 128, r * 1024, (r + 1) * 1024)))
            wb = wst.v((0, 128), (0, 1024))
            kb.copy("p", wb, stg)
            for k in range(8):
                bk = pb[k % 2]
                kb.trans(V(bk, (0, 128), (0, 128)),
                         wst.v((0, 128), (k * 128, (k + 1) * 128)), V(ctx.idb))
                kb.copy("s", wts.v((0, 128), (wi * 8192 + k * 1024 + r * 128,
                                              wi * 8192 + k * 1024 + (r + 1) * 128)),
                        V(bk, (0, 128), (0, 128)))

    F_, I_, H_ = 0, 512, 1024
    S1, S2, S3 = 1536, 2048, 2560
    for tc in range(4):
        cs = tc * 512
        for ci in range(NCT):
            kb.copy("v", xbc.v((0, 128), (ci * 512, (ci + 1) * 512)),
                    V(x, (0, 128), (ci * T + cs, ci * T + cs + 512)))
        for m in range(NCT):
            for wi in range(3):
                ps = pf[2 + wi]
                for k in range(NCT):
                    kb.mm(V(ps), wts.v((0, 128), (wi * 8192 + k * 1024 + m * 128,
                                                  wi * 8192 + k * 1024 + (m + 1) * 128)),
                          xbc.v((0, 128), (k * 512, (k + 1) * 512)),
                          start=(k == 0), stop=(k == NCT - 1))
            FB = accs.v((0, 128), (F_, F_ + 512))
            IB = accs.v((0, 128), (I_, I_ + 512))
            HB = accs.v((0, 128), (H_, H_ + 512))
            kb.act(FB, V(pf[2]), AF.Identity, bias=V(biases, (0, 128), (m, m + 1)))
            kb.act(IB, V(pf[3]), AF.Identity, bias=V(biases, (0, 128), (8 + m, 8 + m + 1)))
            kb.act(HB, V(pf[4]), AF.Identity, bias=V(biases, (0, 128), (16 + m, 16 + m + 1)))
            A = accs.v((0, 128), (S1, S1 + 512))
            B = accs.v((0, 128), (S2, S2 + 512))
            C = accs.v((0, 128), (S3, S3 + 512))
            kb.act(A, FB, AF.Abs)
            kb.act(A, A, AF.Exp, scale=-1.0)
            kb.act(A, A, AF.Ln, bias=1.0)
            kb.ts("v", B, FB, -1.0, 0.0, op0=ALU.mult, op1=ALU.max)
            kb.tt("v", A, A, B, ALU.add)
            kb.act(B, IB, AF.Abs)
            kb.act(B, B, AF.Exp, scale=-1.0)
            kb.act(B, B, AF.Ln, bias=1.0)
            kb.ts("v", C, IB, -1.0, 0.0, op0=ALU.mult, op1=ALU.max)
            kb.tt("v", B, B, C, ALU.add)
            kb.tt("v", A, A, B, ALU.subtract)
            kb.act(B, A, AF.Sigmoid, scale=-1.0)
            kb.act(A, A, AF.Sigmoid)
            kb.act(C, HB, AF.Sigmoid)
            kb.ts("v", FB, HB, 0.5, None, op0=ALU.add)
            kb.tt("v", FB, FB, C, ALU.subtract)
            kb.ts("v", IB, HB, 0.0, None, op0=ALU.is_ge)
            kb.tt("v", FB, FB, IB, ALU.mult)
            kb.tt("v", C, C, FB, ALU.add)
            kb.tt("v", A, A, C, ALU.mult)
            kb.scan(HB, B, A, V(carry, (0, 128), (m, m + 1)), op0=ALU.mult, op1=ALU.add)
            kb.copy("v", V(carry, (0, 128), (m, m + 1)),
                    TV(accs.th, accs.th[0:128, accs.off + H_ + 511:accs.off + H_ + 512],
                       0, 128, accs.off + H_ + 511, accs.off + H_ + 512))
            xv = V(x, (0, 128), (m * T + cs, m * T + cs + 512))
            kb.tt("v", xv, HB, xv, ALU.add)



LF_QCM = 0
LF_KCM = 8192
LF_VTM = 16384
LF_OCM = 24576
LF_HB1 = 24576
LF_XBC = 35840
LF_WS1 = 39936
LF_PA = 41984
LF_PB = 42496
LF_PT = 42624
LF_WS2 = 43264


def emit_l1_ffn(kb, ctx, W):
    x, pf, pb, AB, AFt = ctx.x, ctx.pf, ctx.pb, ctx.AB, ctx.AF
    qcm = Slot(AB, LF_QCM, 8192)
    kcm = Slot(AB, LF_KCM, 8192)
    vtm = Slot(AB, LF_VTM, 8192)
    ocm = Slot(AB, LF_OCM, 8192)
    hb1 = Slot(AB, LF_HB1, 11264)
    xbc = Slot(AB, LF_XBC, 4096)
    ws1 = Slot(AB, LF_WS1, 2048)
    ws2 = Slot(AB, LF_WS2, 2048)
    pA = Slot(AB, LF_PA, 512)
    pB = Slot(AB, LF_PB, 128)
    pts = Slot(AB, LF_PT, 640)
    accs = Slot(AFt, AF_ACC, 4096)
    sqs = Slot(AFt, AF_SQ, 1024)
    wstg = Slot(AFt, AF_WSTG, 1024)
    lnt = Slot(AFt, AF_LNT, 512)
    qks = kb.sbuf("lf_qks", [128, 4], F32)
    lng = kb.sbuf("lf_lng", [128, 16], F32)
    lnb = kb.sbuf("lf_lnb", [128, 16], F32)
    b1a = kb.sbuf("lf_b1a", [128, 44], F32)
    b1g = kb.sbuf("lf_b1g", [128, 44], F32)
    b2c = kb.sbuf("lf_b2", [128, 16], F32)
    tabs = Slot(AFt, AF_ACC + 2048, 2048)

    for c4 in range(4):
        kb.dma("q", DV(W["xres"][0:128, c4 * 4096:(c4 + 1) * 4096], "xres",
                       (0, 128, c4 * 4096, (c4 + 1) * 4096)),
               V(x, (0, 128), (c4 * 4096, (c4 + 1) * 4096)))
    kb.dma("q", V(qks), DV(W["qk_scale_l"], "h_qks"))
    kb.dma("q", V(lng), DV(W["lng_l"], "h_lng"))
    kb.dma("q", V(lnb), DV(W["lnb_l"], "h_lnb"))
    kb.dma("q", V(b1a), DV(W["b1a_l"], "h_b1a"))
    kb.dma("q", V(b1g), DV(W["b1g_l"], "h_b1g"))
    kb.dma("q", V(b2c), DV(W["b2_l"], "h_b2"))

    def shift_chunk(cs, dst_slot):
        for ci in range(NCT):
            dv = dst_slot.v((0, 128), (ci * 512, (ci + 1) * 512))
            if ci < 4:
                kb.copy("v", dv, V(x, (0, 128), (ci * T + cs, ci * T + cs + 512)))
            elif cs > 0:
                kb.copy("v", dv, V(x, (0, 128), (ci * T + cs - 1, ci * T + cs + 511)))
            else:
                kb.memset("v", dst_slot.v((0, 128), (ci * 512, ci * 512 + 1)), 0.0)
                kb.copy("v", dst_slot.v((0, 128), (ci * 512 + 1, (ci + 1) * 512)),
                        V(x, (0, 128), (ci * T, ci * T + 511)))

    for l in range(2):
        wqkv, wo, w1, w2 = W["wqkv"], W["wo"], W["w1"], W["w2"]
        for tc in range(4):
            cs = tc * 512
            shift_chunk(cs, xbc)
            for tb in range(4):
                kb.dma("q", tabs.v((0, 128), (tb * 512, (tb + 1) * 512)),
                       DV(W[("xp_csq", "xp_ssq", "xp_csk", "xp_ssk")[tb]][0:128, cs:cs + 512],
                          "h_xt%d" % tb, (0, 128, cs, cs + 512)))
            for kind in range(2):
                for k in range(NCT):
                    stg = wstg.v((0, 128), ((k % 2) * 512, (k % 2) * 512 + 512))
                    kb.dma("q", stg, DV(wqkv[l, k * 128:(k + 1) * 128,
                                            kind * 512:(kind + 1) * 512], "wqkv",
                                        (l, l + 1, k * 1536 + kind * 512,
                                         k * 1536 + (kind + 1) * 512)))
                    wb = ws1.v((0, 128), ((k % 2) * 512, (k % 2) * 512 + 512))
                    kb.copy("p", wb, stg)
                    for j in range(4):
                        kb.mm(V(pf[j]), ws1.v((0, 128), ((k % 2) * 512 + j * 128,
                                                         (k % 2) * 512 + (j + 1) * 128)),
                              xbc.v((0, 128), (k * 512, (k + 1) * 512)),
                              start=(k == 0), stop=(k == NCT - 1))
                dst = qcm if kind == 0 else kcm
                co, so = (0, 0) if kind == 0 else (2 * 512, 2)
                for j in range(4):
                    ps = pf[j]
                    qf = accs.v((0, 128), (0, 512))
                    kb.act(qf, V(ps), AF.Copy)
                    sq = sqs.v((0, 128), (0, 512))
                    kb.act(sq, V(ps), AF.Square)
                    n2 = pf[4]
                    kb.mm(V(n2, (0, 2), (0, 512)), V(ctx.e2), sq, start=True, stop=True)
                    nr = lnt.v((0, 2), (0, 512))
                    kb.act(nr, V(n2, (0, 2), (0, 512)), AF.Sqrt)
                    kb.ts("v", nr, nr, 1e-12, None, op0=ALU.max)
                    kb.recip(nr, nr)
                    rb = pf[5]
                    kb.mm(V(rb, (0, 128), (0, 512)), V(ctx.e2t, (0, 2), (0, 128)), nr,
                          start=True, stop=True)
                    qh = accs.v((0, 128), (512, 1024))
                    kb.tt("v", qh, qf, V(rb, (0, 128), (0, 512)), ALU.mult)
                    kb.ts("v", qh, qh, V(qks, (0, 128), (l * 2 + so // 2, l * 2 + so // 2 + 1)),
                          None, op0=ALU.mult)
                    qrot = accs.v((0, 128), (1024, 1536))
                    for (pd, psr) in ((0, 32), (32, 0), (64, 96), (96, 64)):
                        kb.copy("v", TV(AFt, AFt[pd:pd + 32, accs.off + 1024:accs.off + 1536],
                                        pd, pd + 32, accs.off + 1024, accs.off + 1536),
                                TV(AFt, AFt[psr:psr + 32, accs.off + 512:accs.off + 1024],
                                   psr, psr + 32, accs.off + 512, accs.off + 1024))
                    kb.tt("v", qh, qh, tabs.v((0, 128), (co, co + 512)), ALU.mult)
                    kb.tt("v", qrot, qrot, tabs.v((0, 128), (co + 512, co + 1024)), ALU.mult)
                    kb.tt("v", dst.v((0, 128), (j * 2048 + cs, j * 2048 + cs + 512)),
                          qh, qrot, ALU.add)
            for half in range(2):
                for kk in range(4):
                    k = half * 4 + kk
                    stg = wstg.v((0, 128), ((k % 2) * 512, (k % 2) * 512 + 512))
                    kb.dma("q", stg, DV(wqkv[l, k * 128:(k + 1) * 128, 1024:1536], "wqkv",
                                        (l, l + 1, k * 1536 + 1024, k * 1536 + 1536)))
                    kb.copy("p", ws2.v((0, 128), (kk * 512, (kk + 1) * 512)), stg)
                for tl in range(4):
                    ps = pf[2 + tl]
                    for kk in range(4):
                        k = half * 4 + kk
                        kb.mm(V(ps), xbc.v((0, 128), (k * 512 + tl * 128, k * 512 + (tl + 1) * 128)),
                              ws2.v((0, 128), (kk * 512, (kk + 1) * 512)),
                              start=(kk == 0 and half == 0), stop=(kk == 3 and half == 1))
                    if half == 1:
                        kb.copy("s", vtm.v((0, 128), ((tc * 4 + tl) * 512, (tc * 4 + tl + 1) * 512)),
                                V(ps))
        for h in range(8):
            hp, ho = h // 2, (h % 2) * 64
            for qtc in range(16):
                jlo = max(0, qtc - 4)
                noff = qtc - jlo
                bA, bB = pf[0], pf[1]
                if noff > 0:
                    kb.mm(V(bA, (0, 128), (0, noff * 128)),
                          qcm.v((ho, ho + 64), (hp * 2048 + qtc * 128, hp * 2048 + (qtc + 1) * 128)),
                          kcm.v((ho, ho + 64), (hp * 2048 + jlo * 128, hp * 2048 + qtc * 128)),
                          start=True, stop=True, tile_position=(ho, 0))
                kb.mm(V(bB, (0, 128), (0, 128)),
                      qcm.v((ho, ho + 64), (hp * 2048 + qtc * 128, hp * 2048 + (qtc + 1) * 128)),
                      kcm.v((ho, ho + 64), (hp * 2048 + qtc * 128, hp * 2048 + (qtc + 1) * 128)),
                      start=True, stop=True, tile_position=(ho, 0))
                if noff == 4:
                    kb.tt("v", V(bA, (0, 128), (0, 128)), V(bA, (0, 128), (0, 128)),
                          V(ctx.tri), ALU.add)
                kb.tt("v", V(bB, (0, 128), (0, 128)), V(bB, (0, 128), (0, 128)),
                      V(ctx.tri2), ALU.add)
                mx = lnt.v((0, 128), (0, 1))
                m2x = lnt.v((0, 128), (1, 2))
                if noff > 0:
                    kb.reduce("v", mx, V(bA, (0, 128), (0, noff * 128)), ALU.max)
                    kb.reduce("v", m2x, V(bB, (0, 128), (0, 128)), ALU.max)
                    kb.tt("v", mx, mx, m2x, ALU.max)
                else:
                    kb.reduce("v", mx, V(bB, (0, 128), (0, 128)), ALU.max)
                negm = lnt.v((0, 128), (2, 3))
                kb.ts("v", negm, mx, -1.0, None, op0=ALU.mult)
                sA = lnt.v((0, 128), (3, 4))
                sB = lnt.v((0, 128), (4, 5))
                if noff > 0:
                    kb.act(pA.v((0, 128), (0, noff * 128)), V(bA, (0, 128), (0, noff * 128)),
                           AF.Exp, bias=negm, accum=sA)
                kb.act(pB.v((0, 128), (0, 128)), V(bB, (0, 128), (0, 128)),
                       AF.Exp, bias=negm, accum=sB)
                den = lnt.v((0, 128), (5, 6))
                if noff > 0:
                    kb.tt("v", den, sA, sB, ALU.add)
                else:
                    kb.copy("v", den, sB)
                kb.recip(den, den)
                if noff > 0:
                    kb.ts("v", pA.v((0, 128), (0, noff * 128)),
                          pA.v((0, 128), (0, noff * 128)), den, None, op0=ALU.mult)
                kb.ts("v", pB.v((0, 128), (0, 128)), pB.v((0, 128), (0, 128)),
                      den, None, op0=ALU.mult)
                for jj in range(noff + 1):
                    bk = pb[jj % 2]
                    src = (pA.v((0, 128), (jj * 128, (jj + 1) * 128)) if jj < noff
                           else pB.v((0, 128), (0, 128)))
                    kb.trans(V(bk, (0, 128), (0, 128)), src, V(ctx.idb))
                    kb.copy("s", pts.v((0, 128), (jj * 128, (jj + 1) * 128)),
                            V(bk, (0, 128), (0, 128)))
                yb = pf[2]
                for jj in range(noff + 1):
                    j = jlo + jj
                    kb.mm(V(yb, (0, 64), (0, 128)),
                          vtm.v((0, 128), (j * 512 + h * 64, j * 512 + (h + 1) * 64)),
                          pts.v((0, 128), (jj * 128, (jj + 1) * 128)),
                          start=(jj == 0), stop=(jj == noff))
                kb.copy("s", ocm.v((ho, ho + 64), (hp * 2048 + qtc * 128,
                                                   hp * 2048 + (qtc + 1) * 128)),
                        V(yb, (0, 64), (0, 128)))
        for tc in range(4):
            cs = tc * 512
            for mg in range(2):
                for k in range(4):
                    stg = wstg.v((0, 128), ((k % 2) * 512, (k % 2) * 512 + 512))
                    kb.dma("q", stg, DV(wo[l, k * 128:(k + 1) * 128, mg * 512:(mg + 1) * 512],
                                        "wo", (l, l + 1, k * 1024 + mg * 512,
                                               k * 1024 + (mg + 1) * 512)))
                    wb = ws1.v((0, 128), ((k % 2) * 512, (k % 2) * 512 + 512))
                    kb.copy("p", wb, stg)
                    for j in range(4):
                        kb.mm(V(pf[j]), ws1.v((0, 128), ((k % 2) * 512 + j * 128,
                                                         (k % 2) * 512 + (j + 1) * 128)),
                              ocm.v((0, 128), (k * 2048 + cs, k * 2048 + cs + 512)),
                              start=(k == 0), stop=(k == 3))
                for j in range(4):
                    oc_ = mg * 4 + j
                    xv = V(x, (0, 128), (oc_ * T + cs, oc_ * T + cs + 512))
                    kb.tt("v", xv, V(pf[j]), xv, ALU.add)
        AW = [(mi * 512, min(512, 2730 - mi * 512)) for mi in range(6)]
        for tc in range(4):
            cs = tc * 512
            ps_a, ps_b = pf[0], pf[1]
            first = True
            for ci in range(NCT):
                xv = (V(x, (0, 128), (ci * T + cs, ci * T + cs + 512)) if ci < 4 else
                      (V(x, (0, 128), (ci * T + cs - 1, ci * T + cs + 511)) if cs > 0 else
                       V(x, (0, 128), (ci * T, ci * T + 511))))
                ov = (V(ps_a, (0, 1), (0, 512)) if (ci < 4 or cs > 0) else
                      V(ps_a, (0, 1), (1, 512)))
                kb.mm(ov, V(ctx.ones_col), xv, start=first, stop=(ci == NCT - 1))
                first = False
            first = True
            for ci in range(NCT):
                xv = (V(x, (0, 128), (ci * T + cs, ci * T + cs + 512)) if ci < 4 else
                      (V(x, (0, 128), (ci * T + cs - 1, ci * T + cs + 511)) if cs > 0 else
                       V(x, (0, 128), (ci * T, ci * T + 511))))
                w_ = 512 if (ci < 4 or cs > 0) else 511
                sq = sqs.v((0, 128), (0, w_))
                kb.act(sq, xv, AF.Square)
                ov = (V(ps_b, (0, 1), (0, 512)) if w_ == 512 else V(ps_b, (0, 1), (1, 512)))
                kb.mm(ov, V(ctx.ones_col), sq, start=first, stop=(ci == NCT - 1))
                first = False
            if cs == 0:
                pass
            A = sqs.v((0, 1), (0, 512))
            B = sqs.v((0, 1), (512, 1024))
            kb.ts("v", A, V(ps_a, (0, 1), (0, 512)), 1.0 / D, op0=ALU.mult)
            kb.tt("v", B, A, A, ALU.mult)
            kb.stt("v", B, V(ps_b, (0, 1), (0, 512)), 1.0 / D, B, ALU.mult, ALU.subtract)
            kb.act(B, B, AF.Sqrt, bias=1e-5)
            kb.recip(B, B)
            kb.tt("v", A, A, B, ALU.mult)
            kb.mm(V(pf[2], (0, 128), (0, 512)), V(ctx.ones, (0, 1), (0, 128)), B,
                  start=True, stop=True)
            kb.mm(V(pf[3], (0, 128), (0, 512)), V(ctx.ones, (0, 1), (0, 128)), A,
                  start=True, stop=True)
            for ci in range(NCT):
                xv = (V(x, (0, 128), (ci * T + cs, ci * T + cs + 512)) if ci < 4 else
                      (V(x, (0, 128), (ci * T + cs - 1, ci * T + cs + 511)) if cs > 0 else
                       V(x, (0, 128), (ci * T, ci * T + 511))))
                w_ = 512 if (ci < 4 or cs > 0) else 511
                off = 512 - w_
                t1 = accs.v((0, 128), (0, 512))
                t1w = accs.v((0, 128), (off, 512))
                if off:
                    kb.memset("v", accs.v((0, 128), (0, 1)), 0.0)
                kb.tt("v", t1w, xv, V(pf[2], (0, 128), (off, 512)), ALU.mult)
                kb.tt("v", t1, t1, V(pf[3], (0, 128), (0, 512)), ALU.subtract)
                kb.ts("v", xbc.v((0, 128), (ci * 512, (ci + 1) * 512)), t1,
                      V(lng, (0, 128), (l * 8 + ci, l * 8 + ci + 1)),
                      V(lnb, (0, 128), (l * 8 + ci, l * 8 + ci + 1)),
                      op0=ALU.mult, op1=ALU.add)
            for gi, (g0, gw) in enumerate(AW):
                nmc = (gw + 127) // 128
                abuf = accs.v((0, 128), (0, 2048))
                for k in range(NCT):
                    stg = wstg.v((0, 128), ((k % 2) * 512, (k % 2) * 512 + gw))
                    kb.dma("q", stg, DV(w1[l, k * 128:(k + 1) * 128, g0:g0 + gw], "w1",
                                        (l, l + 1, k * 5460 + g0, k * 5460 + g0 + gw)))
                    wb = ws1.v((0, 128), ((k % 2) * 512, (k % 2) * 512 + gw))
                    kb.copy("p", wb, stg)
                    for j in range(nmc):
                        mw = min(128, gw - j * 128)
                        kb.mm(V(pf[j], (0, mw), (0, 512)),
                              ws1.v((0, 128), ((k % 2) * 512 + j * 128, (k % 2) * 512 + j * 128 + mw)),
                              xbc.v((0, 128), (k * 512, (k + 1) * 512)),
                              start=(k == 0), stop=(k == NCT - 1))
                for j in range(nmc):
                    mw = min(128, gw - j * 128)
                    mi = (g0 + j * 128) // 128
                    kb.act(TV(AFt, AFt[0:mw, accs.off + j * 512:accs.off + (j + 1) * 512],
                              0, mw, accs.off + j * 512, accs.off + (j + 1) * 512),
                           V(pf[j], (0, mw), (0, 512)), AF.Identity,
                           bias=V(b1a, (0, mw), (l * 22 + mi, l * 22 + mi + 1)))
                for k in range(NCT):
                    stg = wstg.v((0, 128), ((k % 2) * 512, (k % 2) * 512 + gw))
                    kb.dma("q", stg, DV(w1[l, k * 128:(k + 1) * 128, 2730 + g0:2730 + g0 + gw],
                                        "w1", (l, l + 1, k * 5460 + 2730 + g0,
                                               k * 5460 + 2730 + g0 + gw)))
                    kb.copy("p", ws1.v((0, 128), ((k % 2) * 512, (k % 2) * 512 + gw)), stg)
                    for j in range(nmc):
                        mw = min(128, gw - j * 128)
                        kb.mm(V(pf[j], (0, mw), (0, 512)),
                              ws1.v((0, 128), ((k % 2) * 512 + j * 128, (k % 2) * 512 + j * 128 + mw)),
                              xbc.v((0, 128), (k * 512, (k + 1) * 512)),
                              start=(k == 0), stop=(k == NCT - 1))
                for j in range(nmc):
                    mw = min(128, gw - j * 128)
                    mi = (g0 + j * 128) // 128
                    gb = sqs.v((0, mw), (0, 512))
                    kb.act(gb, V(pf[j], (0, mw), (0, 512)), AF.Gelu,
                           bias=V(b1g, (0, mw), (l * 22 + mi, l * 22 + mi + 1)))
                    kb.tt("v", hb1.v((0, mw), (mi * 512, (mi + 1) * 512)),
                          TV(AFt, AFt[0:mw, accs.off + j * 512:accs.off + (j + 1) * 512],
                             0, mw, accs.off + j * 512, accs.off + (j + 1) * 512),
                          gb, ALU.mult)
            for mg in range(2):
                for kk in range(22):
                    kw = min(128, 2730 - kk * 128)
                    stg = wstg.v((0, kw), ((kk % 2) * 512, (kk % 2) * 512 + 512))
                    kb.dma("q", stg, DV(w2[l, kk * 128:kk * 128 + kw, mg * 512:(mg + 1) * 512],
                                        "w2", (l, l + 1, kk * 1024 + mg * 512,
                                               kk * 1024 + (mg + 1) * 512)))
                    kb.copy("p", ws1.v((0, kw), ((kk % 2) * 512, (kk % 2) * 512 + 512)), stg)
                    for j in range(4):
                        kb.mm(V(pf[j]), ws1.v((0, kw), ((kk % 2) * 512 + j * 128,
                                                        (kk % 2) * 512 + (j + 1) * 128)),
                              hb1.v((0, kw), (kk * 512, (kk + 1) * 512)),
                              start=(kk == 0), stop=(kk == 21))
                for j in range(4):
                    oc_ = mg * 4 + j
                    xv = V(x, (0, 128), (oc_ * T + cs, oc_ * T + cs + 512))
                    kb.stt("v", xv, V(pf[j]),
                           V(b2c, (0, 128), (l * 8 + oc_, l * 8 + oc_ + 1)), xv,
                           ALU.add, ALU.add)
    for c8 in range(8):
        xrs = accs.v((0, 128), (0, 2048))
        kb.dma("q", xrs, DV(W["xres"][0:128, c8 * 2048:(c8 + 1) * 2048], "xres",
                            (0, 128, c8 * 2048, (c8 + 1) * 2048)))
        xv = V(x, (0, 128), (c8 * 2048, (c8 + 1) * 2048))
        kb.tt("v", xv, xrs, xv, ALU.add)



_PROG_CACHE = {}


def _np_dt(a):
    return F32 if a.dtype == np.float32 else BF16


def _declare_consts(nc, consts):
    return {n: nc.dram_tensor(n, list(consts[n].shape), _np_dt(consts[n]),
                              kind="ExternalInput").ap() for n in CONST_NAMES}


def _build_block(block):
    consts = host_consts()
    nc = bass.Bass("TRN2", target_bir_lowering=False)
    x_h = nc.dram_tensor("x", [T, D], F32, kind="ExternalInput").ap()
    o_h = nc.dram_tensor("o", [T, D], F32, kind="ExternalOutput").ap()
    hb = _declare_consts(nc, consts)
    W = {}
    if block == "l0a":
        shapes = dict(w_dq=(1024, 4096), w_uq=(4096, 1024), w_dkv=(1024, 256),
                      w_uk=(256, 1024), w_uv=(256, 1024), w_qr=(4096, 512),
                      w_kr=(1024, 32), w_o=(1024, 1024))
        for n, s in shapes.items():
            W[n] = nc.dram_tensor(n, list(s), F32, kind="ExternalInput").ap()
        W["scale_q4"] = hb["l0_scale_q4"]
        W["scale_k"] = hb["l0_scale_k"]
    elif block == "l0f":
        for n, s in dict(w_gate=(2816, 1024), w_up=(2816, 1024),
                         w_down=(1024, 2816)).items():
            W[n] = nc.dram_tensor(n, list(s), F32, kind="ExternalInput").ap()
        for n, s in dict(wqgT=(1024, 2816), wquT=(1024, 2816),
                         wqdT=(2816, 1024)).items():
            W[n] = nc.dram_tensor(n, list(s), BF16, kind="Internal").ap()
    elif block == "l1a":
        for n, s in dict(w_f=(1024, 1024), w_i=(1024, 1024), w_h=(1024, 1024)).items():
            W[n] = nc.dram_tensor(n, list(s), F32, kind="ExternalInput").ap()
        for n in ("b_f", "b_i", "b_h"):
            W[n] = nc.dram_tensor(n, [1024], F32, kind="ExternalInput").ap()
    elif block == "l1f":
        for n, s in dict(wqkv=(2, 1024, 1536), wo=(2, 512, 1024),
                         w1=(2, 1024, 5460), w2=(2, 2730, 1024)).items():
            W[n] = nc.dram_tensor(n, list(s), F32, kind="ExternalInput").ap()
        for n, s in dict(qk_scale_l=(128, 4), lng_l=(128, 16), lnb_l=(128, 16),
                         b1a_l=(128, 44), b1g_l=(128, 44), b2_l=(128, 16)).items():
            W[n] = nc.dram_tensor(n, list(s), F32, kind="ExternalInput").ap()
        for n in ("xp_csq", "xp_ssq", "xp_csk", "xp_ssk"):
            W[n] = hb[n]
        W["xres"] = nc.dram_tensor("xres", [128, NCT * T], F32, kind="Internal").ap()
    kb = KB(nc)
    ctx = Ctx(kb)
    alloc_common(kb, ctx)
    emit_load_consts(kb, ctx, hb)
    emit_load_x(kb, ctx, x_h)
    if block == "l0a":
        emit_l0_attn(kb, ctx, W)
    elif block == "l0f":
        emit_l0_ffn(kb, ctx, W)
    elif block == "l1a":
        emit_l1_attn(kb, ctx, W)
    elif block == "l1f":
        emit_l1_ffn(kb, ctx, W)
    emit_postln(kb, ctx)
    emit_store_x(kb, ctx, o_h)
    kb.finalize()
    return nc, consts


def _l1f_layouts(q_scale, k_scale, ln_g, ln_b, b1, b2):
    d = {}
    d["qk_scale_l"] = np.stack([np.tile(q_scale[0], 2), np.tile(k_scale[0], 2),
                                np.tile(q_scale[1], 2), np.tile(k_scale[1], 2)],
                               axis=1).astype(np.float32)

    def cols(vec, n):
        out = np.zeros((128, n), np.float32)
        for i in range(n):
            seg = vec[i * 128:(i + 1) * 128]
            out[:len(seg), i] = seg
        return out
    d["lng_l"] = np.concatenate([cols(ln_g[l], 8) for l in range(2)], 1)
    d["lnb_l"] = np.concatenate([cols(ln_b[l], 8) for l in range(2)], 1)
    d["b1a_l"] = np.concatenate([cols(b1[l][:2730], 22) for l in range(2)], 1)
    d["b1g_l"] = np.concatenate([cols(b1[l][2730:], 22) for l in range(2)], 1)
    d["b2_l"] = np.concatenate([cols(b2[l], 8) for l in range(2)], 1)
    return d


def kernel(**inputs):
    inp = {k: np.asarray(v) for k, v in inputs.items()}
    if FUSED:
        return kernel_fused(inp)
    f32 = np.float32
    blocks = [
        ("l0a", {"w_dq": inp["L0_attn_W_DQ"], "w_uq": inp["L0_attn_W_UQ"],
                 "w_dkv": inp["L0_attn_W_DKV"], "w_uk": inp["L0_attn_W_UK"],
                 "w_uv": inp["L0_attn_W_UV"], "w_qr": inp["L0_attn_W_QR"],
                 "w_kr": inp["L0_attn_W_KR"], "w_o": inp["L0_attn_W_O"]}),
        ("l0f", {"w_gate": inp["L0_ffn_w_gate"], "w_up": inp["L0_ffn_w_up"],
                 "w_down": inp["L0_ffn_w_down"]}),
        ("l1a", {"w_f": inp["L1_attn_W_f"], "b_f": inp["L1_attn_b_f"],
                 "w_i": inp["L1_attn_W_i"], "b_i": inp["L1_attn_b_i"],
                 "w_h": inp["L1_attn_W_h"], "b_h": inp["L1_attn_b_h"]}),
        ("l1f", {"wqkv": inp["L1_ffn_wqkv"], "wo": inp["L1_ffn_wo"],
                 "w1": inp["L1_ffn_w1"], "w2": inp["L1_ffn_w2"],
                 **_l1f_layouts(inp["L1_ffn_q_scale"], inp["L1_ffn_k_scale"],
                                inp["L1_ffn_ln_g"], inp["L1_ffn_ln_b"],
                                inp["L1_ffn_b1"], inp["L1_ffn_b2"])}),
    ]
    cur = np.ascontiguousarray(inp["L0_attn_x"].astype(f32))
    for bname, wmap in blocks:
        if bname not in _PROG_CACHE:
            _PROG_CACHE[bname] = _build_block(bname)
        nc, consts = _PROG_CACHE[bname]
        cmap = {n: consts[n] for n in CONST_NAMES}
        wmap2 = {k: np.ascontiguousarray(v.astype(f32) if v.dtype != np.float32 else v)
                 for k, v in wmap.items()}
        in_maps = [{"x": np.ascontiguousarray(cur[c]), **wmap2, **cmap}
                   for c in range(8)]
        res = run_bass_kernel_spmd(nc, in_maps, core_ids=list(range(8)))
        cur = np.stack([res.results[c]["o"] for c in range(8)])
    return cur.astype(np.float32)



def _build_full():
    consts = host_consts()
    nc = bass.Bass("TRN2", target_bir_lowering=False)
    x_h = nc.dram_tensor("x", [T, D], F32, kind="ExternalInput").ap()
    o_h = nc.dram_tensor("o", [T, D], F32, kind="ExternalOutput").ap()
    hb = _declare_consts(nc, consts)
    Wa, Wf, Wr, Wl = {}, {}, {}, {}
    for n, s in dict(w_dq=(1024, 4096), w_uq=(4096, 1024), w_dkv=(1024, 256),
                     w_uk=(256, 1024), w_uv=(256, 1024), w_qr=(4096, 512),
                     w_kr=(1024, 32), w_o=(1024, 1024)).items():
        Wa[n] = nc.dram_tensor("l0a_" + n, list(s), F32, kind="ExternalInput").ap()
    Wa["scale_q4"] = hb["l0_scale_q4"]
    Wa["scale_k"] = hb["l0_scale_k"]
    for n, s in dict(w_gate=(2816, 1024), w_up=(2816, 1024), w_down=(1024, 2816)).items():
        Wf[n] = nc.dram_tensor("l0f_" + n, list(s), F32, kind="ExternalInput").ap()
    for n, s in dict(wqgT=(1024, 2816), wquT=(1024, 2816), wqdT=(2816, 1024)).items():
        Wf[n] = nc.dram_tensor("l0f_" + n, list(s), BF16, kind="Internal").ap()
    for n in ("w_f", "w_i", "w_h"):
        Wr[n] = nc.dram_tensor("l1a_" + n, [1024, 1024], F32, kind="ExternalInput").ap()
    for n in ("b_f", "b_i", "b_h"):
        Wr[n] = nc.dram_tensor("l1a_" + n, [1024], F32, kind="ExternalInput").ap()
    for n, s in dict(wqkv=(2, 1024, 1536), wo=(2, 512, 1024), w1=(2, 1024, 5460),
                     w2=(2, 2730, 1024)).items():
        Wl[n] = nc.dram_tensor("l1f_" + n, list(s), F32, kind="ExternalInput").ap()
    for n, s in dict(qk_scale_l=(128, 4), lng_l=(128, 16), lnb_l=(128, 16),
                     b1a_l=(128, 44), b1g_l=(128, 44), b2_l=(128, 16)).items():
        Wl[n] = nc.dram_tensor("l1f_" + n, list(s), F32, kind="ExternalInput").ap()
    for n in ("xp_csq", "xp_ssq", "xp_csk", "xp_ssk"):
        Wl[n] = hb[n]
    Wl["xres"] = nc.dram_tensor("xres", [128, NCT * T], F32, kind="Internal").ap()

    kb = KB(nc)
    ctx = Ctx(kb)
    alloc_common(kb, ctx)
    emit_load_consts(kb, ctx, hb)
    emit_load_x(kb, ctx, x_h)
    emit_l0_attn(kb, ctx, Wa)
    emit_postln(kb, ctx)
    emit_l0_ffn(kb, ctx, Wf)
    emit_postln(kb, ctx)
    emit_l1_attn(kb, ctx, Wr)
    emit_postln(kb, ctx)
    emit_l1_ffn(kb, ctx, Wl)
    emit_postln(kb, ctx)
    emit_store_x(kb, ctx, o_h)
    kb.finalize()
    return nc, consts


FUSED = True


def kernel_fused(inp):
    if "full" not in _PROG_CACHE:
        _PROG_CACHE["full"] = _build_full()
    nc, consts = _PROG_CACHE["full"]
    cmap = {n: consts[n] for n in CONST_NAMES}
    wmap = {}
    for pre, names in (("L0_attn_", ("W_DQ", "W_UQ", "W_DKV", "W_UK", "W_UV",
                                     "W_QR", "W_KR", "W_O")),):
        for n in names:
            wmap["l0a_w_" + n[2:].lower()] = inp[pre + n]
    wmap["l0f_w_gate"] = inp["L0_ffn_w_gate"]
    wmap["l0f_w_up"] = inp["L0_ffn_w_up"]
    wmap["l0f_w_down"] = inp["L0_ffn_w_down"]
    for n in ("W_f", "W_i", "W_h", "b_f", "b_i", "b_h"):
        wmap["l1a_" + n.lower() if n[0] == "W" else "l1a_" + n] = inp["L1_attn_" + n]
    for n in ("wqkv", "wo", "w1", "w2"):
        wmap["l1f_" + n] = inp["L1_ffn_" + n]
    lay = _l1f_layouts(inp["L1_ffn_q_scale"], inp["L1_ffn_k_scale"],
                       inp["L1_ffn_ln_g"], inp["L1_ffn_ln_b"],
                       inp["L1_ffn_b1"], inp["L1_ffn_b2"])
    for k, v in lay.items():
        wmap["l1f_" + k] = v
    wmap = {k: np.ascontiguousarray(np.asarray(v).astype(np.float32)) for k, v in wmap.items()}
    x = np.ascontiguousarray(inp["L0_attn_x"].astype(np.float32))
    in_maps = [{"x": x[c], **wmap, **cmap} for c in range(8)]
    res = run_bass_kernel_spmd(nc, in_maps, core_ids=list(range(8)))
    return np.stack([res.results[c]["o"] for c in range(8)]).astype(np.float32)


# version 05
# speedup vs baseline: 1.0691x; 1.0276x over previous
import numpy as np
import ml_dtypes
from contextlib import ExitStack

import concourse.bass as bass
import concourse.mybir as mybir
from concourse.bass_utils import run_bass_kernel_spmd

F32 = mybir.dt.float32
BF16 = mybir.dt.bfloat16
AF = mybir.ActivationFunctionType
ALU = mybir.AluOpType
AX = mybir.AxisListType

DT_SIZE = {F32: 4, BF16: 2}


class TV:
    __slots__ = ("t", "ap", "p0", "p1", "e0", "e1")

    def __init__(self, t, ap, p0, p1, e0, e1):
        self.t = t
        self.ap = ap
        self.p0, self.p1, self.e0, self.e1 = p0, p1, e0, e1


def V(th, ps=None, es=None):
    shape = list(th.shape)
    p0, p1 = (0, shape[0]) if ps is None else ps
    e0, e1 = (0, shape[1]) if es is None else es
    return TV(th, th[p0:p1, e0:e1], p0, p1, e0, e1)


def V3(th, ps, es, ap):
    return TV(th, ap, ps[0], ps[1], es[0], es[1])


def DV(th_ap, tname, region=None):
    if region is None:
        region = (0, 1, 0, 1)
    return TV(_Named(tname), th_ap, *region)


class _Named:
    __slots__ = ("name",)

    def __init__(self, name):
        self.name = name



ENGINES = ("t", "v", "s", "p", "q")


class KB:
    def __init__(self, nc):
        self.nc = nc
        self.stack = ExitStack()
        self.prog = {e: [] for e in ENGINES}
        self.semh = {}
        for name in ("ct", "cv", "cs", "cp"):
            self.semh[name] = self.stack.enter_context(nc.semaphore("sem_" + name))
        self.csem = {"t": "ct", "v": "cv", "s": "cs", "p": "cp"}
        self.dring = {}
        for q, r in (("q", 8), ("p", 8), ("s", 4), ("v", 2), ("t", 2)):
            names = []
            for i in range(r):
                n = f"d{q}{i}"
                self.semh[n] = self.stack.enter_context(nc.semaphore("sem_" + n))
                names.append(n)
            self.dring[q] = names
        self.dma_i = {q: 0 for q in self.dring}
        self.cnt = {k: 0 for k in self.semh}
        self.clock = {e: {} for e in ENGINES}
        self.snap = {k: {} for k in self.semh}
        self.writes = {}
        self.reads = {}
        self.psum_names = set()
        self.eng_obj = {"t": nc.tensor, "v": nc.vector, "s": nc.scalar,
                        "p": nc.gpsimd, "q": nc.sync}
        self._tensors = []
        self.fconst_t = self.sbuf("fconst_tab", [1, 32], F32)
        self.fconst_vals = {}

    def fconst(self, val):
        val = float(val)
        if val in (0.0, 1.0):
            return val, None
        if val not in self.fconst_vals:
            idx = len(self.fconst_vals)
            assert idx < 32
            self.fconst_vals[val] = idx
            view = V(self.fconst_t, (0, 1), (idx, idx + 1))
            self.memset("p", view, val)
            self.nc.const_aps.register(F32, val, self.fconst_t[0:1, idx:idx + 1])
        idx = self.fconst_vals[val]
        return val, V(self.fconst_t, (0, 1), (idx, idx + 1))

    def sbuf(self, name, shape, dtype):
        th = self.stack.enter_context(self.nc.sbuf_tensor(name, list(shape), dtype))
        self._tensors.append(th)
        return th

    def psum(self, name, shape, dtype):
        th = self.stack.enter_context(self.nc.psum_tensor(name, list(shape), dtype))
        self._tensors.append(th)
        self.psum_names.add(name)
        return th

    @staticmethod
    def _overlap(a, b):
        return not (a[1] <= b[0] or b[1] <= a[0] or a[3] <= b[2] or b[3] <= a[2])

    def _dep_events(self, eng, rviews, wviews):
        evs = []
        own = self.csem.get(eng)
        for v_ in rviews:
            for ent in self.writes.get(v_.t.name, ()):
                if self._overlap((v_.p0, v_.p1, v_.e0, v_.e1), ent):
                    evs.extend(ent[4])
            if v_.t.name in self.psum_names:
                for ent in self.reads.get(v_.t.name, ()):
                    if self._overlap((v_.p0, v_.p1, v_.e0, v_.e1), ent):
                        evs.extend(e for e in ent[4] if e[0] != own)
        for v_ in wviews:
            box = (v_.p0, v_.p1, v_.e0, v_.e1)
            for ent in self.writes.get(v_.t.name, ()):
                if self._overlap(box, ent):
                    evs.extend(ent[4])
            for ent in self.reads.get(v_.t.name, ()):
                if self._overlap(box, ent):
                    evs.extend(ent[4])
        if eng == "t":
            evs = [e for e in evs if e[0] != "ct"]
        return evs

    def _emit_waits(self, eng, evs):
        clk = self.clock[eng]
        need = {}
        for s_, v_ in evs:
            if clk.get(s_, 0) < v_ and need.get(s_, 0) < v_:
                need[s_] = v_
        for s_, v_ in sorted(need.items()):
            semh = self.semh[s_]
            self.prog[eng].append(("w", semh, v_))
            snap = self.snap[s_].get(v_)
            if snap:
                for ks, kv in snap.items():
                    if clk.get(ks, 0) < kv:
                        clk[ks] = kv
            if clk.get(s_, 0) < v_:
                clk[s_] = v_

    def _record(self, views, table, ev):
        for v_ in views:
            lst = table.setdefault(v_.t.name, [])
            box = (v_.p0, v_.p1, v_.e0, v_.e1)
            if table is self.writes:
                for tab in (self.writes, self.reads):
                    l2 = tab.get(v_.t.name)
                    if l2:
                        tab[v_.t.name] = [e for e in l2 if not (
                            box[0] <= e[0] and e[1] <= box[1] and box[2] <= e[2] and e[3] <= box[3])]
                lst = table.setdefault(v_.t.name, [])
            lst.append(box + ((ev,),))
            if len(lst) > 120:
                old, rest = lst[:60], lst[60:]
                p0 = min(e[0] for e in old); p1 = max(e[1] for e in old)
                e0 = min(e[2] for e in old); e1 = max(e[3] for e in old)
                evs = tuple(sorted({x for e in old for x in e[4]}, key=lambda t: (t[0], -t[1])))
                seen = {}
                for s_, vv in evs:
                    seen[s_] = max(seen.get(s_, 0), vv)
                table[v_.t.name] = [(p0, p1, e0, e1, tuple(seen.items()))] + rest

    def op(self, eng, fn, reads=(), writes=(), dma=False):
        evs = self._dep_events(eng, reads, writes)
        if dma:
            ring = self.dring[eng]
            sname = ring[self.dma_i[eng] % len(ring)]
            self.dma_i[eng] += 1
            if self.cnt[sname] > 0:
                evs = list(evs) + [(sname, self.cnt[sname])]
        else:
            sname = self.csem[eng]
        self._emit_waits(eng, evs)
        inc = 16 if dma else 1
        self.cnt[sname] += inc
        ev = (sname, self.cnt[sname])
        semh = self.semh[sname]
        self.prog[eng].append(("c", fn, semh, inc))
        clk = dict(self.clock[eng])
        clk[sname] = self.cnt[sname]
        self.snap[sname][self.cnt[sname]] = clk
        self._record(reads, self.reads, ev)
        self._record(writes, self.writes, ev)
        return ev

    def raw_wait(self, eng, sname, val):
        self.prog[eng].append(("w", self.semh[sname], val))
        clk = self.clock[eng]
        if clk.get(sname, 0) < val:
            clk[sname] = val

    def barrier_all_on(self, eng):
        evs = [(s, c) for s, c in self.cnt.items() if c > 0]
        self._emit_waits(eng, evs)

    def dma(self, q, dst, src):
        return self.op(q, lambda e: e.dma_start(out=dst.ap, in_=src.ap),
                       reads=(src,), writes=(dst,), dma=True)

    def dma_slow(self, q, dst, src):
        return self.op(q, lambda e: e.dma_start(out=dst.ap, in_=src.ap,
                                                allow_slow_non_contiguous=True),
                       reads=(src,), writes=(dst,), dma=True)

    def copy(self, eng, dst, src):
        if eng == "s":
            return self.op(eng, lambda e: e.copy(out=dst.ap, in_=src.ap),
                           reads=(src,), writes=(dst,))
        return self.op(eng, lambda e: e.tensor_copy(out=dst.ap, in_=src.ap),
                       reads=(src,), writes=(dst,))

    def memset(self, eng, dst, c):
        return self.op(eng, lambda e: e.memset(dst.ap, c), writes=(dst,))

    def tt(self, eng, out, in0, in1, alu):
        return self.op(eng, lambda e: e.tensor_tensor(out=out.ap, in0=in0.ap, in1=in1.ap, op=alu),
                       reads=(in0, in1), writes=(out,))

    def ts(self, eng, out, in0, s1, s2=None, op0=ALU.mult, op1=None, s1v=None, s2v=None):
        reads = [in0]
        a1 = s1
        if isinstance(s1, TV):
            reads.append(s1); a1 = s1.ap
        a2 = s2
        if isinstance(s2, TV):
            reads.append(s2); a2 = s2.ap
        if op1 is None:
            fn = lambda e: e.tensor_scalar(out=out.ap, in0=in0.ap, scalar1=a1,
                                           scalar2=None, op0=op0)
        else:
            fn = lambda e: e.tensor_scalar(out=out.ap, in0=in0.ap, scalar1=a1,
                                           scalar2=a2, op0=op0, op1=op1)
        return self.op(eng, fn, reads=tuple(reads), writes=(out,))

    def stt(self, eng, out, in0, scalar, in1, op0, op1):
        reads = [in0, in1]
        a = scalar
        if isinstance(scalar, TV):
            reads.append(scalar); a = scalar.ap
        return self.op(eng, lambda e: e.scalar_tensor_tensor(
            out=out.ap, in0=in0.ap, scalar=a, in1=in1.ap, op0=op0, op1=op1),
            reads=tuple(reads), writes=(out,))

    def scan(self, out, d0, d1, initial, op0=ALU.mult, op1=ALU.add):
        reads = [d0, d1]
        ini = initial
        if isinstance(initial, TV):
            reads.append(initial); ini = initial.ap
        return self.op("v", lambda e: e.tensor_tensor_scan(
            out=out.ap, data0=d0.ap, data1=d1.ap, initial=ini, op0=op0, op1=op1),
            reads=tuple(reads), writes=(out,))

    def act(self, out, in_, func, bias=0.0, scale=1.0, accum=None):
        reads = [in_]
        b = bias
        if isinstance(bias, TV):
            reads.append(bias); b = bias.ap
        else:
            b, cv = self.fconst(bias)
            if cv is not None:
                reads.append(cv)
        sc = scale
        if isinstance(scale, TV):
            reads.append(scale); sc = scale.ap
        else:
            sc, cv = self.fconst(scale)
            if cv is not None:
                reads.append(cv)
        writes = [out]
        if accum is not None:
            writes.append(accum)
            fn = lambda e: e.activation(out=out.ap, in_=in_.ap, func=func, bias=b,
                                        scale=sc, accum_out=accum.ap)
        else:
            fn = lambda e: e.activation(out=out.ap, in_=in_.ap, func=func, bias=b, scale=sc)
        return self.op("s", fn, reads=tuple(reads), writes=tuple(writes))

    def reduce(self, eng, out, in_, alu, negate=False):
        return self.op(eng, lambda e: e.tensor_reduce(
            out=out.ap, in_=in_.ap, axis=AX.X, op=alu, negate=negate),
            reads=(in_,), writes=(out,))

    def recip(self, out, in_):
        return self.op("v", lambda e: e.reciprocal(out=out.ap, in_=in_.ap),
                       reads=(in_,), writes=(out,))

    def mm(self, out, lhsT, rhs, start, stop, tile_position=None):
        return self.op("t", lambda e: e.matmul(out.ap, lhsT.ap, rhs.ap, start=start,
                                               stop=stop, tile_position=tile_position),
                       reads=(lhsT, rhs), writes=(out,))

    def trans(self, out, in_, ident):
        return self.op("t", lambda e: e.transpose(out.ap, in_.ap, ident.ap),
                       reads=(in_, ident), writes=(out,))

    def finalize(self):
        self.barrier_all_on("q")
        nc = self.nc
        with nc.Block() as block:
            def runner(stream):
                def go(engobj):
                    for item in stream:
                        if item[0] == "w":
                            engobj.wait_ge(item[1], item[2])
                        else:
                            _, fn, semh, inc = item
                            fn(engobj).then_inc(semh, inc)
                return go

            block.tensor(runner(self.prog["t"]))
            block.vector(runner(self.prog["v"]))
            block.scalar(runner(self.prog["s"]))
            block.gpsimd(runner(self.prog["p"]))
            block.sync(runner(self.prog["q"]))
        self.stack.close()



T = 2048
D = 1024
NCT = D // 128
NTT = T // 128


class Slot:
    __slots__ = ("th", "off", "ln")

    def __init__(self, th, off, ln):
        self.th, self.off, self.ln = th, off, ln

    def v(self, ps=None, es=None):
        p0, p1 = (0, self.th.shape[0]) if ps is None else ps
        e0, e1 = (0, self.ln) if es is None else es
        a, b = self.off + e0, self.off + e1
        return TV(self.th, self.th[p0:p1, a:b], p0, p1, a, b)

    def vap(self, rel_off, dims, ps, es):
        pitch = self.th.shape[1]
        ap = bass.AP(self.th, self.off + rel_off, [[pitch, ps[1] - ps[0]]] + dims)
        if ps[0] != 0:
            ap = bass.AP(self.th, (self.off + rel_off), [[pitch, ps]] )
        return TV(self.th, ap, ps[0], ps[1], self.off + es[0], self.off + es[1])


AB_KC = 0
AB_VTM = 16384
AB_KR4 = 33024
AB_SCL = 35072
AB_CQ = 35840
AB_XBC = 44032
AB_QC = 46080
AB_QR = 48128
AB_WSM = 49152
AB_END = 56576

AF_ACC = 0
AF_WSTG = 4096
AF_SQ = 5120
AF_LNT = 6144
AF_END = 6656


def host_consts():
    c = {}
    c["ident_f32"] = np.eye(128, dtype=np.float32)
    c["ident_bf16"] = np.eye(128, dtype=ml_dtypes.bfloat16)
    c["ones_f32"] = np.ones((1, 128), dtype=np.float32)
    i = np.arange(128)
    c["tri_mask"] = np.where(i[:, None] <= i[None, :], 0.0, -1e30).astype(np.float32)
    DHR = 32
    t = np.arange(T, dtype=np.float32)
    inv = 1.0 / 10000.0 ** (np.arange(0, DHR, 2, dtype=np.float32) / DHR)
    freqs = t[:, None] * inv[None, :]
    emb = np.concatenate([freqs, freqs], -1)
    scale = (np.cos(emb) + np.sin(emb)).astype(np.float32)
    c["l0_scale_k"] = np.ascontiguousarray(scale.T).astype(ml_dtypes.bfloat16)
    c["l0_scale_q4"] = (np.tile(scale.T, (4, 1)) /
                        np.sqrt(np.float32(96.0))).astype(ml_dtypes.bfloat16)
    c["tri2_mask"] = np.where(i[None, :] <= i[:, None], 0.0, -1e30).astype(np.float32)
    e2 = np.zeros((128, 2), np.float32)
    e2[0:64, 0] = 1.0
    e2[64:128, 1] = 1.0
    c["e2_mat"] = e2
    c["e2t_mat"] = np.ascontiguousarray(e2.T)
    DH, WIN = 64, 512
    half = np.arange(0, DH, 2, dtype=np.float32)
    invf = 1.0 / 10000.0 ** (half / DH)
    tt = np.arange(T, dtype=np.float32)
    fr = tt[:, None] * invf[None, :]
    fr2 = np.concatenate([fr, fr], -1)
    basev = (half + 0.4 * DH) / (1.4 * DH)
    power = (tt - T // 2) / float(WIN // 2)
    sc = basev[None, :] ** power[:, None]
    sc2 = np.concatenate([sc, sc], -1)
    cosv, sinv = np.cos(fr2), np.sin(fr2)
    sgn = np.where(np.arange(DH) < DH // 2, -1.0, 1.0).astype(np.float32)
    csq = (8.0 * cosv * sc2).T
    ssq = (8.0 * sinv * sc2).T * sgn[:, None]
    csk = (cosv / sc2).T
    ssk = (sinv / sc2).T * sgn[:, None]
    c["xp_csq"] = np.ascontiguousarray(np.tile(csq, (2, 1)).astype(np.float32))
    c["xp_ssq"] = np.ascontiguousarray(np.tile(ssq, (2, 1)).astype(np.float32))
    c["xp_csk"] = np.ascontiguousarray(np.tile(csk, (2, 1)).astype(np.float32))
    c["xp_ssk"] = np.ascontiguousarray(np.tile(ssk, (2, 1)).astype(np.float32))
    return c


CONST_NAMES = ("ident_f32", "ident_bf16", "ones_f32", "tri_mask",
               "l0_scale_k", "l0_scale_q4", "tri2_mask", "e2_mat", "e2t_mat",
               "xp_csq", "xp_ssq", "xp_csk", "xp_ssk")


class Ctx:
    def __init__(self, kb):
        self.kb = kb
        self.d = {}

    def __getattr__(self, k):
        try:
            return self.d[k]
        except KeyError:
            raise AttributeError(k)


def alloc_common(kb, ctx):
    ctx.d["x"] = kb.sbuf("x_cm", [128, NCT * T], F32)
    ctx.d["AB"] = kb.sbuf("arena_bf", [128, AB_END], BF16)
    ctx.d["AF"] = kb.sbuf("arena_f32", [128, AF_END], F32)
    ctx.d["pf"] = [kb.psum(f"psf{i}", [128, 512], F32) for i in range(6)]
    ctx.d["pb"] = [kb.psum(f"psb{i}", [128, 1024], BF16) for i in range(2)]
    ctx.d["idf"] = kb.sbuf("idf", [128, 128], F32)
    ctx.d["idb"] = kb.sbuf("idb", [128, 128], BF16)
    ctx.d["ones"] = kb.sbuf("ones_c", [1, 128], F32)
    ctx.d["ones_col"] = kb.sbuf("ones_col", [128, 1], F32)
    ctx.d["tri"] = kb.sbuf("tri_c", [128, 128], F32)
    ctx.d["tri2"] = kb.sbuf("tri2_c", [128, 128], F32)
    ctx.d["e2"] = kb.sbuf("e2_c", [128, 2], F32)
    ctx.d["e2t"] = kb.sbuf("e2t_c", [2, 128], F32)
    AB, AFt = ctx.AB, ctx.AF
    ctx.d["stage"] = Slot(AFt, AF_ACC, 2048)


def emit_load_consts(kb, ctx, hb):
    kb.dma("q", V(ctx.idf), DV(hb["ident_f32"], "h_identf"))
    kb.dma("q", V(ctx.idb), DV(hb["ident_bf16"], "h_identb"))
    kb.dma("q", V(ctx.ones), DV(hb["ones_f32"], "h_ones"))
    kb.dma_slow("q", V(ctx.ones_col), DV(hb["ones_f32"].rearrange("o p -> p o"), "h_ones2"))
    kb.dma("q", V(ctx.tri), DV(hb["tri_mask"], "h_tri"))
    kb.dma("q", V(ctx.tri2), DV(hb["tri2_mask"], "h_tri2"))
    kb.dma("q", V(ctx.e2), DV(hb["e2_mat"], "h_e2"))
    kb.dma("q", V(ctx.e2t), DV(hb["e2t_mat"], "h_e2t"))


def emit_load_x(kb, ctx, x_hbm):
    x, stage, pf = ctx.x, ctx.stage, ctx.pf
    for ti in range(NTT):
        sb = ti % 2
        stv = stage.v(es=(sb * 1024, (sb + 1) * 1024))
        kb.dma("q", stv, DV(x_hbm[ti * 128:(ti + 1) * 128, :], "x_in",
                            (0, 128, ti * 1024, (ti + 1) * 1024)))
        for half in range(2):
            bank = pf[half]
            for cj in range(4):
                ci = half * 4 + cj
                src = stage.v(es=(sb * 1024 + ci * 128, sb * 1024 + (ci + 1) * 128))
                kb.trans(V(bank, (0, 128), (cj * 128, (cj + 1) * 128)), src, V(ctx.idf))
            base = (half * 4) * T + ti * 128
            ap = bass.AP(x, base, [[NCT * T, 128], [T, 4], [1, 128]])
            dtv = TV(x, ap, 0, 128, base, base + 3 * T + 128)
            kb.op("s", lambda e, o=ap, i=bank: e.copy(out=o, in_=i[:, 0:512]),
                  reads=(V(bank, (0, 128), (0, 512)),), writes=(dtv,))


def emit_store_x(kb, ctx, out_hbm):
    x, stage, pf = ctx.x, ctx.stage, ctx.pf
    for ti in range(NTT):
        sb = ti % 2
        for half in range(2):
            bank = pf[2 + half]
            for cj in range(4):
                ci = half * 4 + cj
                src = TV(x, x[:, ci * T + ti * 128: ci * T + (ti + 1) * 128],
                         0, 128, ci * T + ti * 128, ci * T + (ti + 1) * 128)
                kb.trans(V(bank, (0, 128), (cj * 128, (cj + 1) * 128)), src, V(ctx.idf))
            kb.copy("s", stage.v(es=(sb * 1024 + half * 512, sb * 1024 + (half + 1) * 512)),
                    V(bank, (0, 128), (0, 512)))
        kb.dma("q", DV(out_hbm[ti * 128:(ti + 1) * 128, :], "x_out",
                       (0, 128, ti * 1024, (ti + 1) * 1024)),
               stage.v(es=(sb * 1024, (sb + 1) * 1024)))



def emit_postln(kb, ctx, eps=1e-5):
    x, pf, oc = ctx.x, ctx.pf, ctx.ones_col
    AFt = ctx.AF
    for tc in range(T // 512):
        cs = tc * 512
        ps_a, ps_b, ps_c, ps_d = pf[0], pf[1], pf[2], pf[3]
        for ci in range(NCT):
            kb.mm(V(ps_a, (0, 1), (0, 512)), V(oc),
                  V(x, (0, 128), (ci * T + cs, ci * T + cs + 512)),
                  start=(ci == 0), stop=(ci == NCT - 1))
        for ci in range(NCT):
            sq = Slot(AFt, AF_SQ, 1024).v(es=((ci % 2) * 512, (ci % 2) * 512 + 512))
            kb.act(sq, V(x, (0, 128), (ci * T + cs, ci * T + cs + 512)), AF.Square)
            kb.mm(V(ps_b, (0, 1), (0, 512)), V(oc), sq,
                  start=(ci == 0), stop=(ci == NCT - 1))
        A = Slot(ctx.AF, AF_SQ, 1024).v((0, 1), (0, 512))
        B = Slot(ctx.AF, AF_SQ, 1024).v((0, 1), (512, 1024))
        kb.ts("v", A, V(ps_a, (0, 1), (0, 512)), 1.0 / D, op0=ALU.mult)
        kb.tt("v", B, A, A, ALU.mult)
        kb.stt("v", B, V(ps_b, (0, 1), (0, 512)), 1.0 / D, B, ALU.mult, ALU.subtract)
        kb.act(B, B, AF.Sqrt, bias=float(eps))
        kb.recip(B, B)
        kb.tt("v", A, A, B, ALU.mult)
        kb.mm(V(ps_c, (0, 128), (0, 512)), V(ctx.ones, (0, 1), (0, 128)), B,
              start=True, stop=True)
        kb.mm(V(ps_d, (0, 128), (0, 512)), V(ctx.ones, (0, 1), (0, 128)), A,
              start=True, stop=True)
        for ci in range(NCT):
            xv = V(x, (0, 128), (ci * T + cs, ci * T + cs + 512))
            t1 = Slot(AFt, AF_LNT, 512).v()
            kb.tt("v", t1, xv, V(ps_c, (0, 128), (0, 512)), ALU.mult)
            kb.tt("v", xv, t1, V(ps_d, (0, 128), (0, 512)), ALU.subtract)



QCW = 256
NQC = T // QCW


def emit_l0_attn(kb, ctx, W):
    x, pf, AB = ctx.x, ctx.pf, ctx.AB
    kc = Slot(AB, AB_KC, 16384)
    vtm = Slot(AB, AB_VTM, 16640)
    kr4 = Slot(AB, AB_KR4, 2048)
    scl = Slot(AB, AB_SCL, 768)
    cqs = Slot(AB, AB_CQ, 8192)
    expt = Slot(AB, AB_CQ, 1024)
    ycm = Slot(AB, AB_CQ + 1024, 2048)
    xbc = Slot(AB, AB_XBC, 2048)
    qcc = Slot(AB, AB_QC, 2048)
    qrr = Slot(AB, AB_QR, 1024)
    wbf = Slot(AB, AB_WSM, 1024)
    wsm = Slot(AB, AB_WSM + 1024, 6400)
    wstg = Slot(ctx.AF, AF_WSTG, 1024)
    rbsb = Slot(ctx.AF, AF_LNT, 512)
    OF_DKV, OF_UK, OF_UV, OF_KR = 0, 8 * 256, 8 * 256 + 2 * 1024, 8 * 256 + 4 * 1024

    def stream_w(hap, hname, r0, c0, ncols, buf, width):
        wb = wbf.v((0, 128), (buf * 512, buf * 512 + ncols))
        kb.dma("q", wb, DV(hap[r0:r0 + 128, c0:c0 + ncols], hname,
                           (0, 128, r0 * width + c0, r0 * width + c0 + ncols)))
        return wb

    def load_small(hname, hap, ktiles, cols, off):
        for k in range(ktiles):
            stg = wstg.v((0, 128), ((k % 2) * 512, (k % 2) * 512 + cols))
            kb.dma("q", stg, DV(hap[k * 128:(k + 1) * 128, :], hname,
                                (0, 128, k * cols, (k + 1) * cols)))
            kb.copy("p", wsm.v((0, 128), (off + k * cols, off + (k + 1) * cols)), stg)

    load_small("w_dkv", W["w_dkv"], 8, 256, OF_DKV)
    load_small("w_uk", W["w_uk"], 2, 1024, OF_UK)
    load_small("w_uv", W["w_uv"], 2, 1024, OF_UV)
    load_small("w_kr", W["w_kr"], 8, 32, OF_KR)

    ckv = Slot(AB, AB_CQ + 4096, 4096)
    xb1 = Slot(AB, AB_CQ, 4096)

    for tc in range(4):
        cs = tc * 512
        kb.dma("q", scl.v((0, 32), (256, 768)),
               DV(W["scale_k"][0:32, cs:cs + 512], "h_sclk", (0, 32, cs, cs + 512)))
        for ci in range(NCT):
            kb.copy("v", xb1.v((0, 128), (ci * 512, (ci + 1) * 512)),
                    V(x, (0, 128), (ci * T + cs, ci * T + cs + 512)))
        for m in range(2):
            ps = pf[m]
            for ci in range(NCT):
                kb.mm(V(ps), wsm.v((0, 128), (OF_DKV + ci * 256 + m * 128,
                                              OF_DKV + ci * 256 + (m + 1) * 128)),
                      xb1.v((0, 128), (ci * 512, (ci + 1) * 512)),
                      start=(ci == 0), stop=(ci == NCT - 1))
            kb.copy("s", ckv.v((0, 128), (m * T + cs - 0 * T, m * T + cs + 512))
                    if False else ckv.v((0, 128), (m * 2048 + cs, m * 2048 + cs + 512)), V(ps))
        for m in range(8):
            ps = pf[2 + (m % 2)]
            for ci in range(2):
                kb.mm(V(ps), wsm.v((0, 128), (OF_UK + ci * 1024 + m * 128,
                                              OF_UK + ci * 1024 + (m + 1) * 128)),
                      ckv.v((0, 128), (ci * 2048 + cs, ci * 2048 + cs + 512)),
                      start=(ci == 0), stop=(ci == 1))
            kb.copy("s" if m % 2 == 0 else "v",
                    kc.v((0, 128), (m * T + cs, m * T + cs + 512)), V(ps))
        for tl in range(4):
            tt_ = tc * 4 + tl
            for n in range(2):
                ps = pf[4 + n]
                for ci in range(2):
                    kb.mm(V(ps), ckv.v((0, 128), (ci * 2048 + cs + tl * 128,
                                                  ci * 2048 + cs + (tl + 1) * 128)),
                          wsm.v((0, 128), (OF_UV + ci * 1024 + n * 512,
                                           OF_UV + ci * 1024 + (n + 1) * 512)),
                          start=(ci == 0), stop=(ci == 1))
                base = AB_VTM + tt_ * 1040 + 65 * 8 * n
                oap = bass.AP(AB, base, [[AB_END, 128], [65, 8], [1, 64]])
                kb.op("v", lambda e, o=oap, i=ps: e.tensor_copy(out=o, in_=i[:, 0:512]),
                      reads=(V(ps),), writes=(TV(AB, oap, 0, 128, base, base + 8 * 65),))
        ps = pf[0]
        for ci in range(NCT):
            kb.mm(V(ps, (0, 32), (0, 512)),
                  wsm.v((0, 128), (OF_KR + ci * 32, OF_KR + (ci + 1) * 32)),
                  xb1.v((0, 128), (ci * 512, (ci + 1) * 512)),
                  start=(ci == 0), stop=(ci == NCT - 1))
        kb.tt("v", kr4.v((0, 32), (cs, cs + 512)), V(ps, (0, 32), (0, 512)),
              scl.v((0, 32), (256, 768)), ALU.mult)
    for tt_ in range(16):
        base = AB_VTM + tt_ * 1040 + 64
        oap = bass.AP(AB, base, [[AB_END, 128], [65, 16], [1, 1]])
        kb.op("v", lambda e, o=oap: e.memset(o, 1.0),
              writes=(TV(AB, oap, 0, 128, base, base + 16 * 65),))
    for rep in range(1, 4):
        kb.dma("p", kr4.v((rep * 32, (rep + 1) * 32), (0, T)), kr4.v((0, 32), (0, T)))

    for qc in range(NQC):
        qs = qc * QCW
        kb.dma("q", scl.v((0, 128), (0, QCW)),
               DV(W["scale_q4"][0:128, qs:qs + QCW], "h_sclq", (0, 128, qs, qs + QCW)))
        for ci in range(NCT):
            kb.copy("v", xbc.v((0, 128), (ci * QCW, (ci + 1) * QCW)),
                    V(x, (0, 128), (ci * T + qs, ci * T + qs + QCW)))
        for mg in range(8):
            for k in range(NCT):
                wb = stream_w(W["s_w_dq"], "s_w_dq", k * 128, mg * 512, 512, (mg * 8 + k) % 2, 4096)
                for j in range(4):
                    kb.mm(V(pf[j], (0, 128), (0, QCW)),
                          wbf.v((0, 128), (wb.e0 - AB_WSM + j * 128, wb.e0 - AB_WSM + (j + 1) * 128)),
                          xbc.v((0, 128), (k * QCW, (k + 1) * QCW)),
                          start=(k == 0), stop=(k == NCT - 1))
            for j in range(4):
                kb.copy("s" if j % 2 == 0 else "v",
                        cqs.v((0, 128), ((mg * 4 + j) * QCW, (mg * 4 + j + 1) * QCW)),
                        V(pf[j], (0, 128), (0, QCW)))
        for mg in range(2):
            for k in range(32):
                wb = stream_w(W["s_w_uq"], "s_w_uq", k * 128, mg * 512, 512, (mg * 32 + k) % 2, 1024)
                for j in range(4):
                    kb.mm(V(pf[j], (0, 128), (0, QCW)),
                          wbf.v((0, 128), (wb.e0 - AB_WSM + j * 128, wb.e0 - AB_WSM + (j + 1) * 128)),
                          cqs.v((0, 128), (k * QCW, (k + 1) * QCW)),
                          start=(k == 0), stop=(k == 31))
            for j in range(4):
                dst = qcc.v((0, 128), ((mg * 4 + j) * QCW, (mg * 4 + j + 1) * QCW))
                kb.act(dst, V(pf[j], (0, 128), (0, QCW)), AF.Copy,
                       scale=float(1.0 / np.sqrt(96.0)))
        for k in range(32):
            wb = stream_w(W["s_w_qr"], "s_w_qr", k * 128, 0, 512, k % 2, 512)
            for j in range(4):
                kb.mm(V(pf[j], (0, 128), (0, QCW)),
                      wbf.v((0, 128), (wb.e0 - AB_WSM + j * 128, wb.e0 - AB_WSM + (j + 1) * 128)),
                      cqs.v((0, 128), (k * QCW, (k + 1) * QCW)),
                      start=(k == 0), stop=(k == 31))
        for j in range(4):
            kb.tt("v", qrr.v((0, 128), (j * QCW, (j + 1) * QCW)),
                  V(pf[j], (0, 128), (0, QCW)), scl.v((0, 128), (0, QCW)), ALU.mult)
        njt = (qc + 1) * (QCW // 128)
        for h in range(16):
            hp, ho = h // 2, (h % 2) * 64
            rp = (h % 4) * 32
            y_ps = pf[4]
            for j in range(njt):
                st = pf[j % 2]
                kb.mm(V(st, (0, 128), (0, QCW)),
                      kc.v((ho, ho + 64), (hp * T + j * 128, hp * T + (j + 1) * 128)),
                      qcc.v((ho, ho + 64), (hp * QCW, (hp + 1) * QCW)),
                      start=True, stop=False, tile_position=(ho, 0))
                kb.mm(V(st, (0, 128), (0, QCW)),
                      kr4.v((rp, rp + 32), (j * 128, (j + 1) * 128)),
                      qrr.v((rp, rp + 32), ((h // 4) * QCW, (h // 4 + 1) * QCW)),
                      start=False, stop=True, tile_position=(rp, 0))
                exoff = (j % 2) * 512
                nd = qc * (QCW // 128)
                if j >= nd:
                    a = j - nd
                    if a > 0:
                        kb.memset("v", expt.v((0, 128), (exoff, exoff + a * 128)), 0.0)
                    kb.tt("v", V(st, (0, 128), (a * 128, (a + 1) * 128)),
                          V(st, (0, 128), (a * 128, (a + 1) * 128)), V(ctx.tri), ALU.add)
                    kb.act(expt.v((0, 128), (exoff + a * 128, exoff + QCW)),
                           V(st, (0, 128), (a * 128, QCW)), AF.Exp)
                else:
                    kb.act(expt.v((0, 128), (exoff, exoff + QCW)),
                           V(st, (0, 128), (0, QCW)), AF.Exp)
                kb.mm(V(y_ps, (0, 65), (0, QCW)),
                      vtm.v((0, 128), (j * 1040 + 65 * h, j * 1040 + 65 * h + 65)),
                      expt.v((0, 128), (exoff, exoff + QCW)),
                      start=(j == 0), stop=(j == njt - 1))
            kb.recip(Slot(ctx.AF, AF_SQ, 1024).v((0, 1), (0, QCW)),
                     V(y_ps, (64, 65), (0, QCW)))
            rb_ps = pf[3]
            kb.mm(V(rb_ps, (0, 64), (0, QCW)), V(ctx.ones, (0, 1), (0, 64)),
                  Slot(ctx.AF, AF_SQ, 1024).v((0, 1), (0, QCW)), start=True, stop=True)
            kb.copy("s", rbsb.v((0, 64), (0, QCW)), V(rb_ps, (0, 64), (0, QCW)))
            kb.tt("v", ycm.v((ho, ho + 64), (hp * QCW, (hp + 1) * QCW)),
                  V(y_ps, (0, 64), (0, QCW)), rbsb.v((0, 64), (0, QCW)), ALU.mult)
        for ocg in range(2):
            for k in range(NCT):
                wb = stream_w(W["s_w_o"], "s_w_o", k * 128, ocg * 512, 512, (ocg * 8 + k) % 2, 1024)
                for j in range(4):
                    kb.mm(V(pf[j], (0, 128), (0, QCW)),
                          wbf.v((0, 128), (wb.e0 - AB_WSM + j * 128, wb.e0 - AB_WSM + (j + 1) * 128)),
                          ycm.v((0, 128), (k * QCW, (k + 1) * QCW)),
                          start=(k == 0), stop=(k == NCT - 1))
            for j in range(4):
                oc_ = ocg * 4 + j
                xv = V(x, (0, 128), (oc_ * T + qs, oc_ * T + qs + QCW))
                kb.tt("v", xv, V(pf[j], (0, 128), (0, QCW)), xv, ALU.add)



FP4_MIDS = (0.75, 1.25, 1.75, 2.5, 3.5, 5.0)


def emit_fp4_quant(kb, ctx, src, dst, W, bufA, bufB, bmrow, ladder_eng="v"):
    nb = W // 16
    bufA = TV(bufA.t, bufA.t[0:128, bufA.e0:bufA.e0 + W], 0, 128, bufA.e0, bufA.e0 + W)
    bufB = TV(bufB.t, bufB.t[0:128, bufB.e0:bufB.e0 + W], 0, 128, bufB.e0, bufB.e0 + W)

    def b3(view, blk=True):
        if blk:
            ap = bass.AP(view.t, view.e0, [[view.t.shape[1], 128], [16, nb], [1, 16]])
        else:
            ap = bass.AP(view.t, view.e0, [[view.t.shape[1], 128], [1, nb], [0, 16]])
        return TV(view.t, ap, view.p0, view.p1, view.e0, view.e1)

    bm = TV(bmrow.t, bmrow.t[0:128, bmrow.e0:bmrow.e0 + nb], 0, 128, bmrow.e0, bmrow.e0 + nb)
    sc = TV(bmrow.t, bmrow.t[0:128, bmrow.e0 + nb:bmrow.e0 + 2 * nb], 0, 128,
            bmrow.e0 + nb, bmrow.e0 + 2 * nb)
    inv = TV(bmrow.t, bmrow.t[0:128, bmrow.e0 + 2 * nb:bmrow.e0 + 3 * nb], 0, 128,
             bmrow.e0 + 2 * nb, bmrow.e0 + 3 * nb)
    invb = TV(inv.t, bass.AP(inv.t, inv.e0, [[inv.t.shape[1], 128], [1, nb], [0, 16]]),
              0, 128, inv.e0, inv.e1)
    scb = TV(sc.t, bass.AP(sc.t, sc.e0, [[sc.t.shape[1], 128], [1, nb], [0, 16]]),
             0, 128, sc.e0, sc.e1)

    kb.act(bufA, src, AF.Abs)
    kb.reduce("v", bm, b3(bufA), ALU.max)
    kb.ts("v", sc, bm, 1.0 / 6.0, 1e-12, op0=ALU.mult, op1=ALU.max)
    kb.recip(inv, sc)
    kb.tt("v", b3(bufA), b3(src), invb, ALU.mult)
    kb.act(bufB, bufA, AF.Abs)
    kb.ts(ladder_eng, bufA, bufB, 0.25, None, op0=ALU.is_gt)
    for m in FP4_MIDS:
        kb.stt(ladder_eng, bufA, bufB, float(m), bufA, ALU.is_gt, ALU.add)
    kb.ts("v", bufB, bufA, -4.0, 0.0, op0=ALU.add, op1=ALU.max)
    kb.tt("v", bufB, bufA, bufB, ALU.add)
    kb.ts("v", bufA, bufA, -6.0, 0.0, op0=ALU.add, op1=ALU.max)
    kb.stt("v", bufA, bufB, 0.5, bufA, ALU.mult, ALU.add)
    kb.ts("v", bufB, src, 0.0, -2.0, op0=ALU.is_lt, op1=ALU.mult)
    kb.ts("v", bufB, bufB, 1.0, None, op0=ALU.add)
    kb.tt("v", bufA, bufA, bufB, ALU.mult)
    kb.tt("v", b3(dst), b3(bufA), scb, ALU.mult)


FB_XQT = 0
FB_XQC = 4096
FB_HQ = 8192
FB_HQC = 8704
FB_WG = 9216
FB_WD = 17408
FB_WQS = 21504


def emit_l0_ffn(kb, ctx, W):
    x, pf, pb, AB, AFt = ctx.x, ctx.pf, ctx.pb, ctx.AB, ctx.AF
    NCH = [(0, 512), (512, 512), (1024, 512), (1536, 512), (2048, 512), (2560, 256)]

    accs = Slot(AFt, AF_ACC, 4096)
    wqs = Slot(AB, FB_WQS, 1024)

    def quant_w(hap, hname, nrows, ncols, dst_ap, dst_name, dst_cols):
        ccs = []
        c0_ = 0
        while c0_ < ncols:
            cw_ = min(512, ncols - c0_)
            ccs.append((c0_, cw_))
            c0_ += cw_
        for r in range(nrows // 128):
            for c0, cw in ccs:
                piece = accs.v((0, 128), (0, cw))
                kb.dma("q", piece, DV(hap[r * 128:(r + 1) * 128, c0:c0 + cw], hname,
                                      (0, 128, r * ncols + c0, r * ncols + c0 + cw)))
                bufA = accs.v((0, 128), (512, 1024))
                bufB = accs.v((0, 128), (1024, 1536))
                bmr = accs.v((0, 128), (1536, 1536 + 96))
                q = wqs.v((0, 128), (0, cw))
                emit_fp4_quant(kb, ctx, piece, q, cw, bufA, bufB, bmr)
                for j in range(cw // 128):
                    bk = pb[j % 2]
                    kb.trans(V(bk, (0, 128), (0, 128)),
                             wqs.v((0, 128), (j * 128, (j + 1) * 128)), V(ctx.idb))
                    st = wqs.v((0, 128), (512 + j * 128, 512 + (j + 1) * 128))
                    kb.copy("s", st, V(bk, (0, 128), (0, 128)))
                    kb.dma("p", DV(dst_ap[c0 + j * 128:c0 + (j + 1) * 128,
                                          r * 128:(r + 1) * 128], dst_name,
                                   (0, 128, (c0 + j * 128) * dst_cols + r * 128,
                                    (c0 + j * 128) * dst_cols + r * 128 + 128)),
                           st)

    quant_w(W["w_gate"], "w_gate", 2816, 1024, W["wqgT"], "wqgT", 2816)
    quant_w(W["w_up"], "w_up", 2816, 1024, W["wquT"], "wquT", 2816)
    quant_w(W["w_down"], "w_down", 1024, 2816, W["wqdT"], "wqdT", 1024)

    xqt = Slot(AB, FB_XQT, 4096)
    xqc = Slot(AB, FB_XQC, 4096)
    hq = Slot(AB, FB_HQ, 512)
    hqc = Slot(AB, FB_HQC, 512)
    wg = Slot(AB, FB_WG, 8192)
    wd = Slot(AB, FB_WD, 4096)
    sqs = Slot(AFt, AF_SQ, 1024)
    wstg = Slot(AFt, AF_WSTG, 1024)
    lnt = Slot(AFt, AF_LNT, 512)

    for tc in range(4):
        cs = tc * 512
        for tl in range(4):
            for half in range(2):
                bank = pf[half]
                for cj in range(4):
                    ci = half * 4 + cj
                    kb.trans(V(bank, (0, 128), (cj * 128, (cj + 1) * 128)),
                             V(x, (0, 128), (ci * T + cs + tl * 128, ci * T + cs + (tl + 1) * 128)),
                             V(ctx.idf))
                base = tl * 1024 + half * 512
                kb.copy("s", accs.v((0, 128), (base, base + 512)), V(bank, (0, 128), (0, 512)))
        for tl in range(4):
            for half in range(2):
                base = tl * 1024 + half * 512
                emit_fp4_quant(kb, ctx, accs.v((0, 128), (base, base + 512)),
                               xqt.v((0, 128), (base, base + 512)), 512,
                               sqs.v((0, 128), (0, 512)), sqs.v((0, 128), (512, 1024)),
                               lnt.v((0, 128), (0, 96)))
        for tl in range(4):
            for half in range(2):
                bank = pb[half]
                for cj in range(4):
                    ci = half * 4 + cj
                    kb.trans(V(bank, (0, 128), (cj * 128, (cj + 1) * 128)),
                             xqt.v((0, 128), (tl * 1024 + ci * 128, tl * 1024 + (ci + 1) * 128)),
                             V(ctx.idb))
                base = (half * 4) * 512 + tl * 128
                ap = bass.AP(AB, FB_XQC + base, [[AB_END, 128], [512, 4], [1, 128]])
                kb.op("v", lambda e, o=ap, i=bank: e.tensor_copy(out=o, in_=i[:, 0:512]),
                      reads=(V(bank, (0, 128), (0, 512)),),
                      writes=(TV(AB, ap, 0, 128, FB_XQC + base, FB_XQC + base + 3 * 512 + 128),))
        for ci in range(NCT):
            kb.memset("v", accs.v((0, 128), (ci * 512, (ci + 1) * 512)), 0.0)
        for n0, nw in NCH:
            nk = nw // 128
            for k in range(NCT):
                kb.dma("q", wg.v((0, 128), (k * 512, k * 512 + nw)),
                       DV(W["wqgT"][k * 128:(k + 1) * 128, n0:n0 + nw], "wqgT",
                          (0, 128, k * 2816 + n0, k * 2816 + n0 + nw)))
                kb.dma("p", wg.v((0, 128), (4096 + k * 512, 4096 + k * 512 + nw)),
                       DV(W["wquT"][k * 128:(k + 1) * 128, n0:n0 + nw], "wquT",
                          (0, 128, 2 ** 23 + k * 2816 + n0, 2 ** 23 + k * 2816 + n0 + nw)))
            for kk in range(nk):
                kb.dma("q", wd.v((0, 128), (kk * 1024, (kk + 1) * 1024)),
                       DV(W["wqdT"][n0 + kk * 128:n0 + (kk + 1) * 128, 0:1024], "wqdT",
                          (0, 128, (n0 + kk * 128) * 1024, (n0 + kk * 128) * 1024 + 1024)))
            for tl in range(4):
                ps_g, ps_u = pf[2], pf[3]
                for k in range(NCT):
                    kb.mm(V(ps_g, (0, 128), (0, nw)),
                          xqc.v((0, 128), (k * 512 + tl * 128, k * 512 + (tl + 1) * 128)),
                          wg.v((0, 128), (k * 512, k * 512 + nw)),
                          start=(k == 0), stop=(k == NCT - 1))
                for k in range(NCT):
                    kb.mm(V(ps_u, (0, 128), (0, nw)),
                          xqc.v((0, 128), (k * 512 + tl * 128, k * 512 + (tl + 1) * 128)),
                          wg.v((0, 128), (4096 + k * 512, 4096 + k * 512 + nw)),
                          start=(k == 0), stop=(k == NCT - 1))
                sg = wstg.v((0, 128), (512, 512 + nw))
                kb.act(sg, V(ps_g, (0, 128), (0, nw)), AF.Silu)
                hh = wstg.v((0, 128), (0, nw))
                kb.tt("v", hh, sg, V(ps_u, (0, 128), (0, nw)), ALU.mult)
                emit_fp4_quant(kb, ctx, hh, hq.v((0, 128), (0, nw)), nw,
                               sqs.v((0, 128), (0, 512)), sqs.v((0, 128), (512, 1024)),
                               lnt.v((0, 128), (0, 96)), ladder_eng="v")
                for kk in range(nk):
                    bk = pb[kk % 2]
                    kb.trans(V(bk, (0, 128), (0, 128)),
                             hq.v((0, 128), (kk * 128, (kk + 1) * 128)), V(ctx.idb))
                    kb.copy("s", hqc.v((0, 128), (kk * 128, (kk + 1) * 128)),
                            V(bk, (0, 128), (0, 128)))
                for oc in range(NCT):
                    ps_d = pf[4 + (oc % 2)]
                    for kk in range(nk):
                        kb.mm(V(ps_d, (0, 128), (0, 128)),
                              wd.v((0, 128), (kk * 1024 + oc * 128, kk * 1024 + (oc + 1) * 128)),
                              hqc.v((0, 128), (kk * 128, (kk + 1) * 128)),
                              start=(kk == 0), stop=(kk == nk - 1))
                    av = accs.v((0, 128), (oc * 512 + tl * 128, oc * 512 + (tl + 1) * 128))
                    kb.tt("v", av, V(ps_d, (0, 128), (0, 128)), av, ALU.add)
        for ci in range(NCT):
            xv = V(x, (0, 128), (ci * T + cs, ci * T + cs + 512))
            kb.tt("v", xv, accs.v((0, 128), (ci * 512, (ci + 1) * 512)), xv, ALU.add)



LA_WT = 0
LA_XBC = 24576
LA_WST = 28672


def emit_l1_attn(kb, ctx, W):
    x, pf, pb, AB, AFt = ctx.x, ctx.pf, ctx.pb, ctx.AB, ctx.AF
    wts = Slot(AB, LA_WT, 24576)
    xbc = Slot(AB, LA_XBC, 4096)
    wst = Slot(AB, LA_WST, 1024)
    accs = Slot(AFt, AF_ACC, 4096)
    wstg = Slot(AFt, AF_WSTG, 1024)
    sqs = Slot(AFt, AF_SQ, 1024)
    biases = kb.sbuf("l1a_bias", [128, 24], F32)
    carry = kb.sbuf("l1a_carry", [128, 8], F32)

    for wi, bn in enumerate(("b_f", "b_i", "b_h")):
        kb.dma_slow("q", V(biases, (0, 128), (wi * 8, wi * 8 + 8)),
                    DV(W[bn].rearrange("(c p) -> p c", p=128), "h_" + bn))
    kb.memset("v", V(carry), 0.0)

    for wi, wn in enumerate(("w_f", "w_i", "w_h")):
        for r in range(8):
            stg = wstg.v((0, 128), (0, 1024))
            kb.dma("q", stg, DV(W[wn][r * 128:(r + 1) * 128, :], wn,
                                (0, 128, r * 1024, (r + 1) * 1024)))
            wb = wst.v((0, 128), (0, 1024))
            kb.copy("p", wb, stg)
            for k in range(8):
                bk = pb[k % 2]
                kb.trans(V(bk, (0, 128), (0, 128)),
                         wst.v((0, 128), (k * 128, (k + 1) * 128)), V(ctx.idb))
                kb.copy("s", wts.v((0, 128), (wi * 8192 + k * 1024 + r * 128,
                                              wi * 8192 + k * 1024 + (r + 1) * 128)),
                        V(bk, (0, 128), (0, 128)))

    F_, I_, H_ = 0, 512, 1024
    S1, S2, S3 = 1536, 2048, 2560
    for tc in range(4):
        cs = tc * 512
        for ci in range(NCT):
            kb.copy("v", xbc.v((0, 128), (ci * 512, (ci + 1) * 512)),
                    V(x, (0, 128), (ci * T + cs, ci * T + cs + 512)))
        for m in range(NCT):
            for wi in range(3):
                ps = pf[2 + wi]
                for k in range(NCT):
                    kb.mm(V(ps), wts.v((0, 128), (wi * 8192 + k * 1024 + m * 128,
                                                  wi * 8192 + k * 1024 + (m + 1) * 128)),
                          xbc.v((0, 128), (k * 512, (k + 1) * 512)),
                          start=(k == 0), stop=(k == NCT - 1))
            FB = accs.v((0, 128), (F_, F_ + 512))
            IB = accs.v((0, 128), (I_, I_ + 512))
            HB = accs.v((0, 128), (H_, H_ + 512))
            kb.act(FB, V(pf[2]), AF.Identity, bias=V(biases, (0, 128), (m, m + 1)))
            kb.act(IB, V(pf[3]), AF.Identity, bias=V(biases, (0, 128), (8 + m, 8 + m + 1)))
            kb.act(HB, V(pf[4]), AF.Identity, bias=V(biases, (0, 128), (16 + m, 16 + m + 1)))
            A = accs.v((0, 128), (S1, S1 + 512))
            B = accs.v((0, 128), (S2, S2 + 512))
            C = accs.v((0, 128), (S3, S3 + 512))
            kb.act(A, FB, AF.Abs)
            kb.act(A, A, AF.Exp, scale=-1.0)
            kb.act(A, A, AF.Ln, bias=1.0)
            kb.ts("v", B, FB, -1.0, 0.0, op0=ALU.mult, op1=ALU.max)
            kb.tt("v", A, A, B, ALU.add)
            kb.act(B, IB, AF.Abs)
            kb.act(B, B, AF.Exp, scale=-1.0)
            kb.act(B, B, AF.Ln, bias=1.0)
            kb.ts("v", C, IB, -1.0, 0.0, op0=ALU.mult, op1=ALU.max)
            kb.tt("v", B, B, C, ALU.add)
            kb.tt("v", A, A, B, ALU.subtract)
            kb.act(B, A, AF.Sigmoid, scale=-1.0)
            kb.act(A, A, AF.Sigmoid)
            kb.act(C, HB, AF.Sigmoid)
            kb.ts("v", FB, HB, 0.5, None, op0=ALU.add)
            kb.tt("v", FB, FB, C, ALU.subtract)
            kb.ts("v", IB, HB, 0.0, None, op0=ALU.is_ge)
            kb.tt("v", FB, FB, IB, ALU.mult)
            kb.tt("v", C, C, FB, ALU.add)
            kb.tt("v", A, A, C, ALU.mult)
            kb.scan(HB, B, A, V(carry, (0, 128), (m, m + 1)), op0=ALU.mult, op1=ALU.add)
            kb.copy("v", V(carry, (0, 128), (m, m + 1)),
                    TV(accs.th, accs.th[0:128, accs.off + H_ + 511:accs.off + H_ + 512],
                       0, 128, accs.off + H_ + 511, accs.off + H_ + 512))
            xv = V(x, (0, 128), (m * T + cs, m * T + cs + 512))
            kb.tt("v", xv, HB, xv, ALU.add)



LF_QCM = 0
LF_KCM = 8192
LF_VTM = 16384
LF_OCM = 24576
LF_HB1 = 24576
LF_XBC = 35840
LF_WS1 = 39936
LF_PA = 41984
LF_PB = 42496
LF_PT = 42624
LF_WS2 = 43264


def emit_l1_ffn(kb, ctx, W):
    x, pf, pb, AB, AFt = ctx.x, ctx.pf, ctx.pb, ctx.AB, ctx.AF
    qcm = Slot(AB, LF_QCM, 8192)
    kcm = Slot(AB, LF_KCM, 8192)
    vtm = Slot(AB, LF_VTM, 8192)
    ocm = Slot(AB, LF_OCM, 8192)
    hb1 = Slot(AB, LF_HB1, 11264)
    xbc = Slot(AB, LF_XBC, 4096)
    ws1 = Slot(AB, LF_WS1, 2048)
    ws2 = Slot(AB, LF_WS2, 2048)
    pA = Slot(AB, LF_PA, 512)
    pB = Slot(AB, LF_PB, 128)
    pts = Slot(AB, LF_PT, 640)
    accs = Slot(AFt, AF_ACC, 4096)
    sqs = Slot(AFt, AF_SQ, 1024)
    wstg = Slot(AFt, AF_WSTG, 1024)
    lnt = Slot(AFt, AF_LNT, 512)
    qks = kb.sbuf("lf_qks", [128, 4], F32)
    lng = kb.sbuf("lf_lng", [128, 16], F32)
    lnb = kb.sbuf("lf_lnb", [128, 16], F32)
    b1a = kb.sbuf("lf_b1a", [128, 44], F32)
    b1g = kb.sbuf("lf_b1g", [128, 44], F32)
    b2c = kb.sbuf("lf_b2", [128, 16], F32)
    tabs = Slot(AFt, AF_ACC + 2048, 2048)

    for c4 in range(4):
        kb.dma("q", DV(W["xres"][0:128, c4 * 4096:(c4 + 1) * 4096], "xres",
                       (0, 128, c4 * 4096, (c4 + 1) * 4096)),
               V(x, (0, 128), (c4 * 4096, (c4 + 1) * 4096)))
    kb.dma("q", V(qks), DV(W["qk_scale_l"], "h_qks"))
    kb.dma("q", V(lng), DV(W["lng_l"], "h_lng"))
    kb.dma("q", V(lnb), DV(W["lnb_l"], "h_lnb"))
    kb.dma("q", V(b1a), DV(W["b1a_l"], "h_b1a"))
    kb.dma("q", V(b1g), DV(W["b1g_l"], "h_b1g"))
    kb.dma("q", V(b2c), DV(W["b2_l"], "h_b2"))

    def shift_chunk(cs, dst_slot):
        for ci in range(NCT):
            dv = dst_slot.v((0, 128), (ci * 512, (ci + 1) * 512))
            if ci < 4:
                kb.copy("v", dv, V(x, (0, 128), (ci * T + cs, ci * T + cs + 512)))
            elif cs > 0:
                kb.copy("v", dv, V(x, (0, 128), (ci * T + cs - 1, ci * T + cs + 511)))
            else:
                kb.memset("v", dst_slot.v((0, 128), (ci * 512, ci * 512 + 1)), 0.0)
                kb.copy("v", dst_slot.v((0, 128), (ci * 512 + 1, (ci + 1) * 512)),
                        V(x, (0, 128), (ci * T, ci * T + 511)))

    for l in range(2):
        wqkv, wo, w1, w2 = W["wqkv"], W["wo"], W["w1"], W["w2"]
        for tc in range(4):
            cs = tc * 512
            shift_chunk(cs, xbc)
            for tb in range(4):
                kb.dma("q", tabs.v((0, 128), (tb * 512, (tb + 1) * 512)),
                       DV(W[("xp_csq", "xp_ssq", "xp_csk", "xp_ssk")[tb]][0:128, cs:cs + 512],
                          "h_xt%d" % tb, (0, 128, cs, cs + 512)))
            for kind in range(2):
                for k in range(NCT):
                    wb = ws1.v((0, 128), ((k % 2) * 512, (k % 2) * 512 + 512))
                    kb.dma("q", wb, DV(wqkv[l * 1024 + k * 128:l * 1024 + (k + 1) * 128,
                                            kind * 512:(kind + 1) * 512], "s_wqkv",
                                       (l, l + 1, k * 1536 + kind * 512,
                                        k * 1536 + (kind + 1) * 512)))
                    for j in range(4):
                        kb.mm(V(pf[j]), ws1.v((0, 128), ((k % 2) * 512 + j * 128,
                                                         (k % 2) * 512 + (j + 1) * 128)),
                              xbc.v((0, 128), (k * 512, (k + 1) * 512)),
                              start=(k == 0), stop=(k == NCT - 1))
                dst = qcm if kind == 0 else kcm
                co, so = (0, 0) if kind == 0 else (2 * 512, 2)
                for j in range(4):
                    ps = pf[j]
                    qf = accs.v((0, 128), (0, 512))
                    kb.act(qf, V(ps), AF.Copy)
                    sq = sqs.v((0, 128), (0, 512))
                    kb.act(sq, V(ps), AF.Square)
                    n2 = pf[4]
                    kb.mm(V(n2, (0, 2), (0, 512)), V(ctx.e2), sq, start=True, stop=True)
                    nr = lnt.v((0, 2), (0, 512))
                    kb.act(nr, V(n2, (0, 2), (0, 512)), AF.Sqrt)
                    kb.ts("v", nr, nr, 1e-12, None, op0=ALU.max)
                    kb.recip(nr, nr)
                    rb = pf[5]
                    kb.mm(V(rb, (0, 128), (0, 512)), V(ctx.e2t, (0, 2), (0, 128)), nr,
                          start=True, stop=True)
                    qh = accs.v((0, 128), (512, 1024))
                    kb.tt("v", qh, qf, V(rb, (0, 128), (0, 512)), ALU.mult)
                    kb.ts("v", qh, qh, V(qks, (0, 128), (l * 2 + so // 2, l * 2 + so // 2 + 1)),
                          None, op0=ALU.mult)
                    qrot = accs.v((0, 128), (1024, 1536))
                    for (pd, psr) in ((0, 32), (32, 0), (64, 96), (96, 64)):
                        kb.copy("v", TV(AFt, AFt[pd:pd + 32, accs.off + 1024:accs.off + 1536],
                                        pd, pd + 32, accs.off + 1024, accs.off + 1536),
                                TV(AFt, AFt[psr:psr + 32, accs.off + 512:accs.off + 1024],
                                   psr, psr + 32, accs.off + 512, accs.off + 1024))
                    kb.tt("v", qh, qh, tabs.v((0, 128), (co, co + 512)), ALU.mult)
                    kb.tt("v", qrot, qrot, tabs.v((0, 128), (co + 512, co + 1024)), ALU.mult)
                    kb.tt("v", dst.v((0, 128), (j * 2048 + cs, j * 2048 + cs + 512)),
                          qh, qrot, ALU.add)
            for half in range(2):
                for kk in range(4):
                    k = half * 4 + kk
                    kb.dma("q", ws2.v((0, 128), (kk * 512, (kk + 1) * 512)),
                           DV(wqkv[l * 1024 + k * 128:l * 1024 + (k + 1) * 128, 1024:1536],
                              "s_wqkv", (l, l + 1, k * 1536 + 1024, k * 1536 + 1536)))
                for tl in range(4):
                    ps = pf[2 + tl]
                    for kk in range(4):
                        k = half * 4 + kk
                        kb.mm(V(ps), xbc.v((0, 128), (k * 512 + tl * 128, k * 512 + (tl + 1) * 128)),
                              ws2.v((0, 128), (kk * 512, (kk + 1) * 512)),
                              start=(kk == 0 and half == 0), stop=(kk == 3 and half == 1))
                    if half == 1:
                        kb.copy("s", vtm.v((0, 128), ((tc * 4 + tl) * 512, (tc * 4 + tl + 1) * 512)),
                                V(ps))
        for h in range(8):
            hp, ho = h // 2, (h % 2) * 64
            for qtc in range(16):
                jlo = max(0, qtc - 4)
                noff = qtc - jlo
                bA, bB = pf[0], pf[1]
                if noff > 0:
                    kb.mm(V(bA, (0, 128), (0, noff * 128)),
                          qcm.v((ho, ho + 64), (hp * 2048 + qtc * 128, hp * 2048 + (qtc + 1) * 128)),
                          kcm.v((ho, ho + 64), (hp * 2048 + jlo * 128, hp * 2048 + qtc * 128)),
                          start=True, stop=True, tile_position=(ho, 0))
                kb.mm(V(bB, (0, 128), (0, 128)),
                      qcm.v((ho, ho + 64), (hp * 2048 + qtc * 128, hp * 2048 + (qtc + 1) * 128)),
                      kcm.v((ho, ho + 64), (hp * 2048 + qtc * 128, hp * 2048 + (qtc + 1) * 128)),
                      start=True, stop=True, tile_position=(ho, 0))
                if noff == 4:
                    kb.tt("v", V(bA, (0, 128), (0, 128)), V(bA, (0, 128), (0, 128)),
                          V(ctx.tri), ALU.add)
                kb.tt("v", V(bB, (0, 128), (0, 128)), V(bB, (0, 128), (0, 128)),
                      V(ctx.tri2), ALU.add)
                mx = lnt.v((0, 128), (0, 1))
                m2x = lnt.v((0, 128), (1, 2))
                if noff > 0:
                    kb.reduce("v", mx, V(bA, (0, 128), (0, noff * 128)), ALU.max)
                    kb.reduce("v", m2x, V(bB, (0, 128), (0, 128)), ALU.max)
                    kb.tt("v", mx, mx, m2x, ALU.max)
                else:
                    kb.reduce("v", mx, V(bB, (0, 128), (0, 128)), ALU.max)
                negm = lnt.v((0, 128), (2, 3))
                kb.ts("v", negm, mx, -1.0, None, op0=ALU.mult)
                sA = lnt.v((0, 128), (3, 4))
                sB = lnt.v((0, 128), (4, 5))
                if noff > 0:
                    kb.act(pA.v((0, 128), (0, noff * 128)), V(bA, (0, 128), (0, noff * 128)),
                           AF.Exp, bias=negm, accum=sA)
                kb.act(pB.v((0, 128), (0, 128)), V(bB, (0, 128), (0, 128)),
                       AF.Exp, bias=negm, accum=sB)
                den = lnt.v((0, 128), (5, 6))
                if noff > 0:
                    kb.tt("v", den, sA, sB, ALU.add)
                else:
                    kb.copy("v", den, sB)
                kb.recip(den, den)
                if noff > 0:
                    kb.ts("v", pA.v((0, 128), (0, noff * 128)),
                          pA.v((0, 128), (0, noff * 128)), den, None, op0=ALU.mult)
                kb.ts("v", pB.v((0, 128), (0, 128)), pB.v((0, 128), (0, 128)),
                      den, None, op0=ALU.mult)
                for jj in range(noff + 1):
                    bk = pb[jj % 2]
                    src = (pA.v((0, 128), (jj * 128, (jj + 1) * 128)) if jj < noff
                           else pB.v((0, 128), (0, 128)))
                    kb.trans(V(bk, (0, 128), (0, 128)), src, V(ctx.idb))
                    kb.copy("s", pts.v((0, 128), (jj * 128, (jj + 1) * 128)),
                            V(bk, (0, 128), (0, 128)))
                yb = pf[2]
                for jj in range(noff + 1):
                    j = jlo + jj
                    kb.mm(V(yb, (0, 64), (0, 128)),
                          vtm.v((0, 128), (j * 512 + h * 64, j * 512 + (h + 1) * 64)),
                          pts.v((0, 128), (jj * 128, (jj + 1) * 128)),
                          start=(jj == 0), stop=(jj == noff))
                kb.copy("s", ocm.v((ho, ho + 64), (hp * 2048 + qtc * 128,
                                                   hp * 2048 + (qtc + 1) * 128)),
                        V(yb, (0, 64), (0, 128)))
        for tc in range(4):
            cs = tc * 512
            for mg in range(2):
                for k in range(4):
                    wb = ws1.v((0, 128), ((k % 2) * 512, (k % 2) * 512 + 512))
                    kb.dma("q", wb, DV(wo[l * 512 + k * 128:l * 512 + (k + 1) * 128,
                                          mg * 512:(mg + 1) * 512],
                                       "s_wo", (l, l + 1, k * 1024 + mg * 512,
                                                k * 1024 + (mg + 1) * 512)))
                    for j in range(4):
                        kb.mm(V(pf[j]), ws1.v((0, 128), ((k % 2) * 512 + j * 128,
                                                         (k % 2) * 512 + (j + 1) * 128)),
                              ocm.v((0, 128), (k * 2048 + cs, k * 2048 + cs + 512)),
                              start=(k == 0), stop=(k == 3))
                for j in range(4):
                    oc_ = mg * 4 + j
                    xv = V(x, (0, 128), (oc_ * T + cs, oc_ * T + cs + 512))
                    kb.tt("v", xv, V(pf[j]), xv, ALU.add)
        AW = [(mi * 512, min(512, 2730 - mi * 512)) for mi in range(6)]
        for tc in range(4):
            cs = tc * 512
            ps_a, ps_b = pf[0], pf[1]
            first = True
            for ci in range(NCT):
                xv = (V(x, (0, 128), (ci * T + cs, ci * T + cs + 512)) if ci < 4 else
                      (V(x, (0, 128), (ci * T + cs - 1, ci * T + cs + 511)) if cs > 0 else
                       V(x, (0, 128), (ci * T, ci * T + 511))))
                ov = (V(ps_a, (0, 1), (0, 512)) if (ci < 4 or cs > 0) else
                      V(ps_a, (0, 1), (1, 512)))
                kb.mm(ov, V(ctx.ones_col), xv, start=first, stop=(ci == NCT - 1))
                first = False
            first = True
            for ci in range(NCT):
                xv = (V(x, (0, 128), (ci * T + cs, ci * T + cs + 512)) if ci < 4 else
                      (V(x, (0, 128), (ci * T + cs - 1, ci * T + cs + 511)) if cs > 0 else
                       V(x, (0, 128), (ci * T, ci * T + 511))))
                w_ = 512 if (ci < 4 or cs > 0) else 511
                sq = sqs.v((0, 128), (0, w_))
                kb.act(sq, xv, AF.Square)
                ov = (V(ps_b, (0, 1), (0, 512)) if w_ == 512 else V(ps_b, (0, 1), (1, 512)))
                kb.mm(ov, V(ctx.ones_col), sq, start=first, stop=(ci == NCT - 1))
                first = False
            if cs == 0:
                pass
            A = sqs.v((0, 1), (0, 512))
            B = sqs.v((0, 1), (512, 1024))
            kb.ts("v", A, V(ps_a, (0, 1), (0, 512)), 1.0 / D, op0=ALU.mult)
            kb.tt("v", B, A, A, ALU.mult)
            kb.stt("v", B, V(ps_b, (0, 1), (0, 512)), 1.0 / D, B, ALU.mult, ALU.subtract)
            kb.act(B, B, AF.Sqrt, bias=1e-5)
            kb.recip(B, B)
            kb.tt("v", A, A, B, ALU.mult)
            kb.mm(V(pf[2], (0, 128), (0, 512)), V(ctx.ones, (0, 1), (0, 128)), B,
                  start=True, stop=True)
            kb.mm(V(pf[3], (0, 128), (0, 512)), V(ctx.ones, (0, 1), (0, 128)), A,
                  start=True, stop=True)
            for ci in range(NCT):
                xv = (V(x, (0, 128), (ci * T + cs, ci * T + cs + 512)) if ci < 4 else
                      (V(x, (0, 128), (ci * T + cs - 1, ci * T + cs + 511)) if cs > 0 else
                       V(x, (0, 128), (ci * T, ci * T + 511))))
                w_ = 512 if (ci < 4 or cs > 0) else 511
                off = 512 - w_
                t1 = accs.v((0, 128), (0, 512))
                t1w = accs.v((0, 128), (off, 512))
                if off:
                    kb.memset("v", accs.v((0, 128), (0, 1)), 0.0)
                kb.tt("v", t1w, xv, V(pf[2], (0, 128), (off, 512)), ALU.mult)
                kb.tt("v", t1, t1, V(pf[3], (0, 128), (0, 512)), ALU.subtract)
                kb.ts("v", xbc.v((0, 128), (ci * 512, (ci + 1) * 512)), t1,
                      V(lng, (0, 128), (l * 8 + ci, l * 8 + ci + 1)),
                      V(lnb, (0, 128), (l * 8 + ci, l * 8 + ci + 1)),
                      op0=ALU.mult, op1=ALU.add)
            for gi, (g0, gw) in enumerate(AW):
                nmc = (gw + 127) // 128
                abuf = accs.v((0, 128), (0, 2048))
                for k in range(NCT):
                    wb = ws1.v((0, 128), ((k % 2) * 512, (k % 2) * 512 + gw))
                    kb.dma("q", wb, DV(w1[l * 1024 + k * 128:l * 1024 + (k + 1) * 128,
                                          g0:g0 + gw], "s_w1",
                                       (l, l + 1, k * 5460 + g0, k * 5460 + g0 + gw)))
                    for j in range(nmc):
                        mw = min(128, gw - j * 128)
                        kb.mm(V(pf[j], (0, mw), (0, 512)),
                              ws1.v((0, 128), ((k % 2) * 512 + j * 128, (k % 2) * 512 + j * 128 + mw)),
                              xbc.v((0, 128), (k * 512, (k + 1) * 512)),
                              start=(k == 0), stop=(k == NCT - 1))
                for j in range(nmc):
                    mw = min(128, gw - j * 128)
                    mi = (g0 + j * 128) // 128
                    kb.act(TV(AFt, AFt[0:mw, accs.off + j * 512:accs.off + (j + 1) * 512],
                              0, mw, accs.off + j * 512, accs.off + (j + 1) * 512),
                           V(pf[j], (0, mw), (0, 512)), AF.Identity,
                           bias=V(b1a, (0, mw), (l * 22 + mi, l * 22 + mi + 1)))
                for k in range(NCT):
                    wb = ws1.v((0, 128), ((k % 2) * 512, (k % 2) * 512 + gw))
                    kb.dma("q", wb, DV(w1[l * 1024 + k * 128:l * 1024 + (k + 1) * 128,
                                          2730 + g0:2730 + g0 + gw], "s_w1",
                                       (l, l + 1, k * 5460 + 2730 + g0,
                                        k * 5460 + 2730 + g0 + gw)))
                    for j in range(nmc):
                        mw = min(128, gw - j * 128)
                        kb.mm(V(pf[j], (0, mw), (0, 512)),
                              ws1.v((0, 128), ((k % 2) * 512 + j * 128, (k % 2) * 512 + j * 128 + mw)),
                              xbc.v((0, 128), (k * 512, (k + 1) * 512)),
                              start=(k == 0), stop=(k == NCT - 1))
                for j in range(nmc):
                    mw = min(128, gw - j * 128)
                    mi = (g0 + j * 128) // 128
                    gb = sqs.v((0, mw), (0, 512))
                    kb.act(gb, V(pf[j], (0, mw), (0, 512)), AF.Gelu,
                           bias=V(b1g, (0, mw), (l * 22 + mi, l * 22 + mi + 1)))
                    kb.tt("v", hb1.v((0, mw), (mi * 512, (mi + 1) * 512)),
                          TV(AFt, AFt[0:mw, accs.off + j * 512:accs.off + (j + 1) * 512],
                             0, mw, accs.off + j * 512, accs.off + (j + 1) * 512),
                          gb, ALU.mult)
            for mg in range(2):
                for kk in range(22):
                    kw = min(128, 2730 - kk * 128)
                    kb.dma("q", ws1.v((0, kw), ((kk % 2) * 512, (kk % 2) * 512 + 512)),
                           DV(w2[l * 2730 + kk * 128:l * 2730 + kk * 128 + kw,
                                 mg * 512:(mg + 1) * 512],
                              "s_w2", (l, l + 1, kk * 1024 + mg * 512,
                                       kk * 1024 + (mg + 1) * 512)))
                    for j in range(4):
                        kb.mm(V(pf[j]), ws1.v((0, kw), ((kk % 2) * 512 + j * 128,
                                                        (kk % 2) * 512 + (j + 1) * 128)),
                              hb1.v((0, kw), (kk * 512, (kk + 1) * 512)),
                              start=(kk == 0), stop=(kk == 21))
                for j in range(4):
                    oc_ = mg * 4 + j
                    xv = V(x, (0, 128), (oc_ * T + cs, oc_ * T + cs + 512))
                    kb.stt("v", xv, V(pf[j]),
                           V(b2c, (0, 128), (l * 8 + oc_, l * 8 + oc_ + 1)), xv,
                           ALU.add, ALU.add)
    for c8 in range(8):
        xrs = accs.v((0, 128), (0, 2048))
        kb.dma("q", xrs, DV(W["xres"][0:128, c8 * 2048:(c8 + 1) * 2048], "xres",
                            (0, 128, c8 * 2048, (c8 + 1) * 2048)))
        xv = V(x, (0, 128), (c8 * 2048, (c8 + 1) * 2048))
        kb.tt("v", xv, xrs, xv, ALU.add)



_PROG_CACHE = {}


def _np_dt(a):
    return F32 if a.dtype == np.float32 else BF16


def _declare_consts(nc, consts):
    return {n: nc.dram_tensor(n, list(consts[n].shape), _np_dt(consts[n]),
                              kind="ExternalInput").ap() for n in CONST_NAMES}


def _build_block(block):
    consts = host_consts()
    nc = bass.Bass("TRN2", target_bir_lowering=False)
    x_h = nc.dram_tensor("x", [T, D], F32, kind="ExternalInput").ap()
    o_h = nc.dram_tensor("o", [T, D], F32, kind="ExternalOutput").ap()
    hb = _declare_consts(nc, consts)
    W = {}
    if block == "l0a":
        shapes = dict(w_dq=(1024, 4096), w_uq=(4096, 1024), w_dkv=(1024, 256),
                      w_uk=(256, 1024), w_uv=(256, 1024), w_qr=(4096, 512),
                      w_kr=(1024, 32), w_o=(1024, 1024))
        for n, s in shapes.items():
            W[n] = nc.dram_tensor(n, list(s), F32, kind="ExternalInput").ap()
        W["scale_q4"] = hb["l0_scale_q4"]
        W["scale_k"] = hb["l0_scale_k"]
        for sname, nrows, ncols in (("s_w_dq", 1024, 4096), ("s_w_uq", 4096, 1024),
                                    ("s_w_qr", 4096, 512), ("s_w_o", 1024, 1024)):
            W[sname] = nc.dram_tensor(sname, [nrows, ncols], BF16, kind="Internal").ap()
    elif block == "l0f":
        for n, s in dict(w_gate=(2816, 1024), w_up=(2816, 1024),
                         w_down=(1024, 2816)).items():
            W[n] = nc.dram_tensor(n, list(s), F32, kind="ExternalInput").ap()
        for n, s in dict(wqgT=(1024, 2816), wquT=(1024, 2816),
                         wqdT=(2816, 1024)).items():
            W[n] = nc.dram_tensor(n, list(s), BF16, kind="Internal").ap()
    elif block == "l1a":
        for n, s in dict(w_f=(1024, 1024), w_i=(1024, 1024), w_h=(1024, 1024)).items():
            W[n] = nc.dram_tensor(n, list(s), F32, kind="ExternalInput").ap()
        for n in ("b_f", "b_i", "b_h"):
            W[n] = nc.dram_tensor(n, [1024], F32, kind="ExternalInput").ap()
    elif block == "l1f":
        for n, s in dict(wqkv=(2, 1024, 1536), wo=(2, 512, 1024),
                         w1=(2, 1024, 5460), w2=(2, 2730, 1024)).items():
            W[n] = nc.dram_tensor(n, list(s), F32, kind="ExternalInput").ap()
        for n, s in dict(qk_scale_l=(128, 4), lng_l=(128, 16), lnb_l=(128, 16),
                         b1a_l=(128, 44), b1g_l=(128, 44), b2_l=(128, 16)).items():
            W[n] = nc.dram_tensor(n, list(s), F32, kind="ExternalInput").ap()
        for n in ("xp_csq", "xp_ssq", "xp_csk", "xp_ssk"):
            W[n] = hb[n]
        W["xres"] = nc.dram_tensor("xres", [128, NCT * T], F32, kind="Internal").ap()
        for sname, nrows, ncols in (("s_wqkv", 2048, 1536), ("s_wo", 1024, 1024),
                                    ("s_w1", 2048, 5460), ("s_w2", 5460, 1024)):
            W[sname] = nc.dram_tensor(sname, [nrows, ncols], BF16, kind="Internal").ap()
    kb = KB(nc)
    ctx = Ctx(kb)
    alloc_common(kb, ctx)
    emit_load_consts(kb, ctx, hb)
    if block == "l0a":
        _pp = [p for p in PREPASS_WEIGHTS if p[0].startswith("l0a")]
        srcs = {"l0a_w_dq": W["w_dq"], "l0a_w_uq": W["w_uq"], "l0a_w_qr": W["w_qr"],
                "l0a_w_o": W["w_o"]}
        import kernel as _K
        _saved = globals()["PREPASS_WEIGHTS"]
        globals()["PREPASS_WEIGHTS"] = _pp
        emit_prepass(kb, ctx, srcs, W)
        globals()["PREPASS_WEIGHTS"] = _saved
    elif block == "l1f":
        _pp = [p for p in PREPASS_WEIGHTS if p[0].startswith("l1f")]
        srcs = {"l1f_wqkv": W["wqkv"].rearrange("l a b -> (l a) b"),
                "l1f_wo": W["wo"].rearrange("l a b -> (l a) b"),
                "l1f_w1": W["w1"].rearrange("l a b -> (l a) b"),
                "l1f_w2": W["w2"].rearrange("l a b -> (l a) b")}
        _saved = globals()["PREPASS_WEIGHTS"]
        globals()["PREPASS_WEIGHTS"] = _pp
        emit_prepass(kb, ctx, srcs, W)
        globals()["PREPASS_WEIGHTS"] = _saved
        W["wqkv"] = W["s_wqkv"]
        W["wo"] = W["s_wo"]
        W["w1"] = W["s_w1"]
        W["w2"] = W["s_w2"]
    emit_load_x(kb, ctx, x_h)
    if block == "l0a":
        emit_l0_attn(kb, ctx, W)
    elif block == "l0f":
        emit_l0_ffn(kb, ctx, W)
    elif block == "l1a":
        emit_l1_attn(kb, ctx, W)
    elif block == "l1f":
        emit_l1_ffn(kb, ctx, W)
    emit_postln(kb, ctx)
    emit_store_x(kb, ctx, o_h)
    kb.finalize()
    return nc, consts


def _l1f_layouts(q_scale, k_scale, ln_g, ln_b, b1, b2):
    d = {}
    d["qk_scale_l"] = np.stack([np.tile(q_scale[0], 2), np.tile(k_scale[0], 2),
                                np.tile(q_scale[1], 2), np.tile(k_scale[1], 2)],
                               axis=1).astype(np.float32)

    def cols(vec, n):
        out = np.zeros((128, n), np.float32)
        for i in range(n):
            seg = vec[i * 128:(i + 1) * 128]
            out[:len(seg), i] = seg
        return out
    d["lng_l"] = np.concatenate([cols(ln_g[l], 8) for l in range(2)], 1)
    d["lnb_l"] = np.concatenate([cols(ln_b[l], 8) for l in range(2)], 1)
    d["b1a_l"] = np.concatenate([cols(b1[l][:2730], 22) for l in range(2)], 1)
    d["b1g_l"] = np.concatenate([cols(b1[l][2730:], 22) for l in range(2)], 1)
    d["b2_l"] = np.concatenate([cols(b2[l], 8) for l in range(2)], 1)
    return d


def kernel(**inputs):
    inp = {k: np.asarray(v) for k, v in inputs.items()}
    if FUSED:
        return kernel_fused(inp)
    f32 = np.float32
    blocks = [
        ("l0a", {"w_dq": inp["L0_attn_W_DQ"], "w_uq": inp["L0_attn_W_UQ"],
                 "w_dkv": inp["L0_attn_W_DKV"], "w_uk": inp["L0_attn_W_UK"],
                 "w_uv": inp["L0_attn_W_UV"], "w_qr": inp["L0_attn_W_QR"],
                 "w_kr": inp["L0_attn_W_KR"], "w_o": inp["L0_attn_W_O"]}),
        ("l0f", {"w_gate": inp["L0_ffn_w_gate"], "w_up": inp["L0_ffn_w_up"],
                 "w_down": inp["L0_ffn_w_down"]}),
        ("l1a", {"w_f": inp["L1_attn_W_f"], "b_f": inp["L1_attn_b_f"],
                 "w_i": inp["L1_attn_W_i"], "b_i": inp["L1_attn_b_i"],
                 "w_h": inp["L1_attn_W_h"], "b_h": inp["L1_attn_b_h"]}),
        ("l1f", {"wqkv": inp["L1_ffn_wqkv"], "wo": inp["L1_ffn_wo"],
                 "w1": inp["L1_ffn_w1"], "w2": inp["L1_ffn_w2"],
                 **_l1f_layouts(inp["L1_ffn_q_scale"], inp["L1_ffn_k_scale"],
                                inp["L1_ffn_ln_g"], inp["L1_ffn_ln_b"],
                                inp["L1_ffn_b1"], inp["L1_ffn_b2"])}),
    ]
    cur = np.ascontiguousarray(inp["L0_attn_x"].astype(f32))
    for bname, wmap in blocks:
        if bname not in _PROG_CACHE:
            _PROG_CACHE[bname] = _build_block(bname)
        nc, consts = _PROG_CACHE[bname]
        cmap = {n: consts[n] for n in CONST_NAMES}
        wmap2 = {k: np.ascontiguousarray(v.astype(f32) if v.dtype != np.float32 else v)
                 for k, v in wmap.items()}
        in_maps = [{"x": np.ascontiguousarray(cur[c]), **wmap2, **cmap}
                   for c in range(8)]
        res = run_bass_kernel_spmd(nc, in_maps, core_ids=list(range(8)))
        cur = np.stack([res.results[c]["o"] for c in range(8)])
    return cur.astype(np.float32)



def _build_full():
    consts = host_consts()
    nc = bass.Bass("TRN2", target_bir_lowering=False)
    x_h = nc.dram_tensor("x", [T, D], F32, kind="ExternalInput").ap()
    o_h = nc.dram_tensor("o", [T, D], F32, kind="ExternalOutput").ap()
    hb = _declare_consts(nc, consts)
    Wa, Wf, Wr, Wl = {}, {}, {}, {}
    for n, s in dict(w_dq=(1024, 4096), w_uq=(4096, 1024), w_dkv=(1024, 256),
                     w_uk=(256, 1024), w_uv=(256, 1024), w_qr=(4096, 512),
                     w_kr=(1024, 32), w_o=(1024, 1024)).items():
        Wa[n] = nc.dram_tensor("l0a_" + n, list(s), F32, kind="ExternalInput").ap()
    Wa["scale_q4"] = hb["l0_scale_q4"]
    Wa["scale_k"] = hb["l0_scale_k"]
    for n, s in dict(w_gate=(2816, 1024), w_up=(2816, 1024), w_down=(1024, 2816)).items():
        Wf[n] = nc.dram_tensor("l0f_" + n, list(s), F32, kind="ExternalInput").ap()
    for n, s in dict(wqgT=(1024, 2816), wquT=(1024, 2816), wqdT=(2816, 1024)).items():
        Wf[n] = nc.dram_tensor("l0f_" + n, list(s), BF16, kind="Internal").ap()
    for n in ("w_f", "w_i", "w_h"):
        Wr[n] = nc.dram_tensor("l1a_" + n, [1024, 1024], F32, kind="ExternalInput").ap()
    for n in ("b_f", "b_i", "b_h"):
        Wr[n] = nc.dram_tensor("l1a_" + n, [1024], F32, kind="ExternalInput").ap()
    for n, s in dict(wqkv=(2, 1024, 1536), wo=(2, 512, 1024), w1=(2, 1024, 5460),
                     w2=(2, 2730, 1024)).items():
        Wl[n] = nc.dram_tensor("l1f_" + n, list(s), F32, kind="ExternalInput").ap()
    for n, s in dict(qk_scale_l=(128, 4), lng_l=(128, 16), lnb_l=(128, 16),
                     b1a_l=(128, 44), b1g_l=(128, 44), b2_l=(128, 16)).items():
        Wl[n] = nc.dram_tensor("l1f_" + n, list(s), F32, kind="ExternalInput").ap()
    for n in ("xp_csq", "xp_ssq", "xp_csk", "xp_ssk"):
        Wl[n] = hb[n]
    Wl["xres"] = nc.dram_tensor("xres", [128, NCT * T], F32, kind="Internal").ap()
    scr = {}
    for iname, sname, nrows, ncols in PREPASS_WEIGHTS:
        scr[sname] = nc.dram_tensor(sname, [nrows, ncols], BF16, kind="Internal").ap()
    srcs = {"l0a_w_dq": Wa["w_dq"], "l0a_w_uq": Wa["w_uq"], "l0a_w_qr": Wa["w_qr"],
            "l0a_w_o": Wa["w_o"], "l1f_wqkv": Wl["wqkv"].rearrange("l a b -> (l a) b"),
            "l1f_wo": Wl["wo"].rearrange("l a b -> (l a) b"),
            "l1f_w1": Wl["w1"].rearrange("l a b -> (l a) b"),
            "l1f_w2": Wl["w2"].rearrange("l a b -> (l a) b")}
    for sname in ("s_w_dq", "s_w_uq", "s_w_qr", "s_w_o"):
        Wa[sname] = scr[sname]
    Wl["wqkv"] = scr["s_wqkv"]
    Wl["wo"] = scr["s_wo"]
    Wl["w1"] = scr["s_w1"]
    Wl["w2"] = scr["s_w2"]

    kb = KB(nc)
    ctx = Ctx(kb)
    alloc_common(kb, ctx)
    emit_load_consts(kb, ctx, hb)
    emit_prepass(kb, ctx, srcs, scr)
    emit_load_x(kb, ctx, x_h)
    emit_l0_attn(kb, ctx, Wa)
    emit_postln(kb, ctx)
    emit_l0_ffn(kb, ctx, Wf)
    emit_postln(kb, ctx)
    emit_l1_attn(kb, ctx, Wr)
    emit_postln(kb, ctx)
    emit_l1_ffn(kb, ctx, Wl)
    emit_postln(kb, ctx)
    emit_store_x(kb, ctx, o_h)
    kb.finalize()
    return nc, consts


FUSED = True


def kernel_fused(inp):
    if "full" not in _PROG_CACHE:
        _PROG_CACHE["full"] = _build_full()
    nc, consts = _PROG_CACHE["full"]
    cmap = {n: consts[n] for n in CONST_NAMES}
    wmap = {}
    for pre, names in (("L0_attn_", ("W_DQ", "W_UQ", "W_DKV", "W_UK", "W_UV",
                                     "W_QR", "W_KR", "W_O")),):
        for n in names:
            wmap["l0a_w_" + n[2:].lower()] = inp[pre + n]
    wmap["l0f_w_gate"] = inp["L0_ffn_w_gate"]
    wmap["l0f_w_up"] = inp["L0_ffn_w_up"]
    wmap["l0f_w_down"] = inp["L0_ffn_w_down"]
    for n in ("W_f", "W_i", "W_h", "b_f", "b_i", "b_h"):
        wmap["l1a_" + n.lower() if n[0] == "W" else "l1a_" + n] = inp["L1_attn_" + n]
    for n in ("wqkv", "wo", "w1", "w2"):
        wmap["l1f_" + n] = inp["L1_ffn_" + n]
    lay = _l1f_layouts(inp["L1_ffn_q_scale"], inp["L1_ffn_k_scale"],
                       inp["L1_ffn_ln_g"], inp["L1_ffn_ln_b"],
                       inp["L1_ffn_b1"], inp["L1_ffn_b2"])
    for k, v in lay.items():
        wmap["l1f_" + k] = v
    wmap = {k: np.ascontiguousarray(np.asarray(v).astype(np.float32)) for k, v in wmap.items()}
    x = np.ascontiguousarray(inp["L0_attn_x"].astype(np.float32))
    in_maps = [{"x": x[c], **wmap, **cmap} for c in range(8)]
    res = run_bass_kernel_spmd(nc, in_maps, core_ids=list(range(8)))
    return np.stack([res.results[c]["o"] for c in range(8)]).astype(np.float32)



PREPASS_WEIGHTS = [
    ("l0a_w_dq", "s_w_dq", 1024, 4096),
    ("l0a_w_uq", "s_w_uq", 4096, 1024),
    ("l0a_w_qr", "s_w_qr", 4096, 512),
    ("l0a_w_o", "s_w_o", 1024, 1024),
    ("l1f_wqkv", "s_wqkv", 2 * 1024, 1536),
    ("l1f_wo", "s_wo", 2 * 512, 1024),
    ("l1f_w1", "s_w1", 2 * 1024, 5460),
    ("l1f_w2", "s_w2", 2 * 2730, 1024),
]


def emit_prepass(kb, ctx, srcs, dsts):
    stage_f = Slot(ctx.AF, AF_ACC, 2048)
    stage_b = Slot(ctx.AB, 0, 4096)
    bi = 0
    for iname, sname, nrows, ncols in PREPASS_WEIGHTS:
        src, dst = srcs[iname], dsts[sname]
        for r in range((nrows + 127) // 128):
            rh = min(128, nrows - r * 128)
            c0 = 0
            while c0 < ncols:
                cw = min(2048, ncols - c0)
                sf = stage_f.v((0, rh), ((bi % 1) * 0, cw)) if False else stage_f.v((0, rh), (0, cw))
                kb.dma("q", sf, DV(src[r * 128:r * 128 + rh, c0:c0 + cw], "pp_" + iname,
                                   (0, rh, r * ncols + c0, r * ncols + c0 + cw)))
                sb = stage_b.v((0, rh), ((bi % 2) * 2048, (bi % 2) * 2048 + cw))
                kb.copy("s" if bi % 2 == 0 else "v", sb, sf)
                kb.dma("p", DV(dst[r * 128:r * 128 + rh, c0:c0 + cw], sname,
                               (0, rh, r * ncols + c0, r * ncols + c0 + cw)), sb)
                bi += 1
                c0 += cw


# version 06
# speedup vs baseline: 1.0757x; 1.0062x over previous
import numpy as np
import ml_dtypes
from contextlib import ExitStack

import concourse.bass as bass
import concourse.mybir as mybir
from concourse.bass_utils import run_bass_kernel_spmd

F32 = mybir.dt.float32
BF16 = mybir.dt.bfloat16
AF = mybir.ActivationFunctionType
ALU = mybir.AluOpType
AX = mybir.AxisListType

DT_SIZE = {F32: 4, BF16: 2}


class TV:
    __slots__ = ("t", "ap", "p0", "p1", "e0", "e1")

    def __init__(self, t, ap, p0, p1, e0, e1):
        self.t = t
        self.ap = ap
        self.p0, self.p1, self.e0, self.e1 = p0, p1, e0, e1


def V(th, ps=None, es=None):
    shape = list(th.shape)
    p0, p1 = (0, shape[0]) if ps is None else ps
    e0, e1 = (0, shape[1]) if es is None else es
    return TV(th, th[p0:p1, e0:e1], p0, p1, e0, e1)


def V3(th, ps, es, ap):
    return TV(th, ap, ps[0], ps[1], es[0], es[1])


def DV(th_ap, tname, region=None):
    if region is None:
        region = (0, 1, 0, 1)
    return TV(_Named(tname), th_ap, *region)


class _Named:
    __slots__ = ("name",)

    def __init__(self, name):
        self.name = name



ENGINES = ("t", "v", "s", "p", "q")


class KB:
    def __init__(self, nc):
        self.nc = nc
        self.stack = ExitStack()
        self.prog = {e: [] for e in ENGINES}
        self.semh = {}
        for name in ("ct", "cv", "cs", "cp"):
            self.semh[name] = self.stack.enter_context(nc.semaphore("sem_" + name))
        self.csem = {"t": "ct", "v": "cv", "s": "cs", "p": "cp"}
        self.dring = {}
        for q, r in (("q", 8), ("p", 8), ("s", 4), ("v", 2), ("t", 2)):
            names = []
            for i in range(r):
                n = f"d{q}{i}"
                self.semh[n] = self.stack.enter_context(nc.semaphore("sem_" + n))
                names.append(n)
            self.dring[q] = names
        self.dma_i = {q: 0 for q in self.dring}
        self.cnt = {k: 0 for k in self.semh}
        self.clock = {e: {} for e in ENGINES}
        self.snap = {k: {} for k in self.semh}
        self.writes = {}
        self.reads = {}
        self.psum_names = set()
        self.eng_obj = {"t": nc.tensor, "v": nc.vector, "s": nc.scalar,
                        "p": nc.gpsimd, "q": nc.sync}
        self._tensors = []
        self.fconst_t = self.sbuf("fconst_tab", [1, 32], F32)
        self.fconst_vals = {}

    def fconst(self, val):
        val = float(val)
        if val in (0.0, 1.0):
            return val, None
        if val not in self.fconst_vals:
            idx = len(self.fconst_vals)
            assert idx < 32
            self.fconst_vals[val] = idx
            view = V(self.fconst_t, (0, 1), (idx, idx + 1))
            self.memset("p", view, val)
            self.nc.const_aps.register(F32, val, self.fconst_t[0:1, idx:idx + 1])
        idx = self.fconst_vals[val]
        return val, V(self.fconst_t, (0, 1), (idx, idx + 1))

    def sbuf(self, name, shape, dtype):
        th = self.stack.enter_context(self.nc.sbuf_tensor(name, list(shape), dtype))
        self._tensors.append(th)
        return th

    def psum(self, name, shape, dtype):
        th = self.stack.enter_context(self.nc.psum_tensor(name, list(shape), dtype))
        self._tensors.append(th)
        self.psum_names.add(name)
        return th

    @staticmethod
    def _overlap(a, b):
        return not (a[1] <= b[0] or b[1] <= a[0] or a[3] <= b[2] or b[3] <= a[2])

    def _dep_events(self, eng, rviews, wviews):
        evs = []
        own = self.csem.get(eng)
        for v_ in rviews:
            for ent in self.writes.get(v_.t.name, ()):
                if self._overlap((v_.p0, v_.p1, v_.e0, v_.e1), ent):
                    evs.extend(ent[4])
            if v_.t.name in self.psum_names:
                for ent in self.reads.get(v_.t.name, ()):
                    if self._overlap((v_.p0, v_.p1, v_.e0, v_.e1), ent):
                        evs.extend(e for e in ent[4] if e[0] != own)
        for v_ in wviews:
            box = (v_.p0, v_.p1, v_.e0, v_.e1)
            for ent in self.writes.get(v_.t.name, ()):
                if self._overlap(box, ent):
                    evs.extend(ent[4])
            for ent in self.reads.get(v_.t.name, ()):
                if self._overlap(box, ent):
                    evs.extend(ent[4])
        if eng == "t":
            evs = [e for e in evs if e[0] != "ct"]
        return evs

    def _emit_waits(self, eng, evs):
        clk = self.clock[eng]
        need = {}
        for s_, v_ in evs:
            if clk.get(s_, 0) < v_ and need.get(s_, 0) < v_:
                need[s_] = v_
        for s_, v_ in sorted(need.items()):
            semh = self.semh[s_]
            self.prog[eng].append(("w", semh, v_))
            snap = self.snap[s_].get(v_)
            if snap:
                for ks, kv in snap.items():
                    if clk.get(ks, 0) < kv:
                        clk[ks] = kv
            if clk.get(s_, 0) < v_:
                clk[s_] = v_

    def _record(self, views, table, ev):
        for v_ in views:
            lst = table.setdefault(v_.t.name, [])
            box = (v_.p0, v_.p1, v_.e0, v_.e1)
            if table is self.writes:
                for tab in (self.writes, self.reads):
                    l2 = tab.get(v_.t.name)
                    if l2:
                        tab[v_.t.name] = [e for e in l2 if not (
                            box[0] <= e[0] and e[1] <= box[1] and box[2] <= e[2] and e[3] <= box[3])]
                lst = table.setdefault(v_.t.name, [])
            lst.append(box + ((ev,),))
            if len(lst) > 120:
                old, rest = lst[:60], lst[60:]
                p0 = min(e[0] for e in old); p1 = max(e[1] for e in old)
                e0 = min(e[2] for e in old); e1 = max(e[3] for e in old)
                evs = tuple(sorted({x for e in old for x in e[4]}, key=lambda t: (t[0], -t[1])))
                seen = {}
                for s_, vv in evs:
                    seen[s_] = max(seen.get(s_, 0), vv)
                table[v_.t.name] = [(p0, p1, e0, e1, tuple(seen.items()))] + rest

    def op(self, eng, fn, reads=(), writes=(), dma=False):
        evs = self._dep_events(eng, reads, writes)
        if dma:
            ring = self.dring[eng]
            sname = ring[self.dma_i[eng] % len(ring)]
            self.dma_i[eng] += 1
            if self.cnt[sname] > 0:
                evs = list(evs) + [(sname, self.cnt[sname])]
        else:
            sname = self.csem[eng]
        self._emit_waits(eng, evs)
        inc = 16 if dma else 1
        self.cnt[sname] += inc
        ev = (sname, self.cnt[sname])
        semh = self.semh[sname]
        self.prog[eng].append(("c", fn, semh, inc))
        clk = dict(self.clock[eng])
        clk[sname] = self.cnt[sname]
        self.snap[sname][self.cnt[sname]] = clk
        self._record(reads, self.reads, ev)
        self._record(writes, self.writes, ev)
        return ev

    def raw_wait(self, eng, sname, val):
        self.prog[eng].append(("w", self.semh[sname], val))
        clk = self.clock[eng]
        if clk.get(sname, 0) < val:
            clk[sname] = val

    def barrier_all_on(self, eng):
        evs = [(s, c) for s, c in self.cnt.items() if c > 0]
        self._emit_waits(eng, evs)

    def dma(self, q, dst, src):
        return self.op(q, lambda e: e.dma_start(out=dst.ap, in_=src.ap),
                       reads=(src,), writes=(dst,), dma=True)

    def dma_slow(self, q, dst, src):
        return self.op(q, lambda e: e.dma_start(out=dst.ap, in_=src.ap,
                                                allow_slow_non_contiguous=True),
                       reads=(src,), writes=(dst,), dma=True)

    def copy(self, eng, dst, src):
        if eng == "s":
            return self.op(eng, lambda e: e.copy(out=dst.ap, in_=src.ap),
                           reads=(src,), writes=(dst,))
        return self.op(eng, lambda e: e.tensor_copy(out=dst.ap, in_=src.ap),
                       reads=(src,), writes=(dst,))

    def memset(self, eng, dst, c):
        return self.op(eng, lambda e: e.memset(dst.ap, c), writes=(dst,))

    def tt(self, eng, out, in0, in1, alu):
        return self.op(eng, lambda e: e.tensor_tensor(out=out.ap, in0=in0.ap, in1=in1.ap, op=alu),
                       reads=(in0, in1), writes=(out,))

    def ts(self, eng, out, in0, s1, s2=None, op0=ALU.mult, op1=None, s1v=None, s2v=None):
        reads = [in0]
        a1 = s1
        if isinstance(s1, TV):
            reads.append(s1); a1 = s1.ap
        a2 = s2
        if isinstance(s2, TV):
            reads.append(s2); a2 = s2.ap
        if op1 is None:
            fn = lambda e: e.tensor_scalar(out=out.ap, in0=in0.ap, scalar1=a1,
                                           scalar2=None, op0=op0)
        else:
            fn = lambda e: e.tensor_scalar(out=out.ap, in0=in0.ap, scalar1=a1,
                                           scalar2=a2, op0=op0, op1=op1)
        return self.op(eng, fn, reads=tuple(reads), writes=(out,))

    def stt(self, eng, out, in0, scalar, in1, op0, op1):
        reads = [in0, in1]
        a = scalar
        if isinstance(scalar, TV):
            reads.append(scalar); a = scalar.ap
        return self.op(eng, lambda e: e.scalar_tensor_tensor(
            out=out.ap, in0=in0.ap, scalar=a, in1=in1.ap, op0=op0, op1=op1),
            reads=tuple(reads), writes=(out,))

    def scan(self, out, d0, d1, initial, op0=ALU.mult, op1=ALU.add):
        reads = [d0, d1]
        ini = initial
        if isinstance(initial, TV):
            reads.append(initial); ini = initial.ap
        return self.op("v", lambda e: e.tensor_tensor_scan(
            out=out.ap, data0=d0.ap, data1=d1.ap, initial=ini, op0=op0, op1=op1),
            reads=tuple(reads), writes=(out,))

    def act(self, out, in_, func, bias=0.0, scale=1.0, accum=None):
        reads = [in_]
        b = bias
        if isinstance(bias, TV):
            reads.append(bias); b = bias.ap
        else:
            b, cv = self.fconst(bias)
            if cv is not None:
                reads.append(cv)
        sc = scale
        if isinstance(scale, TV):
            reads.append(scale); sc = scale.ap
        else:
            sc, cv = self.fconst(scale)
            if cv is not None:
                reads.append(cv)
        writes = [out]
        if accum is not None:
            writes.append(accum)
            fn = lambda e: e.activation(out=out.ap, in_=in_.ap, func=func, bias=b,
                                        scale=sc, accum_out=accum.ap)
        else:
            fn = lambda e: e.activation(out=out.ap, in_=in_.ap, func=func, bias=b, scale=sc)
        return self.op("s", fn, reads=tuple(reads), writes=tuple(writes))

    def reduce(self, eng, out, in_, alu, negate=False):
        return self.op(eng, lambda e: e.tensor_reduce(
            out=out.ap, in_=in_.ap, axis=AX.X, op=alu, negate=negate),
            reads=(in_,), writes=(out,))

    def recip(self, out, in_):
        return self.op("v", lambda e: e.reciprocal(out=out.ap, in_=in_.ap),
                       reads=(in_,), writes=(out,))

    def mm(self, out, lhsT, rhs, start, stop, tile_position=None):
        return self.op("t", lambda e: e.matmul(out.ap, lhsT.ap, rhs.ap, start=start,
                                               stop=stop, tile_position=tile_position),
                       reads=(lhsT, rhs), writes=(out,))

    def trans(self, out, in_, ident):
        return self.op("t", lambda e: e.transpose(out.ap, in_.ap, ident.ap),
                       reads=(in_, ident), writes=(out,))

    def finalize(self):
        self.barrier_all_on("q")
        nc = self.nc
        with nc.Block() as block:
            def runner(stream):
                def go(engobj):
                    for item in stream:
                        if item[0] == "w":
                            engobj.wait_ge(item[1], item[2])
                        else:
                            _, fn, semh, inc = item
                            fn(engobj).then_inc(semh, inc)
                return go

            block.tensor(runner(self.prog["t"]))
            block.vector(runner(self.prog["v"]))
            block.scalar(runner(self.prog["s"]))
            block.gpsimd(runner(self.prog["p"]))
            block.sync(runner(self.prog["q"]))
        self.stack.close()



T = 2048
D = 1024
NCT = D // 128
NTT = T // 128


class Slot:
    __slots__ = ("th", "off", "ln")

    def __init__(self, th, off, ln):
        self.th, self.off, self.ln = th, off, ln

    def v(self, ps=None, es=None):
        p0, p1 = (0, self.th.shape[0]) if ps is None else ps
        e0, e1 = (0, self.ln) if es is None else es
        a, b = self.off + e0, self.off + e1
        return TV(self.th, self.th[p0:p1, a:b], p0, p1, a, b)

    def vap(self, rel_off, dims, ps, es):
        pitch = self.th.shape[1]
        ap = bass.AP(self.th, self.off + rel_off, [[pitch, ps[1] - ps[0]]] + dims)
        if ps[0] != 0:
            ap = bass.AP(self.th, (self.off + rel_off), [[pitch, ps]] )
        return TV(self.th, ap, ps[0], ps[1], self.off + es[0], self.off + es[1])


AB_KC = 0
AB_VTM = 16384
AB_KR4 = 33024
AB_SCL = 35072
AB_CQ = 35840
AB_XBC = 44032
AB_QC = 46080
AB_QR = 48128
AB_WSM = 49152
AB_END = 56576

AF_ACC = 0
AF_WSTG = 4096
AF_SQ = 5120
AF_LNT = 6144
AF_END = 6656


def host_consts():
    c = {}
    c["ident_f32"] = np.eye(128, dtype=np.float32)
    c["ident_bf16"] = np.eye(128, dtype=ml_dtypes.bfloat16)
    c["ones_f32"] = np.ones((1, 128), dtype=np.float32)
    i = np.arange(128)
    c["tri_mask"] = np.where(i[:, None] <= i[None, :], 0.0, -1e30).astype(np.float32)
    DHR = 32
    t = np.arange(T, dtype=np.float32)
    inv = 1.0 / 10000.0 ** (np.arange(0, DHR, 2, dtype=np.float32) / DHR)
    freqs = t[:, None] * inv[None, :]
    emb = np.concatenate([freqs, freqs], -1)
    scale = (np.cos(emb) + np.sin(emb)).astype(np.float32)
    c["l0_scale_k"] = np.ascontiguousarray(scale.T).astype(ml_dtypes.bfloat16)
    c["l0_scale_q4"] = (np.tile(scale.T, (4, 1)) /
                        np.sqrt(np.float32(96.0))).astype(ml_dtypes.bfloat16)
    c["tri2_mask"] = np.where(i[None, :] <= i[:, None], 0.0, -1e30).astype(np.float32)
    e2 = np.zeros((128, 2), np.float32)
    e2[0:64, 0] = 1.0
    e2[64:128, 1] = 1.0
    c["e2_mat"] = e2
    c["e2t_mat"] = np.ascontiguousarray(e2.T)
    DH, WIN = 64, 512
    half = np.arange(0, DH, 2, dtype=np.float32)
    invf = 1.0 / 10000.0 ** (half / DH)
    tt = np.arange(T, dtype=np.float32)
    fr = tt[:, None] * invf[None, :]
    fr2 = np.concatenate([fr, fr], -1)
    basev = (half + 0.4 * DH) / (1.4 * DH)
    power = (tt - T // 2) / float(WIN // 2)
    sc = basev[None, :] ** power[:, None]
    sc2 = np.concatenate([sc, sc], -1)
    cosv, sinv = np.cos(fr2), np.sin(fr2)
    sgn = np.where(np.arange(DH) < DH // 2, -1.0, 1.0).astype(np.float32)
    csq = (8.0 * cosv * sc2).T
    ssq = (8.0 * sinv * sc2).T * sgn[:, None]
    csk = (cosv / sc2).T
    ssk = (sinv / sc2).T * sgn[:, None]
    c["xp_csq"] = np.ascontiguousarray(np.tile(csq, (2, 1)).astype(np.float32))
    c["xp_ssq"] = np.ascontiguousarray(np.tile(ssq, (2, 1)).astype(np.float32))
    c["xp_csk"] = np.ascontiguousarray(np.tile(csk, (2, 1)).astype(np.float32))
    c["xp_ssk"] = np.ascontiguousarray(np.tile(ssk, (2, 1)).astype(np.float32))
    return c


CONST_NAMES = ("ident_f32", "ident_bf16", "ones_f32", "tri_mask",
               "l0_scale_k", "l0_scale_q4", "tri2_mask", "e2_mat", "e2t_mat",
               "xp_csq", "xp_ssq", "xp_csk", "xp_ssk")


class Ctx:
    def __init__(self, kb):
        self.kb = kb
        self.d = {}

    def __getattr__(self, k):
        try:
            return self.d[k]
        except KeyError:
            raise AttributeError(k)


def alloc_common(kb, ctx):
    ctx.d["x"] = kb.sbuf("x_cm", [128, NCT * T], F32)
    ctx.d["AB"] = kb.sbuf("arena_bf", [128, AB_END], BF16)
    ctx.d["AF"] = kb.sbuf("arena_f32", [128, AF_END], F32)
    ctx.d["pf"] = [kb.psum(f"psf{i}", [128, 512], F32) for i in range(6)]
    ctx.d["pb"] = [kb.psum(f"psb{i}", [128, 1024], BF16) for i in range(2)]
    ctx.d["idf"] = kb.sbuf("idf", [128, 128], F32)
    ctx.d["idb"] = kb.sbuf("idb", [128, 128], BF16)
    ctx.d["ones"] = kb.sbuf("ones_c", [1, 128], F32)
    ctx.d["ones_col"] = kb.sbuf("ones_col", [128, 1], F32)
    ctx.d["tri"] = kb.sbuf("tri_c", [128, 128], F32)
    ctx.d["tri2"] = kb.sbuf("tri2_c", [128, 128], F32)
    ctx.d["e2"] = kb.sbuf("e2_c", [128, 2], F32)
    ctx.d["e2t"] = kb.sbuf("e2t_c", [2, 128], F32)
    AB, AFt = ctx.AB, ctx.AF
    ctx.d["stage"] = Slot(AFt, AF_ACC, 2048)


def emit_load_consts(kb, ctx, hb):
    kb.dma("q", V(ctx.idf), DV(hb["ident_f32"], "h_identf"))
    kb.dma("q", V(ctx.idb), DV(hb["ident_bf16"], "h_identb"))
    kb.dma("q", V(ctx.ones), DV(hb["ones_f32"], "h_ones"))
    kb.dma_slow("q", V(ctx.ones_col), DV(hb["ones_f32"].rearrange("o p -> p o"), "h_ones2"))
    kb.dma("q", V(ctx.tri), DV(hb["tri_mask"], "h_tri"))
    kb.dma("q", V(ctx.tri2), DV(hb["tri2_mask"], "h_tri2"))
    kb.dma("q", V(ctx.e2), DV(hb["e2_mat"], "h_e2"))
    kb.dma("q", V(ctx.e2t), DV(hb["e2t_mat"], "h_e2t"))


def emit_load_x(kb, ctx, x_hbm):
    x, stage, pf = ctx.x, ctx.stage, ctx.pf
    for ti in range(NTT):
        sb = ti % 2
        stv = stage.v(es=(sb * 1024, (sb + 1) * 1024))
        kb.dma("q", stv, DV(x_hbm[ti * 128:(ti + 1) * 128, :], "x_in",
                            (0, 128, ti * 1024, (ti + 1) * 1024)))
        for half in range(2):
            bank = pf[half]
            for cj in range(4):
                ci = half * 4 + cj
                src = stage.v(es=(sb * 1024 + ci * 128, sb * 1024 + (ci + 1) * 128))
                kb.trans(V(bank, (0, 128), (cj * 128, (cj + 1) * 128)), src, V(ctx.idf))
            base = (half * 4) * T + ti * 128
            ap = bass.AP(x, base, [[NCT * T, 128], [T, 4], [1, 128]])
            dtv = TV(x, ap, 0, 128, base, base + 3 * T + 128)
            kb.op("s", lambda e, o=ap, i=bank: e.copy(out=o, in_=i[:, 0:512]),
                  reads=(V(bank, (0, 128), (0, 512)),), writes=(dtv,))


def emit_store_x(kb, ctx, out_hbm):
    x, stage, pf = ctx.x, ctx.stage, ctx.pf
    for ti in range(NTT):
        sb = ti % 2
        for half in range(2):
            bank = pf[2 + half]
            for cj in range(4):
                ci = half * 4 + cj
                src = TV(x, x[:, ci * T + ti * 128: ci * T + (ti + 1) * 128],
                         0, 128, ci * T + ti * 128, ci * T + (ti + 1) * 128)
                kb.trans(V(bank, (0, 128), (cj * 128, (cj + 1) * 128)), src, V(ctx.idf))
            kb.copy("s", stage.v(es=(sb * 1024 + half * 512, sb * 1024 + (half + 1) * 512)),
                    V(bank, (0, 128), (0, 512)))
        kb.dma("q", DV(out_hbm[ti * 128:(ti + 1) * 128, :], "x_out",
                       (0, 128, ti * 1024, (ti + 1) * 1024)),
               stage.v(es=(sb * 1024, (sb + 1) * 1024)))



def emit_postln(kb, ctx, eps=1e-5):
    x, pf, oc = ctx.x, ctx.pf, ctx.ones_col
    AFt = ctx.AF
    for tc in range(T // 512):
        cs = tc * 512
        ps_a, ps_b, ps_c, ps_d = pf[0], pf[1], pf[2], pf[3]
        for ci in range(NCT):
            kb.mm(V(ps_a, (0, 1), (0, 512)), V(oc),
                  V(x, (0, 128), (ci * T + cs, ci * T + cs + 512)),
                  start=(ci == 0), stop=(ci == NCT - 1))
        for ci in range(NCT):
            sq = Slot(AFt, AF_SQ, 1024).v(es=((ci % 2) * 512, (ci % 2) * 512 + 512))
            kb.act(sq, V(x, (0, 128), (ci * T + cs, ci * T + cs + 512)), AF.Square)
            kb.mm(V(ps_b, (0, 1), (0, 512)), V(oc), sq,
                  start=(ci == 0), stop=(ci == NCT - 1))
        A = Slot(ctx.AF, AF_SQ, 1024).v((0, 1), (0, 512))
        B = Slot(ctx.AF, AF_SQ, 1024).v((0, 1), (512, 1024))
        kb.ts("v", A, V(ps_a, (0, 1), (0, 512)), 1.0 / D, op0=ALU.mult)
        kb.tt("v", B, A, A, ALU.mult)
        kb.stt("v", B, V(ps_b, (0, 1), (0, 512)), 1.0 / D, B, ALU.mult, ALU.subtract)
        kb.act(B, B, AF.Sqrt, bias=float(eps))
        kb.recip(B, B)
        kb.tt("v", A, A, B, ALU.mult)
        kb.mm(V(ps_c, (0, 128), (0, 512)), V(ctx.ones, (0, 1), (0, 128)), B,
              start=True, stop=True)
        kb.mm(V(ps_d, (0, 128), (0, 512)), V(ctx.ones, (0, 1), (0, 128)), A,
              start=True, stop=True)
        for ci in range(NCT):
            xv = V(x, (0, 128), (ci * T + cs, ci * T + cs + 512))
            t1 = Slot(AFt, AF_LNT, 512).v()
            kb.tt("v", t1, xv, V(ps_c, (0, 128), (0, 512)), ALU.mult)
            kb.tt("v", xv, t1, V(ps_d, (0, 128), (0, 512)), ALU.subtract)



QCW = 256
NQC = T // QCW


def emit_l0_attn(kb, ctx, W):
    x, pf, AB = ctx.x, ctx.pf, ctx.AB
    kc = Slot(AB, AB_KC, 16384)
    vtm = Slot(AB, AB_VTM, 16640)
    kr4 = Slot(AB, AB_KR4, 2048)
    scl = Slot(AB, AB_SCL, 768)
    cqs = Slot(AB, AB_CQ, 8192)
    expt = Slot(AB, AB_CQ, 1024)
    ycm = Slot(AB, AB_CQ + 1024, 2048)
    xbc = Slot(AB, AB_XBC, 2048)
    qcc = Slot(AB, AB_QC, 2048)
    qrr = Slot(AB, AB_QR, 1024)
    wbf = Slot(AB, AB_WSM, 3072)
    wsm = Slot(AB, AB_WSM + 1024, 6400)
    wstg = Slot(ctx.AF, AF_WSTG, 1024)
    rbsb = Slot(ctx.AF, AF_LNT, 512)
    OF_DKV, OF_UK, OF_UV, OF_KR = 0, 8 * 256, 8 * 256 + 2 * 1024, 8 * 256 + 4 * 1024

    sw_i = [0]

    def stream_w(hap, hname, r0, c0, ncols, buf, width):
        buf = sw_i[0] % 6
        sw_i[0] += 1
        wb = wbf.v((0, 128), (buf * 512, buf * 512 + ncols))
        kb.dma("q" if buf % 2 == 0 else "s", wb,
               DV(hap[r0:r0 + 128, c0:c0 + ncols], hname,
                  (0, 128, r0 * width + c0, r0 * width + c0 + ncols)))
        return wb

    def load_small(hname, hap, ktiles, cols, off):
        for k in range(ktiles):
            stg = wstg.v((0, 128), ((k % 2) * 512, (k % 2) * 512 + cols))
            kb.dma("q", stg, DV(hap[k * 128:(k + 1) * 128, :], hname,
                                (0, 128, k * cols, (k + 1) * cols)))
            kb.copy("p", wsm.v((0, 128), (off + k * cols, off + (k + 1) * cols)), stg)

    load_small("w_dkv", W["w_dkv"], 8, 256, OF_DKV)
    load_small("w_uk", W["w_uk"], 2, 1024, OF_UK)
    load_small("w_uv", W["w_uv"], 2, 1024, OF_UV)
    load_small("w_kr", W["w_kr"], 8, 32, OF_KR)

    ckv = Slot(AB, AB_CQ + 4096, 4096)
    xb1 = Slot(AB, AB_CQ, 4096)

    for tc in range(4):
        cs = tc * 512
        kb.dma("q", scl.v((0, 32), (256, 768)),
               DV(W["scale_k"][0:32, cs:cs + 512], "h_sclk", (0, 32, cs, cs + 512)))
        for ci in range(NCT):
            kb.copy("v", xb1.v((0, 128), (ci * 512, (ci + 1) * 512)),
                    V(x, (0, 128), (ci * T + cs, ci * T + cs + 512)))
        for m in range(2):
            ps = pf[m]
            for ci in range(NCT):
                kb.mm(V(ps), wsm.v((0, 128), (OF_DKV + ci * 256 + m * 128,
                                              OF_DKV + ci * 256 + (m + 1) * 128)),
                      xb1.v((0, 128), (ci * 512, (ci + 1) * 512)),
                      start=(ci == 0), stop=(ci == NCT - 1))
            kb.copy("s", ckv.v((0, 128), (m * T + cs - 0 * T, m * T + cs + 512))
                    if False else ckv.v((0, 128), (m * 2048 + cs, m * 2048 + cs + 512)), V(ps))
        for m in range(8):
            ps = pf[2 + (m % 2)]
            for ci in range(2):
                kb.mm(V(ps), wsm.v((0, 128), (OF_UK + ci * 1024 + m * 128,
                                              OF_UK + ci * 1024 + (m + 1) * 128)),
                      ckv.v((0, 128), (ci * 2048 + cs, ci * 2048 + cs + 512)),
                      start=(ci == 0), stop=(ci == 1))
            kb.copy("s" if m % 2 == 0 else "v",
                    kc.v((0, 128), (m * T + cs, m * T + cs + 512)), V(ps))
        for tl in range(4):
            tt_ = tc * 4 + tl
            for n in range(2):
                ps = pf[4 + n]
                for ci in range(2):
                    kb.mm(V(ps), ckv.v((0, 128), (ci * 2048 + cs + tl * 128,
                                                  ci * 2048 + cs + (tl + 1) * 128)),
                          wsm.v((0, 128), (OF_UV + ci * 1024 + n * 512,
                                           OF_UV + ci * 1024 + (n + 1) * 512)),
                          start=(ci == 0), stop=(ci == 1))
                base = AB_VTM + tt_ * 1040 + 65 * 8 * n
                oap = bass.AP(AB, base, [[AB_END, 128], [65, 8], [1, 64]])
                kb.op("v", lambda e, o=oap, i=ps: e.tensor_copy(out=o, in_=i[:, 0:512]),
                      reads=(V(ps),), writes=(TV(AB, oap, 0, 128, base, base + 8 * 65),))
        ps = pf[0]
        for ci in range(NCT):
            kb.mm(V(ps, (0, 32), (0, 512)),
                  wsm.v((0, 128), (OF_KR + ci * 32, OF_KR + (ci + 1) * 32)),
                  xb1.v((0, 128), (ci * 512, (ci + 1) * 512)),
                  start=(ci == 0), stop=(ci == NCT - 1))
        kb.tt("v", kr4.v((0, 32), (cs, cs + 512)), V(ps, (0, 32), (0, 512)),
              scl.v((0, 32), (256, 768)), ALU.mult)
    for tt_ in range(16):
        base = AB_VTM + tt_ * 1040 + 64
        oap = bass.AP(AB, base, [[AB_END, 128], [65, 16], [1, 1]])
        kb.op("v", lambda e, o=oap: e.memset(o, 1.0),
              writes=(TV(AB, oap, 0, 128, base, base + 16 * 65),))
    for rep in range(1, 4):
        kb.dma("p", kr4.v((rep * 32, (rep + 1) * 32), (0, T)), kr4.v((0, 32), (0, T)))

    for qc in range(NQC):
        qs = qc * QCW
        kb.dma("q", scl.v((0, 128), (0, QCW)),
               DV(W["scale_q4"][0:128, qs:qs + QCW], "h_sclq", (0, 128, qs, qs + QCW)))
        for ci in range(NCT):
            kb.copy("v", xbc.v((0, 128), (ci * QCW, (ci + 1) * QCW)),
                    V(x, (0, 128), (ci * T + qs, ci * T + qs + QCW)))
        for mg in range(8):
            for k in range(NCT):
                wb = stream_w(W["s_w_dq"], "s_w_dq", k * 128, mg * 512, 512, (mg * 8 + k) % 2, 4096)
                for j in range(4):
                    kb.mm(V(pf[j], (0, 128), (0, QCW)),
                          wbf.v((0, 128), (wb.e0 - AB_WSM + j * 128, wb.e0 - AB_WSM + (j + 1) * 128)),
                          xbc.v((0, 128), (k * QCW, (k + 1) * QCW)),
                          start=(k == 0), stop=(k == NCT - 1))
            for j in range(4):
                kb.copy("s" if j % 2 == 0 else "v",
                        cqs.v((0, 128), ((mg * 4 + j) * QCW, (mg * 4 + j + 1) * QCW)),
                        V(pf[j], (0, 128), (0, QCW)))
        for mg in range(2):
            for k in range(32):
                wb = stream_w(W["s_w_uq"], "s_w_uq", k * 128, mg * 512, 512, (mg * 32 + k) % 2, 1024)
                for j in range(4):
                    kb.mm(V(pf[j], (0, 128), (0, QCW)),
                          wbf.v((0, 128), (wb.e0 - AB_WSM + j * 128, wb.e0 - AB_WSM + (j + 1) * 128)),
                          cqs.v((0, 128), (k * QCW, (k + 1) * QCW)),
                          start=(k == 0), stop=(k == 31))
            for j in range(4):
                dst = qcc.v((0, 128), ((mg * 4 + j) * QCW, (mg * 4 + j + 1) * QCW))
                kb.act(dst, V(pf[j], (0, 128), (0, QCW)), AF.Copy,
                       scale=float(1.0 / np.sqrt(96.0)))
        for k in range(32):
            wb = stream_w(W["s_w_qr"], "s_w_qr", k * 128, 0, 512, k % 2, 512)
            for j in range(4):
                kb.mm(V(pf[j], (0, 128), (0, QCW)),
                      wbf.v((0, 128), (wb.e0 - AB_WSM + j * 128, wb.e0 - AB_WSM + (j + 1) * 128)),
                      cqs.v((0, 128), (k * QCW, (k + 1) * QCW)),
                      start=(k == 0), stop=(k == 31))
        for j in range(4):
            kb.tt("v", qrr.v((0, 128), (j * QCW, (j + 1) * QCW)),
                  V(pf[j], (0, 128), (0, QCW)), scl.v((0, 128), (0, QCW)), ALU.mult)
        njt = (qc + 1) * (QCW // 128)
        for h in range(16):
            hp, ho = h // 2, (h % 2) * 64
            rp = (h % 4) * 32
            y_ps = pf[4]
            for j in range(njt):
                st = pf[j % 2]
                kb.mm(V(st, (0, 128), (0, QCW)),
                      kc.v((ho, ho + 64), (hp * T + j * 128, hp * T + (j + 1) * 128)),
                      qcc.v((ho, ho + 64), (hp * QCW, (hp + 1) * QCW)),
                      start=True, stop=False, tile_position=(ho, 0))
                kb.mm(V(st, (0, 128), (0, QCW)),
                      kr4.v((rp, rp + 32), (j * 128, (j + 1) * 128)),
                      qrr.v((rp, rp + 32), ((h // 4) * QCW, (h // 4 + 1) * QCW)),
                      start=False, stop=True, tile_position=(rp, 0))
                exoff = (j % 2) * 512
                nd = qc * (QCW // 128)
                if j >= nd:
                    a = j - nd
                    if a > 0:
                        kb.memset("v", expt.v((0, 128), (exoff, exoff + a * 128)), 0.0)
                    kb.tt("v", V(st, (0, 128), (a * 128, (a + 1) * 128)),
                          V(st, (0, 128), (a * 128, (a + 1) * 128)), V(ctx.tri), ALU.add)
                    kb.act(expt.v((0, 128), (exoff + a * 128, exoff + QCW)),
                           V(st, (0, 128), (a * 128, QCW)), AF.Exp)
                else:
                    kb.act(expt.v((0, 128), (exoff, exoff + QCW)),
                           V(st, (0, 128), (0, QCW)), AF.Exp)
                kb.mm(V(y_ps, (0, 65), (0, QCW)),
                      vtm.v((0, 128), (j * 1040 + 65 * h, j * 1040 + 65 * h + 65)),
                      expt.v((0, 128), (exoff, exoff + QCW)),
                      start=(j == 0), stop=(j == njt - 1))
            kb.recip(Slot(ctx.AF, AF_SQ, 1024).v((0, 1), (0, QCW)),
                     V(y_ps, (64, 65), (0, QCW)))
            rb_ps = pf[3]
            kb.mm(V(rb_ps, (0, 64), (0, QCW)), V(ctx.ones, (0, 1), (0, 64)),
                  Slot(ctx.AF, AF_SQ, 1024).v((0, 1), (0, QCW)), start=True, stop=True)
            kb.copy("s", rbsb.v((0, 64), (0, QCW)), V(rb_ps, (0, 64), (0, QCW)))
            kb.tt("v", ycm.v((ho, ho + 64), (hp * QCW, (hp + 1) * QCW)),
                  V(y_ps, (0, 64), (0, QCW)), rbsb.v((0, 64), (0, QCW)), ALU.mult)
        for ocg in range(2):
            for k in range(NCT):
                wb = stream_w(W["s_w_o"], "s_w_o", k * 128, ocg * 512, 512, (ocg * 8 + k) % 2, 1024)
                for j in range(4):
                    kb.mm(V(pf[j], (0, 128), (0, QCW)),
                          wbf.v((0, 128), (wb.e0 - AB_WSM + j * 128, wb.e0 - AB_WSM + (j + 1) * 128)),
                          ycm.v((0, 128), (k * QCW, (k + 1) * QCW)),
                          start=(k == 0), stop=(k == NCT - 1))
            for j in range(4):
                oc_ = ocg * 4 + j
                xv = V(x, (0, 128), (oc_ * T + qs, oc_ * T + qs + QCW))
                kb.tt("v", xv, V(pf[j], (0, 128), (0, QCW)), xv, ALU.add)



FP4_MIDS = (0.75, 1.25, 1.75, 2.5, 3.5, 5.0)


def emit_fp4_quant(kb, ctx, src, dst, W, bufA, bufB, bmrow, ladder_eng="v"):
    nb = W // 16
    bufA = TV(bufA.t, bufA.t[0:128, bufA.e0:bufA.e0 + W], 0, 128, bufA.e0, bufA.e0 + W)
    bufB = TV(bufB.t, bufB.t[0:128, bufB.e0:bufB.e0 + W], 0, 128, bufB.e0, bufB.e0 + W)

    def b3(view, blk=True):
        if blk:
            ap = bass.AP(view.t, view.e0, [[view.t.shape[1], 128], [16, nb], [1, 16]])
        else:
            ap = bass.AP(view.t, view.e0, [[view.t.shape[1], 128], [1, nb], [0, 16]])
        return TV(view.t, ap, view.p0, view.p1, view.e0, view.e1)

    bm = TV(bmrow.t, bmrow.t[0:128, bmrow.e0:bmrow.e0 + nb], 0, 128, bmrow.e0, bmrow.e0 + nb)
    sc = TV(bmrow.t, bmrow.t[0:128, bmrow.e0 + nb:bmrow.e0 + 2 * nb], 0, 128,
            bmrow.e0 + nb, bmrow.e0 + 2 * nb)
    inv = TV(bmrow.t, bmrow.t[0:128, bmrow.e0 + 2 * nb:bmrow.e0 + 3 * nb], 0, 128,
             bmrow.e0 + 2 * nb, bmrow.e0 + 3 * nb)
    invb = TV(inv.t, bass.AP(inv.t, inv.e0, [[inv.t.shape[1], 128], [1, nb], [0, 16]]),
              0, 128, inv.e0, inv.e1)
    scb = TV(sc.t, bass.AP(sc.t, sc.e0, [[sc.t.shape[1], 128], [1, nb], [0, 16]]),
             0, 128, sc.e0, sc.e1)

    kb.act(bufA, src, AF.Abs)
    kb.reduce("v", bm, b3(bufA), ALU.max)
    kb.ts("v", sc, bm, 1.0 / 6.0, 1e-12, op0=ALU.mult, op1=ALU.max)
    kb.recip(inv, sc)
    kb.tt("v", b3(bufA), b3(src), invb, ALU.mult)
    kb.act(bufB, bufA, AF.Abs)
    kb.ts(ladder_eng, bufA, bufB, 0.25, None, op0=ALU.is_gt)
    for m in FP4_MIDS:
        kb.stt(ladder_eng, bufA, bufB, float(m), bufA, ALU.is_gt, ALU.add)
    kb.ts("v", bufB, bufA, -4.0, 0.0, op0=ALU.add, op1=ALU.max)
    kb.tt("v", bufB, bufA, bufB, ALU.add)
    kb.ts("v", bufA, bufA, -6.0, 0.0, op0=ALU.add, op1=ALU.max)
    kb.stt("v", bufA, bufB, 0.5, bufA, ALU.mult, ALU.add)
    kb.ts("v", bufB, src, 0.0, -2.0, op0=ALU.is_lt, op1=ALU.mult)
    kb.ts("v", bufB, bufB, 1.0, None, op0=ALU.add)
    kb.tt("v", bufA, bufA, bufB, ALU.mult)
    kb.tt("v", b3(dst), b3(bufA), scb, ALU.mult)


FB_XQT = 0
FB_XQC = 4096
FB_HQ = 8192
FB_HQC = 8704
FB_WG = 10752
FB_WD = 18944
FB_WQS = 23040


def emit_l0_ffn(kb, ctx, W):
    x, pf, pb, AB, AFt = ctx.x, ctx.pf, ctx.pb, ctx.AB, ctx.AF
    NCH = [(0, 512), (512, 512), (1024, 512), (1536, 512), (2048, 512), (2560, 256)]

    accs = Slot(AFt, AF_ACC, 4096)
    wqs = Slot(AB, FB_WQS, 1024)

    def quant_w(hap, hname, nrows, ncols, dst_ap, dst_name, dst_cols):
        ccs = []
        c0_ = 0
        while c0_ < ncols:
            cw_ = min(512, ncols - c0_)
            ccs.append((c0_, cw_))
            c0_ += cw_
        for r in range(nrows // 128):
            for c0, cw in ccs:
                piece = accs.v((0, 128), (0, cw))
                kb.dma("q", piece, DV(hap[r * 128:(r + 1) * 128, c0:c0 + cw], hname,
                                      (0, 128, r * ncols + c0, r * ncols + c0 + cw)))
                bufA = accs.v((0, 128), (512, 1024))
                bufB = accs.v((0, 128), (1024, 1536))
                bmr = accs.v((0, 128), (1536, 1536 + 96))
                q = wqs.v((0, 128), (0, cw))
                emit_fp4_quant(kb, ctx, piece, q, cw, bufA, bufB, bmr)
                for j in range(cw // 128):
                    bk = pb[j % 2]
                    kb.trans(V(bk, (0, 128), (0, 128)),
                             wqs.v((0, 128), (j * 128, (j + 1) * 128)), V(ctx.idb))
                    st = wqs.v((0, 128), (512 + j * 128, 512 + (j + 1) * 128))
                    kb.copy("s", st, V(bk, (0, 128), (0, 128)))
                    kb.dma("p", DV(dst_ap[c0 + j * 128:c0 + (j + 1) * 128,
                                          r * 128:(r + 1) * 128], dst_name,
                                   (0, 128, (c0 + j * 128) * dst_cols + r * 128,
                                    (c0 + j * 128) * dst_cols + r * 128 + 128)),
                           st)

    quant_w(W["w_gate"], "w_gate", 2816, 1024, W["wqgT"], "wqgT", 2816)
    quant_w(W["w_up"], "w_up", 2816, 1024, W["wquT"], "wquT", 2816)
    quant_w(W["w_down"], "w_down", 1024, 2816, W["wqdT"], "wqdT", 1024)

    xqt = Slot(AB, FB_XQT, 4096)
    xqc = Slot(AB, FB_XQC, 4096)
    hq = Slot(AB, FB_HQ, 512)
    hqc = Slot(AB, FB_HQC, 2048)
    wg = Slot(AB, FB_WG, 8192)
    wd = Slot(AB, FB_WD, 4096)
    sqs = Slot(AFt, AF_SQ, 1024)
    wstg = Slot(AFt, AF_WSTG, 1024)
    lnt = Slot(AFt, AF_LNT, 512)

    for tc in range(4):
        cs = tc * 512
        for tl in range(4):
            for half in range(2):
                bank = pf[half]
                for cj in range(4):
                    ci = half * 4 + cj
                    kb.trans(V(bank, (0, 128), (cj * 128, (cj + 1) * 128)),
                             V(x, (0, 128), (ci * T + cs + tl * 128, ci * T + cs + (tl + 1) * 128)),
                             V(ctx.idf))
                base = tl * 1024 + half * 512
                kb.copy("s", accs.v((0, 128), (base, base + 512)), V(bank, (0, 128), (0, 512)))
        for tl in range(4):
            for half in range(2):
                base = tl * 1024 + half * 512
                emit_fp4_quant(kb, ctx, accs.v((0, 128), (base, base + 512)),
                               xqt.v((0, 128), (base, base + 512)), 512,
                               sqs.v((0, 128), (0, 512)), sqs.v((0, 128), (512, 1024)),
                               lnt.v((0, 128), (0, 96)))
        for tl in range(4):
            for half in range(2):
                bank = pb[half]
                for cj in range(4):
                    ci = half * 4 + cj
                    kb.trans(V(bank, (0, 128), (cj * 128, (cj + 1) * 128)),
                             xqt.v((0, 128), (tl * 1024 + ci * 128, tl * 1024 + (ci + 1) * 128)),
                             V(ctx.idb))
                base = (half * 4) * 512 + tl * 128
                ap = bass.AP(AB, FB_XQC + base, [[AB_END, 128], [512, 4], [1, 128]])
                kb.op("v", lambda e, o=ap, i=bank: e.tensor_copy(out=o, in_=i[:, 0:512]),
                      reads=(V(bank, (0, 128), (0, 512)),),
                      writes=(TV(AB, ap, 0, 128, FB_XQC + base, FB_XQC + base + 3 * 512 + 128),))
        for ci in range(NCT):
            kb.memset("v", accs.v((0, 128), (ci * 512, (ci + 1) * 512)), 0.0)
        for n0, nw in NCH:
            nk = nw // 128
            for k in range(NCT):
                kb.dma("q", wg.v((0, 128), (k * 512, k * 512 + nw)),
                       DV(W["wqgT"][k * 128:(k + 1) * 128, n0:n0 + nw], "wqgT",
                          (0, 128, k * 2816 + n0, k * 2816 + n0 + nw)))
                kb.dma("p", wg.v((0, 128), (4096 + k * 512, 4096 + k * 512 + nw)),
                       DV(W["wquT"][k * 128:(k + 1) * 128, n0:n0 + nw], "wquT",
                          (0, 128, 2 ** 23 + k * 2816 + n0, 2 ** 23 + k * 2816 + n0 + nw)))
            for kk in range(nk):
                kb.dma("q", wd.v((0, 128), (kk * 1024, (kk + 1) * 1024)),
                       DV(W["wqdT"][n0 + kk * 128:n0 + (kk + 1) * 128, 0:1024], "wqdT",
                          (0, 128, (n0 + kk * 128) * 1024, (n0 + kk * 128) * 1024 + 1024)))
            for tl in range(4):
                ps_g, ps_u = pf[2], pf[3]
                for k in range(NCT):
                    kb.mm(V(ps_g, (0, 128), (0, nw)),
                          xqc.v((0, 128), (k * 512 + tl * 128, k * 512 + (tl + 1) * 128)),
                          wg.v((0, 128), (k * 512, k * 512 + nw)),
                          start=(k == 0), stop=(k == NCT - 1))
                for k in range(NCT):
                    kb.mm(V(ps_u, (0, 128), (0, nw)),
                          xqc.v((0, 128), (k * 512 + tl * 128, k * 512 + (tl + 1) * 128)),
                          wg.v((0, 128), (4096 + k * 512, 4096 + k * 512 + nw)),
                          start=(k == 0), stop=(k == NCT - 1))
                sg = wstg.v((0, 128), (512, 512 + nw))
                kb.act(sg, V(ps_g, (0, 128), (0, nw)), AF.Silu)
                hh = wstg.v((0, 128), (0, nw))
                kb.tt("v", hh, sg, V(ps_u, (0, 128), (0, nw)), ALU.mult)
                emit_fp4_quant(kb, ctx, hh, hq.v((0, 128), (0, nw)), nw,
                               sqs.v((0, 128), (0, 512)), sqs.v((0, 128), (512, 1024)),
                               lnt.v((0, 128), (0, 96)), ladder_eng="v")
                for kk in range(nk):
                    bk = pb[kk % 2]
                    kb.trans(V(bk, (0, 128), (0, 128)),
                             hq.v((0, 128), (kk * 128, (kk + 1) * 128)), V(ctx.idb))
                    kb.copy("s", hqc.v((0, 128), (kk * 512 + tl * 128, kk * 512 + (tl + 1) * 128)),
                            V(bk, (0, 128), (0, 128)))
            for oc in range(NCT):
                ps_d = pf[4 + (oc % 2)]
                for kk in range(nk):
                    kb.mm(V(ps_d, (0, 128), (0, 512)),
                          wd.v((0, 128), (kk * 1024 + oc * 128, kk * 1024 + (oc + 1) * 128)),
                          hqc.v((0, 128), (kk * 512, (kk + 1) * 512)),
                          start=(kk == 0), stop=(kk == nk - 1))
                av = accs.v((0, 128), (oc * 512, (oc + 1) * 512))
                kb.tt("v", av, V(ps_d, (0, 128), (0, 512)), av, ALU.add)
        for ci in range(NCT):
            xv = V(x, (0, 128), (ci * T + cs, ci * T + cs + 512))
            kb.tt("v", xv, accs.v((0, 128), (ci * 512, (ci + 1) * 512)), xv, ALU.add)



LA_WT = 0
LA_XBC = 24576
LA_WST = 28672


def emit_l1_attn(kb, ctx, W):
    x, pf, pb, AB, AFt = ctx.x, ctx.pf, ctx.pb, ctx.AB, ctx.AF
    wts = Slot(AB, LA_WT, 24576)
    xbc = Slot(AB, LA_XBC, 4096)
    wst = Slot(AB, LA_WST, 1024)
    accs = Slot(AFt, AF_ACC, 4096)
    wstg = Slot(AFt, AF_WSTG, 1024)
    sqs = Slot(AFt, AF_SQ, 1024)
    biases = kb.sbuf("l1a_bias", [128, 24], F32)
    carry = kb.sbuf("l1a_carry", [128, 8], F32)

    for wi, bn in enumerate(("b_f", "b_i", "b_h")):
        kb.dma_slow("q", V(biases, (0, 128), (wi * 8, wi * 8 + 8)),
                    DV(W[bn].rearrange("(c p) -> p c", p=128), "h_" + bn))
    kb.memset("v", V(carry), 0.0)

    for wi, wn in enumerate(("w_f", "w_i", "w_h")):
        for r in range(8):
            stg = wstg.v((0, 128), (0, 1024))
            kb.dma("q", stg, DV(W[wn][r * 128:(r + 1) * 128, :], wn,
                                (0, 128, r * 1024, (r + 1) * 1024)))
            wb = wst.v((0, 128), (0, 1024))
            kb.copy("p", wb, stg)
            for k in range(8):
                bk = pb[k % 2]
                kb.trans(V(bk, (0, 128), (0, 128)),
                         wst.v((0, 128), (k * 128, (k + 1) * 128)), V(ctx.idb))
                kb.copy("s", wts.v((0, 128), (wi * 8192 + k * 1024 + r * 128,
                                              wi * 8192 + k * 1024 + (r + 1) * 128)),
                        V(bk, (0, 128), (0, 128)))

    F_, I_, H_ = 0, 512, 1024
    S1, S2, S3 = 1536, 2048, 2560
    for tc in range(4):
        cs = tc * 512
        for ci in range(NCT):
            kb.copy("v", xbc.v((0, 128), (ci * 512, (ci + 1) * 512)),
                    V(x, (0, 128), (ci * T + cs, ci * T + cs + 512)))
        for m in range(NCT):
            for wi in range(3):
                ps = pf[2 + wi]
                for k in range(NCT):
                    kb.mm(V(ps), wts.v((0, 128), (wi * 8192 + k * 1024 + m * 128,
                                                  wi * 8192 + k * 1024 + (m + 1) * 128)),
                          xbc.v((0, 128), (k * 512, (k + 1) * 512)),
                          start=(k == 0), stop=(k == NCT - 1))
            FB = accs.v((0, 128), (F_, F_ + 512))
            IB = accs.v((0, 128), (I_, I_ + 512))
            HB = accs.v((0, 128), (H_, H_ + 512))
            kb.act(FB, V(pf[2]), AF.Identity, bias=V(biases, (0, 128), (m, m + 1)))
            kb.act(IB, V(pf[3]), AF.Identity, bias=V(biases, (0, 128), (8 + m, 8 + m + 1)))
            kb.act(HB, V(pf[4]), AF.Identity, bias=V(biases, (0, 128), (16 + m, 16 + m + 1)))
            A = accs.v((0, 128), (S1, S1 + 512))
            B = accs.v((0, 128), (S2, S2 + 512))
            C = accs.v((0, 128), (S3, S3 + 512))
            kb.act(A, FB, AF.Abs)
            kb.act(A, A, AF.Exp, scale=-1.0)
            kb.act(A, A, AF.Ln, bias=1.0)
            kb.ts("v", B, FB, -1.0, 0.0, op0=ALU.mult, op1=ALU.max)
            kb.tt("v", A, A, B, ALU.add)
            kb.act(B, IB, AF.Abs)
            kb.act(B, B, AF.Exp, scale=-1.0)
            kb.act(B, B, AF.Ln, bias=1.0)
            kb.ts("v", C, IB, -1.0, 0.0, op0=ALU.mult, op1=ALU.max)
            kb.tt("v", B, B, C, ALU.add)
            kb.tt("v", A, A, B, ALU.subtract)
            kb.act(B, A, AF.Sigmoid, scale=-1.0)
            kb.act(A, A, AF.Sigmoid)
            kb.act(C, HB, AF.Sigmoid)
            kb.ts("v", FB, HB, 0.5, None, op0=ALU.add)
            kb.tt("v", FB, FB, C, ALU.subtract)
            kb.ts("v", IB, HB, 0.0, None, op0=ALU.is_ge)
            kb.tt("v", FB, FB, IB, ALU.mult)
            kb.tt("v", C, C, FB, ALU.add)
            kb.tt("v", A, A, C, ALU.mult)
            kb.scan(HB, B, A, V(carry, (0, 128), (m, m + 1)), op0=ALU.mult, op1=ALU.add)
            kb.copy("v", V(carry, (0, 128), (m, m + 1)),
                    TV(accs.th, accs.th[0:128, accs.off + H_ + 511:accs.off + H_ + 512],
                       0, 128, accs.off + H_ + 511, accs.off + H_ + 512))
            xv = V(x, (0, 128), (m * T + cs, m * T + cs + 512))
            kb.tt("v", xv, HB, xv, ALU.add)



LF_QCM = 0
LF_KCM = 8192
LF_VTM = 16384
LF_OCM = 24576
LF_HB1 = 24576
LF_XBC = 35840
LF_WS1 = 39936
LF_PA = 41984
LF_PB = 42496
LF_PT = 42624
LF_WS2 = 43264


def emit_l1_ffn(kb, ctx, W):
    x, pf, pb, AB, AFt = ctx.x, ctx.pf, ctx.pb, ctx.AB, ctx.AF
    qcm = Slot(AB, LF_QCM, 8192)
    kcm = Slot(AB, LF_KCM, 8192)
    vtm = Slot(AB, LF_VTM, 8192)
    ocm = Slot(AB, LF_OCM, 8192)
    hb1 = Slot(AB, LF_HB1, 11264)
    xbc = Slot(AB, LF_XBC, 4096)
    ws1 = Slot(AB, LF_WS1, 2048)
    ws2 = Slot(AB, LF_WS2, 2048)
    pA = Slot(AB, LF_PA, 512)
    pB = Slot(AB, LF_PB, 128)
    pts = Slot(AB, LF_PT, 640)
    accs = Slot(AFt, AF_ACC, 4096)
    sqs = Slot(AFt, AF_SQ, 1024)
    wstg = Slot(AFt, AF_WSTG, 1024)
    lnt = Slot(AFt, AF_LNT, 512)
    qks = kb.sbuf("lf_qks", [128, 4], F32)
    lng = kb.sbuf("lf_lng", [128, 16], F32)
    lnb = kb.sbuf("lf_lnb", [128, 16], F32)
    b1a = kb.sbuf("lf_b1a", [128, 44], F32)
    b1g = kb.sbuf("lf_b1g", [128, 44], F32)
    b2c = kb.sbuf("lf_b2", [128, 16], F32)
    tabs = Slot(AFt, AF_ACC + 2048, 2048)

    for c4 in range(4):
        kb.dma("q", DV(W["xres"][0:128, c4 * 4096:(c4 + 1) * 4096], "xres",
                       (0, 128, c4 * 4096, (c4 + 1) * 4096)),
               V(x, (0, 128), (c4 * 4096, (c4 + 1) * 4096)))
    kb.dma("q", V(qks), DV(W["qk_scale_l"], "h_qks"))
    kb.dma("q", V(lng), DV(W["lng_l"], "h_lng"))
    kb.dma("q", V(lnb), DV(W["lnb_l"], "h_lnb"))
    kb.dma("q", V(b1a), DV(W["b1a_l"], "h_b1a"))
    kb.dma("q", V(b1g), DV(W["b1g_l"], "h_b1g"))
    kb.dma("q", V(b2c), DV(W["b2_l"], "h_b2"))

    def shift_chunk(cs, dst_slot):
        for ci in range(NCT):
            dv = dst_slot.v((0, 128), (ci * 512, (ci + 1) * 512))
            if ci < 4:
                kb.copy("v", dv, V(x, (0, 128), (ci * T + cs, ci * T + cs + 512)))
            elif cs > 0:
                kb.copy("v", dv, V(x, (0, 128), (ci * T + cs - 1, ci * T + cs + 511)))
            else:
                kb.memset("v", dst_slot.v((0, 128), (ci * 512, ci * 512 + 1)), 0.0)
                kb.copy("v", dst_slot.v((0, 128), (ci * 512 + 1, (ci + 1) * 512)),
                        V(x, (0, 128), (ci * T, ci * T + 511)))

    for l in range(2):
        wqkv, wo, w1, w2 = W["wqkv"], W["wo"], W["w1"], W["w2"]
        for tc in range(4):
            cs = tc * 512
            shift_chunk(cs, xbc)
            for tb in range(4):
                kb.dma("q", tabs.v((0, 128), (tb * 512, (tb + 1) * 512)),
                       DV(W[("xp_csq", "xp_ssq", "xp_csk", "xp_ssk")[tb]][0:128, cs:cs + 512],
                          "h_xt%d" % tb, (0, 128, cs, cs + 512)))
            for kind in range(2):
                for k in range(NCT):
                    wb = ws1.v((0, 128), ((k % 4) * 512, (k % 4) * 512 + 512))
                    kb.dma("q" if k % 2 == 0 else "s", wb,
                           DV(wqkv[l * 1024 + k * 128:l * 1024 + (k + 1) * 128,
                                   kind * 512:(kind + 1) * 512], "s_wqkv",
                              (l, l + 1, k * 1536 + kind * 512,
                               k * 1536 + (kind + 1) * 512)))
                    for j in range(4):
                        kb.mm(V(pf[j]), ws1.v((0, 128), ((k % 4) * 512 + j * 128,
                                                         (k % 4) * 512 + (j + 1) * 128)),
                              xbc.v((0, 128), (k * 512, (k + 1) * 512)),
                              start=(k == 0), stop=(k == NCT - 1))
                dst = qcm if kind == 0 else kcm
                co, so = (0, 0) if kind == 0 else (2 * 512, 2)
                for j in range(4):
                    ps = pf[j]
                    qf = accs.v((0, 128), (0, 512))
                    kb.act(qf, V(ps), AF.Copy)
                    sq = sqs.v((0, 128), (0, 512))
                    kb.act(sq, V(ps), AF.Square)
                    n2 = pf[4]
                    kb.mm(V(n2, (0, 2), (0, 512)), V(ctx.e2), sq, start=True, stop=True)
                    nr = lnt.v((0, 2), (0, 512))
                    kb.act(nr, V(n2, (0, 2), (0, 512)), AF.Sqrt)
                    kb.ts("v", nr, nr, 1e-12, None, op0=ALU.max)
                    kb.recip(nr, nr)
                    rb = pf[5]
                    kb.mm(V(rb, (0, 128), (0, 512)), V(ctx.e2t, (0, 2), (0, 128)), nr,
                          start=True, stop=True)
                    qh = accs.v((0, 128), (512, 1024))
                    kb.tt("v", qh, qf, V(rb, (0, 128), (0, 512)), ALU.mult)
                    kb.ts("v", qh, qh, V(qks, (0, 128), (l * 2 + so // 2, l * 2 + so // 2 + 1)),
                          None, op0=ALU.mult)
                    qrot = accs.v((0, 128), (1024, 1536))
                    for (pd, psr) in ((0, 32), (32, 0), (64, 96), (96, 64)):
                        kb.copy("v", TV(AFt, AFt[pd:pd + 32, accs.off + 1024:accs.off + 1536],
                                        pd, pd + 32, accs.off + 1024, accs.off + 1536),
                                TV(AFt, AFt[psr:psr + 32, accs.off + 512:accs.off + 1024],
                                   psr, psr + 32, accs.off + 512, accs.off + 1024))
                    kb.tt("v", qh, qh, tabs.v((0, 128), (co, co + 512)), ALU.mult)
                    kb.tt("v", qrot, qrot, tabs.v((0, 128), (co + 512, co + 1024)), ALU.mult)
                    kb.tt("v", dst.v((0, 128), (j * 2048 + cs, j * 2048 + cs + 512)),
                          qh, qrot, ALU.add)
            for half in range(2):
                for kk in range(4):
                    k = half * 4 + kk
                    kb.dma("q", ws2.v((0, 128), (kk * 512, (kk + 1) * 512)),
                           DV(wqkv[l * 1024 + k * 128:l * 1024 + (k + 1) * 128, 1024:1536],
                              "s_wqkv", (l, l + 1, k * 1536 + 1024, k * 1536 + 1536)))
                for tl in range(4):
                    ps = pf[2 + tl]
                    for kk in range(4):
                        k = half * 4 + kk
                        kb.mm(V(ps), xbc.v((0, 128), (k * 512 + tl * 128, k * 512 + (tl + 1) * 128)),
                              ws2.v((0, 128), (kk * 512, (kk + 1) * 512)),
                              start=(kk == 0 and half == 0), stop=(kk == 3 and half == 1))
                    if half == 1:
                        kb.copy("s", vtm.v((0, 128), ((tc * 4 + tl) * 512, (tc * 4 + tl + 1) * 512)),
                                V(ps))
        for h in range(8):
            hp, ho = h // 2, (h % 2) * 64
            for qtc in range(16):
                jlo = max(0, qtc - 4)
                noff = qtc - jlo
                bA, bB = pf[0], pf[1]
                if noff > 0:
                    kb.mm(V(bA, (0, 128), (0, noff * 128)),
                          qcm.v((ho, ho + 64), (hp * 2048 + qtc * 128, hp * 2048 + (qtc + 1) * 128)),
                          kcm.v((ho, ho + 64), (hp * 2048 + jlo * 128, hp * 2048 + qtc * 128)),
                          start=True, stop=True, tile_position=(ho, 0))
                kb.mm(V(bB, (0, 128), (0, 128)),
                      qcm.v((ho, ho + 64), (hp * 2048 + qtc * 128, hp * 2048 + (qtc + 1) * 128)),
                      kcm.v((ho, ho + 64), (hp * 2048 + qtc * 128, hp * 2048 + (qtc + 1) * 128)),
                      start=True, stop=True, tile_position=(ho, 0))
                if noff == 4:
                    kb.tt("v", V(bA, (0, 128), (0, 128)), V(bA, (0, 128), (0, 128)),
                          V(ctx.tri), ALU.add)
                kb.tt("v", V(bB, (0, 128), (0, 128)), V(bB, (0, 128), (0, 128)),
                      V(ctx.tri2), ALU.add)
                mx = lnt.v((0, 128), (0, 1))
                m2x = lnt.v((0, 128), (1, 2))
                if noff > 0:
                    kb.reduce("v", mx, V(bA, (0, 128), (0, noff * 128)), ALU.max)
                    kb.reduce("v", m2x, V(bB, (0, 128), (0, 128)), ALU.max)
                    kb.tt("v", mx, mx, m2x, ALU.max)
                else:
                    kb.reduce("v", mx, V(bB, (0, 128), (0, 128)), ALU.max)
                negm = lnt.v((0, 128), (2, 3))
                kb.ts("v", negm, mx, -1.0, None, op0=ALU.mult)
                sA = lnt.v((0, 128), (3, 4))
                sB = lnt.v((0, 128), (4, 5))
                if noff > 0:
                    kb.act(pA.v((0, 128), (0, noff * 128)), V(bA, (0, 128), (0, noff * 128)),
                           AF.Exp, bias=negm, accum=sA)
                kb.act(pB.v((0, 128), (0, 128)), V(bB, (0, 128), (0, 128)),
                       AF.Exp, bias=negm, accum=sB)
                den = lnt.v((0, 128), (5, 6))
                if noff > 0:
                    kb.tt("v", den, sA, sB, ALU.add)
                else:
                    kb.copy("v", den, sB)
                kb.recip(den, den)
                if noff > 0:
                    kb.ts("v", pA.v((0, 128), (0, noff * 128)),
                          pA.v((0, 128), (0, noff * 128)), den, None, op0=ALU.mult)
                kb.ts("v", pB.v((0, 128), (0, 128)), pB.v((0, 128), (0, 128)),
                      den, None, op0=ALU.mult)
                for jj in range(noff + 1):
                    bk = pb[jj % 2]
                    src = (pA.v((0, 128), (jj * 128, (jj + 1) * 128)) if jj < noff
                           else pB.v((0, 128), (0, 128)))
                    kb.trans(V(bk, (0, 128), (0, 128)), src, V(ctx.idb))
                    kb.copy("s", pts.v((0, 128), (jj * 128, (jj + 1) * 128)),
                            V(bk, (0, 128), (0, 128)))
                yb = pf[2]
                for jj in range(noff + 1):
                    j = jlo + jj
                    kb.mm(V(yb, (0, 64), (0, 128)),
                          vtm.v((0, 128), (j * 512 + h * 64, j * 512 + (h + 1) * 64)),
                          pts.v((0, 128), (jj * 128, (jj + 1) * 128)),
                          start=(jj == 0), stop=(jj == noff))
                kb.copy("s", ocm.v((ho, ho + 64), (hp * 2048 + qtc * 128,
                                                   hp * 2048 + (qtc + 1) * 128)),
                        V(yb, (0, 64), (0, 128)))
        for tc in range(4):
            cs = tc * 512
            for mg in range(2):
                for k in range(4):
                    wb = ws1.v((0, 128), ((k % 4) * 512, (k % 4) * 512 + 512))
                    kb.dma("q" if k % 2 == 0 else "s", wb,
                           DV(wo[l * 512 + k * 128:l * 512 + (k + 1) * 128,
                                 mg * 512:(mg + 1) * 512],
                              "s_wo", (l, l + 1, k * 1024 + mg * 512,
                                       k * 1024 + (mg + 1) * 512)))
                    for j in range(4):
                        kb.mm(V(pf[j]), ws1.v((0, 128), ((k % 4) * 512 + j * 128,
                                                         (k % 4) * 512 + (j + 1) * 128)),
                              ocm.v((0, 128), (k * 2048 + cs, k * 2048 + cs + 512)),
                              start=(k == 0), stop=(k == 3))
                for j in range(4):
                    oc_ = mg * 4 + j
                    xv = V(x, (0, 128), (oc_ * T + cs, oc_ * T + cs + 512))
                    kb.tt("v", xv, V(pf[j]), xv, ALU.add)
        AW = [(mi * 512, min(512, 2730 - mi * 512)) for mi in range(6)]
        for tc in range(4):
            cs = tc * 512
            ps_a, ps_b = pf[0], pf[1]
            first = True
            for ci in range(NCT):
                xv = (V(x, (0, 128), (ci * T + cs, ci * T + cs + 512)) if ci < 4 else
                      (V(x, (0, 128), (ci * T + cs - 1, ci * T + cs + 511)) if cs > 0 else
                       V(x, (0, 128), (ci * T, ci * T + 511))))
                ov = (V(ps_a, (0, 1), (0, 512)) if (ci < 4 or cs > 0) else
                      V(ps_a, (0, 1), (1, 512)))
                kb.mm(ov, V(ctx.ones_col), xv, start=first, stop=(ci == NCT - 1))
                first = False
            first = True
            for ci in range(NCT):
                xv = (V(x, (0, 128), (ci * T + cs, ci * T + cs + 512)) if ci < 4 else
                      (V(x, (0, 128), (ci * T + cs - 1, ci * T + cs + 511)) if cs > 0 else
                       V(x, (0, 128), (ci * T, ci * T + 511))))
                w_ = 512 if (ci < 4 or cs > 0) else 511
                sq = sqs.v((0, 128), (0, w_))
                kb.act(sq, xv, AF.Square)
                ov = (V(ps_b, (0, 1), (0, 512)) if w_ == 512 else V(ps_b, (0, 1), (1, 512)))
                kb.mm(ov, V(ctx.ones_col), sq, start=first, stop=(ci == NCT - 1))
                first = False
            if cs == 0:
                pass
            A = sqs.v((0, 1), (0, 512))
            B = sqs.v((0, 1), (512, 1024))
            kb.ts("v", A, V(ps_a, (0, 1), (0, 512)), 1.0 / D, op0=ALU.mult)
            kb.tt("v", B, A, A, ALU.mult)
            kb.stt("v", B, V(ps_b, (0, 1), (0, 512)), 1.0 / D, B, ALU.mult, ALU.subtract)
            kb.act(B, B, AF.Sqrt, bias=1e-5)
            kb.recip(B, B)
            kb.tt("v", A, A, B, ALU.mult)
            kb.mm(V(pf[2], (0, 128), (0, 512)), V(ctx.ones, (0, 1), (0, 128)), B,
                  start=True, stop=True)
            kb.mm(V(pf[3], (0, 128), (0, 512)), V(ctx.ones, (0, 1), (0, 128)), A,
                  start=True, stop=True)
            for ci in range(NCT):
                xv = (V(x, (0, 128), (ci * T + cs, ci * T + cs + 512)) if ci < 4 else
                      (V(x, (0, 128), (ci * T + cs - 1, ci * T + cs + 511)) if cs > 0 else
                       V(x, (0, 128), (ci * T, ci * T + 511))))
                w_ = 512 if (ci < 4 or cs > 0) else 511
                off = 512 - w_
                t1 = accs.v((0, 128), (0, 512))
                t1w = accs.v((0, 128), (off, 512))
                if off:
                    kb.memset("v", accs.v((0, 128), (0, 1)), 0.0)
                kb.tt("v", t1w, xv, V(pf[2], (0, 128), (off, 512)), ALU.mult)
                kb.tt("v", t1, t1, V(pf[3], (0, 128), (0, 512)), ALU.subtract)
                kb.ts("v", xbc.v((0, 128), (ci * 512, (ci + 1) * 512)), t1,
                      V(lng, (0, 128), (l * 8 + ci, l * 8 + ci + 1)),
                      V(lnb, (0, 128), (l * 8 + ci, l * 8 + ci + 1)),
                      op0=ALU.mult, op1=ALU.add)
            for gi, (g0, gw) in enumerate(AW):
                nmc = (gw + 127) // 128
                abuf = accs.v((0, 128), (0, 2048))
                for k in range(NCT):
                    wb = ws1.v((0, 128), ((k % 4) * 512, (k % 4) * 512 + gw))
                    kb.dma("q" if k % 2 == 0 else "s", wb,
                           DV(w1[l * 1024 + k * 128:l * 1024 + (k + 1) * 128,
                                 g0:g0 + gw], "s_w1",
                              (l, l + 1, k * 5460 + g0, k * 5460 + g0 + gw)))
                    for j in range(nmc):
                        mw = min(128, gw - j * 128)
                        kb.mm(V(pf[j], (0, mw), (0, 512)),
                              ws1.v((0, 128), ((k % 4) * 512 + j * 128, (k % 4) * 512 + j * 128 + mw)),
                              xbc.v((0, 128), (k * 512, (k + 1) * 512)),
                              start=(k == 0), stop=(k == NCT - 1))
                for j in range(nmc):
                    mw = min(128, gw - j * 128)
                    mi = (g0 + j * 128) // 128
                    kb.act(TV(AFt, AFt[0:mw, accs.off + j * 512:accs.off + (j + 1) * 512],
                              0, mw, accs.off + j * 512, accs.off + (j + 1) * 512),
                           V(pf[j], (0, mw), (0, 512)), AF.Identity,
                           bias=V(b1a, (0, mw), (l * 22 + mi, l * 22 + mi + 1)))
                for k in range(NCT):
                    wb = ws1.v((0, 128), ((k % 4) * 512, (k % 4) * 512 + gw))
                    kb.dma("q" if k % 2 == 0 else "s", wb,
                           DV(w1[l * 1024 + k * 128:l * 1024 + (k + 1) * 128,
                                 2730 + g0:2730 + g0 + gw], "s_w1",
                              (l, l + 1, k * 5460 + 2730 + g0,
                               k * 5460 + 2730 + g0 + gw)))
                    for j in range(nmc):
                        mw = min(128, gw - j * 128)
                        kb.mm(V(pf[j], (0, mw), (0, 512)),
                              ws1.v((0, 128), ((k % 4) * 512 + j * 128, (k % 4) * 512 + j * 128 + mw)),
                              xbc.v((0, 128), (k * 512, (k + 1) * 512)),
                              start=(k == 0), stop=(k == NCT - 1))
                for j in range(nmc):
                    mw = min(128, gw - j * 128)
                    mi = (g0 + j * 128) // 128
                    gb = sqs.v((0, mw), (0, 512))
                    kb.act(gb, V(pf[j], (0, mw), (0, 512)), AF.Gelu,
                           bias=V(b1g, (0, mw), (l * 22 + mi, l * 22 + mi + 1)))
                    kb.tt("v", hb1.v((0, mw), (mi * 512, (mi + 1) * 512)),
                          TV(AFt, AFt[0:mw, accs.off + j * 512:accs.off + (j + 1) * 512],
                             0, mw, accs.off + j * 512, accs.off + (j + 1) * 512),
                          gb, ALU.mult)
            for mg in range(2):
                for kk in range(22):
                    kw = min(128, 2730 - kk * 128)
                    kb.dma("q" if kk % 2 == 0 else "s",
                           ws1.v((0, kw), ((kk % 4) * 512, (kk % 4) * 512 + 512)),
                           DV(w2[l * 2730 + kk * 128:l * 2730 + kk * 128 + kw,
                                 mg * 512:(mg + 1) * 512],
                              "s_w2", (l, l + 1, kk * 1024 + mg * 512,
                                       kk * 1024 + (mg + 1) * 512)))
                    for j in range(4):
                        kb.mm(V(pf[j]), ws1.v((0, kw), ((kk % 4) * 512 + j * 128,
                                                        (kk % 4) * 512 + (j + 1) * 128)),
                              hb1.v((0, kw), (kk * 512, (kk + 1) * 512)),
                              start=(kk == 0), stop=(kk == 21))
                for j in range(4):
                    oc_ = mg * 4 + j
                    xv = V(x, (0, 128), (oc_ * T + cs, oc_ * T + cs + 512))
                    kb.stt("v", xv, V(pf[j]),
                           V(b2c, (0, 128), (l * 8 + oc_, l * 8 + oc_ + 1)), xv,
                           ALU.add, ALU.add)
    for c8 in range(8):
        xrs = accs.v((0, 128), (0, 2048))
        kb.dma("q", xrs, DV(W["xres"][0:128, c8 * 2048:(c8 + 1) * 2048], "xres",
                            (0, 128, c8 * 2048, (c8 + 1) * 2048)))
        xv = V(x, (0, 128), (c8 * 2048, (c8 + 1) * 2048))
        kb.tt("v", xv, xrs, xv, ALU.add)



_PROG_CACHE = {}


def _np_dt(a):
    return F32 if a.dtype == np.float32 else BF16


def _declare_consts(nc, consts):
    return {n: nc.dram_tensor(n, list(consts[n].shape), _np_dt(consts[n]),
                              kind="ExternalInput").ap() for n in CONST_NAMES}


def _build_block(block):
    consts = host_consts()
    nc = bass.Bass("TRN2", target_bir_lowering=False)
    x_h = nc.dram_tensor("x", [T, D], F32, kind="ExternalInput").ap()
    o_h = nc.dram_tensor("o", [T, D], F32, kind="ExternalOutput").ap()
    hb = _declare_consts(nc, consts)
    W = {}
    if block == "l0a":
        shapes = dict(w_dq=(1024, 4096), w_uq=(4096, 1024), w_dkv=(1024, 256),
                      w_uk=(256, 1024), w_uv=(256, 1024), w_qr=(4096, 512),
                      w_kr=(1024, 32), w_o=(1024, 1024))
        for n, s in shapes.items():
            W[n] = nc.dram_tensor(n, list(s), F32, kind="ExternalInput").ap()
        W["scale_q4"] = hb["l0_scale_q4"]
        W["scale_k"] = hb["l0_scale_k"]
        for sname, nrows, ncols in (("s_w_dq", 1024, 4096), ("s_w_uq", 4096, 1024),
                                    ("s_w_qr", 4096, 512), ("s_w_o", 1024, 1024)):
            W[sname] = nc.dram_tensor(sname, [nrows, ncols], BF16, kind="Internal").ap()
    elif block == "l0f":
        for n, s in dict(w_gate=(2816, 1024), w_up=(2816, 1024),
                         w_down=(1024, 2816)).items():
            W[n] = nc.dram_tensor(n, list(s), F32, kind="ExternalInput").ap()
        for n, s in dict(wqgT=(1024, 2816), wquT=(1024, 2816),
                         wqdT=(2816, 1024)).items():
            W[n] = nc.dram_tensor(n, list(s), BF16, kind="Internal").ap()
    elif block == "l1a":
        for n, s in dict(w_f=(1024, 1024), w_i=(1024, 1024), w_h=(1024, 1024)).items():
            W[n] = nc.dram_tensor(n, list(s), F32, kind="ExternalInput").ap()
        for n in ("b_f", "b_i", "b_h"):
            W[n] = nc.dram_tensor(n, [1024], F32, kind="ExternalInput").ap()
    elif block == "l1f":
        for n, s in dict(wqkv=(2, 1024, 1536), wo=(2, 512, 1024),
                         w1=(2, 1024, 5460), w2=(2, 2730, 1024)).items():
            W[n] = nc.dram_tensor(n, list(s), F32, kind="ExternalInput").ap()
        for n, s in dict(qk_scale_l=(128, 4), lng_l=(128, 16), lnb_l=(128, 16),
                         b1a_l=(128, 44), b1g_l=(128, 44), b2_l=(128, 16)).items():
            W[n] = nc.dram_tensor(n, list(s), F32, kind="ExternalInput").ap()
        for n in ("xp_csq", "xp_ssq", "xp_csk", "xp_ssk"):
            W[n] = hb[n]
        W["xres"] = nc.dram_tensor("xres", [128, NCT * T], F32, kind="Internal").ap()
        for sname, nrows, ncols in (("s_wqkv", 2048, 1536), ("s_wo", 1024, 1024),
                                    ("s_w1", 2048, 5460), ("s_w2", 5460, 1024)):
            W[sname] = nc.dram_tensor(sname, [nrows, ncols], BF16, kind="Internal").ap()
    kb = KB(nc)
    ctx = Ctx(kb)
    alloc_common(kb, ctx)
    emit_load_consts(kb, ctx, hb)
    if block == "l0a":
        _pp = [p for p in PREPASS_WEIGHTS if p[0].startswith("l0a")]
        srcs = {"l0a_w_dq": W["w_dq"], "l0a_w_uq": W["w_uq"], "l0a_w_qr": W["w_qr"],
                "l0a_w_o": W["w_o"]}
        import kernel as _K
        _saved = globals()["PREPASS_WEIGHTS"]
        globals()["PREPASS_WEIGHTS"] = _pp
        emit_prepass(kb, ctx, srcs, W)
        globals()["PREPASS_WEIGHTS"] = _saved
    elif block == "l1f":
        _pp = [p for p in PREPASS_WEIGHTS if p[0].startswith("l1f")]
        srcs = {"l1f_wqkv": W["wqkv"].rearrange("l a b -> (l a) b"),
                "l1f_wo": W["wo"].rearrange("l a b -> (l a) b"),
                "l1f_w1": W["w1"].rearrange("l a b -> (l a) b"),
                "l1f_w2": W["w2"].rearrange("l a b -> (l a) b")}
        _saved = globals()["PREPASS_WEIGHTS"]
        globals()["PREPASS_WEIGHTS"] = _pp
        emit_prepass(kb, ctx, srcs, W)
        globals()["PREPASS_WEIGHTS"] = _saved
        W["wqkv"] = W["s_wqkv"]
        W["wo"] = W["s_wo"]
        W["w1"] = W["s_w1"]
        W["w2"] = W["s_w2"]
    emit_load_x(kb, ctx, x_h)
    if block == "l0a":
        emit_l0_attn(kb, ctx, W)
    elif block == "l0f":
        emit_l0_ffn(kb, ctx, W)
    elif block == "l1a":
        emit_l1_attn(kb, ctx, W)
    elif block == "l1f":
        emit_l1_ffn(kb, ctx, W)
    emit_postln(kb, ctx)
    emit_store_x(kb, ctx, o_h)
    kb.finalize()
    return nc, consts


def _l1f_layouts(q_scale, k_scale, ln_g, ln_b, b1, b2):
    d = {}
    d["qk_scale_l"] = np.stack([np.tile(q_scale[0], 2), np.tile(k_scale[0], 2),
                                np.tile(q_scale[1], 2), np.tile(k_scale[1], 2)],
                               axis=1).astype(np.float32)

    def cols(vec, n):
        out = np.zeros((128, n), np.float32)
        for i in range(n):
            seg = vec[i * 128:(i + 1) * 128]
            out[:len(seg), i] = seg
        return out
    d["lng_l"] = np.concatenate([cols(ln_g[l], 8) for l in range(2)], 1)
    d["lnb_l"] = np.concatenate([cols(ln_b[l], 8) for l in range(2)], 1)
    d["b1a_l"] = np.concatenate([cols(b1[l][:2730], 22) for l in range(2)], 1)
    d["b1g_l"] = np.concatenate([cols(b1[l][2730:], 22) for l in range(2)], 1)
    d["b2_l"] = np.concatenate([cols(b2[l], 8) for l in range(2)], 1)
    return d


def kernel(**inputs):
    inp = {k: np.asarray(v) for k, v in inputs.items()}
    if FUSED:
        return kernel_fused(inp)
    f32 = np.float32
    blocks = [
        ("l0a", {"w_dq": inp["L0_attn_W_DQ"], "w_uq": inp["L0_attn_W_UQ"],
                 "w_dkv": inp["L0_attn_W_DKV"], "w_uk": inp["L0_attn_W_UK"],
                 "w_uv": inp["L0_attn_W_UV"], "w_qr": inp["L0_attn_W_QR"],
                 "w_kr": inp["L0_attn_W_KR"], "w_o": inp["L0_attn_W_O"]}),
        ("l0f", {"w_gate": inp["L0_ffn_w_gate"], "w_up": inp["L0_ffn_w_up"],
                 "w_down": inp["L0_ffn_w_down"]}),
        ("l1a", {"w_f": inp["L1_attn_W_f"], "b_f": inp["L1_attn_b_f"],
                 "w_i": inp["L1_attn_W_i"], "b_i": inp["L1_attn_b_i"],
                 "w_h": inp["L1_attn_W_h"], "b_h": inp["L1_attn_b_h"]}),
        ("l1f", {"wqkv": inp["L1_ffn_wqkv"], "wo": inp["L1_ffn_wo"],
                 "w1": inp["L1_ffn_w1"], "w2": inp["L1_ffn_w2"],
                 **_l1f_layouts(inp["L1_ffn_q_scale"], inp["L1_ffn_k_scale"],
                                inp["L1_ffn_ln_g"], inp["L1_ffn_ln_b"],
                                inp["L1_ffn_b1"], inp["L1_ffn_b2"])}),
    ]
    cur = np.ascontiguousarray(inp["L0_attn_x"].astype(f32))
    for bname, wmap in blocks:
        if bname not in _PROG_CACHE:
            _PROG_CACHE[bname] = _build_block(bname)
        nc, consts = _PROG_CACHE[bname]
        cmap = {n: consts[n] for n in CONST_NAMES}
        wmap2 = {k: np.ascontiguousarray(v.astype(f32) if v.dtype != np.float32 else v)
                 for k, v in wmap.items()}
        in_maps = [{"x": np.ascontiguousarray(cur[c]), **wmap2, **cmap}
                   for c in range(8)]
        res = run_bass_kernel_spmd(nc, in_maps, core_ids=list(range(8)))
        cur = np.stack([res.results[c]["o"] for c in range(8)])
    return cur.astype(np.float32)



def _build_full():
    consts = host_consts()
    nc = bass.Bass("TRN2", target_bir_lowering=False)
    x_h = nc.dram_tensor("x", [T, D], F32, kind="ExternalInput").ap()
    o_h = nc.dram_tensor("o", [T, D], F32, kind="ExternalOutput").ap()
    hb = _declare_consts(nc, consts)
    Wa, Wf, Wr, Wl = {}, {}, {}, {}
    for n, s in dict(w_dq=(1024, 4096), w_uq=(4096, 1024), w_dkv=(1024, 256),
                     w_uk=(256, 1024), w_uv=(256, 1024), w_qr=(4096, 512),
                     w_kr=(1024, 32), w_o=(1024, 1024)).items():
        Wa[n] = nc.dram_tensor("l0a_" + n, list(s), F32, kind="ExternalInput").ap()
    Wa["scale_q4"] = hb["l0_scale_q4"]
    Wa["scale_k"] = hb["l0_scale_k"]
    for n, s in dict(w_gate=(2816, 1024), w_up=(2816, 1024), w_down=(1024, 2816)).items():
        Wf[n] = nc.dram_tensor("l0f_" + n, list(s), F32, kind="ExternalInput").ap()
    for n, s in dict(wqgT=(1024, 2816), wquT=(1024, 2816), wqdT=(2816, 1024)).items():
        Wf[n] = nc.dram_tensor("l0f_" + n, list(s), BF16, kind="Internal").ap()
    for n in ("w_f", "w_i", "w_h"):
        Wr[n] = nc.dram_tensor("l1a_" + n, [1024, 1024], F32, kind="ExternalInput").ap()
    for n in ("b_f", "b_i", "b_h"):
        Wr[n] = nc.dram_tensor("l1a_" + n, [1024], F32, kind="ExternalInput").ap()
    for n, s in dict(wqkv=(2, 1024, 1536), wo=(2, 512, 1024), w1=(2, 1024, 5460),
                     w2=(2, 2730, 1024)).items():
        Wl[n] = nc.dram_tensor("l1f_" + n, list(s), F32, kind="ExternalInput").ap()
    for n, s in dict(qk_scale_l=(128, 4), lng_l=(128, 16), lnb_l=(128, 16),
                     b1a_l=(128, 44), b1g_l=(128, 44), b2_l=(128, 16)).items():
        Wl[n] = nc.dram_tensor("l1f_" + n, list(s), F32, kind="ExternalInput").ap()
    for n in ("xp_csq", "xp_ssq", "xp_csk", "xp_ssk"):
        Wl[n] = hb[n]
    Wl["xres"] = nc.dram_tensor("xres", [128, NCT * T], F32, kind="Internal").ap()
    scr = {}
    for iname, sname, nrows, ncols in PREPASS_WEIGHTS:
        scr[sname] = nc.dram_tensor(sname, [nrows, ncols], BF16, kind="Internal").ap()
    srcs = {"l0a_w_dq": Wa["w_dq"], "l0a_w_uq": Wa["w_uq"], "l0a_w_qr": Wa["w_qr"],
            "l0a_w_o": Wa["w_o"], "l1f_wqkv": Wl["wqkv"].rearrange("l a b -> (l a) b"),
            "l1f_wo": Wl["wo"].rearrange("l a b -> (l a) b"),
            "l1f_w1": Wl["w1"].rearrange("l a b -> (l a) b"),
            "l1f_w2": Wl["w2"].rearrange("l a b -> (l a) b")}
    for sname in ("s_w_dq", "s_w_uq", "s_w_qr", "s_w_o"):
        Wa[sname] = scr[sname]
    Wl["wqkv"] = scr["s_wqkv"]
    Wl["wo"] = scr["s_wo"]
    Wl["w1"] = scr["s_w1"]
    Wl["w2"] = scr["s_w2"]

    kb = KB(nc)
    ctx = Ctx(kb)
    alloc_common(kb, ctx)
    emit_load_consts(kb, ctx, hb)
    emit_prepass(kb, ctx, srcs, scr)
    emit_load_x(kb, ctx, x_h)
    emit_l0_attn(kb, ctx, Wa)
    emit_postln(kb, ctx)
    emit_l0_ffn(kb, ctx, Wf)
    emit_postln(kb, ctx)
    emit_l1_attn(kb, ctx, Wr)
    emit_postln(kb, ctx)
    emit_l1_ffn(kb, ctx, Wl)
    emit_postln(kb, ctx)
    emit_store_x(kb, ctx, o_h)
    kb.finalize()
    return nc, consts


FUSED = True


def kernel_fused(inp):
    if "full" not in _PROG_CACHE:
        _PROG_CACHE["full"] = _build_full()
    nc, consts = _PROG_CACHE["full"]
    cmap = {n: consts[n] for n in CONST_NAMES}
    wmap = {}
    for pre, names in (("L0_attn_", ("W_DQ", "W_UQ", "W_DKV", "W_UK", "W_UV",
                                     "W_QR", "W_KR", "W_O")),):
        for n in names:
            wmap["l0a_w_" + n[2:].lower()] = inp[pre + n]
    wmap["l0f_w_gate"] = inp["L0_ffn_w_gate"]
    wmap["l0f_w_up"] = inp["L0_ffn_w_up"]
    wmap["l0f_w_down"] = inp["L0_ffn_w_down"]
    for n in ("W_f", "W_i", "W_h", "b_f", "b_i", "b_h"):
        wmap["l1a_" + n.lower() if n[0] == "W" else "l1a_" + n] = inp["L1_attn_" + n]
    for n in ("wqkv", "wo", "w1", "w2"):
        wmap["l1f_" + n] = inp["L1_ffn_" + n]
    lay = _l1f_layouts(inp["L1_ffn_q_scale"], inp["L1_ffn_k_scale"],
                       inp["L1_ffn_ln_g"], inp["L1_ffn_ln_b"],
                       inp["L1_ffn_b1"], inp["L1_ffn_b2"])
    for k, v in lay.items():
        wmap["l1f_" + k] = v
    wmap = {k: np.ascontiguousarray(np.asarray(v).astype(np.float32)) for k, v in wmap.items()}
    x = np.ascontiguousarray(inp["L0_attn_x"].astype(np.float32))
    in_maps = [{"x": x[c], **wmap, **cmap} for c in range(8)]
    res = run_bass_kernel_spmd(nc, in_maps, core_ids=list(range(8)))
    return np.stack([res.results[c]["o"] for c in range(8)]).astype(np.float32)



PREPASS_WEIGHTS = [
    ("l0a_w_dq", "s_w_dq", 1024, 4096),
    ("l0a_w_uq", "s_w_uq", 4096, 1024),
    ("l0a_w_qr", "s_w_qr", 4096, 512),
    ("l0a_w_o", "s_w_o", 1024, 1024),
    ("l1f_wqkv", "s_wqkv", 2 * 1024, 1536),
    ("l1f_wo", "s_wo", 2 * 512, 1024),
    ("l1f_w1", "s_w1", 2 * 1024, 5460),
    ("l1f_w2", "s_w2", 2 * 2730, 1024),
]


def emit_prepass(kb, ctx, srcs, dsts):
    stage_b = Slot(ctx.AB, 0, 4096)
    bi = 0
    for iname, sname, nrows, ncols in PREPASS_WEIGHTS:
        src, dst = srcs[iname], dsts[sname]
        for r in range((nrows + 127) // 128):
            rh = min(128, nrows - r * 128)
            c0 = 0
            while c0 < ncols:
                cw = min(2048, ncols - c0)
                sf = Slot(ctx.AF, AF_ACC + (bi % 2) * 2048, 2048).v((0, rh), (0, cw))
                kb.dma("q" if bi % 2 == 0 else "s", sf,
                       DV(src[r * 128:r * 128 + rh, c0:c0 + cw], "pp_" + iname,
                          (0, rh, r * ncols + c0, r * ncols + c0 + cw)))
                sb = stage_b.v((0, rh), ((bi % 2) * 2048, (bi % 2) * 2048 + cw))
                kb.copy("s" if bi % 2 == 0 else "v", sb, sf)
                kb.dma("p", DV(dst[r * 128:r * 128 + rh, c0:c0 + cw], sname,
                               (0, rh, r * ncols + c0, r * ncols + c0 + cw)), sb)
                bi += 1
                c0 += cw


# version 07
# speedup vs baseline: 1.3809x; 1.2837x over previous
import numpy as np
import ml_dtypes
from contextlib import ExitStack

import concourse.bass as bass
import concourse.mybir as mybir
from concourse.bass_utils import run_bass_kernel_spmd

F32 = mybir.dt.float32
BF16 = mybir.dt.bfloat16
AF = mybir.ActivationFunctionType
ALU = mybir.AluOpType
AX = mybir.AxisListType

DT_SIZE = {F32: 4, BF16: 2}


class TV:
    __slots__ = ("t", "ap", "p0", "p1", "e0", "e1")

    def __init__(self, t, ap, p0, p1, e0, e1):
        self.t = t
        self.ap = ap
        self.p0, self.p1, self.e0, self.e1 = p0, p1, e0, e1


def V(th, ps=None, es=None):
    shape = list(th.shape)
    p0, p1 = (0, shape[0]) if ps is None else ps
    e0, e1 = (0, shape[1]) if es is None else es
    return TV(th, th[p0:p1, e0:e1], p0, p1, e0, e1)


def V3(th, ps, es, ap):
    return TV(th, ap, ps[0], ps[1], es[0], es[1])


def DV(th_ap, tname, region=None):
    if region is None:
        region = (0, 1, 0, 1)
    return TV(_Named(tname), th_ap, *region)


class _Named:
    __slots__ = ("name",)

    def __init__(self, name):
        self.name = name



ENGINES = ("t", "v", "s", "p", "q")


class KB:
    def __init__(self, nc):
        self.nc = nc
        self.stack = ExitStack()
        self.prog = {e: [] for e in ENGINES}
        self.semh = {}
        for name in ("ct", "cv", "cs", "cp"):
            self.semh[name] = self.stack.enter_context(nc.semaphore("sem_" + name))
        self.csem = {"t": "ct", "v": "cv", "s": "cs", "p": "cp"}
        self.dring = {}
        for q, r in (("q", 8), ("p", 8), ("s", 4), ("v", 2), ("t", 2)):
            names = []
            for i in range(r):
                n = f"d{q}{i}"
                self.semh[n] = self.stack.enter_context(nc.semaphore("sem_" + n))
                names.append(n)
            self.dring[q] = names
        self.dma_i = {q: 0 for q in self.dring}
        self.cnt = {k: 0 for k in self.semh}
        self.clock = {e: {} for e in ENGINES}
        self.snap = {k: {} for k in self.semh}
        self.writes = {}
        self.reads = {}
        self.psum_names = set()
        self.eng_obj = {"t": nc.tensor, "v": nc.vector, "s": nc.scalar,
                        "p": nc.gpsimd, "q": nc.sync}
        self._tensors = []
        self.fconst_t = self.sbuf("fconst_tab", [1, 32], F32)
        self.fconst_vals = {}

    def fconst(self, val):
        val = float(val)
        if val in (0.0, 1.0):
            return val, None
        if val not in self.fconst_vals:
            idx = len(self.fconst_vals)
            assert idx < 32
            self.fconst_vals[val] = idx
            view = V(self.fconst_t, (0, 1), (idx, idx + 1))
            self.memset("p", view, val)
            self.nc.const_aps.register(F32, val, self.fconst_t[0:1, idx:idx + 1])
        idx = self.fconst_vals[val]
        return val, V(self.fconst_t, (0, 1), (idx, idx + 1))

    def sbuf(self, name, shape, dtype):
        th = self.stack.enter_context(self.nc.sbuf_tensor(name, list(shape), dtype))
        self._tensors.append(th)
        return th

    def psum(self, name, shape, dtype):
        th = self.stack.enter_context(self.nc.psum_tensor(name, list(shape), dtype))
        self._tensors.append(th)
        self.psum_names.add(name)
        return th

    @staticmethod
    def _overlap(a, b):
        return not (a[1] <= b[0] or b[1] <= a[0] or a[3] <= b[2] or b[3] <= a[2])

    def _dep_events(self, eng, rviews, wviews):
        evs = []
        own = self.csem.get(eng)
        for v_ in rviews:
            for ent in self.writes.get(v_.t.name, ()):
                if self._overlap((v_.p0, v_.p1, v_.e0, v_.e1), ent):
                    evs.extend(ent[4])
            if v_.t.name in self.psum_names:
                for ent in self.reads.get(v_.t.name, ()):
                    if self._overlap((v_.p0, v_.p1, v_.e0, v_.e1), ent):
                        evs.extend(e for e in ent[4] if e[0] != own)
        for v_ in wviews:
            box = (v_.p0, v_.p1, v_.e0, v_.e1)
            for ent in self.writes.get(v_.t.name, ()):
                if self._overlap(box, ent):
                    evs.extend(ent[4])
            for ent in self.reads.get(v_.t.name, ()):
                if self._overlap(box, ent):
                    evs.extend(ent[4])
        if eng == "t":
            evs = [e for e in evs if e[0] != "ct"]
        return evs

    def _emit_waits(self, eng, evs):
        clk = self.clock[eng]
        need = {}
        for s_, v_ in evs:
            if clk.get(s_, 0) < v_ and need.get(s_, 0) < v_:
                need[s_] = v_
        for s_, v_ in sorted(need.items()):
            semh = self.semh[s_]
            self.prog[eng].append(("w", semh, v_))
            snap = self.snap[s_].get(v_)
            if snap:
                for ks, kv in snap.items():
                    if clk.get(ks, 0) < kv:
                        clk[ks] = kv
            if clk.get(s_, 0) < v_:
                clk[s_] = v_

    def _record(self, views, table, ev):
        for v_ in views:
            lst = table.setdefault(v_.t.name, [])
            box = (v_.p0, v_.p1, v_.e0, v_.e1)
            if table is self.writes:
                for tab in (self.writes, self.reads):
                    l2 = tab.get(v_.t.name)
                    if l2:
                        tab[v_.t.name] = [e for e in l2 if not (
                            box[0] <= e[0] and e[1] <= box[1] and box[2] <= e[2] and e[3] <= box[3])]
                lst = table.setdefault(v_.t.name, [])
            lst.append(box + ((ev,),))
            if len(lst) > 120:
                old, rest = lst[:60], lst[60:]
                p0 = min(e[0] for e in old); p1 = max(e[1] for e in old)
                e0 = min(e[2] for e in old); e1 = max(e[3] for e in old)
                evs = tuple(sorted({x for e in old for x in e[4]}, key=lambda t: (t[0], -t[1])))
                seen = {}
                for s_, vv in evs:
                    seen[s_] = max(seen.get(s_, 0), vv)
                table[v_.t.name] = [(p0, p1, e0, e1, tuple(seen.items()))] + rest

    def op(self, eng, fn, reads=(), writes=(), dma=False):
        evs = self._dep_events(eng, reads, writes)
        if dma:
            ring = self.dring[eng]
            sname = ring[self.dma_i[eng] % len(ring)]
            self.dma_i[eng] += 1
            if self.cnt[sname] > 0:
                evs = list(evs) + [(sname, self.cnt[sname])]
        else:
            sname = self.csem[eng]
        self._emit_waits(eng, evs)
        inc = 16 if dma else 1
        self.cnt[sname] += inc
        ev = (sname, self.cnt[sname])
        semh = self.semh[sname]
        self.prog[eng].append(("c", fn, semh, inc))
        clk = dict(self.clock[eng])
        clk[sname] = self.cnt[sname]
        self.snap[sname][self.cnt[sname]] = clk
        self._record(reads, self.reads, ev)
        self._record(writes, self.writes, ev)
        return ev

    def raw_wait(self, eng, sname, val):
        self.prog[eng].append(("w", self.semh[sname], val))
        clk = self.clock[eng]
        if clk.get(sname, 0) < val:
            clk[sname] = val

    def barrier_all_on(self, eng):
        evs = [(s, c) for s, c in self.cnt.items() if c > 0]
        self._emit_waits(eng, evs)

    def dma(self, q, dst, src):
        return self.op(q, lambda e: e.dma_start(out=dst.ap, in_=src.ap),
                       reads=(src,), writes=(dst,), dma=True)

    def dma_slow(self, q, dst, src):
        return self.op(q, lambda e: e.dma_start(out=dst.ap, in_=src.ap,
                                                allow_slow_non_contiguous=True),
                       reads=(src,), writes=(dst,), dma=True)

    def copy(self, eng, dst, src):
        if eng == "s":
            return self.op(eng, lambda e: e.copy(out=dst.ap, in_=src.ap),
                           reads=(src,), writes=(dst,))
        return self.op(eng, lambda e: e.tensor_copy(out=dst.ap, in_=src.ap),
                       reads=(src,), writes=(dst,))

    def memset(self, eng, dst, c):
        return self.op(eng, lambda e: e.memset(dst.ap, c), writes=(dst,))

    def tt(self, eng, out, in0, in1, alu):
        return self.op(eng, lambda e: e.tensor_tensor(out=out.ap, in0=in0.ap, in1=in1.ap, op=alu),
                       reads=(in0, in1), writes=(out,))

    def ts(self, eng, out, in0, s1, s2=None, op0=ALU.mult, op1=None, s1v=None, s2v=None):
        reads = [in0]
        a1 = s1
        if isinstance(s1, TV):
            reads.append(s1); a1 = s1.ap
        a2 = s2
        if isinstance(s2, TV):
            reads.append(s2); a2 = s2.ap
        if op1 is None:
            fn = lambda e: e.tensor_scalar(out=out.ap, in0=in0.ap, scalar1=a1,
                                           scalar2=None, op0=op0)
        else:
            fn = lambda e: e.tensor_scalar(out=out.ap, in0=in0.ap, scalar1=a1,
                                           scalar2=a2, op0=op0, op1=op1)
        return self.op(eng, fn, reads=tuple(reads), writes=(out,))

    def stt(self, eng, out, in0, scalar, in1, op0, op1):
        reads = [in0, in1]
        a = scalar
        if isinstance(scalar, TV):
            reads.append(scalar); a = scalar.ap
        return self.op(eng, lambda e: e.scalar_tensor_tensor(
            out=out.ap, in0=in0.ap, scalar=a, in1=in1.ap, op0=op0, op1=op1),
            reads=tuple(reads), writes=(out,))

    def scan(self, out, d0, d1, initial, op0=ALU.mult, op1=ALU.add):
        reads = [d0, d1]
        ini = initial
        if isinstance(initial, TV):
            reads.append(initial); ini = initial.ap
        return self.op("v", lambda e: e.tensor_tensor_scan(
            out=out.ap, data0=d0.ap, data1=d1.ap, initial=ini, op0=op0, op1=op1),
            reads=tuple(reads), writes=(out,))

    def act(self, out, in_, func, bias=0.0, scale=1.0, accum=None):
        reads = [in_]
        b = bias
        if isinstance(bias, TV):
            reads.append(bias); b = bias.ap
        else:
            b, cv = self.fconst(bias)
            if cv is not None:
                reads.append(cv)
        sc = scale
        if isinstance(scale, TV):
            reads.append(scale); sc = scale.ap
        else:
            sc, cv = self.fconst(scale)
            if cv is not None:
                reads.append(cv)
        writes = [out]
        if accum is not None:
            writes.append(accum)
            fn = lambda e: e.activation(out=out.ap, in_=in_.ap, func=func, bias=b,
                                        scale=sc, accum_out=accum.ap)
        else:
            fn = lambda e: e.activation(out=out.ap, in_=in_.ap, func=func, bias=b, scale=sc)
        return self.op("s", fn, reads=tuple(reads), writes=tuple(writes))

    def reduce(self, eng, out, in_, alu, negate=False):
        return self.op(eng, lambda e: e.tensor_reduce(
            out=out.ap, in_=in_.ap, axis=AX.X, op=alu, negate=negate),
            reads=(in_,), writes=(out,))

    def recip(self, out, in_):
        return self.op("v", lambda e: e.reciprocal(out=out.ap, in_=in_.ap),
                       reads=(in_,), writes=(out,))

    def recip_fast(self, out, in_, scratch):
        from concourse import dve_ops
        self.op("v", lambda e: e.reciprocal_approx_fast(out=scratch.ap, in_=in_.ap),
                reads=(in_,), writes=(scratch,))
        return self.op("v", lambda e: e._custom_dve(
            dve_ops.RECIPROCAL_APPROX_NR, out=out.ap, in0=in_.ap, in1=scratch.ap,
            s0=2.0), reads=(in_, scratch), writes=(out,))

    def mm(self, out, lhsT, rhs, start, stop, tile_position=None):
        return self.op("t", lambda e: e.matmul(out.ap, lhsT.ap, rhs.ap, start=start,
                                               stop=stop, tile_position=tile_position),
                       reads=(lhsT, rhs), writes=(out,))

    def trans(self, out, in_, ident):
        return self.op("t", lambda e: e.transpose(out.ap, in_.ap, ident.ap),
                       reads=(in_, ident), writes=(out,))

    def finalize(self):
        self.barrier_all_on("q")
        nc = self.nc
        with nc.Block() as block:
            def runner(stream):
                def go(engobj):
                    for item in stream:
                        if item[0] == "w":
                            engobj.wait_ge(item[1], item[2])
                        else:
                            _, fn, semh, inc = item
                            fn(engobj).then_inc(semh, inc)
                return go

            block.tensor(runner(self.prog["t"]))
            block.vector(runner(self.prog["v"]))
            block.scalar(runner(self.prog["s"]))
            block.gpsimd(runner(self.prog["p"]))
            block.sync(runner(self.prog["q"]))
        self.stack.close()



T = 2048
D = 1024
NCT = D // 128
NTT = T // 128


class Slot:
    __slots__ = ("th", "off", "ln")

    def __init__(self, th, off, ln):
        self.th, self.off, self.ln = th, off, ln

    def v(self, ps=None, es=None):
        p0, p1 = (0, self.th.shape[0]) if ps is None else ps
        e0, e1 = (0, self.ln) if es is None else es
        a, b = self.off + e0, self.off + e1
        return TV(self.th, self.th[p0:p1, a:b], p0, p1, a, b)

    def vap(self, rel_off, dims, ps, es):
        pitch = self.th.shape[1]
        ap = bass.AP(self.th, self.off + rel_off, [[pitch, ps[1] - ps[0]]] + dims)
        if ps[0] != 0:
            ap = bass.AP(self.th, (self.off + rel_off), [[pitch, ps]] )
        return TV(self.th, ap, ps[0], ps[1], self.off + es[0], self.off + es[1])


AB_KC = 0
AB_VTM = 16384
AB_KR4 = 33024
AB_SCL = 35072
AB_CQ = 35840
AB_XBC = 44032
AB_QC = 46080
AB_QR = 48128
AB_WSM = 49152
AB_END = 56576

AF_ACC = 0
AF_WSTG = 4096
AF_SQ = 5120
AF_LNT = 6144
AF_END = 6656


def host_consts():
    c = {}
    c["ident_f32"] = np.eye(128, dtype=np.float32)
    c["ident_bf16"] = np.eye(128, dtype=ml_dtypes.bfloat16)
    c["ones_f32"] = np.ones((1, 128), dtype=np.float32)
    i = np.arange(128)
    c["tri_mask"] = np.where(i[:, None] <= i[None, :], 0.0, -1e30).astype(np.float32)
    DHR = 32
    t = np.arange(T, dtype=np.float32)
    inv = 1.0 / 10000.0 ** (np.arange(0, DHR, 2, dtype=np.float32) / DHR)
    freqs = t[:, None] * inv[None, :]
    emb = np.concatenate([freqs, freqs], -1)
    scale = (np.cos(emb) + np.sin(emb)).astype(np.float32)
    c["l0_scale_k"] = np.ascontiguousarray(scale.T).astype(ml_dtypes.bfloat16)
    c["l0_scale_q4"] = (np.tile(scale.T, (4, 1)) /
                        np.sqrt(np.float32(96.0))).astype(ml_dtypes.bfloat16)
    c["tri2_mask"] = np.where(i[None, :] <= i[:, None], 0.0, -1e30).astype(np.float32)
    e2 = np.zeros((128, 2), np.float32)
    e2[0:64, 0] = 1.0
    e2[64:128, 1] = 1.0
    c["e2_mat"] = e2
    c["e2t_mat"] = np.ascontiguousarray(e2.T)
    DH, WIN = 64, 512
    half = np.arange(0, DH, 2, dtype=np.float32)
    invf = 1.0 / 10000.0 ** (half / DH)
    tt = np.arange(T, dtype=np.float32)
    fr = tt[:, None] * invf[None, :]
    fr2 = np.concatenate([fr, fr], -1)
    basev = (half + 0.4 * DH) / (1.4 * DH)
    power = (tt - T // 2) / float(WIN // 2)
    sc = basev[None, :] ** power[:, None]
    sc2 = np.concatenate([sc, sc], -1)
    cosv, sinv = np.cos(fr2), np.sin(fr2)
    sgn = np.where(np.arange(DH) < DH // 2, -1.0, 1.0).astype(np.float32)
    csq = (8.0 * cosv * sc2).T
    ssq = (8.0 * sinv * sc2).T * sgn[:, None]
    csk = (cosv / sc2).T
    ssk = (sinv / sc2).T * sgn[:, None]
    c["xp_csq"] = np.ascontiguousarray(np.tile(csq, (2, 1)).astype(np.float32))
    c["xp_ssq"] = np.ascontiguousarray(np.tile(ssq, (2, 1)).astype(np.float32))
    c["xp_csk"] = np.ascontiguousarray(np.tile(csk, (2, 1)).astype(np.float32))
    c["xp_ssk"] = np.ascontiguousarray(np.tile(ssk, (2, 1)).astype(np.float32))
    return c


CONST_NAMES = ("ident_f32", "ident_bf16", "ones_f32", "tri_mask",
               "l0_scale_k", "l0_scale_q4", "tri2_mask", "e2_mat", "e2t_mat",
               "xp_csq", "xp_ssq", "xp_csk", "xp_ssk")


class Ctx:
    def __init__(self, kb):
        self.kb = kb
        self.d = {}

    def __getattr__(self, k):
        try:
            return self.d[k]
        except KeyError:
            raise AttributeError(k)


def alloc_common(kb, ctx):
    ctx.d["x"] = kb.sbuf("x_cm", [128, NCT * T], F32)
    ctx.d["AB"] = kb.sbuf("arena_bf", [128, AB_END], BF16)
    ctx.d["AF"] = kb.sbuf("arena_f32", [128, AF_END], F32)
    ctx.d["pf"] = [kb.psum(f"psf{i}", [128, 512], F32) for i in range(6)]
    ctx.d["pb"] = [kb.psum(f"psb{i}", [128, 1024], BF16) for i in range(2)]
    ctx.d["idf"] = kb.sbuf("idf", [128, 128], F32)
    ctx.d["idb"] = kb.sbuf("idb", [128, 128], BF16)
    ctx.d["ones"] = kb.sbuf("ones_c", [1, 128], F32)
    ctx.d["ones_col"] = kb.sbuf("ones_col", [128, 1], F32)
    ctx.d["tri"] = kb.sbuf("tri_c", [128, 128], F32)
    ctx.d["tri2"] = kb.sbuf("tri2_c", [128, 128], F32)
    ctx.d["e2"] = kb.sbuf("e2_c", [128, 2], F32)
    ctx.d["e2t"] = kb.sbuf("e2t_c", [2, 128], F32)
    AB, AFt = ctx.AB, ctx.AF
    ctx.d["stage"] = Slot(AFt, AF_ACC, 2048)


def emit_load_consts(kb, ctx, hb):
    kb.dma("q", V(ctx.idf), DV(hb["ident_f32"], "h_identf"))
    kb.dma("q", V(ctx.idb), DV(hb["ident_bf16"], "h_identb"))
    kb.dma("q", V(ctx.ones), DV(hb["ones_f32"], "h_ones"))
    kb.dma_slow("q", V(ctx.ones_col), DV(hb["ones_f32"].rearrange("o p -> p o"), "h_ones2"))
    kb.dma("q", V(ctx.tri), DV(hb["tri_mask"], "h_tri"))
    kb.dma("q", V(ctx.tri2), DV(hb["tri2_mask"], "h_tri2"))
    kb.dma("q", V(ctx.e2), DV(hb["e2_mat"], "h_e2"))
    kb.dma("q", V(ctx.e2t), DV(hb["e2t_mat"], "h_e2t"))


def emit_load_x(kb, ctx, x_hbm):
    x, stage, pf = ctx.x, ctx.stage, ctx.pf
    for ti in range(NTT):
        sb = ti % 2
        stv = stage.v(es=(sb * 1024, (sb + 1) * 1024))
        kb.dma("q", stv, DV(x_hbm[ti * 128:(ti + 1) * 128, :], "x_in",
                            (0, 128, ti * 1024, (ti + 1) * 1024)))
        for half in range(2):
            bank = pf[half]
            for cj in range(4):
                ci = half * 4 + cj
                src = stage.v(es=(sb * 1024 + ci * 128, sb * 1024 + (ci + 1) * 128))
                kb.trans(V(bank, (0, 128), (cj * 128, (cj + 1) * 128)), src, V(ctx.idf))
            base = (half * 4) * T + ti * 128
            ap = bass.AP(x, base, [[NCT * T, 128], [T, 4], [1, 128]])
            dtv = TV(x, ap, 0, 128, base, base + 3 * T + 128)
            kb.op("s", lambda e, o=ap, i=bank: e.copy(out=o, in_=i[:, 0:512]),
                  reads=(V(bank, (0, 128), (0, 512)),), writes=(dtv,))


def emit_store_x(kb, ctx, out_hbm):
    x, stage, pf = ctx.x, ctx.stage, ctx.pf
    for ti in range(NTT):
        sb = ti % 2
        for half in range(2):
            bank = pf[2 + half]
            for cj in range(4):
                ci = half * 4 + cj
                src = TV(x, x[:, ci * T + ti * 128: ci * T + (ti + 1) * 128],
                         0, 128, ci * T + ti * 128, ci * T + (ti + 1) * 128)
                kb.trans(V(bank, (0, 128), (cj * 128, (cj + 1) * 128)), src, V(ctx.idf))
            kb.copy("s", stage.v(es=(sb * 1024 + half * 512, sb * 1024 + (half + 1) * 512)),
                    V(bank, (0, 128), (0, 512)))
        kb.dma("q", DV(out_hbm[ti * 128:(ti + 1) * 128, :], "x_out",
                       (0, 128, ti * 1024, (ti + 1) * 1024)),
               stage.v(es=(sb * 1024, (sb + 1) * 1024)))



def emit_postln(kb, ctx, eps=1e-5):
    x, pf, oc = ctx.x, ctx.pf, ctx.ones_col
    AFt = ctx.AF
    for tc in range(T // 512):
        cs = tc * 512
        ps_a, ps_b, ps_c, ps_d = pf[0], pf[1], pf[2], pf[3]
        for ci in range(NCT):
            kb.mm(V(ps_a, (0, 1), (0, 512)), V(oc),
                  V(x, (0, 128), (ci * T + cs, ci * T + cs + 512)),
                  start=(ci == 0), stop=(ci == NCT - 1))
        for ci in range(NCT):
            sq = Slot(AFt, AF_SQ, 1024).v(es=((ci % 2) * 512, (ci % 2) * 512 + 512))
            kb.act(sq, V(x, (0, 128), (ci * T + cs, ci * T + cs + 512)), AF.Square)
            kb.mm(V(ps_b, (0, 1), (0, 512)), V(oc), sq,
                  start=(ci == 0), stop=(ci == NCT - 1))
        A = Slot(ctx.AF, AF_SQ, 1024).v((0, 1), (0, 512))
        B = Slot(ctx.AF, AF_SQ, 1024).v((0, 1), (512, 1024))
        kb.ts("v", A, V(ps_a, (0, 1), (0, 512)), 1.0 / D, op0=ALU.mult)
        kb.tt("v", B, A, A, ALU.mult)
        kb.stt("v", B, V(ps_b, (0, 1), (0, 512)), 1.0 / D, B, ALU.mult, ALU.subtract)
        kb.act(B, B, AF.Sqrt, bias=float(eps))
        kb.recip(B, B)
        kb.tt("v", A, A, B, ALU.mult)
        kb.mm(V(ps_c, (0, 128), (0, 512)), V(ctx.ones, (0, 1), (0, 128)), B,
              start=True, stop=True)
        kb.mm(V(ps_d, (0, 128), (0, 512)), V(ctx.ones, (0, 1), (0, 128)), A,
              start=True, stop=True)
        for ci in range(NCT):
            xv = V(x, (0, 128), (ci * T + cs, ci * T + cs + 512))
            t1 = Slot(AFt, AF_LNT, 512).v()
            kb.tt("v", t1, xv, V(ps_c, (0, 128), (0, 512)), ALU.mult)
            kb.tt("v", xv, t1, V(ps_d, (0, 128), (0, 512)), ALU.subtract)



QCW = 256
NQC = T // QCW


def emit_l0_attn(kb, ctx, W):
    x, pf, AB = ctx.x, ctx.pf, ctx.AB
    kc = Slot(AB, AB_KC, 16384)
    vtm = Slot(AB, AB_VTM, 16640)
    kr4 = Slot(AB, AB_KR4, 2048)
    scl = Slot(AB, AB_SCL, 768)
    cqs = Slot(AB, AB_CQ, 8192)
    expt = Slot(AB, AB_CQ, 1024)
    ycm = Slot(AB, AB_CQ + 1024, 2048)
    xbc = Slot(AB, AB_XBC, 2048)
    qcc = Slot(AB, AB_QC, 2048)
    qrr = Slot(AB, AB_QR, 1024)
    wbf = Slot(AB, AB_WSM, 3072)
    wsm = Slot(AB, AB_WSM + 1024, 6400)
    wstg = Slot(ctx.AF, AF_WSTG, 1024)
    rbsb = Slot(ctx.AF, AF_LNT, 512)
    OF_DKV, OF_UK, OF_UV, OF_KR = 0, 8 * 256, 8 * 256 + 2 * 1024, 8 * 256 + 4 * 1024

    sw_i = [0]

    def stream_w(hap, hname, r0, c0, ncols, buf, width):
        buf = sw_i[0] % 6
        sw_i[0] += 1
        wb = wbf.v((0, 128), (buf * 512, buf * 512 + ncols))
        kb.dma("q" if buf % 2 == 0 else "s", wb,
               DV(hap[r0:r0 + 128, c0:c0 + ncols], hname,
                  (0, 128, r0 * width + c0, r0 * width + c0 + ncols)))
        return wb

    def load_small(hname, hap, ktiles, cols, off):
        for k in range(ktiles):
            stg = wstg.v((0, 128), ((k % 2) * 512, (k % 2) * 512 + cols))
            kb.dma("q", stg, DV(hap[k * 128:(k + 1) * 128, :], hname,
                                (0, 128, k * cols, (k + 1) * cols)))
            kb.copy("p", wsm.v((0, 128), (off + k * cols, off + (k + 1) * cols)), stg)

    load_small("w_dkv", W["w_dkv"], 8, 256, OF_DKV)
    load_small("w_uk", W["w_uk"], 2, 1024, OF_UK)
    load_small("w_uv", W["w_uv"], 2, 1024, OF_UV)
    load_small("w_kr", W["w_kr"], 8, 32, OF_KR)

    ckv = Slot(AB, AB_CQ + 4096, 4096)
    xb1 = Slot(AB, AB_CQ, 4096)

    for tc in range(4):
        cs = tc * 512
        kb.dma("q", scl.v((0, 32), (256, 768)),
               DV(W["scale_k"][0:32, cs:cs + 512], "h_sclk", (0, 32, cs, cs + 512)))
        for ci in range(NCT):
            kb.copy("v", xb1.v((0, 128), (ci * 512, (ci + 1) * 512)),
                    V(x, (0, 128), (ci * T + cs, ci * T + cs + 512)))
        for m in range(2):
            ps = pf[m]
            for ci in range(NCT):
                kb.mm(V(ps), wsm.v((0, 128), (OF_DKV + ci * 256 + m * 128,
                                              OF_DKV + ci * 256 + (m + 1) * 128)),
                      xb1.v((0, 128), (ci * 512, (ci + 1) * 512)),
                      start=(ci == 0), stop=(ci == NCT - 1))
            kb.copy("s", ckv.v((0, 128), (m * T + cs - 0 * T, m * T + cs + 512))
                    if False else ckv.v((0, 128), (m * 2048 + cs, m * 2048 + cs + 512)), V(ps))
        for m in range(8):
            ps = pf[2 + (m % 2)]
            for ci in range(2):
                kb.mm(V(ps), wsm.v((0, 128), (OF_UK + ci * 1024 + m * 128,
                                              OF_UK + ci * 1024 + (m + 1) * 128)),
                      ckv.v((0, 128), (ci * 2048 + cs, ci * 2048 + cs + 512)),
                      start=(ci == 0), stop=(ci == 1))
            kb.copy("s" if m % 2 == 0 else "v",
                    kc.v((0, 128), (m * T + cs, m * T + cs + 512)), V(ps))
        for tl in range(4):
            tt_ = tc * 4 + tl
            for n in range(2):
                ps = pf[4 + n]
                for ci in range(2):
                    kb.mm(V(ps), ckv.v((0, 128), (ci * 2048 + cs + tl * 128,
                                                  ci * 2048 + cs + (tl + 1) * 128)),
                          wsm.v((0, 128), (OF_UV + ci * 1024 + n * 512,
                                           OF_UV + ci * 1024 + (n + 1) * 512)),
                          start=(ci == 0), stop=(ci == 1))
                base = AB_VTM + tt_ * 1040 + 65 * 8 * n
                oap = bass.AP(AB, base, [[AB_END, 128], [65, 8], [1, 64]])
                kb.op("v", lambda e, o=oap, i=ps: e.tensor_copy(out=o, in_=i[:, 0:512]),
                      reads=(V(ps),), writes=(TV(AB, oap, 0, 128, base, base + 8 * 65),))
        ps = pf[0]
        for ci in range(NCT):
            kb.mm(V(ps, (0, 32), (0, 512)),
                  wsm.v((0, 128), (OF_KR + ci * 32, OF_KR + (ci + 1) * 32)),
                  xb1.v((0, 128), (ci * 512, (ci + 1) * 512)),
                  start=(ci == 0), stop=(ci == NCT - 1))
        kb.tt("v", kr4.v((0, 32), (cs, cs + 512)), V(ps, (0, 32), (0, 512)),
              scl.v((0, 32), (256, 768)), ALU.mult)
    for tt_ in range(16):
        base = AB_VTM + tt_ * 1040 + 64
        oap = bass.AP(AB, base, [[AB_END, 128], [65, 16], [1, 1]])
        kb.op("v", lambda e, o=oap: e.memset(o, 1.0),
              writes=(TV(AB, oap, 0, 128, base, base + 16 * 65),))
    for rep in range(1, 4):
        kb.dma("p", kr4.v((rep * 32, (rep + 1) * 32), (0, T)), kr4.v((0, 32), (0, T)))

    for qc in range(NQC):
        qs = qc * QCW
        kb.dma("q", scl.v((0, 128), (0, QCW)),
               DV(W["scale_q4"][0:128, qs:qs + QCW], "h_sclq", (0, 128, qs, qs + QCW)))
        for ci in range(NCT):
            kb.copy("v", xbc.v((0, 128), (ci * QCW, (ci + 1) * QCW)),
                    V(x, (0, 128), (ci * T + qs, ci * T + qs + QCW)))
        for mg in range(8):
            for k in range(NCT):
                wb = stream_w(W["s_w_dq"], "s_w_dq", k * 128, mg * 512, 512, (mg * 8 + k) % 2, 4096)
                for j in range(4):
                    kb.mm(V(pf[j], (0, 128), (0, QCW)),
                          wbf.v((0, 128), (wb.e0 - AB_WSM + j * 128, wb.e0 - AB_WSM + (j + 1) * 128)),
                          xbc.v((0, 128), (k * QCW, (k + 1) * QCW)),
                          start=(k == 0), stop=(k == NCT - 1))
            for j in range(4):
                kb.copy("s" if j % 2 == 0 else "v",
                        cqs.v((0, 128), ((mg * 4 + j) * QCW, (mg * 4 + j + 1) * QCW)),
                        V(pf[j], (0, 128), (0, QCW)))
        for mg in range(2):
            for k in range(32):
                wb = stream_w(W["s_w_uq"], "s_w_uq", k * 128, mg * 512, 512, (mg * 32 + k) % 2, 1024)
                for j in range(4):
                    kb.mm(V(pf[j], (0, 128), (0, QCW)),
                          wbf.v((0, 128), (wb.e0 - AB_WSM + j * 128, wb.e0 - AB_WSM + (j + 1) * 128)),
                          cqs.v((0, 128), (k * QCW, (k + 1) * QCW)),
                          start=(k == 0), stop=(k == 31))
            for j in range(4):
                dst = qcc.v((0, 128), ((mg * 4 + j) * QCW, (mg * 4 + j + 1) * QCW))
                kb.act(dst, V(pf[j], (0, 128), (0, QCW)), AF.Copy,
                       scale=float(1.0 / np.sqrt(96.0)))
        for k in range(32):
            wb = stream_w(W["s_w_qr"], "s_w_qr", k * 128, 0, 512, k % 2, 512)
            for j in range(4):
                kb.mm(V(pf[j], (0, 128), (0, QCW)),
                      wbf.v((0, 128), (wb.e0 - AB_WSM + j * 128, wb.e0 - AB_WSM + (j + 1) * 128)),
                      cqs.v((0, 128), (k * QCW, (k + 1) * QCW)),
                      start=(k == 0), stop=(k == 31))
        for j in range(4):
            kb.tt("v", qrr.v((0, 128), (j * QCW, (j + 1) * QCW)),
                  V(pf[j], (0, 128), (0, QCW)), scl.v((0, 128), (0, QCW)), ALU.mult)
        njt = (qc + 1) * (QCW // 128)
        nd = qc * (QCW // 128)

        def head_gen(h, si):
            hp, ho = h // 2, (h % 2) * 64
            rp = (h % 4) * 32
            y_ps = pf[4 + si]
            order = list(range(nd, njt)) + list(range(nd))
            for ji, j in enumerate(order):
                st = pf[2 * si + (ji % 2)]
                kb.mm(V(st, (0, 128), (0, QCW)),
                      kc.v((ho, ho + 64), (hp * T + j * 128, hp * T + (j + 1) * 128)),
                      qcc.v((ho, ho + 64), (hp * QCW, (hp + 1) * QCW)),
                      start=True, stop=False, tile_position=(ho, 0))
                kb.mm(V(st, (0, 128), (0, QCW)),
                      kr4.v((rp, rp + 32), (j * 128, (j + 1) * 128)),
                      qrr.v((rp, rp + 32), ((h // 4) * QCW, (h // 4 + 1) * QCW)),
                      start=False, stop=True, tile_position=(rp, 0))
                yield
                exoff = si * 512 + (ji % 2) * 256
                if j >= nd:
                    a = j - nd
                    if a > 0:
                        kb.memset("v", expt.v((0, 128), (exoff, exoff + a * 128)), 0.0)
                    kb.tt("v", V(st, (0, 128), (a * 128, (a + 1) * 128)),
                          V(st, (0, 128), (a * 128, (a + 1) * 128)), V(ctx.tri), ALU.add)
                    kb.act(expt.v((0, 128), (exoff + a * 128, exoff + QCW)),
                           V(st, (0, 128), (a * 128, QCW)), AF.Exp)
                else:
                    kb.act(expt.v((0, 128), (exoff, exoff + QCW)),
                           V(st, (0, 128), (0, QCW)), AF.Exp)
                yield
                kb.mm(V(y_ps, (0, 65), (0, QCW)),
                      vtm.v((0, 128), (j * 1040 + 65 * h, j * 1040 + 65 * h + 65)),
                      expt.v((0, 128), (exoff, exoff + QCW)),
                      start=(ji == 0), stop=(ji == njt - 1))
                yield
            rrow = Slot(ctx.AF, AF_SQ, 1024).v((0, 1), (si * 512, si * 512 + QCW))
            kb.recip(rrow, V(y_ps, (64, 65), (0, QCW)))
            yield
            rb_ps = pf[2 * si]
            kb.mm(V(rb_ps, (0, 64), (0, QCW)), V(ctx.ones, (0, 1), (0, 64)),
                  rrow, start=True, stop=True)
            yield
            kb.copy("s", rbsb.v((0, 64), (si * 256, si * 256 + QCW)),
                    V(rb_ps, (0, 64), (0, QCW)))
            yield
            kb.tt("v", ycm.v((ho, ho + 64), (hp * QCW, (hp + 1) * QCW)),
                  V(y_ps, (0, 64), (0, QCW)),
                  rbsb.v((0, 64), (si * 256, si * 256 + QCW)), ALU.mult)

        for h2 in range(0, 16, 2):
            g0 = head_gen(h2, 0)
            g1 = head_gen(h2 + 1, 1)
            gens = [g0, g1]
            while gens:
                nxt = []
                for g in gens:
                    try:
                        next(g)
                        nxt.append(g)
                    except StopIteration:
                        pass
                gens = nxt
        for ocg in range(2):
            for k in range(NCT):
                wb = stream_w(W["s_w_o"], "s_w_o", k * 128, ocg * 512, 512, (ocg * 8 + k) % 2, 1024)
                for j in range(4):
                    kb.mm(V(pf[j], (0, 128), (0, QCW)),
                          wbf.v((0, 128), (wb.e0 - AB_WSM + j * 128, wb.e0 - AB_WSM + (j + 1) * 128)),
                          ycm.v((0, 128), (k * QCW, (k + 1) * QCW)),
                          start=(k == 0), stop=(k == NCT - 1))
            for j in range(4):
                oc_ = ocg * 4 + j
                xv = V(x, (0, 128), (oc_ * T + qs, oc_ * T + qs + QCW))
                kb.tt("v", xv, V(pf[j], (0, 128), (0, QCW)), xv, ALU.add)



FP4_MIDS = (0.75, 1.25, 1.75, 2.5, 3.5, 5.0)


def emit_fp4_quant(kb, ctx, src, dst, W, bufA, bufB, bmrow, ladder_eng="v"):
    nb = W // 16
    bufA = TV(bufA.t, bufA.t[0:128, bufA.e0:bufA.e0 + W], 0, 128, bufA.e0, bufA.e0 + W)
    bufB = TV(bufB.t, bufB.t[0:128, bufB.e0:bufB.e0 + W], 0, 128, bufB.e0, bufB.e0 + W)

    def b3(view, blk=True):
        if blk:
            ap = bass.AP(view.t, view.e0, [[view.t.shape[1], 128], [16, nb], [1, 16]])
        else:
            ap = bass.AP(view.t, view.e0, [[view.t.shape[1], 128], [1, nb], [0, 16]])
        return TV(view.t, ap, view.p0, view.p1, view.e0, view.e1)

    bm = TV(bmrow.t, bmrow.t[0:128, bmrow.e0:bmrow.e0 + nb], 0, 128, bmrow.e0, bmrow.e0 + nb)
    sc = TV(bmrow.t, bmrow.t[0:128, bmrow.e0 + nb:bmrow.e0 + 2 * nb], 0, 128,
            bmrow.e0 + nb, bmrow.e0 + 2 * nb)
    inv = TV(bmrow.t, bmrow.t[0:128, bmrow.e0 + 2 * nb:bmrow.e0 + 3 * nb], 0, 128,
             bmrow.e0 + 2 * nb, bmrow.e0 + 3 * nb)
    invb = TV(inv.t, bass.AP(inv.t, inv.e0, [[inv.t.shape[1], 128], [1, nb], [0, 16]]),
              0, 128, inv.e0, inv.e1)
    scb = TV(sc.t, bass.AP(sc.t, sc.e0, [[sc.t.shape[1], 128], [1, nb], [0, 16]]),
             0, 128, sc.e0, sc.e1)

    kb.act(bufA, src, AF.Abs)
    kb.reduce("v", bm, b3(bufA), ALU.max)
    kb.ts("v", sc, bm, 1.0 / 6.0, 1e-12, op0=ALU.mult, op1=ALU.max)
    kb.recip(inv, sc)
    kb.tt("v", b3(bufA), b3(src), invb, ALU.mult)
    kb.act(bufB, bufA, AF.Abs)
    kb.ts(ladder_eng, bufA, bufB, 0.25, None, op0=ALU.is_gt)
    for m in FP4_MIDS:
        kb.stt(ladder_eng, bufA, bufB, float(m), bufA, ALU.is_gt, ALU.add)
    kb.ts("v", bufB, bufA, -4.0, 0.0, op0=ALU.add, op1=ALU.max)
    kb.tt("v", bufB, bufA, bufB, ALU.add)
    kb.ts("v", bufA, bufA, -6.0, 0.0, op0=ALU.add, op1=ALU.max)
    kb.stt("v", bufA, bufB, 0.5, bufA, ALU.mult, ALU.add)
    kb.ts("v", bufB, src, 0.0, -2.0, op0=ALU.is_lt, op1=ALU.mult)
    kb.ts("v", bufB, bufB, 1.0, None, op0=ALU.add)
    kb.tt("v", bufA, bufA, bufB, ALU.mult)
    kb.tt("v", b3(dst), b3(bufA), scb, ALU.mult)


FB_XQT = 0
FB_XQC = 4096
FB_HQ = 8192
FB_HQC = 8704
FB_WG = 10752
FB_WD = 18944
FB_WQS = 23040


def emit_l0_ffn(kb, ctx, W):
    x, pf, pb, AB, AFt = ctx.x, ctx.pf, ctx.pb, ctx.AB, ctx.AF
    NCH = [(0, 512), (512, 512), (1024, 512), (1536, 512), (2048, 512), (2560, 256)]

    accs = Slot(AFt, AF_ACC, 4096)
    wqs = Slot(AB, FB_WQS, 1024)

    def quant_w(hap, hname, nrows, ncols, dst_ap, dst_name, dst_cols):
        ccs = []
        c0_ = 0
        while c0_ < ncols:
            cw_ = min(512, ncols - c0_)
            ccs.append((c0_, cw_))
            c0_ += cw_
        for r in range(nrows // 128):
            for c0, cw in ccs:
                piece = accs.v((0, 128), (0, cw))
                kb.dma("q", piece, DV(hap[r * 128:(r + 1) * 128, c0:c0 + cw], hname,
                                      (0, 128, r * ncols + c0, r * ncols + c0 + cw)))
                bufA = accs.v((0, 128), (512, 1024))
                bufB = accs.v((0, 128), (1024, 1536))
                bmr = accs.v((0, 128), (1536, 1536 + 96))
                q = wqs.v((0, 128), (0, cw))
                emit_fp4_quant(kb, ctx, piece, q, cw, bufA, bufB, bmr)
                for j in range(cw // 128):
                    bk = pb[j % 2]
                    kb.trans(V(bk, (0, 128), (0, 128)),
                             wqs.v((0, 128), (j * 128, (j + 1) * 128)), V(ctx.idb))
                    st = wqs.v((0, 128), (512 + j * 128, 512 + (j + 1) * 128))
                    kb.copy("s", st, V(bk, (0, 128), (0, 128)))
                    kb.dma("p", DV(dst_ap[c0 + j * 128:c0 + (j + 1) * 128,
                                          r * 128:(r + 1) * 128], dst_name,
                                   (0, 128, (c0 + j * 128) * dst_cols + r * 128,
                                    (c0 + j * 128) * dst_cols + r * 128 + 128)),
                           st)

    quant_w(W["w_gate"], "w_gate", 2816, 1024, W["wqgT"], "wqgT", 2816)
    quant_w(W["w_up"], "w_up", 2816, 1024, W["wquT"], "wquT", 2816)
    quant_w(W["w_down"], "w_down", 1024, 2816, W["wqdT"], "wqdT", 1024)

    xqt = Slot(AB, FB_XQT, 4096)
    xqc = Slot(AB, FB_XQC, 4096)
    hq = Slot(AB, FB_HQ, 512)
    hqc = Slot(AB, FB_HQC, 2048)
    wg = Slot(AB, FB_WG, 8192)
    wd = Slot(AB, FB_WD, 4096)
    sqs = Slot(AFt, AF_SQ, 1024)
    wstg = Slot(AFt, AF_WSTG, 1024)
    lnt = Slot(AFt, AF_LNT, 512)

    for tc in range(4):
        cs = tc * 512
        for tl in range(4):
            for half in range(2):
                bank = pf[half]
                for cj in range(4):
                    ci = half * 4 + cj
                    kb.trans(V(bank, (0, 128), (cj * 128, (cj + 1) * 128)),
                             V(x, (0, 128), (ci * T + cs + tl * 128, ci * T + cs + (tl + 1) * 128)),
                             V(ctx.idf))
                base = tl * 1024 + half * 512
                kb.copy("s", accs.v((0, 128), (base, base + 512)), V(bank, (0, 128), (0, 512)))
        for tl in range(4):
            for half in range(2):
                base = tl * 1024 + half * 512
                emit_fp4_quant(kb, ctx, accs.v((0, 128), (base, base + 512)),
                               xqt.v((0, 128), (base, base + 512)), 512,
                               sqs.v((0, 128), (0, 512)), sqs.v((0, 128), (512, 1024)),
                               lnt.v((0, 128), (0, 96)))
        for tl in range(4):
            for half in range(2):
                bank = pb[half]
                for cj in range(4):
                    ci = half * 4 + cj
                    kb.trans(V(bank, (0, 128), (cj * 128, (cj + 1) * 128)),
                             xqt.v((0, 128), (tl * 1024 + ci * 128, tl * 1024 + (ci + 1) * 128)),
                             V(ctx.idb))
                base = (half * 4) * 512 + tl * 128
                ap = bass.AP(AB, FB_XQC + base, [[AB_END, 128], [512, 4], [1, 128]])
                kb.op("v", lambda e, o=ap, i=bank: e.tensor_copy(out=o, in_=i[:, 0:512]),
                      reads=(V(bank, (0, 128), (0, 512)),),
                      writes=(TV(AB, ap, 0, 128, FB_XQC + base, FB_XQC + base + 3 * 512 + 128),))
        for ci in range(NCT):
            kb.memset("v", accs.v((0, 128), (ci * 512, (ci + 1) * 512)), 0.0)
        for n0, nw in NCH:
            nk = nw // 128
            for k in range(NCT):
                kb.dma("q", wg.v((0, 128), (k * 512, k * 512 + nw)),
                       DV(W["wqgT"][k * 128:(k + 1) * 128, n0:n0 + nw], "wqgT",
                          (0, 128, k * 2816 + n0, k * 2816 + n0 + nw)))
                kb.dma("p", wg.v((0, 128), (4096 + k * 512, 4096 + k * 512 + nw)),
                       DV(W["wquT"][k * 128:(k + 1) * 128, n0:n0 + nw], "wquT",
                          (0, 128, 2 ** 23 + k * 2816 + n0, 2 ** 23 + k * 2816 + n0 + nw)))
            for kk in range(nk):
                kb.dma("q", wd.v((0, 128), (kk * 1024, (kk + 1) * 1024)),
                       DV(W["wqdT"][n0 + kk * 128:n0 + (kk + 1) * 128, 0:1024], "wqdT",
                          (0, 128, (n0 + kk * 128) * 1024, (n0 + kk * 128) * 1024 + 1024)))
            for tl in range(4):
                ps_g, ps_u = pf[2], pf[3]
                for k in range(NCT):
                    kb.mm(V(ps_g, (0, 128), (0, nw)),
                          xqc.v((0, 128), (k * 512 + tl * 128, k * 512 + (tl + 1) * 128)),
                          wg.v((0, 128), (k * 512, k * 512 + nw)),
                          start=(k == 0), stop=(k == NCT - 1))
                for k in range(NCT):
                    kb.mm(V(ps_u, (0, 128), (0, nw)),
                          xqc.v((0, 128), (k * 512 + tl * 128, k * 512 + (tl + 1) * 128)),
                          wg.v((0, 128), (4096 + k * 512, 4096 + k * 512 + nw)),
                          start=(k == 0), stop=(k == NCT - 1))
                sg = wstg.v((0, 128), (512, 512 + nw))
                kb.act(sg, V(ps_g, (0, 128), (0, nw)), AF.Silu)
                hh = wstg.v((0, 128), (0, nw))
                kb.tt("v", hh, sg, V(ps_u, (0, 128), (0, nw)), ALU.mult)
                emit_fp4_quant(kb, ctx, hh, hq.v((0, 128), (0, nw)), nw,
                               sqs.v((0, 128), (0, 512)), sqs.v((0, 128), (512, 1024)),
                               lnt.v((0, 128), (0, 96)), ladder_eng="v")
                for kk in range(nk):
                    bk = pb[kk % 2]
                    kb.trans(V(bk, (0, 128), (0, 128)),
                             hq.v((0, 128), (kk * 128, (kk + 1) * 128)), V(ctx.idb))
                    kb.copy("s", hqc.v((0, 128), (kk * 512 + tl * 128, kk * 512 + (tl + 1) * 128)),
                            V(bk, (0, 128), (0, 128)))
            for oc in range(NCT):
                ps_d = pf[4 + (oc % 2)]
                for kk in range(nk):
                    kb.mm(V(ps_d, (0, 128), (0, 512)),
                          wd.v((0, 128), (kk * 1024 + oc * 128, kk * 1024 + (oc + 1) * 128)),
                          hqc.v((0, 128), (kk * 512, (kk + 1) * 512)),
                          start=(kk == 0), stop=(kk == nk - 1))
                av = accs.v((0, 128), (oc * 512, (oc + 1) * 512))
                kb.tt("v", av, V(ps_d, (0, 128), (0, 512)), av, ALU.add)
        for ci in range(NCT):
            xv = V(x, (0, 128), (ci * T + cs, ci * T + cs + 512))
            kb.tt("v", xv, accs.v((0, 128), (ci * 512, (ci + 1) * 512)), xv, ALU.add)



LA_WT = 0
LA_XBC = 24576
LA_WST = 28672


def emit_l1_attn(kb, ctx, W):
    x, pf, pb, AB, AFt = ctx.x, ctx.pf, ctx.pb, ctx.AB, ctx.AF
    wts = Slot(AB, LA_WT, 24576)
    xbc = Slot(AB, LA_XBC, 4096)
    wst = Slot(AB, LA_WST, 1024)
    accs = Slot(AFt, AF_ACC, 4096)
    wstg = Slot(AFt, AF_WSTG, 1024)
    sqs = Slot(AFt, AF_SQ, 1024)
    biases = kb.sbuf("l1a_bias", [128, 24], F32)
    carry = kb.sbuf("l1a_carry", [128, 8], F32)

    for wi, bn in enumerate(("b_f", "b_i", "b_h")):
        kb.dma_slow("q", V(biases, (0, 128), (wi * 8, wi * 8 + 8)),
                    DV(W[bn].rearrange("(c p) -> p c", p=128), "h_" + bn))
    kb.memset("v", V(carry), 0.0)

    for wi, wn in enumerate(("w_f", "w_i", "w_h")):
        for r in range(8):
            stg = wstg.v((0, 128), (0, 1024))
            kb.dma("q", stg, DV(W[wn][r * 128:(r + 1) * 128, :], wn,
                                (0, 128, r * 1024, (r + 1) * 1024)))
            wb = wst.v((0, 128), (0, 1024))
            kb.copy("p", wb, stg)
            for k in range(8):
                bk = pb[k % 2]
                kb.trans(V(bk, (0, 128), (0, 128)),
                         wst.v((0, 128), (k * 128, (k + 1) * 128)), V(ctx.idb))
                kb.copy("s", wts.v((0, 128), (wi * 8192 + k * 1024 + r * 128,
                                              wi * 8192 + k * 1024 + (r + 1) * 128)),
                        V(bk, (0, 128), (0, 128)))

    F_, I_, H_ = 0, 512, 1024
    S1, S2, S3 = 1536, 2048, 2560
    for tc in range(4):
        cs = tc * 512
        for ci in range(NCT):
            kb.copy("v", xbc.v((0, 128), (ci * 512, (ci + 1) * 512)),
                    V(x, (0, 128), (ci * T + cs, ci * T + cs + 512)))
        for m in range(NCT):
            for wi in range(3):
                ps = pf[2 + wi]
                for k in range(NCT):
                    kb.mm(V(ps), wts.v((0, 128), (wi * 8192 + k * 1024 + m * 128,
                                                  wi * 8192 + k * 1024 + (m + 1) * 128)),
                          xbc.v((0, 128), (k * 512, (k + 1) * 512)),
                          start=(k == 0), stop=(k == NCT - 1))
            FB = accs.v((0, 128), (F_, F_ + 512))
            IB = accs.v((0, 128), (I_, I_ + 512))
            HB = accs.v((0, 128), (H_, H_ + 512))
            kb.act(FB, V(pf[2]), AF.Identity, bias=V(biases, (0, 128), (m, m + 1)))
            kb.act(IB, V(pf[3]), AF.Identity, bias=V(biases, (0, 128), (8 + m, 8 + m + 1)))
            kb.act(HB, V(pf[4]), AF.Identity, bias=V(biases, (0, 128), (16 + m, 16 + m + 1)))
            A = accs.v((0, 128), (S1, S1 + 512))
            B = accs.v((0, 128), (S2, S2 + 512))
            C = accs.v((0, 128), (S3, S3 + 512))
            kb.act(A, FB, AF.Abs)
            kb.act(A, A, AF.Exp, scale=-1.0)
            kb.act(A, A, AF.Ln, bias=1.0)
            kb.ts("v", B, FB, -1.0, 0.0, op0=ALU.mult, op1=ALU.max)
            kb.tt("v", A, A, B, ALU.add)
            kb.act(B, IB, AF.Abs)
            kb.act(B, B, AF.Exp, scale=-1.0)
            kb.act(B, B, AF.Ln, bias=1.0)
            kb.ts("v", C, IB, -1.0, 0.0, op0=ALU.mult, op1=ALU.max)
            kb.tt("v", B, B, C, ALU.add)
            kb.tt("v", A, A, B, ALU.subtract)
            kb.act(B, A, AF.Sigmoid, scale=-1.0)
            kb.act(A, A, AF.Sigmoid)
            kb.act(C, HB, AF.Sigmoid)
            kb.ts("v", FB, HB, 0.5, None, op0=ALU.add)
            kb.tt("v", FB, FB, C, ALU.subtract)
            kb.ts("v", IB, HB, 0.0, None, op0=ALU.is_ge)
            kb.tt("v", FB, FB, IB, ALU.mult)
            kb.tt("v", C, C, FB, ALU.add)
            kb.tt("v", A, A, C, ALU.mult)
            kb.scan(HB, B, A, V(carry, (0, 128), (m, m + 1)), op0=ALU.mult, op1=ALU.add)
            kb.copy("v", V(carry, (0, 128), (m, m + 1)),
                    TV(accs.th, accs.th[0:128, accs.off + H_ + 511:accs.off + H_ + 512],
                       0, 128, accs.off + H_ + 511, accs.off + H_ + 512))
            xv = V(x, (0, 128), (m * T + cs, m * T + cs + 512))
            kb.tt("v", xv, HB, xv, ALU.add)



LF_QCM = 0
LF_KCM = 8192
LF_VTM = 16384
LF_OCM = 24576
LF_HB1 = 24576
LF_XBC = 35840
LF_WS1 = 39936
LF_PA = 41984
LF_PB = 42496
LF_PT = 42624
LF_WS2 = 43264
LF_PA2 = 45312



def emit_l1_ffn(kb, ctx, W):
    x, pf, pb, AB, AFt = ctx.x, ctx.pf, ctx.pb, ctx.AB, ctx.AF
    qcm = Slot(AB, LF_QCM, 8192)
    kcm = Slot(AB, LF_KCM, 8192)
    vtm = Slot(AB, LF_VTM, 8192)
    ocm = Slot(AB, LF_OCM, 8192)
    hb1 = Slot(AB, LF_HB1, 11264)
    xbc = Slot(AB, LF_XBC, 4096)
    ws1 = Slot(AB, LF_WS1, 2048)
    ws2 = Slot(AB, LF_WS2, 2048)
    pA = Slot(AB, LF_PA, 512)
    pB = Slot(AB, LF_PB, 128)
    pts = Slot(AB, LF_PT, 640)
    accs = Slot(AFt, AF_ACC, 4096)
    sqs = Slot(AFt, AF_SQ, 1024)
    wstg = Slot(AFt, AF_WSTG, 1024)
    lnt = Slot(AFt, AF_LNT, 512)
    qks = kb.sbuf("lf_qks", [128, 4], F32)
    lng = kb.sbuf("lf_lng", [128, 16], F32)
    lnb = kb.sbuf("lf_lnb", [128, 16], F32)
    b1a = kb.sbuf("lf_b1a", [128, 44], F32)
    b1g = kb.sbuf("lf_b1g", [128, 44], F32)
    b2c = kb.sbuf("lf_b2", [128, 16], F32)
    tabs = Slot(AFt, AF_ACC + 2048, 2048)

    for c4 in range(4):
        kb.dma("q", DV(W["xres"][0:128, c4 * 4096:(c4 + 1) * 4096], "xres",
                       (0, 128, c4 * 4096, (c4 + 1) * 4096)),
               V(x, (0, 128), (c4 * 4096, (c4 + 1) * 4096)))
    kb.dma("q", V(qks), DV(W["qk_scale_l"], "h_qks"))
    kb.dma("q", V(lng), DV(W["lng_l"], "h_lng"))
    kb.dma("q", V(lnb), DV(W["lnb_l"], "h_lnb"))
    kb.dma("q", V(b1a), DV(W["b1a_l"], "h_b1a"))
    kb.dma("q", V(b1g), DV(W["b1g_l"], "h_b1g"))
    kb.dma("q", V(b2c), DV(W["b2_l"], "h_b2"))

    def shift_chunk(cs, dst_slot):
        for ci in range(NCT):
            dv = dst_slot.v((0, 128), (ci * 512, (ci + 1) * 512))
            if ci < 4:
                kb.copy("v", dv, V(x, (0, 128), (ci * T + cs, ci * T + cs + 512)))
            elif cs > 0:
                kb.copy("v", dv, V(x, (0, 128), (ci * T + cs - 1, ci * T + cs + 511)))
            else:
                kb.memset("v", dst_slot.v((0, 128), (ci * 512, ci * 512 + 1)), 0.0)
                kb.copy("v", dst_slot.v((0, 128), (ci * 512 + 1, (ci + 1) * 512)),
                        V(x, (0, 128), (ci * T, ci * T + 511)))

    for l in range(2):
        wqkv, wo, w1, w2 = W["wqkv"], W["wo"], W["w1"], W["w2"]
        for tc in range(4):
            cs = tc * 512
            shift_chunk(cs, xbc)
            for tb in range(4):
                kb.dma("q", tabs.v((0, 128), (tb * 512, (tb + 1) * 512)),
                       DV(W[("xp_csq", "xp_ssq", "xp_csk", "xp_ssk")[tb]][0:128, cs:cs + 512],
                          "h_xt%d" % tb, (0, 128, cs, cs + 512)))
            for kind in range(2):
                for k in range(NCT):
                    wb = ws1.v((0, 128), ((k % 4) * 512, (k % 4) * 512 + 512))
                    kb.dma("q" if k % 2 == 0 else "s", wb,
                           DV(wqkv[l * 1024 + k * 128:l * 1024 + (k + 1) * 128,
                                   kind * 512:(kind + 1) * 512], "s_wqkv",
                              (l, l + 1, k * 1536 + kind * 512,
                               k * 1536 + (kind + 1) * 512)))
                    for j in range(4):
                        kb.mm(V(pf[j]), ws1.v((0, 128), ((k % 4) * 512 + j * 128,
                                                         (k % 4) * 512 + (j + 1) * 128)),
                              xbc.v((0, 128), (k * 512, (k + 1) * 512)),
                              start=(k == 0), stop=(k == NCT - 1))
                dst = qcm if kind == 0 else kcm
                co, so = (0, 0) if kind == 0 else (2 * 512, 2)
                for j in range(4):
                    ps = pf[j]
                    qf = accs.v((0, 128), (0, 512))
                    kb.act(qf, V(ps), AF.Copy)
                    sq = sqs.v((0, 128), (0, 512))
                    kb.act(sq, V(ps), AF.Square)
                    n2 = pf[4]
                    kb.mm(V(n2, (0, 2), (0, 512)), V(ctx.e2), sq, start=True, stop=True)
                    nr = lnt.v((0, 2), (0, 512))
                    kb.act(nr, V(n2, (0, 2), (0, 512)), AF.Sqrt)
                    kb.ts("v", nr, nr, 1e-12, None, op0=ALU.max)
                    kb.recip(nr, nr)
                    rb = pf[5]
                    kb.mm(V(rb, (0, 128), (0, 512)), V(ctx.e2t, (0, 2), (0, 128)), nr,
                          start=True, stop=True)
                    qh = accs.v((0, 128), (512, 1024))
                    kb.tt("v", qh, qf, V(rb, (0, 128), (0, 512)), ALU.mult)
                    kb.ts("v", qh, qh, V(qks, (0, 128), (l * 2 + so // 2, l * 2 + so // 2 + 1)),
                          None, op0=ALU.mult)
                    qrot = accs.v((0, 128), (1024, 1536))
                    for (pd, psr) in ((0, 32), (32, 0), (64, 96), (96, 64)):
                        kb.copy("v", TV(AFt, AFt[pd:pd + 32, accs.off + 1024:accs.off + 1536],
                                        pd, pd + 32, accs.off + 1024, accs.off + 1536),
                                TV(AFt, AFt[psr:psr + 32, accs.off + 512:accs.off + 1024],
                                   psr, psr + 32, accs.off + 512, accs.off + 1024))
                    kb.tt("v", qh, qh, tabs.v((0, 128), (co, co + 512)), ALU.mult)
                    kb.tt("v", qrot, qrot, tabs.v((0, 128), (co + 512, co + 1024)), ALU.mult)
                    kb.tt("v", dst.v((0, 128), (j * 2048 + cs, j * 2048 + cs + 512)),
                          qh, qrot, ALU.add)
            for half in range(2):
                for kk in range(4):
                    k = half * 4 + kk
                    kb.dma("q", ws2.v((0, 128), (kk * 512, (kk + 1) * 512)),
                           DV(wqkv[l * 1024 + k * 128:l * 1024 + (k + 1) * 128, 1024:1536],
                              "s_wqkv", (l, l + 1, k * 1536 + 1024, k * 1536 + 1536)))
                for tl in range(4):
                    ps = pf[2 + tl]
                    for kk in range(4):
                        k = half * 4 + kk
                        kb.mm(V(ps), xbc.v((0, 128), (k * 512 + tl * 128, k * 512 + (tl + 1) * 128)),
                              ws2.v((0, 128), (kk * 512, (kk + 1) * 512)),
                              start=(kk == 0 and half == 0), stop=(kk == 3 and half == 1))
                    if half == 1:
                        kb.copy("s", vtm.v((0, 128), ((tc * 4 + tl) * 512, (tc * 4 + tl + 1) * 512)),
                                V(ps))
        def att_gen(h, qtc, si):
            hp, ho = h // 2, (h % 2) * 64
            jlo = max(0, qtc - 4)
            noff = qtc - jlo
            bA, bB = pf[2 * si], pf[2 * si + 1]
            pA_ = Slot(AB, LF_PA + si * (LF_PA2 - LF_PA), 512)
            pB_ = Slot(AB, LF_PB + si * (LF_PA2 - LF_PA), 128)
            pts_ = Slot(AB, LF_PT + si * (LF_PA2 - LF_PA), 640)
            lo = 8 * si
            if noff > 0:
                kb.mm(V(bA, (0, 128), (0, noff * 128)),
                      qcm.v((ho, ho + 64), (hp * 2048 + qtc * 128, hp * 2048 + (qtc + 1) * 128)),
                      kcm.v((ho, ho + 64), (hp * 2048 + jlo * 128, hp * 2048 + qtc * 128)),
                      start=True, stop=True, tile_position=(ho, 0))
            kb.mm(V(bB, (0, 128), (0, 128)),
                  qcm.v((ho, ho + 64), (hp * 2048 + qtc * 128, hp * 2048 + (qtc + 1) * 128)),
                  kcm.v((ho, ho + 64), (hp * 2048 + qtc * 128, hp * 2048 + (qtc + 1) * 128)),
                  start=True, stop=True, tile_position=(ho, 0))
            yield
            if noff == 4:
                kb.tt("v", V(bA, (0, 128), (0, 128)), V(bA, (0, 128), (0, 128)),
                      V(ctx.tri), ALU.add)
            kb.tt("v", V(bB, (0, 128), (0, 128)), V(bB, (0, 128), (0, 128)),
                  V(ctx.tri2), ALU.add)
            yield
            mx = lnt.v((0, 128), (lo, lo + 1))
            m2x = lnt.v((0, 128), (lo + 1, lo + 2))
            if noff > 0:
                kb.reduce("v", mx, V(bA, (0, 128), (0, noff * 128)), ALU.max)
                kb.reduce("v", m2x, V(bB, (0, 128), (0, 128)), ALU.max)
                kb.tt("v", mx, mx, m2x, ALU.max)
            else:
                kb.reduce("v", mx, V(bB, (0, 128), (0, 128)), ALU.max)
            negm = lnt.v((0, 128), (lo + 2, lo + 3))
            kb.ts("v", negm, mx, -1.0, None, op0=ALU.mult)
            yield
            sA = lnt.v((0, 128), (lo + 3, lo + 4))
            sB = lnt.v((0, 128), (lo + 4, lo + 5))
            if noff > 0:
                kb.act(pA_.v((0, 128), (0, noff * 128)), V(bA, (0, 128), (0, noff * 128)),
                       AF.Exp, bias=negm, accum=sA)
            kb.act(pB_.v((0, 128), (0, 128)), V(bB, (0, 128), (0, 128)),
                   AF.Exp, bias=negm, accum=sB)
            yield
            den = lnt.v((0, 128), (lo + 5, lo + 6))
            if noff > 0:
                kb.tt("v", den, sA, sB, ALU.add)
            else:
                kb.copy("v", den, sB)
            kb.recip(den, den)
            if noff > 0:
                kb.ts("v", pA_.v((0, 128), (0, noff * 128)),
                      pA_.v((0, 128), (0, noff * 128)), den, None, op0=ALU.mult)
            kb.ts("v", pB_.v((0, 128), (0, 128)), pB_.v((0, 128), (0, 128)),
                  den, None, op0=ALU.mult)
            yield
            for jj in range(noff + 1):
                bk = pb[si]
                srcv = (pA_.v((0, 128), (jj * 128, (jj + 1) * 128)) if jj < noff
                        else pB_.v((0, 128), (0, 128)))
                kb.trans(V(bk, (0, 128), (si * 512, si * 512 + 128)), srcv, V(ctx.idb))
                kb.copy("s", pts_.v((0, 128), (jj * 128, (jj + 1) * 128)),
                        V(bk, (0, 128), (si * 512, si * 512 + 128)))
                yield
            yb = pf[4 + si]
            for jj in range(noff + 1):
                j = jlo + jj
                kb.mm(V(yb, (0, 64), (0, 128)),
                      vtm.v((0, 128), (j * 512 + h * 64, j * 512 + (h + 1) * 64)),
                      pts_.v((0, 128), (jj * 128, (jj + 1) * 128)),
                      start=(jj == 0), stop=(jj == noff))
            yield
            kb.copy("s", ocm.v((ho, ho + 64), (hp * 2048 + qtc * 128,
                                               hp * 2048 + (qtc + 1) * 128)),
                    V(yb, (0, 64), (0, 128)))

        iters = [(h, qtc) for h in range(8) for qtc in range(16)]
        for i0 in range(0, len(iters), 2):
            gens = [att_gen(iters[i0][0], iters[i0][1], 0)]
            if i0 + 1 < len(iters):
                gens.append(att_gen(iters[i0 + 1][0], iters[i0 + 1][1], 1))
            while gens:
                nxt = []
                for g in gens:
                    try:
                        next(g)
                        nxt.append(g)
                    except StopIteration:
                        pass
                gens = nxt
        for tc in range(4):
            cs = tc * 512
            for mg in range(2):
                for k in range(4):
                    wb = ws1.v((0, 128), ((k % 4) * 512, (k % 4) * 512 + 512))
                    kb.dma("q" if k % 2 == 0 else "s", wb,
                           DV(wo[l * 512 + k * 128:l * 512 + (k + 1) * 128,
                                 mg * 512:(mg + 1) * 512],
                              "s_wo", (l, l + 1, k * 1024 + mg * 512,
                                       k * 1024 + (mg + 1) * 512)))
                    for j in range(4):
                        kb.mm(V(pf[j]), ws1.v((0, 128), ((k % 4) * 512 + j * 128,
                                                         (k % 4) * 512 + (j + 1) * 128)),
                              ocm.v((0, 128), (k * 2048 + cs, k * 2048 + cs + 512)),
                              start=(k == 0), stop=(k == 3))
                for j in range(4):
                    oc_ = mg * 4 + j
                    xv = V(x, (0, 128), (oc_ * T + cs, oc_ * T + cs + 512))
                    kb.tt("v", xv, V(pf[j]), xv, ALU.add)
        AW = [(mi * 512, min(512, 2730 - mi * 512)) for mi in range(6)]
        for tc in range(4):
            cs = tc * 512
            ps_a, ps_b = pf[0], pf[1]
            first = True
            for ci in range(NCT):
                xv = (V(x, (0, 128), (ci * T + cs, ci * T + cs + 512)) if ci < 4 else
                      (V(x, (0, 128), (ci * T + cs - 1, ci * T + cs + 511)) if cs > 0 else
                       V(x, (0, 128), (ci * T, ci * T + 511))))
                ov = (V(ps_a, (0, 1), (0, 512)) if (ci < 4 or cs > 0) else
                      V(ps_a, (0, 1), (1, 512)))
                kb.mm(ov, V(ctx.ones_col), xv, start=first, stop=(ci == NCT - 1))
                first = False
            first = True
            for ci in range(NCT):
                xv = (V(x, (0, 128), (ci * T + cs, ci * T + cs + 512)) if ci < 4 else
                      (V(x, (0, 128), (ci * T + cs - 1, ci * T + cs + 511)) if cs > 0 else
                       V(x, (0, 128), (ci * T, ci * T + 511))))
                w_ = 512 if (ci < 4 or cs > 0) else 511
                sq = sqs.v((0, 128), (0, w_))
                kb.act(sq, xv, AF.Square)
                ov = (V(ps_b, (0, 1), (0, 512)) if w_ == 512 else V(ps_b, (0, 1), (1, 512)))
                kb.mm(ov, V(ctx.ones_col), sq, start=first, stop=(ci == NCT - 1))
                first = False
            if cs == 0:
                pass
            A = sqs.v((0, 1), (0, 512))
            B = sqs.v((0, 1), (512, 1024))
            kb.ts("v", A, V(ps_a, (0, 1), (0, 512)), 1.0 / D, op0=ALU.mult)
            kb.tt("v", B, A, A, ALU.mult)
            kb.stt("v", B, V(ps_b, (0, 1), (0, 512)), 1.0 / D, B, ALU.mult, ALU.subtract)
            kb.act(B, B, AF.Sqrt, bias=1e-5)
            kb.recip(B, B)
            kb.tt("v", A, A, B, ALU.mult)
            kb.mm(V(pf[2], (0, 128), (0, 512)), V(ctx.ones, (0, 1), (0, 128)), B,
                  start=True, stop=True)
            kb.mm(V(pf[3], (0, 128), (0, 512)), V(ctx.ones, (0, 1), (0, 128)), A,
                  start=True, stop=True)
            for ci in range(NCT):
                xv = (V(x, (0, 128), (ci * T + cs, ci * T + cs + 512)) if ci < 4 else
                      (V(x, (0, 128), (ci * T + cs - 1, ci * T + cs + 511)) if cs > 0 else
                       V(x, (0, 128), (ci * T, ci * T + 511))))
                w_ = 512 if (ci < 4 or cs > 0) else 511
                off = 512 - w_
                t1 = accs.v((0, 128), (0, 512))
                t1w = accs.v((0, 128), (off, 512))
                if off:
                    kb.memset("v", accs.v((0, 128), (0, 1)), 0.0)
                kb.tt("v", t1w, xv, V(pf[2], (0, 128), (off, 512)), ALU.mult)
                kb.tt("v", t1, t1, V(pf[3], (0, 128), (0, 512)), ALU.subtract)
                kb.ts("v", xbc.v((0, 128), (ci * 512, (ci + 1) * 512)), t1,
                      V(lng, (0, 128), (l * 8 + ci, l * 8 + ci + 1)),
                      V(lnb, (0, 128), (l * 8 + ci, l * 8 + ci + 1)),
                      op0=ALU.mult, op1=ALU.add)
            for gi, (g0, gw) in enumerate(AW):
                nmc = (gw + 127) // 128
                abuf = accs.v((0, 128), (0, 2048))
                for k in range(NCT):
                    wb = ws1.v((0, 128), ((k % 4) * 512, (k % 4) * 512 + gw))
                    kb.dma("q" if k % 2 == 0 else "s", wb,
                           DV(w1[l * 1024 + k * 128:l * 1024 + (k + 1) * 128,
                                 g0:g0 + gw], "s_w1",
                              (l, l + 1, k * 5460 + g0, k * 5460 + g0 + gw)))
                    for j in range(nmc):
                        mw = min(128, gw - j * 128)
                        kb.mm(V(pf[j], (0, mw), (0, 512)),
                              ws1.v((0, 128), ((k % 4) * 512 + j * 128, (k % 4) * 512 + j * 128 + mw)),
                              xbc.v((0, 128), (k * 512, (k + 1) * 512)),
                              start=(k == 0), stop=(k == NCT - 1))
                for j in range(nmc):
                    mw = min(128, gw - j * 128)
                    mi = (g0 + j * 128) // 128
                    kb.act(TV(AFt, AFt[0:mw, accs.off + j * 512:accs.off + (j + 1) * 512],
                              0, mw, accs.off + j * 512, accs.off + (j + 1) * 512),
                           V(pf[j], (0, mw), (0, 512)), AF.Identity,
                           bias=V(b1a, (0, mw), (l * 22 + mi, l * 22 + mi + 1)))
                for k in range(NCT):
                    wb = ws1.v((0, 128), ((k % 4) * 512, (k % 4) * 512 + gw))
                    kb.dma("q" if k % 2 == 0 else "s", wb,
                           DV(w1[l * 1024 + k * 128:l * 1024 + (k + 1) * 128,
                                 2730 + g0:2730 + g0 + gw], "s_w1",
                              (l, l + 1, k * 5460 + 2730 + g0,
                               k * 5460 + 2730 + g0 + gw)))
                    for j in range(nmc):
                        mw = min(128, gw - j * 128)
                        kb.mm(V(pf[j], (0, mw), (0, 512)),
                              ws1.v((0, 128), ((k % 4) * 512 + j * 128, (k % 4) * 512 + j * 128 + mw)),
                              xbc.v((0, 128), (k * 512, (k + 1) * 512)),
                              start=(k == 0), stop=(k == NCT - 1))
                for j in range(nmc):
                    mw = min(128, gw - j * 128)
                    mi = (g0 + j * 128) // 128
                    gb = sqs.v((0, mw), (0, 512))
                    kb.act(gb, V(pf[j], (0, mw), (0, 512)), AF.Gelu,
                           bias=V(b1g, (0, mw), (l * 22 + mi, l * 22 + mi + 1)))
                    kb.tt("v", hb1.v((0, mw), (mi * 512, (mi + 1) * 512)),
                          TV(AFt, AFt[0:mw, accs.off + j * 512:accs.off + (j + 1) * 512],
                             0, mw, accs.off + j * 512, accs.off + (j + 1) * 512),
                          gb, ALU.mult)
            for mg in range(2):
                for kk in range(22):
                    kw = min(128, 2730 - kk * 128)
                    kb.dma("q" if kk % 2 == 0 else "s",
                           ws1.v((0, kw), ((kk % 4) * 512, (kk % 4) * 512 + 512)),
                           DV(w2[l * 2730 + kk * 128:l * 2730 + kk * 128 + kw,
                                 mg * 512:(mg + 1) * 512],
                              "s_w2", (l, l + 1, kk * 1024 + mg * 512,
                                       kk * 1024 + (mg + 1) * 512)))
                    for j in range(4):
                        kb.mm(V(pf[j]), ws1.v((0, kw), ((kk % 4) * 512 + j * 128,
                                                        (kk % 4) * 512 + (j + 1) * 128)),
                              hb1.v((0, kw), (kk * 512, (kk + 1) * 512)),
                              start=(kk == 0), stop=(kk == 21))
                for j in range(4):
                    oc_ = mg * 4 + j
                    xv = V(x, (0, 128), (oc_ * T + cs, oc_ * T + cs + 512))
                    kb.stt("v", xv, V(pf[j]),
                           V(b2c, (0, 128), (l * 8 + oc_, l * 8 + oc_ + 1)), xv,
                           ALU.add, ALU.add)
    for c8 in range(8):
        xrs = accs.v((0, 128), (0, 2048))
        kb.dma("q", xrs, DV(W["xres"][0:128, c8 * 2048:(c8 + 1) * 2048], "xres",
                            (0, 128, c8 * 2048, (c8 + 1) * 2048)))
        xv = V(x, (0, 128), (c8 * 2048, (c8 + 1) * 2048))
        kb.tt("v", xv, xrs, xv, ALU.add)



_PROG_CACHE = {}


def _np_dt(a):
    return F32 if a.dtype == np.float32 else BF16


def _declare_consts(nc, consts):
    return {n: nc.dram_tensor(n, list(consts[n].shape), _np_dt(consts[n]),
                              kind="ExternalInput").ap() for n in CONST_NAMES}


def _build_block(block):
    consts = host_consts()
    nc = bass.Bass("TRN2", target_bir_lowering=False)
    x_h = nc.dram_tensor("x", [T, D], F32, kind="ExternalInput").ap()
    o_h = nc.dram_tensor("o", [T, D], F32, kind="ExternalOutput").ap()
    hb = _declare_consts(nc, consts)
    W = {}
    if block == "l0a":
        shapes = dict(w_dq=(1024, 4096), w_uq=(4096, 1024), w_dkv=(1024, 256),
                      w_uk=(256, 1024), w_uv=(256, 1024), w_qr=(4096, 512),
                      w_kr=(1024, 32), w_o=(1024, 1024))
        for n, s in shapes.items():
            W[n] = nc.dram_tensor(n, list(s), F32, kind="ExternalInput").ap()
        W["scale_q4"] = hb["l0_scale_q4"]
        W["scale_k"] = hb["l0_scale_k"]
        for sname, nrows, ncols in (("s_w_dq", 1024, 4096), ("s_w_uq", 4096, 1024),
                                    ("s_w_qr", 4096, 512), ("s_w_o", 1024, 1024)):
            W[sname] = nc.dram_tensor(sname, [nrows, ncols], BF16, kind="Internal").ap()
    elif block == "l0f":
        for n, s in dict(w_gate=(2816, 1024), w_up=(2816, 1024),
                         w_down=(1024, 2816)).items():
            W[n] = nc.dram_tensor(n, list(s), F32, kind="ExternalInput").ap()
        for n, s in dict(wqgT=(1024, 2816), wquT=(1024, 2816),
                         wqdT=(2816, 1024)).items():
            W[n] = nc.dram_tensor(n, list(s), BF16, kind="Internal").ap()
    elif block == "l1a":
        for n, s in dict(w_f=(1024, 1024), w_i=(1024, 1024), w_h=(1024, 1024)).items():
            W[n] = nc.dram_tensor(n, list(s), F32, kind="ExternalInput").ap()
        for n in ("b_f", "b_i", "b_h"):
            W[n] = nc.dram_tensor(n, [1024], F32, kind="ExternalInput").ap()
    elif block == "l1f":
        for n, s in dict(wqkv=(2, 1024, 1536), wo=(2, 512, 1024),
                         w1=(2, 1024, 5460), w2=(2, 2730, 1024)).items():
            W[n] = nc.dram_tensor(n, list(s), F32, kind="ExternalInput").ap()
        for n, s in dict(qk_scale_l=(128, 4), lng_l=(128, 16), lnb_l=(128, 16),
                         b1a_l=(128, 44), b1g_l=(128, 44), b2_l=(128, 16)).items():
            W[n] = nc.dram_tensor(n, list(s), F32, kind="ExternalInput").ap()
        for n in ("xp_csq", "xp_ssq", "xp_csk", "xp_ssk"):
            W[n] = hb[n]
        W["xres"] = nc.dram_tensor("xres", [128, NCT * T], F32, kind="Internal").ap()
        for sname, nrows, ncols in (("s_wqkv", 2048, 1536), ("s_wo", 1024, 1024),
                                    ("s_w1", 2048, 5460), ("s_w2", 5460, 1024)):
            W[sname] = nc.dram_tensor(sname, [nrows, ncols], BF16, kind="Internal").ap()
    kb = KB(nc)
    ctx = Ctx(kb)
    alloc_common(kb, ctx)
    emit_load_consts(kb, ctx, hb)
    if block == "l0a":
        _pp = [p for p in PREPASS_WEIGHTS if p[0].startswith("l0a")]
        srcs = {"l0a_w_dq": W["w_dq"], "l0a_w_uq": W["w_uq"], "l0a_w_qr": W["w_qr"],
                "l0a_w_o": W["w_o"]}
        import kernel as _K
        _saved = globals()["PREPASS_WEIGHTS"]
        globals()["PREPASS_WEIGHTS"] = _pp
        emit_prepass(kb, ctx, srcs, W)
        globals()["PREPASS_WEIGHTS"] = _saved
    elif block == "l1f":
        _pp = [p for p in PREPASS_WEIGHTS if p[0].startswith("l1f")]
        srcs = {"l1f_wqkv": W["wqkv"].rearrange("l a b -> (l a) b"),
                "l1f_wo": W["wo"].rearrange("l a b -> (l a) b"),
                "l1f_w1": W["w1"].rearrange("l a b -> (l a) b"),
                "l1f_w2": W["w2"].rearrange("l a b -> (l a) b")}
        _saved = globals()["PREPASS_WEIGHTS"]
        globals()["PREPASS_WEIGHTS"] = _pp
        emit_prepass(kb, ctx, srcs, W)
        globals()["PREPASS_WEIGHTS"] = _saved
        W["wqkv"] = W["s_wqkv"]
        W["wo"] = W["s_wo"]
        W["w1"] = W["s_w1"]
        W["w2"] = W["s_w2"]
    emit_load_x(kb, ctx, x_h)
    if block == "l0a":
        emit_l0_attn(kb, ctx, W)
    elif block == "l0f":
        emit_l0_ffn(kb, ctx, W)
    elif block == "l1a":
        emit_l1_attn(kb, ctx, W)
    elif block == "l1f":
        emit_l1_ffn(kb, ctx, W)
    emit_postln(kb, ctx)
    emit_store_x(kb, ctx, o_h)
    kb.finalize()
    return nc, consts


def _l1f_layouts(q_scale, k_scale, ln_g, ln_b, b1, b2):
    d = {}
    d["qk_scale_l"] = np.stack([np.tile(q_scale[0], 2), np.tile(k_scale[0], 2),
                                np.tile(q_scale[1], 2), np.tile(k_scale[1], 2)],
                               axis=1).astype(np.float32)

    def cols(vec, n):
        out = np.zeros((128, n), np.float32)
        for i in range(n):
            seg = vec[i * 128:(i + 1) * 128]
            out[:len(seg), i] = seg
        return out
    d["lng_l"] = np.concatenate([cols(ln_g[l], 8) for l in range(2)], 1)
    d["lnb_l"] = np.concatenate([cols(ln_b[l], 8) for l in range(2)], 1)
    d["b1a_l"] = np.concatenate([cols(b1[l][:2730], 22) for l in range(2)], 1)
    d["b1g_l"] = np.concatenate([cols(b1[l][2730:], 22) for l in range(2)], 1)
    d["b2_l"] = np.concatenate([cols(b2[l], 8) for l in range(2)], 1)
    return d


def kernel(**inputs):
    inp = {k: np.asarray(v) for k, v in inputs.items()}
    if FUSED:
        return kernel_fused(inp)
    f32 = np.float32
    blocks = [
        ("l0a", {"w_dq": inp["L0_attn_W_DQ"], "w_uq": inp["L0_attn_W_UQ"],
                 "w_dkv": inp["L0_attn_W_DKV"], "w_uk": inp["L0_attn_W_UK"],
                 "w_uv": inp["L0_attn_W_UV"], "w_qr": inp["L0_attn_W_QR"],
                 "w_kr": inp["L0_attn_W_KR"], "w_o": inp["L0_attn_W_O"]}),
        ("l0f", {"w_gate": inp["L0_ffn_w_gate"], "w_up": inp["L0_ffn_w_up"],
                 "w_down": inp["L0_ffn_w_down"]}),
        ("l1a", {"w_f": inp["L1_attn_W_f"], "b_f": inp["L1_attn_b_f"],
                 "w_i": inp["L1_attn_W_i"], "b_i": inp["L1_attn_b_i"],
                 "w_h": inp["L1_attn_W_h"], "b_h": inp["L1_attn_b_h"]}),
        ("l1f", {"wqkv": inp["L1_ffn_wqkv"], "wo": inp["L1_ffn_wo"],
                 "w1": inp["L1_ffn_w1"], "w2": inp["L1_ffn_w2"],
                 **_l1f_layouts(inp["L1_ffn_q_scale"], inp["L1_ffn_k_scale"],
                                inp["L1_ffn_ln_g"], inp["L1_ffn_ln_b"],
                                inp["L1_ffn_b1"], inp["L1_ffn_b2"])}),
    ]
    cur = np.ascontiguousarray(inp["L0_attn_x"].astype(f32))
    for bname, wmap in blocks:
        if bname not in _PROG_CACHE:
            _PROG_CACHE[bname] = _build_block(bname)
        nc, consts = _PROG_CACHE[bname]
        cmap = {n: consts[n] for n in CONST_NAMES}
        wmap2 = {k: np.ascontiguousarray(v.astype(f32) if v.dtype != np.float32 else v)
                 for k, v in wmap.items()}
        in_maps = [{"x": np.ascontiguousarray(cur[c]), **wmap2, **cmap}
                   for c in range(8)]
        res = run_bass_kernel_spmd(nc, in_maps, core_ids=list(range(8)))
        cur = np.stack([res.results[c]["o"] for c in range(8)])
    return cur.astype(np.float32)



def _build_full():
    consts = host_consts()
    nc = bass.Bass("TRN2", target_bir_lowering=False)
    x_h = nc.dram_tensor("x", [T, D], F32, kind="ExternalInput").ap()
    o_h = nc.dram_tensor("o", [T, D], F32, kind="ExternalOutput").ap()
    hb = _declare_consts(nc, consts)
    Wa, Wf, Wr, Wl = {}, {}, {}, {}
    for n, s in dict(w_dq=(1024, 4096), w_uq=(4096, 1024), w_dkv=(1024, 256),
                     w_uk=(256, 1024), w_uv=(256, 1024), w_qr=(4096, 512),
                     w_kr=(1024, 32), w_o=(1024, 1024)).items():
        Wa[n] = nc.dram_tensor("l0a_" + n, list(s), F32, kind="ExternalInput").ap()
    Wa["scale_q4"] = hb["l0_scale_q4"]
    Wa["scale_k"] = hb["l0_scale_k"]
    for n, s in dict(w_gate=(2816, 1024), w_up=(2816, 1024), w_down=(1024, 2816)).items():
        Wf[n] = nc.dram_tensor("l0f_" + n, list(s), F32, kind="ExternalInput").ap()
    for n, s in dict(wqgT=(1024, 2816), wquT=(1024, 2816), wqdT=(2816, 1024)).items():
        Wf[n] = nc.dram_tensor("l0f_" + n, list(s), BF16, kind="Internal").ap()
    for n in ("w_f", "w_i", "w_h"):
        Wr[n] = nc.dram_tensor("l1a_" + n, [1024, 1024], F32, kind="ExternalInput").ap()
    for n in ("b_f", "b_i", "b_h"):
        Wr[n] = nc.dram_tensor("l1a_" + n, [1024], F32, kind="ExternalInput").ap()
    for n, s in dict(wqkv=(2, 1024, 1536), wo=(2, 512, 1024), w1=(2, 1024, 5460),
                     w2=(2, 2730, 1024)).items():
        Wl[n] = nc.dram_tensor("l1f_" + n, list(s), F32, kind="ExternalInput").ap()
    for n, s in dict(qk_scale_l=(128, 4), lng_l=(128, 16), lnb_l=(128, 16),
                     b1a_l=(128, 44), b1g_l=(128, 44), b2_l=(128, 16)).items():
        Wl[n] = nc.dram_tensor("l1f_" + n, list(s), F32, kind="ExternalInput").ap()
    for n in ("xp_csq", "xp_ssq", "xp_csk", "xp_ssk"):
        Wl[n] = hb[n]
    Wl["xres"] = nc.dram_tensor("xres", [128, NCT * T], F32, kind="Internal").ap()
    scr = {}
    for iname, sname, nrows, ncols in PREPASS_WEIGHTS:
        scr[sname] = nc.dram_tensor(sname, [nrows, ncols], BF16, kind="Internal").ap()
    srcs = {"l0a_w_dq": Wa["w_dq"], "l0a_w_uq": Wa["w_uq"], "l0a_w_qr": Wa["w_qr"],
            "l0a_w_o": Wa["w_o"], "l1f_wqkv": Wl["wqkv"].rearrange("l a b -> (l a) b"),
            "l1f_wo": Wl["wo"].rearrange("l a b -> (l a) b"),
            "l1f_w1": Wl["w1"].rearrange("l a b -> (l a) b"),
            "l1f_w2": Wl["w2"].rearrange("l a b -> (l a) b")}
    for sname in ("s_w_dq", "s_w_uq", "s_w_qr", "s_w_o"):
        Wa[sname] = scr[sname]
    Wl["wqkv"] = scr["s_wqkv"]
    Wl["wo"] = scr["s_wo"]
    Wl["w1"] = scr["s_w1"]
    Wl["w2"] = scr["s_w2"]

    kb = KB(nc)
    ctx = Ctx(kb)
    alloc_common(kb, ctx)
    emit_load_consts(kb, ctx, hb)
    emit_prepass(kb, ctx, srcs, scr)
    emit_load_x(kb, ctx, x_h)
    emit_l0_attn(kb, ctx, Wa)
    emit_postln(kb, ctx)
    emit_l0_ffn(kb, ctx, Wf)
    emit_postln(kb, ctx)
    emit_l1_attn(kb, ctx, Wr)
    emit_postln(kb, ctx)
    emit_l1_ffn(kb, ctx, Wl)
    emit_postln(kb, ctx)
    emit_store_x(kb, ctx, o_h)
    kb.finalize()
    return nc, consts


FUSED = True


def kernel_fused(inp):
    if "full" not in _PROG_CACHE:
        _PROG_CACHE["full"] = _build_full()
    nc, consts = _PROG_CACHE["full"]
    cmap = {n: consts[n] for n in CONST_NAMES}
    wmap = {}
    for pre, names in (("L0_attn_", ("W_DQ", "W_UQ", "W_DKV", "W_UK", "W_UV",
                                     "W_QR", "W_KR", "W_O")),):
        for n in names:
            wmap["l0a_w_" + n[2:].lower()] = inp[pre + n]
    wmap["l0f_w_gate"] = inp["L0_ffn_w_gate"]
    wmap["l0f_w_up"] = inp["L0_ffn_w_up"]
    wmap["l0f_w_down"] = inp["L0_ffn_w_down"]
    for n in ("W_f", "W_i", "W_h", "b_f", "b_i", "b_h"):
        wmap["l1a_" + n.lower() if n[0] == "W" else "l1a_" + n] = inp["L1_attn_" + n]
    for n in ("wqkv", "wo", "w1", "w2"):
        wmap["l1f_" + n] = inp["L1_ffn_" + n]
    lay = _l1f_layouts(inp["L1_ffn_q_scale"], inp["L1_ffn_k_scale"],
                       inp["L1_ffn_ln_g"], inp["L1_ffn_ln_b"],
                       inp["L1_ffn_b1"], inp["L1_ffn_b2"])
    for k, v in lay.items():
        wmap["l1f_" + k] = v
    wmap = {k: np.ascontiguousarray(np.asarray(v).astype(np.float32)) for k, v in wmap.items()}
    x = np.ascontiguousarray(inp["L0_attn_x"].astype(np.float32))
    in_maps = [{"x": x[c], **wmap, **cmap} for c in range(8)]
    res = run_bass_kernel_spmd(nc, in_maps, core_ids=list(range(8)))
    return np.stack([res.results[c]["o"] for c in range(8)]).astype(np.float32)



PREPASS_WEIGHTS = [
    ("l0a_w_dq", "s_w_dq", 1024, 4096),
    ("l0a_w_uq", "s_w_uq", 4096, 1024),
    ("l0a_w_qr", "s_w_qr", 4096, 512),
    ("l0a_w_o", "s_w_o", 1024, 1024),
    ("l1f_wqkv", "s_wqkv", 2 * 1024, 1536),
    ("l1f_wo", "s_wo", 2 * 512, 1024),
    ("l1f_w1", "s_w1", 2 * 1024, 5460),
    ("l1f_w2", "s_w2", 2 * 2730, 1024),
]


def emit_prepass(kb, ctx, srcs, dsts):
    stage_b = Slot(ctx.AB, 0, 4096)
    bi = 0
    for iname, sname, nrows, ncols in PREPASS_WEIGHTS:
        src, dst = srcs[iname], dsts[sname]
        for r in range((nrows + 127) // 128):
            rh = min(128, nrows - r * 128)
            c0 = 0
            while c0 < ncols:
                cw = min(2048, ncols - c0)
                sf = Slot(ctx.AF, AF_ACC + (bi % 2) * 2048, 2048).v((0, rh), (0, cw))
                kb.dma("q" if bi % 2 == 0 else "s", sf,
                       DV(src[r * 128:r * 128 + rh, c0:c0 + cw], "pp_" + iname,
                          (0, rh, r * ncols + c0, r * ncols + c0 + cw)))
                sb = stage_b.v((0, rh), ((bi % 2) * 2048, (bi % 2) * 2048 + cw))
                kb.copy("s" if bi % 2 == 0 else "v", sb, sf)
                kb.dma("p", DV(dst[r * 128:r * 128 + rh, c0:c0 + cw], sname,
                               (0, rh, r * ncols + c0, r * ncols + c0 + cw)), sb)
                bi += 1
                c0 += cw
